# version 36
# baseline (speedup 1.0000x reference)
import numpy as np
import concourse.bass as bass
import concourse.mybir as mybir
from concourse.bass_utils import run_bass_kernel_spmd

F32 = mybir.dt.float32
BF16 = mybir.dt.bfloat16
AF = mybir.ActivationFunctionType
ALU = mybir.AluOpType
AX = mybir.AxisListType

NCORES = 8
CC_WORLD = 8
B, T, D, I = 16, 2048, 1024, 2816
BL = B // NCORES
R = BL * T
SEG = 512
NH = T // SEG
TPH = SEG // 128
DC = D // 128
IC = I // 128
GT = 512
NG = R // GT
GPS = NG // BL
MD = 6 * D


class Sched:
    def __init__(self):
        self.ops = []
        self.cnt = {}

    def op(self, eng, fn, waits=(), inc=None, d=1):
        tok = None
        if inc is not None:
            self.cnt[inc] = self.cnt.get(inc, 0) + d
            tok = (inc, self.cnt[inc])
        self.ops.append((eng, fn, [w for w in waits if w is not None], inc, d))
        return tok

    def emit(self, name, eng, sems, group_sems=()):
        last = {}
        for e, fn, waits, inc, d in self.ops:
            if e != name:
                continue
            need = {}
            for sn, val in waits:
                if sn in group_sems:
                    val = self.cnt[sn]
                if val > need.get(sn, -1):
                    need[sn] = val
            for sn, val in need.items():
                if last.get(sn, -1) >= val:
                    continue
                eng.wait_ge(sems[sn], val)
                last[sn] = val
            inst = fn(eng) if fn is not None else None
            if inc is not None:
                inst.then_inc(sems[inc], d)


class Pool:
    def __init__(self, n, seed=None):
        self.n = n
        self.wr = [seed] * n
        self.rd = [[] for _ in range(n)]
        self.i = 0

    def acquire(self):
        s = self.i % self.n
        self.i += 1
        w = list(self.rd[s]) + ([self.wr[s]] if self.wr[s] else [])
        self.rd[s] = []
        return s, w

    def wrote(self, s, tok):
        self.wr[s] = tok

    def read(self, s, tok):
        if tok is not None:
            self.rd[s].append(tok)


def build_nc():
    nc = bass.Bass("TRN2", target_bir_lowering=False, debug=False)
    P = Sched()

    hZ = nc.dram_tensor("Z", [R, D], F32, kind="ExternalInput")
    hdec = nc.dram_tensor("decay", [1, 1], F32, kind="ExternalInput")
    hW = nc.dram_tensor("W", [D, D], F32, kind="ExternalInput")
    hb = nc.dram_tensor("bias", [1, D], F32, kind="ExternalInput")
    hg = nc.dram_tensor("gamma", [1, D], F32, kind="ExternalInput")
    hbe = nc.dram_tensor("beta", [1, D], F32, kind="ExternalInput")
    hcx = nc.dram_tensor("clip_x", [1, 1], F32, kind="ExternalInput")
    hcd = nc.dram_tensor("clip_d", [1, 1], F32, kind="ExternalInput")
    hwg = nc.dram_tensor("wg", [I, D], F32, kind="ExternalInput")
    hwu = nc.dram_tensor("wu", [I, D], F32, kind="ExternalInput")
    hwd = nc.dram_tensor("wd", [D, I], F32, kind="ExternalInput")
    hc = nc.dram_tensor("cc_", [BL, D], F32, kind="ExternalInput")
    hwm = nc.dram_tensor("wmod", [D, MD], F32, kind="ExternalInput")
    hpw = nc.dram_tensor("pw", [1, 48], F32, kind="ExternalInput")
    hcv = nc.dram_tensor("cvals", [1, 48], F32, kind="ExternalInput")
    hey = nc.dram_tensor("eye", [128, 128], F32, kind="ExternalInput")
    hy = nc.dram_tensor("y", [R, D], F32, kind="ExternalOutput")

    pkW = nc.dram_tensor("pkW", [D, D], BF16, kind="Internal")
    pkg = nc.dram_tensor("pkg", [I, D], BF16, kind="Internal")
    pku = nc.dram_tensor("pku", [I, D], BF16, kind="Internal")
    pkd = nc.dram_tensor("pkd", [D, I], BF16, kind="Internal")
    spx2 = nc.dram_tensor("spx2", [R, D], F32, kind="Internal")
    sph2 = nc.dram_tensor("sph2", [R, D], BF16, kind="Internal")
    sphh = nc.dram_tensor("sphh", [NG * IC, 128, GT], BF16, kind="Internal")
    hsf = nc.dram_tensor("hsf", [16, 1024], F32, kind="Internal")
    hsb = nc.dram_tensor("hsb", [96, 128], BF16, kind="Internal")
    hsc = nc.dram_tensor("hsc", [12, 1024], BF16, kind="Internal")
    cc1i = nc.dram_tensor("cc1i", [1, 1], F32, kind="Internal")
    cc1o = nc.dram_tensor("cc1o", [1, 1], F32, kind="Internal", addr_space="Shared" if CC_WORLD > 4 else "Local")
    cc2i = nc.dram_tensor("cc2i", [1, 1], F32, kind="Internal")
    cc2o = nc.dram_tensor("cc2o", [1, 1], F32, kind="Internal", addr_space="Shared" if CC_WORLD > 4 else "Local")
    pkuT = nc.dram_tensor("pkuT", [128, 8192], BF16, kind="Internal")
    cc1p = nc.dram_tensor("cc1p", [1, 1], F32, kind="Internal", addr_space="Shared" if CC_WORLD > 4 else "Local")
    cc2p = nc.dram_tensor("cc2p", [1, 1], F32, kind="Internal", addr_space="Shared" if CC_WORLD > 4 else "Local")

    Z, Wm, bia, gam, bet = hZ.ap(), hW.ap(), hb.ap(), hg.ap(), hbe.ap()
    dec, cx, cd = hdec.ap(), hcx.ap(), hcd.ap()
    wg_, wu_, wd_, c_, wm_ = hwg.ap(), hwu.ap(), hwd.ap(), hc.ap(), hwm.ap()
    pw_, cv_, y_, ey_ = hpw.ap(), hcv.ap(), hy.ap(), hey.ap()

    from contextlib import ExitStack
    with ExitStack() as stack:
        en = stack.enter_context
        wa = en(nc.sbuf_tensor("wa", [128, 45056], BF16))
        fmb = en(nc.sbuf_tensor("fmb", [128, 8192], BF16))
        sfm = en(nc.sbuf_tensor("sfm", [128, 8192], BF16))
        x0w = en(nc.sbuf_tensor("x0w", [128, 4 * 1024], F32))
        scr = en(nc.sbuf_tensor("scr", [128, 3072], F32))
        h1sb = en(nc.sbuf_tensor("h1sb", [128, 4 * 1024], BF16))
        xhsb = en(nc.sbuf_tensor("xhsb", [128, 4 * 1024], BF16))
        ynb2 = en(nc.sbuf_tensor("ynb2", [128, 2 * 1024], BF16))
        repb = en(nc.sbuf_tensor("repb", [128, 10 * 1024], BF16))
        sep = en(nc.sbuf_tensor("sep", [128, 5248], BF16))
        smf = en(nc.sbuf_tensor("smf", [128, 512], F32))
        hgb = en(nc.sbuf_tensor("hgb", [128, 2048], BF16))
        hho = en(nc.sbuf_tensor("hho", [128, 1024], BF16))
        ebuf = en(nc.sbuf_tensor("ebuf", [128, 128], BF16))
        p0 = en(nc.psum_tensor("p0", [128, 512], F32))
        p1 = en(nc.psum_tensor("p1", [128, 512], F32))
        p2 = en(nc.psum_tensor("p2", [128, 512], F32))
        p3 = en(nc.psum_tensor("p3", [128, 512], F32))
        p4 = en(nc.psum_tensor("p4", [128, 512], F32))
        p5 = en(nc.psum_tensor("p5", [128, 512], F32))
        p6 = en(nc.psum_tensor("p6", [128, 512], F32))
        p7 = en(nc.psum_tensor("p7", [128, 512], F32))
        SEM_NAMES = ["pe", "ve", "se", "ge", "cc",
                     "sZ0", "sZ1", "sE0", "sE1", "sT0", "sT1",
                     "sS0", "sB0", "sS2", "sS3", "sWT", "sWG", "sWU", "sWD",
                     "sCT", "sMB", "sMO", "sRP",
                     "sH0", "sH1", "sL0", "sL1", "sX0", "sX1", "sQ0", "sQ1", "sTA0", "sTA1",
                     "aX20", "aX21", "aH20", "aH21", "aHH0", "aHH1", "aP0", "aP1", "aO0", "aO1", "sBE", "sGR",
                     "gWP", "gWM0", "gWM1", "gWM2", "gWM3", "gWF0", "gWF1", "gCB", "gRB", "gBI", "sEY", "sGR2", "aPT0", "aPT1"]
        sems = {n: en(nc.semaphore(n)) for n in SEM_NAMES}
        block = en(nc.Block())
        psum = [p0, p1, p2, p3, p4, p5, p6, p7]
        wt = wa[:, 36864:45056]

        C_CB, C_HI, C_LO, C_A, C_SX, C_SD = 0, 1, 2, 3, 4, 5
        C_IS1, C_S2B, C_IS2 = 6, 7, 8
        C_AM, C_AN, C_T0, C_T1, C_EPS = 9, 10, 11, 12, 13
        O_CA1, O_CB1, O_SB = 16, 32, 48
        O_SCG, O_SCU, O_SCD = 64, 86, 108
        O_SGX, O_SUX = 128, 150
        O_TAIL = 176
        O_MX = 184
        O_ST = 192
        O_EPS, O_DLT, O_CV = 336, 384, 432
        O_SCL = 480
        O_CT = 488

        def smv(c):
            return smf[:, c:c + 1]

        S_S1, S_CT, S_MODT, S_CTL = 0, 3072, 3088, 5232
        SWB = [2048, 3184]
        S_CSIL = 3184
        ga_tm = x0w[0:2, 1024:2048]
        sgn = x0w[0:2, 3072:4096]
        repr_ = x0w[0:2, 0:1024]
        gb_tm = sep[0:2, 4208:5232]
        c_sil = sep[0:2, S_CSIL:S_CSIL + 1024]
        modT = sep[:, S_MODT:S_MODT + 96]
        gam2 = scr[0:2, 2048:3072]

        teps = P.op("v", lambda v: v.memset(smf[:, 13:14], 1e-5), inc="ve")
        t_pw = P.op("q", lambda q: q.dma_start(out=smf[0:1, O_EPS:O_EPS + 48], in_=pw_[:, :]), inc="sS0", d=16)
        t_cv = P.op("q", lambda q: q.dma_start(out=smf[0:1, O_CV:O_CV + 48], in_=cv_[:, :]), inc="sS0", d=16)
        t_dc = P.op("q", lambda q: q.dma_start(out=smf[0:1, O_SCL:O_SCL + 1], in_=dec[:, :]), inc="sS0", d=16)
        t_cx = P.op("q", lambda q: q.dma_start(out=smf[0:1, O_SCL + 1:O_SCL + 2], in_=cx[:, :]), inc="sS0", d=16)
        t_cd = P.op("q", lambda q: q.dma_start(out=smf[0:1, O_SCL + 2:O_SCL + 3], in_=cd[:, :]), inc="sS0", d=16)
        t_g2 = P.op("q", lambda q: q.dma_start(out=gam2, in_=bass.AP(hg, 0, [[0, 2], [1, 1024]])), inc="sS0", d=16)
        t_cl = P.op("q", lambda q: q.dma_start(out=x0w[0:2, 0:1024], in_=c_[:, :]), inc="sS0", d=16)

        tg1 = P.op("v", lambda v: v.tensor_scalar(out=smf[0:1, O_EPS:O_EPS + 48], in0=smf[0:1, O_EPS:O_EPS + 48], scalar1=1.0, scalar2=1.0, op0=ALU.add, op1=ALU.is_gt), waits=[t_pw], inc="ve")
        tg2 = P.op("v", lambda v: v.scalar_tensor_tensor(out=smf[0:1, O_DLT:O_DLT + 47], in0=smf[0:1, O_EPS + 1:O_EPS + 48], scalar=-1.0, in1=smf[0:1, O_EPS:O_EPS + 47], op0=ALU.mult, op1=ALU.add), waits=[tg1], inc="ve")
        tg3 = P.op("v", lambda v: v.tensor_tensor(out=smf[0:1, O_DLT:O_DLT + 47], in0=smf[0:1, O_DLT:O_DLT + 47], in1=smf[0:1, O_CV:O_CV + 47], op=ALU.mult), waits=[tg2, t_cv], inc="ve")
        tC = P.op("v", lambda v: v.tensor_reduce(out=smf[0:1, O_CT:O_CT + 1], in_=smf[0:1, O_DLT:O_DLT + 47], axis=AX.X, op=ALU.add), waits=[tg3], inc="ve")
        tsg = P.op("s", lambda s: s.activation(out=smf[0:1, O_SCL + 4:O_SCL + 7], in_=smf[0:1, O_SCL:O_SCL + 3], func=AF.Sigmoid), waits=[t_dc, t_cx, t_cd], inc="se")
        tb1 = P.op("q", lambda q: q.dma_start(out=hsf.ap()[0:1, 0:1], in_=smf[0:1, O_CT:O_CT + 1]), waits=[tC], inc="sB0", d=16)
        tb2 = P.op("q", lambda q: q.dma_start(out=hsf.ap()[0:1, 1:4], in_=smf[0:1, O_SCL + 4:O_SCL + 7]), waits=[tsg], inc="sB0", d=16)
        tCb = P.op("q", lambda q: q.dma_start(out=smv(C_CB), in_=bass.AP(hsf, 0, [[0, 128], [1, 1]])), waits=[tb1], inc="sS2", d=16)
        tac = P.op("q", lambda q: q.dma_start(out=smv(C_A), in_=bass.AP(hsf, 1, [[0, 128], [1, 1]])), waits=[tb2], inc="sS2", d=16)
        tsx = P.op("q", lambda q: q.dma_start(out=smv(C_SX), in_=bass.AP(hsf, 2, [[0, 128], [1, 1]])), waits=[tb2], inc="sS2", d=16)
        tsd = P.op("q", lambda q: q.dma_start(out=smv(C_SD), in_=bass.AP(hsf, 3, [[0, 128], [1, 1]])), waits=[tb2], inc="sS2", d=16)
        thi = P.op("v", lambda v: v.tensor_scalar(out=smv(C_HI), in0=smv(C_CB), scalar1=127.0, scalar2=None, op0=ALU.add), waits=[tCb], inc="ve")
        tlo = P.op("v", lambda v: v.tensor_scalar(out=smv(C_LO), in0=smv(C_CB), scalar1=-128.0, scalar2=None, op0=ALU.add), waits=[tCb], inc="ve")

        tones = P.op("v", lambda v: v.memset(scr[:, 0:2048], 1.0), inc="ve")
        tone2 = P.op("v", lambda v: v.memset(sep[0:2, S_S1:S_S1 + 2048], 1.0), inc="ve")
        ts1w = P.op("v", lambda v: v.tensor_tensor_scan(out=fmb[:, 0:2048], data0=bass.AP(smf, C_A, [[512, 128], [0, 2048]]), data1=scr[:, 0:2048], initial=0.0, op0=ALU.mult, op1=ALU.add), waits=[tones, tac], inc="ve")
        ts1 = P.op("v", lambda v: v.tensor_copy(out=sep[0:1, S_S1:S_S1 + 2048], in_=fmb[0:1, 0:2048]), waits=[ts1w, tone2], inc="ve")

        tWp = P.op("g", lambda g: g.dma_start(out=pkW.ap()[:, :], in_=Wm[:, :]), inc="gWP", d=16)
        tWt = []
        for dcc in range(DC):
            tWt.append(P.op("q", lambda q, dcc=dcc: q.dma_start_transpose(wt[:, dcc * 1024:(dcc + 1) * 1024], pkW.ap()[:, dcc * 128:(dcc + 1) * 128]), waits=[tWp], inc="sWT", d=16))

        c32 = x0w[0:2, 1024:2048]
        tcs = P.op("s", lambda s: s.activation(out=c32, in_=x0w[0:2, 0:1024], func=AF.Silu), waits=[t_cl], inc="se")
        tch = P.op("v", lambda v: v.tensor_copy(out=c_sil, in_=c32), waits=[tcs], inc="ve")
        c_lo = x0w[0:2, 0:1024]
        tclo = P.op("v", lambda v: v.tensor_tensor(out=c_lo, in0=c32, in1=c_sil, op=ALU.subtract), waits=[tch], inc="ve")
        tcb = P.op("g", lambda g: g.dma_start(out=hsc.ap()[0:2, :], in_=c_sil), waits=[tch], inc="gCB", d=16)
        tcb2 = P.op("g", lambda g: g.dma_start(out=hsc.ap()[8:10, :], in_=c_lo), waits=[tclo], inc="gRB", d=16)
        tct = P.op("q", lambda q: q.dma_start_transpose(sep[:, S_CT:S_CT + 16], bass.AP(hsc, 0, [[128, 16], [1, 128]])), waits=[tcb], inc="sCT", d=16)
        tct2 = P.op("q", lambda q: q.dma_start_transpose(sep[:, S_CTL:S_CTL + 16], bass.AP(hsc, 8 * 1024, [[128, 16], [1, 128]])), waits=[tcb2], inc="sCT", d=16)

        mod_tm = wa[0:2, 26624:26624 + MD]
        tmodev = [None, None]
        wm_slot_tok = [None, None, None, None]
        wm32war = [None, None]
        GAGB_N = {4, 5, 10, 11}
        for n in range(12):
            pb = psum[n % 2]
            mmtok = None
            for dcc in range(DC):
                slot = (n * DC + dcc) % 4
                hi = wa[:, 8192 + 1024 * slot:8192 + 1024 * slot + 512]
                lo = wa[:, 8192 + 1024 * slot + 512:8192 + 1024 * (slot + 1)]
                wld = [wm_slot_tok[slot]] if wm_slot_tok[slot] else []
                if n in GAGB_N:
                    f32s = slot % 2
                    wm32 = x0w[:, 2048 + 512 * f32s:2048 + 512 * (f32s + 1)]
                    w32 = [wm32war[f32s]] if wm32war[f32s] else []
                    tld = P.op("g", lambda g, n=n, dcc=dcc, wm32=wm32: g.dma_start(out=wm32, in_=wm_[dcc * 128:(dcc + 1) * 128, n * 512:(n + 1) * 512]), waits=w32, inc="gWF0" if f32s == 0 else "gWF1", d=16)
                    thh_ = P.op("v", lambda v, wm32=wm32, hi=hi: v.tensor_copy(out=hi, in_=wm32), waits=[tld] + wld, inc="ve")
                    tll = P.op("v", lambda v, wm32=wm32, hi=hi, lo=lo: v.tensor_tensor(out=lo, in0=wm32, in1=hi, op=ALU.subtract), waits=[thh_], inc="ve")
                    wm32war[f32s] = tll
                    def mm1(t, dcc=dcc, hi=hi, pb=pb):
                        return t.matmul(pb[0:2, 0:512], bass.AP(sep, S_CT + dcc, [[5248, 128], [8, 2]]), hi, start=(dcc == 0), stop=False)
                    def mm2(t, dcc=dcc, hi=hi, pb=pb):
                        return t.matmul(pb[0:2, 0:512], bass.AP(sep, S_CTL + dcc, [[5248, 128], [8, 2]]), hi, start=False, stop=False)
                    def mm3(t, dcc=dcc, lo=lo, pb=pb):
                        return t.matmul(pb[0:2, 0:512], bass.AP(sep, S_CT + dcc, [[5248, 128], [8, 2]]), lo, start=False, stop=(dcc == DC - 1))
                    w = [thh_, tct, tct2]
                    if dcc == 0 and tmodev[n % 2] is not None:
                        w.append(tmodev[n % 2])
                    P.op("t", mm1, waits=w, inc="pe")
                    P.op("t", mm2, waits=[thh_, tct2], inc="pe")
                    mmtok = P.op("t", mm3, waits=[tll], inc="pe")
                    wm_slot_tok[slot] = mmtok
                else:
                    tld = P.op("g", lambda g, n=n, dcc=dcc, hi=hi: g.dma_start(out=hi, in_=wm_[dcc * 128:(dcc + 1) * 128, n * 512:(n + 1) * 512]), waits=wld, inc="gWM%d" % slot, d=16)
                    def mmp(t, dcc=dcc, hi=hi, pb=pb):
                        return t.matmul(pb[0:2, 0:512], bass.AP(sep, S_CT + dcc, [[5248, 128], [8, 2]]), hi, start=(dcc == 0), stop=(dcc == DC - 1))
                    w = [tld, tct, tct2]
                    if dcc == 0 and tmodev[n % 2] is not None:
                        w.append(tmodev[n % 2])
                    mmtok = P.op("t", mmp, waits=w, inc="pe")
                    wm_slot_tok[slot] = mmtok
            tmodev[n % 2] = P.op("s", lambda s, n=n, pb=pb: s.activation(out=mod_tm[:, n * 512:(n + 1) * 512], in_=pb[0:2, 0:512], func=AF.Identity), waits=[mmtok], inc="se")
        tmodA, tmodB = tmodev

        tmb = []
        for s_ in range(2):
            tmb.append(P.op("q", lambda q, s_=s_: q.dma_start(out=bass.AP(hsb, s_ * 128, [[256, 48], [1, 128]]), in_=mod_tm[s_:s_ + 1, :]), waits=[tmodA, tmodB], inc="sMB", d=16))
        tmodT = P.op("q", lambda q: q.dma_start_transpose(modT, hsb.ap()[:, :]), waits=tmb, inc="sMO", d=16)
        tca1 = P.op("v", lambda v: v.tensor_scalar(out=smf[:, O_CA1:O_CA1 + 16], in0=modT[:, 16:32], scalar1=1.0, scalar2=None, op0=ALU.add), waits=[tmodT], inc="ve")
        tcb1 = P.op("v", lambda v: v.tensor_scalar(out=smf[:, O_CB1:O_CB1 + 16], in0=modT[:, 64:80], scalar1=1.0, scalar2=None, op0=ALU.add), waits=[tmodT], inc="ve")
        tsbf = P.op("v", lambda v: v.tensor_copy(out=smf[:, O_SB:O_SB + 16], in_=modT[:, 48:64]), waits=[tmodT], inc="ve")

        def eff(src_off, dst, prev):
            t1 = P.op("s", lambda s, o=src_off, dst=dst: s.activation(out=dst, in_=mod_tm[0:2, o:o + 1024], func=AF.Abs), waits=[tmodA, tmodB] + prev, inc="se")
            t2 = P.op("v", lambda v, dst=dst: v.tensor_scalar(out=dst, in0=dst, scalar1=0.25, scalar2=None, op0=ALU.max), waits=[t1], inc="ve")
            t3 = P.op("v", lambda v, o=src_off: v.tensor_scalar(out=sgn, in0=mod_tm[0:2, o:o + 1024], scalar1=0.0, scalar2=None, op0=ALU.is_ge), waits=[tmodA, tmodB] + prev, inc="ve")
            t4 = P.op("v", lambda v: v.tensor_scalar(out=sgn, in0=sgn, scalar1=2.0, scalar2=-1.0, op0=ALU.mult, op1=ALU.add), waits=[t3], inc="ve")
            t5 = P.op("v", lambda v, dst=dst: v.tensor_tensor(out=dst, in0=dst, in1=sgn, op=ALU.mult), waits=[t2, t4], inc="ve")
            return t5
        tga = eff(2048, ga_tm, [])
        tgb = eff(5120, gb_tm, [tga])

        rep_toks = {}

        def make_rep(name, rrow, mk, roff, srcap=None):
            tmk = mk()
            sap = repr_ if srcap is None else srcap
            tb = P.op("g", lambda g, rrow=rrow, sap=sap: g.dma_start(out=hsc.ap()[rrow:rrow + 2, :], in_=sap), waits=[tmk], inc="gRB", d=16)
            toks = []
            for s_ in range(2):
                dst = repb[:, (roff + s_) * 1024:(roff + s_ + 1) * 1024]
                toks.append(P.op("q", lambda q, s_=s_, rrow=rrow, dst=dst: q.dma_start(out=dst, in_=bass.AP(hsc, (rrow + s_) * 1024, [[0, 128], [1, 1024]])), waits=[tb], inc="sRP", d=16))
            rep_toks[name] = toks
            return tmk

        tgg = make_rep("gga", 0, lambda: P.op("v", lambda v: v.tensor_tensor(out=repr_, in0=gam2, in1=ga_tm, op=ALU.mult), waits=[tga, t_g2, tgb, tcs, tcb, tcb2], inc="ve"), 0)
        t_b2 = P.op("q", lambda q: q.dma_start(out=gam2, in_=bass.AP(hbe, 0, [[0, 2], [1, 1024]])), waits=[tgg] + rep_toks["gga"], inc="sBE", d=16)
        tbgm = make_rep("bga", 2, lambda: P.op("v", lambda v: v.tensor_tensor(out=repr_, in0=gam2, in1=ga_tm, op=ALU.mult), waits=[t_b2, tga, tgg] + rep_toks["gga"], inc="ve"), 2)
        tcb1r = make_rep("cb1", 6, lambda: P.op("v", lambda v: v.tensor_scalar(out=repr_, in0=mod_tm[0:2, 4096:5120], scalar1=1.0, scalar2=None, op0=ALU.add), waits=[tmodA, tmodB, tbgm] + rep_toks["bga"], inc="ve"), 6)
        tsbr = make_rep("sbr", 10, lambda: P.op("v", lambda v: v.tensor_copy(out=repr_, in_=mod_tm[0:2, 3072:4096]), waits=[tmodA, tmodB, tcb1r] + rep_toks["cb1"], inc="ve"), 8)


        swb_tok = []
        for s_ in range(2):
            for eh in range(2):
                pb = psum[4 + eh]
                toks = []
                for dcc in range(DC):
                    def mmsw(t, s_=s_, dcc=dcc, eh=eh, pb=pb):
                        return t.matmul(pb[0:1, 0:512], modT[:, 2 * dcc + s_:2 * dcc + s_ + 1], wt[:, dcc * 1024 + eh * 512:dcc * 1024 + eh * 512 + 512], start=(dcc == 0), stop=(dcc == DC - 1))
                    w = [tmodT, tWt[dcc]]
                    if dcc == 0 and s_ == 1:
                        w.append(swb_tok[eh])
                    toks.append(P.op("t", mmsw, waits=w, inc="pe"))
                swb_tok.append(P.op("s", lambda s, s_=s_, eh=eh, pb=pb: s.activation(out=sep[0:1, SWB[s_] + eh * 512:SWB[s_] + eh * 512 + 512], in_=pb[0:1, 0:512], func=AF.Identity), waits=[toks[-1]], inc="se"))
        tbias = []
        for s_ in range(2):
            tbias.append(P.op("g", lambda g, s_=s_: g.dma_start(out=sep[1:2, SWB[s_]:SWB[s_] + 1024], in_=bia[:, :]), waits=[tcs] if s_ == 0 else [], inc="gBI", d=16))

        wq_raw = Pool(2)
        nat_pool = Pool(2)
        park_done = {"wg": [], "wu": [], "wd": []}
        wgT_tok = [[] for _ in range(DC)]
        wuT_tok = [[] for _ in range(DC)]
        wTR_war = [[None] * 4, [None] * 4]
        wlast_bank = [None, None]
        stg_pool = Pool(2)
        stg_last = [None, None]

        def prep_gu_tile(name, src, park, scoff, ic):
                slot, w = wq_raw.acquire()
                raw = scr[:, slot * 1024:(slot + 1) * 1024]
                tl = P.op("g", lambda g, ic=ic, raw=raw, src=src: g.dma_start(out=raw, in_=src[ic * 128:(ic + 1) * 128, :]), waits=w + ([tones, ts1] if ic < 2 else []), inc="sQ0" if slot == 0 else "sQ1", d=16)
                wq_raw.wrote(slot, tl)
                tmx = P.op("v", lambda v, raw=raw: v.tensor_reduce(out=smv(C_T0), in_=raw, axis=AX.X, op=ALU.max, apply_absolute_value=True), waits=[tl], inc="ve")
                tsc = P.op("v", lambda v, ic=ic: v.tensor_scalar(out=smf[:, scoff + ic:scoff + ic + 1], in0=smv(C_T0), scalar1=1.0 / 127.0, scalar2=None, op0=ALU.mult), waits=[tmx], inc="ve")
                trc = P.op("v", lambda v, ic=ic: v.reciprocal(out=smv(C_T1), in_=smf[:, scoff + ic:scoff + ic + 1]), waits=[tsc], inc="ve")
                tq = P.op("s", lambda s, raw=raw: s.activation(out=raw, in_=raw, func=AF.Identity, scale=smv(C_T1), bias=smv(C_CB)), waits=[trc, tCb, tmx], inc="se")
                nslot, wn = nat_pool.acquire()
                nat = hgb[:, nslot * 1024:(nslot + 1) * 1024]
                tqq = P.op("v", lambda v, raw=raw, nat=nat: v.tensor_scalar(out=nat, in0=raw, scalar1=smv(C_CB), scalar2=None, op0=ALU.subtract), waits=[tq] + wn, inc="ve")
                nat_pool.wrote(nslot, tqq)
                wq_raw.read(slot, tqq)
                if name in ("wu", "wg"):
                    tpk = P.op("s", lambda s, ic=ic, nat=nat, park=park: s.dma_start(out=park.ap()[ic * 128:(ic + 1) * 128, :], in_=nat), waits=[tqq], inc="aP0" if nslot == 0 else "aP1", d=16)
                    nat_pool.read(nslot, tpk)
                    park_done[name].append(tpk)
                    return
                bank = psum[2 + nslot]
                rw = ([tmodA, tmodB] + [t for t in wm_slot_tok if t]) if name == "wg" else list(mod_tm_readers)
                for dcc in range(DC):
                    creg = dcc % 4
                    wtr = P.op("t", lambda t, bank=bank, creg=creg, nat=nat, dcc=dcc: t.matmul(bank[:, creg * 128:(creg + 1) * 128], nat[:, dcc * 128:(dcc + 1) * 128], ebuf[:, 0:128], start=True, stop=True), waits=[tqq, teye] + ([wTR_war[nslot][creg]] if wTR_war[nslot][creg] else []), inc="pe")
                    nat_pool.read(nslot, wtr)
                    dst = (dcc * 2816 + ic * 128) if name == "wg" else (22528 + dcc * 2816 + ic * 128)
                    if dst + 128 <= 36864:
                        tev = P.op("s", lambda s, bank=bank, creg=creg, dst=dst: s.activation(out=wa[:, dst:dst + 128], in_=bank[:, creg * 128:(creg + 1) * 128], func=AF.Identity), waits=[wtr] + rw, inc="se")
                        (wgT_tok if name == "wg" else wuT_tok)[dcc].append(tev)
                    else:
                        ssl, sw = stg_pool.acquire()
                        hst = hho[:, ssl * 128:(ssl + 1) * 128]
                        tev = P.op("s", lambda s, bank=bank, creg=creg, hst=hst: s.activation(out=hst, in_=bank[:, creg * 128:(creg + 1) * 128], func=AF.Identity), waits=[wtr] + sw, inc="se")
                        tpkT = P.op("s", lambda s, hst=hst, dst=dst: s.dma_start(out=pkuT.ap()[:, dst - 36864:dst - 36864 + 128], in_=hst), waits=[tev], inc="aPT0" if ssl == 0 else "aPT1", d=16)
                        stg_pool.wrote(ssl, tpkT)
                        stg_last[ssl] = tpkT
                    wTR_war[nslot][creg] = tev
                    wlast_bank[nslot] = tev
        PREP_JOBS = []
        for _ic in range(IC):
            PREP_JOBS.append(lambda ic=_ic: prep_gu_tile("wg", wg_, pkg, O_SCG, ic))
        for _ic in range(IC):
            PREP_JOBS.append(lambda ic=_ic: prep_gu_tile("wu", wu_, pku, O_SCU, ic))
        PREP_WD = []
        for _hcn in range(DC):
            PREP_WD.append(lambda hcn=_hcn: prep_wd_tile(hcn))

        def prep_wd_tile(hcn):
            mxp = []
            for j, (o, wdt) in enumerate([(0, 1024), (1024, 1024), (2048, 768)]):
                slot, w = wq_raw.acquire()
                raw = scr[:, slot * 1024:slot * 1024 + wdt]
                tl = P.op("g", lambda g, hcn=hcn, o=o, wdt=wdt, raw=raw: g.dma_start(out=raw, in_=wd_[hcn * 128:(hcn + 1) * 128, o:o + wdt]), waits=w, inc="sQ0" if slot == 0 else "sQ1", d=16)
                wq_raw.wrote(slot, tl)
                tmx = P.op("v", lambda v, raw=raw, j=j: v.tensor_reduce(out=smf[:, O_MX + j:O_MX + j + 1], in_=raw, axis=AX.X, op=ALU.max, apply_absolute_value=True), waits=[tl] + ([mxp[0]] if (j and False) else []), inc="ve")
                wq_raw.read(slot, tmx)
                mxp.append(tmx)
            tm2 = P.op("v", lambda v: v.tensor_tensor(out=smv(O_MX), in0=smf[:, O_MX:O_MX + 1], in1=smf[:, O_MX + 1:O_MX + 2], op=ALU.max), waits=mxp, inc="ve")
            tm3 = P.op("v", lambda v: v.tensor_tensor(out=smv(C_T0), in0=smv(O_MX), in1=smf[:, O_MX + 2:O_MX + 3], op=ALU.max), waits=[tm2], inc="ve")
            tsc = P.op("v", lambda v, hcn=hcn: v.tensor_scalar(out=smf[:, O_SCD + hcn:O_SCD + hcn + 1], in0=smv(C_T0), scalar1=1.0 / 127.0, scalar2=None, op0=ALU.mult), waits=[tm3], inc="ve")
            trc = P.op("v", lambda v: v.reciprocal(out=smv(C_T1), in_=smf[:, O_SCD + hcn:O_SCD + hcn + 1] if False else smv(C_T0)), waits=[tsc], inc="ve")
            trc = P.op("v", lambda v, hcn=hcn: v.reciprocal(out=smv(C_T1), in_=smf[:, O_SCD + hcn:O_SCD + hcn + 1]), waits=[tsc, trc], inc="ve")
            for j, (o, wdt) in enumerate([(0, 1024), (1024, 1024), (2048, 768)]):
                slot, w = wq_raw.acquire()
                raw = scr[:, slot * 1024:slot * 1024 + wdt]
                tl = P.op("g", lambda g, hcn=hcn, o=o, wdt=wdt, raw=raw: g.dma_start(out=raw, in_=wd_[hcn * 128:(hcn + 1) * 128, o:o + wdt]), waits=w + [trc], inc="sQ0" if slot == 0 else "sQ1", d=16)
                wq_raw.wrote(slot, tl)
                tq = P.op("s", lambda s, raw=raw: s.activation(out=raw, in_=raw, func=AF.Identity, scale=smv(C_T1), bias=smv(C_CB)), waits=[tl], inc="se")
                nslot, wn = nat_pool.acquire()
                nat = hgb[:, nslot * 1024:nslot * 1024 + wdt]
                tqq = P.op("v", lambda v, raw=raw, nat=nat: v.tensor_scalar(out=nat, in0=raw, scalar1=smv(C_CB), scalar2=None, op0=ALU.subtract), waits=[tq] + wn, inc="ve")
                nat_pool.wrote(nslot, tqq)
                wq_raw.read(slot, tqq)
                tpk = P.op("s", lambda s, hcn=hcn, o=o, wdt=wdt, nat=nat: s.dma_start(out=pkd.ap()[hcn * 128:(hcn + 1) * 128, o:o + wdt], in_=nat), waits=[tqq], inc="aP0" if nslot == 0 else "aP1", d=16)
                nat_pool.read(nslot, tpk)
                park_done["wd"].append(tpk)

        mod_tm_readers = [tga, tgb, tmodA, tmodB] + tmb
        teye = P.op("g", lambda g: g.dma_start(out=ebuf[:, 0:128], in_=ey_), inc="sEY", d=16)
        prev_evac = [[] for _ in range(DC)]
        prev_evac[0] = [swb_tok[0], swb_tok[2]]
        prev_evac[1] = [swb_tok[0], swb_tok[2]]
        prev_evac[2] = [swb_tok[1], swb_tok[3]]
        prev_evac[3] = [swb_tok[1], swb_tok[3]]
        tr_tok = [[None] * DC, [None] * DC]
        seedA = rep_toks["sbr"][1]
        zpool = Pool(2, seed=seedA)
        epool = Pool(2, seed=seedA)
        ynpool = Pool(2)
        h1pool = Pool(3)
        stpool = Pool(6)
        fm_hist = [[ts1], [ts1]]
        sfm_hist = [[], []]
        h2sp_done = []
        x2sp_done = {}
        h1war = []
        xhwar = []
        xhwar_prev = []
        ypar_reader = [tmodA, tmodB]
        scan_tail = [None] * DC
        amax_last = None
        tam0 = P.op("v", lambda v: v.memset(smv(C_AM), -3e38), inc="ve")
        tan0 = P.op("v", lambda v: v.memset(smv(C_AN), 3e38), waits=[tam0], inc="ve")
        amax_last = tan0

        def ln_stats(src, w, k, wb=None):
            o = O_ST + 24 * k
            t1 = P.op("v", lambda v, src=src, o=o: v.bn_stats(out=smf[:, o:o + 6], in_=src[:, 0:512]), waits=w, inc="ve")
            t2 = P.op("v", lambda v, src=src, o=o: v.bn_stats(out=smf[:, o + 6:o + 12], in_=src[:, 512:1024]), waits=(w if wb is None else wb), inc="ve")
            t3 = P.op("v", lambda v, o=o: v.bn_aggr(out=smf[:, o + 12:o + 14], in_=smf[:, o:o + 12]), waits=[t1, t2], inc="ve")
            t4 = P.op("s", lambda s, o=o: s.activation(out=smf[:, o + 14:o + 15], in_=smf[:, o + 13:o + 14], func=AF.Sqrt, bias=smv(C_EPS)), waits=[t3, teps], inc="se")
            t5 = P.op("v", lambda v, o=o: v.reciprocal(out=smf[:, o + 15:o + 16], in_=smf[:, o + 14:o + 15]), waits=[t4], inc="ve")
            t6 = P.op("v", lambda v, o=o: v.tensor_scalar(out=smf[:, o + 16:o + 17], in0=smf[:, o + 12:o + 13], scalar1=smf[:, o + 15:o + 16], scalar2=-1.0, op0=ALU.mult, op1=ALU.mult), waits=[t5], inc="ve")
            return smf[:, o + 15:o + 16], smf[:, o + 16:o + 17], t6

        for s_ in range(BL):
            for h in range(NH):
                par = (s_ * NH + h) % 2
                fo = par * 4096
                fmbwar = fm_hist[par]
                fm_hist[par] = []
                evac_half = [[None] * DC, [None] * DC]
                tx2_list = []
                for tt in range(TPH):
                    gt = s_ * (NH * TPH) + h * TPH + tt
                    zslot, w = zpool.acquire()
                    zt = x0w[:, zslot * 1024:(zslot + 1) * 1024]
                    extra = [tcs] if (zslot == 0 and gt < 2) else []
                    tz = P.op("q", lambda q, gt=gt, zt=zt: q.dma_start(out=zt, in_=Z[gt * 128:(gt + 1) * 128, :]), waits=w + extra, inc="sZ0" if zslot == 0 else "sZ1", d=16)
                    zpool.wrote(zslot, tz)
                    kst, wst = stpool.acquire()
                    r_ap, mr_ap, tstat = ln_stats(zt, [tz] + wst, kst)
                    zpool.read(zslot, tstat)
                    h1 = h1sb[:, tt * 1024:(tt + 1) * 1024]
                    th1 = P.op("s", lambda s, zt=zt, h1=h1, r_ap=r_ap, mr_ap=mr_ap: s.activation(out=h1, in_=zt, func=AF.Identity, scale=r_ap, bias=mr_ap), waits=[tstat] + h1war, inc="se")
                    zpool.read(zslot, th1)
                    stpool.wrote(kst, th1)
                    for dcc in range(DC):
                        pb = psum[4 + dcc // 2]
                        col = (dcc % 2) * 256 + (tt % 2) * 128
                        ttr = P.op("t", lambda t, pb=pb, col=col, h1=h1, dcc=dcc: t.matmul(pb[:, col:col + 128], h1[:, dcc * 128:(dcc + 1) * 128], ebuf[:, 0:128], start=True, stop=True), waits=[th1, teye] + prev_evac[dcc], inc="pe")
                        tr_tok[tt % 2][dcc] = ttr
                    if tt % 2 == 1:
                        p_ = tt // 2
                        for dcc in range(DC):
                            pb = psum[4 + dcc // 2]
                            tev = P.op("s", lambda s, pb=pb, dcc=dcc, fo=fo, p_=p_, s_=s_: s.activation(out=fmb[:, fo + dcc * SEG + p_ * 256:fo + dcc * SEG + p_ * 256 + 256], in_=pb[:, (dcc % 2) * 256:(dcc % 2) * 256 + 256], func=AF.Identity, scale=smf[:, O_CA1 + 2 * dcc + s_:O_CA1 + 2 * dcc + s_ + 1]), waits=[tr_tok[0][dcc], tr_tok[1][dcc], tca1] + fmbwar, inc="se")
                            prev_evac[dcc] = [tev]
                            evac_half[p_][dcc] = tev
                scans = []
                sfw = sfm_hist[par]
                sfm_hist[par] = []
                for dcc in range(DC):
                    init = 0.0 if h == 0 else smf[:, O_TAIL + dcc:O_TAIL + dcc + 1]
                    winit = [] if h == 0 else [scan_tail[dcc]]
                    tscan = P.op("v", lambda v, dcc=dcc, fo=fo, init=init: v.tensor_tensor_scan(out=sfm[:, fo + dcc * SEG:fo + (dcc + 1) * SEG], data0=bass.AP(smf, C_A, [[512, 128], [0, SEG]]), data1=fmb[:, fo + dcc * SEG:fo + (dcc + 1) * SEG], initial=init, op0=ALU.mult, op1=ALU.add), waits=[evac_half[0][dcc], evac_half[1][dcc], tac] + sfw + winit, inc="ve")
                    ttl = P.op("v", lambda v, dcc=dcc, fo=fo: v.tensor_copy(out=smf[:, O_TAIL + dcc:O_TAIL + dcc + 1], in_=sfm[:, fo + (dcc + 1) * SEG - 1:fo + (dcc + 1) * SEG]), waits=[tscan] + ([scan_tail[dcc]] if h > 0 else []), inc="ve")
                    scan_tail[dcc] = ttl
                    fm_hist[par].append(tscan)
                    sfm_hist[par].append(ttl)
                    scans.append(tscan)
                pending_post = None
                for tt in range(TPH):
                    gt = s_ * (NH * TPH) + h * TPH + tt
                    mm_l = []
                    for eh in range(2):
                        pb = psum[eh]
                        toks = []
                        for dcc in range(DC):
                            def mmy(t, dcc=dcc, eh=eh, tt=tt, pb=pb, fo=fo):
                                return t.matmul(pb[:, 0:512], sfm[:, fo + dcc * SEG + tt * 128:fo + dcc * SEG + tt * 128 + 128], wt[:, dcc * 1024 + eh * 512:dcc * 1024 + eh * 512 + 512], start=(dcc == 0), stop=False)
                            w = [scans[dcc], tWt[dcc]]
                            if dcc == 0 and ypar_reader[eh] is not None:
                                w.append(ypar_reader[eh])
                            toks.append(P.op("t", mmy, waits=w, inc="pe"))
                        def mmk2(t, eh=eh, pb=pb, s_=s_, gt=gt):
                            trel = (gt % (NH * TPH)) * 128
                            return t.matmul(pb[:, 0:512], sep[0:2, S_S1 + trel:S_S1 + trel + 128], sep[0:2, SWB[s_] + eh * 512:SWB[s_] + eh * 512 + 512], start=False, stop=True)
                        tk2 = P.op("t", mmk2, waits=[ts1, tone2, swb_tok[2 * s_ + eh], tbias[s_], toks[-1]], inc="pe")
                        last_y_mm = [tk2]
                        mm_l.append(tk2)
                        sfm_hist[par].append(tk2)
                    kst, wst = stpool.acquire()
                    oY = O_ST + 24 * kst
                    pbl, pbh = psum[0], psum[1]
                    ty1 = P.op("v", lambda v, pbl=pbl, oY=oY: v.bn_stats(out=smf[:, oY:oY + 6], in_=pbl[:, 0:512]), waits=[mm_l[0]] + wst, inc="ve")
                    ty2 = P.op("v", lambda v, pbh=pbh, oY=oY: v.bn_stats(out=smf[:, oY + 6:oY + 12], in_=pbh[:, 0:512]), waits=[mm_l[1]], inc="ve")
                    ty3 = P.op("v", lambda v, oY=oY: v.bn_aggr(out=smf[:, oY + 12:oY + 14], in_=smf[:, oY:oY + 12]), waits=[ty1, ty2], inc="ve")
                    ty4 = P.op("s", lambda s, oY=oY: s.activation(out=smf[:, oY + 14:oY + 15], in_=smf[:, oY + 13:oY + 14], func=AF.Sqrt, bias=smv(C_EPS)), waits=[ty3, teps], inc="se")
                    ty5 = P.op("v", lambda v, oY=oY: v.reciprocal(out=smf[:, oY + 15:oY + 16], in_=smf[:, oY + 14:oY + 15]), waits=[ty4], inc="ve")
                    ty6 = P.op("v", lambda v, oY=oY: v.tensor_scalar(out=smf[:, oY + 16:oY + 17], in0=smf[:, oY + 12:oY + 13], scalar1=smf[:, oY + 15:oY + 16], scalar2=-1.0, op0=ALU.mult, op1=ALU.mult), waits=[ty5], inc="ve")
                    ysl, wyn = ynpool.acquire()
                    yt = ynb2[:, ysl * 1024:(ysl + 1) * 1024]
                    tyn = []
                    for eh, pb in ((0, pbl), (1, pbh)):
                        tok = P.op("v", lambda v, pb=pb, eh=eh, yt=yt, oY=oY: v.tensor_scalar(out=yt[:, eh * 512:eh * 512 + 512], in0=pb[:, 0:512], scalar1=smf[:, oY + 15:oY + 16], scalar2=smf[:, oY + 16:oY + 17], op0=ALU.mult, op1=ALU.add), waits=[ty6] + wyn, inc="ve")
                        tyn.append(tok)
                        ypar_reader[eh] = tok
                    stpool.wrote(kst, ty6)
                    tt2 = P.op("v", lambda v, yt=yt, s_=s_: v.tensor_tensor(out=yt, in0=yt, in1=repb[:, s_ * 1024:(s_ + 1) * 1024], op=ALU.mult), waits=tyn + rep_toks["gga"], inc="ve")
                    tu2 = P.op("v", lambda v, yt=yt, s_=s_: v.tensor_tensor(out=yt, in0=yt, in1=repb[:, 2048 + s_ * 1024:2048 + (s_ + 1) * 1024], op=ALU.add), waits=[tt2] + rep_toks["bga"], inc="ve")
                    eslot, we = epool.acquire()
                    x2t = x0w[:, (2 + eslot) * 1024:(3 + eslot) * 1024]
                    tz2 = P.op("q", lambda q, gt=gt, x2t=x2t: q.dma_start(out=x2t, in_=Z[gt * 128:(gt + 1) * 128, :]), waits=we, inc="sE0" if eslot == 0 else "sE1", d=16)
                    tx2a = P.op("g", lambda g, x2t=x2t, yt=yt: g.tensor_tensor(out=x2t[:, 0:512], in0=x2t[:, 0:512], in1=yt[:, 0:512], op=ALU.add), waits=[tz2, tu2], inc="ge")
                    tx2b = P.op("g", lambda g, x2t=x2t, yt=yt: g.tensor_tensor(out=x2t[:, 512:1024], in0=x2t[:, 512:1024], in1=yt[:, 512:1024], op=ALU.add), inc="ge")
                    epool.wrote(eslot, tx2b)
                    epool.read(eslot, tx2a)
                    ynpool.wrote(ysl, tx2b)
                    ynpool.read(ysl, tx2a)
                    tx2_list.append(tx2b)
                    tsp = P.op("s", lambda s, gt=gt, x2t=x2t: s.dma_start(out=spx2.ap()[gt * 128:(gt + 1) * 128, :], in_=x2t), waits=[tx2a, tx2b], inc="aX20" if eslot == 0 else "aX21", d=16)
                    x2sp_done[gt] = tsp
                    epool.read(eslot, tsp)

                    def mk_post(gt=gt, tt=tt, x2t=x2t, eslot=eslot, tx2a=tx2a, tx2b=tx2b, s_=s_, par=par):
                        def post():
                            kst2, wst2 = stpool.acquire()
                            r2, mr2, tst2 = ln_stats(x2t, [tx2a] + wst2, kst2, wb=[tx2b] + wst2)
                            epool.read(eslot, tst2)
                            xh = xhsb[:, tt * 1024:(tt + 1) * 1024]
                            txh = P.op("v", lambda v, x2t=x2t, xh=xh, r2=r2, mr2=mr2: v.tensor_scalar(out=xh, in0=x2t, scalar1=r2, scalar2=mr2, op0=ALU.mult, op1=ALU.add), waits=[tst2] + xhwar_prev, inc="ve")
                            epool.read(eslot, txh)
                            stpool.wrote(kst2, txh)
                            tt4 = P.op("v", lambda v, xh=xh, s_=s_: v.tensor_tensor(out=xh, in0=xh, in1=repb[:, 6144 + s_ * 1024:6144 + (s_ + 1) * 1024], op=ALU.mult), waits=[txh] + rep_toks["cb1"], inc="ve")
                            th2 = P.op("v", lambda v, xh=xh, s_=s_: v.tensor_tensor(out=xh, in0=xh, in1=repb[:, 8192 + s_ * 1024:8192 + (s_ + 1) * 1024], op=ALU.add), waits=[tt4] + rep_toks["sbr"], inc="ve")
                            tamx = P.op("v", lambda v, xh=xh, gt=gt: v.tensor_reduce(out=smf[:, 384 + gt:385 + gt], in_=xh, axis=AX.X, op=ALU.max, apply_absolute_value=True), waits=[th2], inc="ve")
                            tspl = P.op("s", lambda s, gt=gt, xh=xh: s.dma_start(out=sph2.ap()[gt * 128:(gt + 1) * 128, :], in_=xh), waits=[th2], inc="aH20" if par == 0 else "aH21", d=16)
                            h2sp_done.append(tspl)
                            xhwar.append(tamx)
                            xhwar.append(tspl)
                            return tamx
                        return post

                    if pending_post is not None:
                        amax_last = pending_post()
                    pending_post = mk_post()
                    for _ in range(2):
                        if PREP_JOBS:
                            PREP_JOBS.pop(0)()
                if pending_post is not None:
                    amax_last = pending_post()
                    pending_post = None
                h1war = list(scans)
                xhwar_prev = list(xhwar)
                xhwar = []

        while PREP_JOBS:
            PREP_JOBS.pop(0)()
        while PREP_WD:
            PREP_WD.pop(0)()
        wd_scr_guard = list(wq_raw.rd[0]) + list(wq_raw.rd[1]) + [t for t in (wq_raw.wr[0], wq_raw.wr[1]) if t]
        parks_all = [("aP0", P.cnt.get("aP0", 0)), ("aP1", P.cnt.get("aP1", 0))]
        tscdw = P.op("q", lambda q: q.dma_start(out=bass.AP(hsf, 8 * 1024, [[1, 128], [128, 8]]), in_=smf[:, O_SCD:O_SCD + 8], allow_slow_non_contiguous=True), waits=parks_all, inc="sS3", d=16)
        tscd2 = P.op("q", lambda q: q.dma_start(out=gam2, in_=bass.AP(hsf, 8 * 1024, [[0, 2], [1, 1024]])), waits=[tscdw, tbgm] + rep_toks["bga"], inc="sS3", d=16)
        make_rep("scdgb", 4, lambda: P.op("v", lambda v: v.tensor_tensor(out=gam2, in0=gam2, in1=gb_tm, op=ALU.mult), waits=[tscd2, tgb, tbgm] + rep_toks["sbr"], inc="ve"), 4, srcap=gam2)

        for dcc in range(DC):
            wgT_tok[dcc].append(P.op("q", lambda q, dcc=dcc: q.dma_start_transpose(wa[:, dcc * 2816:(dcc + 1) * 2816], pkg.ap()[:, dcc * 128:(dcc + 1) * 128]), waits=park_done["wg"] + [tmodA, tmodB] + [t for t in wm_slot_tok if t], inc="sWG", d=16))
        for dcc in range(DC):
            wuT_tok[dcc].append(P.op("q", lambda q, dcc=dcc: q.dma_start_transpose(wa[:, 22528 + dcc * 2816:22528 + (dcc + 1) * 2816], pku.ap()[:, dcc * 128:(dcc + 1) * 128]), waits=park_done["wu"] + mod_tm_readers + last_y_mm, inc="sWU", d=16))

        def global_reduce(prev_tok, cci, cco, ccp, hrow, scol, sigcol, iscol, extra_q=(), extra_t8=()):
            t1 = P.op("v", lambda v: v.tensor_scalar(out=smv(C_T0), in0=smv(C_AN), scalar1=-1.0, scalar2=None, op0=ALU.mult), waits=[prev_tok], inc="ve")
            t2 = P.op("v", lambda v: v.tensor_tensor(out=smv(C_T0), in0=smv(C_T0), in1=smv(C_AM), op=ALU.max), waits=[t1], inc="ve")
            t3 = P.op("q", lambda q, hrow=hrow: q.dma_start(out=hsf.ap()[hrow:hrow + 1, 0:128], in_=smv(C_T0)), waits=[t2] + list(extra_q), inc="sGR", d=16)
            t4 = P.op("q", lambda q, hrow=hrow: q.dma_start(out=scr[0:1, 0:128], in_=hsf.ap()[hrow:hrow + 1, 0:128]), waits=[t3, tones, ts1], inc="sGR", d=16)
            t5 = P.op("v", lambda v: v.tensor_reduce(out=scr[0:1, 128:129], in_=scr[0:1, 0:128], axis=AX.X, op=ALU.max), waits=[t4], inc="ve")
            t6 = P.op("q", lambda q, cci=cci: q.dma_start(out=cci.ap()[:, :], in_=scr[0:1, 128:129]), waits=[t5], inc="sGR", d=16)
            t7 = P.op("g", lambda g, cci=cci, cco=cco: g.collective_compute("AllReduce", ALU.max, ins=[cci.ap()[:, :]], outs=[cco.ap()[:, :]], replica_groups=[list(range(CC_WORLD))]), waits=[t6], inc="cc")
            t7b = P.op("g", lambda g, cci=cci, ccp=ccp: g.collective_compute("AllReduce", ALU.max, ins=[cci.ap()[:, :]], outs=[ccp.ap()[:, :]], replica_groups=[list(range(CC_WORLD))]), waits=[t7], inc="cc")
            t8 = P.op("g", lambda g, cco=cco: g.dma_start(out=smv(scol), in_=bass.AP(cco, 0, [[0, 128], [1, 1]])), waits=[t7b] + list(extra_t8), inc="sGR2", d=16)
            t9 = P.op("v", lambda v: v.tensor_scalar(out=smv(scol), in0=smv(scol), scalar1=smv(sigcol), scalar2=1.0 / 127.0, op0=ALU.mult, op1=ALU.mult), waits=[t8, tsx, tsd], inc="ve")
            t9b = P.op("v", lambda v: v.tensor_scalar(out=smv(scol), in0=smv(scol), scalar1=1e-35, scalar2=None, op0=ALU.max), waits=[t9], inc="ve")
            t10 = P.op("v", lambda v: v.reciprocal(out=smv(iscol), in_=smv(scol)), waits=[t9b], inc="ve")
            return t9b, t10
        tcmb = P.op("v", lambda v: v.tensor_reduce(out=smv(C_AM), in_=smf[:, 384:416], axis=AX.X, op=ALU.max), waits=[amax_last], inc="ve")
        h2_all = [("aH20", P.cnt.get("aH20", 0)), ("aH21", P.cnt.get("aH21", 0))]
        x2_all = [("aX20", P.cnt.get("aX20", 0)), ("aX21", P.cnt.get("aX21", 0))]
        fmb_hist = [list(fm_hist[0]), list(fm_hist[1])]
        xq_hist = [list(sfm_hist[0]), list(sfm_hist[1])]
        last_mm_xq = [None, None]

        def emit_tl(g):
            par = g % 2
            base = par * 4096
            w_h2 = fmb_hist[par]
            fmb_hist[par] = []
            toks = []
            for dcc in range(DC):
                h2in = fmb[:, base + dcc * 512:base + (dcc + 1) * 512]
                tl = P.op("q", lambda q, g=g, dcc=dcc, h2in=h2in: q.dma_start_transpose(h2in, sph2.ap()[g * GT:(g + 1) * GT, dcc * 128:(dcc + 1) * 128]), waits=h2_all + w_h2 + ([last_mm_xq[par]] if last_mm_xq[par] else []), inc="sH0" if par == 0 else "sH1", d=16)
                toks.append(tl)
            return toks
        tl_pref = {0: emit_tl(0), 1: emit_tl(1)}
        ts1x, tis1 = global_reduce(tcmb, cc1i, cc1o, cc1p, 9, C_T1, C_SX, C_IS1, extra_t8=[tl_pref[1][-1]])
        tsgx = P.op("v", lambda v: v.tensor_scalar(out=smf[:, O_SGX:O_SGX + 22], in0=smf[:, O_SCG:O_SCG + 22], scalar1=smv(C_T1), scalar2=None, op0=ALU.mult), waits=[ts1x], inc="ve")
        tsux = P.op("v", lambda v: v.tensor_scalar(out=smf[:, O_SUX:O_SUX + 22], in0=smf[:, O_SCU:O_SCU + 22], scalar1=smv(C_T1), scalar2=None, op0=ALU.mult), waits=[ts1x], inc="ve")

        tam2 = P.op("v", lambda v: v.memset(smv(C_AM), -3e38), waits=[ts1x], inc="ve")
        tan2 = P.op("v", lambda v: v.memset(smv(C_AN), 3e38), waits=[tam2], inc="ve")
        amax_last = tan2
        hh_sp = []
        gu_reader = [None, None, wlast_bank[0], wlast_bank[1]]
        hg_war = [None, None]
        hu_war = [None, None]
        hh_war = [stg_last[0], stg_last[1]]
        tq1_last = [None, None]
        for g in range(NG):
            s_ = g // GPS
            par = g % 2
            base = par * 4096
            w_xq = xq_hist[par]
            xq_hist[par] = []
            xq_tok = []
            tl_toks = tl_pref.pop(g) if g in tl_pref else emit_tl(g)
            for dcc in range(DC):
                h2in = fmb[:, base + dcc * 512:base + (dcc + 1) * 512]
                tq1 = P.op("s", lambda s, h2in=h2in, par=par: s.activation(out=scr[:, par * 512:par * 512 + 512], in_=h2in, func=AF.Identity, scale=smv(C_IS1), bias=smv(C_CB)), waits=[tl_toks[-1], tis1] + ([tq1_last[par]] if tq1_last[par] else wd_scr_guard), inc="se")
                tq2 = P.op("v", lambda v, par=par: v.tensor_scalar(out=scr[:, par * 512:par * 512 + 512], in0=scr[:, par * 512:par * 512 + 512], scalar1=smv(C_HI), scalar2=smv(C_LO), op0=ALU.min, op1=ALU.max), waits=[tq1, thi, tlo], inc="ve")
                xq = sfm[:, base + dcc * 512:base + (dcc + 1) * 512]
                tq3 = P.op("v", lambda v, xq=xq, par=par: v.tensor_scalar(out=xq, in0=scr[:, par * 512:par * 512 + 512], scalar1=smv(C_CB), scalar2=None, op0=ALU.subtract), waits=[tq2] + w_xq, inc="ve")
                w_xq = []
                tq1_last[par] = tq3
                xq_tok.append(tq3)
                fmb_hist[par].append(tq1)
            for ic in range(IC):
                icp = ic % 2
                pg = psum[icp]
                pu = psum[2 + icp]
                gtoks = []
                for dcc in range(DC):
                    def mmg(t, ic=ic, dcc=dcc, pg=pg, base=base):
                        return t.matmul(pg[:, 0:512], wa[:, dcc * 2816 + ic * 128:dcc * 2816 + ic * 128 + 128], sfm[:, base + dcc * 512:base + (dcc + 1) * 512], start=(dcc == 0), stop=(dcc == DC - 1))
                    w = [xq_tok[dcc]] + wgT_tok[dcc]
                    if dcc == 0 and gu_reader[icp] is not None:
                        w.append(gu_reader[icp])
                    gtoks.append(P.op("t", mmg, waits=w, inc="pe"))
                utoks = []
                for dcc in range(DC):
                    def mmu(t, ic=ic, dcc=dcc, pu=pu, base=base):
                        return t.matmul(pu[:, 0:512], wa[:, 22528 + dcc * 2816 + ic * 128:22528 + dcc * 2816 + ic * 128 + 128], sfm[:, base + dcc * 512:base + (dcc + 1) * 512], start=(dcc == 0), stop=(dcc == DC - 1))
                    w = [xq_tok[dcc]] + wuT_tok[dcc]
                    if dcc == 0 and gu_reader[2 + icp] is not None:
                        w.append(gu_reader[2 + icp])
                    utoks.append(P.op("t", mmu, waits=w, inc="pe"))
                hgp = hgb[:, icp * 512:icp * 512 + 512]
                hup = hgb[:, 1024 + icp * 512:1024 + icp * 512 + 512]
                thg = P.op("s", lambda s, pg=pg, hgp=hgp, ic=ic: s.activation(out=hgp, in_=pg[:, 0:512], func=AF.Silu, scale=smf[:, O_SGX + ic:O_SGX + ic + 1]), waits=[gtoks[-1], tsgx] + ([hg_war[icp]] if hg_war[icp] else []), inc="se")
                gu_reader[icp] = thg
                thu = P.op("s", lambda s, pu=pu, hup=hup, ic=ic: s.activation(out=hup, in_=pu[:, 0:512], func=AF.Identity, scale=smf[:, O_SUX + ic:O_SUX + ic + 1]), waits=[utoks[-1], tsux] + ([hu_war[icp]] if hu_war[icp] else []), inc="se")
                gu_reader[2 + icp] = thu
                hh = hho[:, icp * 512:icp * 512 + 512]
                thh = P.op("v", lambda v, hgp=hgp, hup=hup, hh=hh: v.tensor_tensor(out=hh, in0=hgp, in1=hup, op=ALU.mult), waits=[thg, thu] + ([hh_war[icp]] if hh_war[icp] else []), inc="ve")
                hg_war[icp] = thh
                tm = P.op("v", lambda v, hh=hh, hup=hup: v.tensor_scalar(out=hup, in0=hh, scalar1=1.0, scalar2=smv(C_AM), op0=ALU.mult, op1=ALU.max, accum_out=smv(C_AM)), waits=[thh, amax_last], inc="ve")
                tn = P.op("v", lambda v, hh=hh, hup=hup: v.tensor_scalar(out=hup, in0=hh, scalar1=1.0, scalar2=smv(C_AN), op0=ALU.min, op1=ALU.min, accum_out=smv(C_AN)), waits=[tm], inc="ve")
                amax_last = tn
                hu_war[icp] = tn
                tsp = P.op("s", lambda s, g=g, ic=ic, hh=hh: s.dma_start(out=sphh.ap()[g * IC + ic, :, :], in_=hh), waits=[thh], inc="aHH0" if icp == 0 else "aHH1", d=16)
                hh_sp.append(tsp)
                hh_war[icp] = tsp
            last_mm_xq[par] = gtoks[-1]
            xq_hist[par] = [utoks[-1], gtoks[-1]]

        wdT_tok = []
        for icc in range(IC):
            wdT_tok.append(P.op("q", lambda q, icc=icc: q.dma_start_transpose(wa[:, icc * 1024:(icc + 1) * 1024], pkd.ap()[:, icc * 128:(icc + 1) * 128]), waits=parks_all + [amax_last, gu_reader[0], gu_reader[1]], inc="sWD", d=16))
        ts2x, tis2 = global_reduce(amax_last, cc2i, cc2o, cc2p, 10, C_S2B, C_SD, C_IS2, extra_t8=[wdT_tok[0]])

        dn_reader = [None] * 4
        t3_war = [None, None]
        dq_war_prev = {}
        hq_mm_last = [[], []]
        x2pool = Pool(2, seed=tis2)
        opool = Pool(2, seed=tis2)
        out_toks = []
        hh_all = [("aHH0", P.cnt.get("aHH0", 0)), ("aHH1", P.cnt.get("aHH1", 0))]
        for g in range(NG):
            s_ = g // GPS
            par = g % 2
            hqb = 22528 + par * 11264
            wclear = hq_mm_last[par]
            hq_mm_last[par] = []
            def mk_tlh(q, g=g, hqb=hqb):
                src = bass.AP(sphh, g * IC * 128 * GT, [[GT, 128], [128 * GT, IC], [1, GT]])
                return q.dma_start(out=wa[:, hqb:hqb + 11264].rearrange("p (i t) -> p i t", t=GT), in_=src)
            tlh = P.op("q", mk_tlh, waits=hh_all + [tis2] + wclear + (([t for l in wuT_tok for t in l]) if g < 2 else []), inc="sL0" if par == 0 else "sL1", d=16)
            hq_tok = []
            dq_war = dq_war_prev
            for ic in range(IC):
                sl = (ic % 2) * 512
                tq1 = P.op("s", lambda s, hqb=hqb, ic=ic, sl=sl: s.activation(out=scr[:, sl:sl + 512], in_=wa[:, hqb + ic * 512:hqb + (ic + 1) * 512], func=AF.Identity, scale=smv(C_IS2), bias=smv(C_CB)), waits=[tlh, tis2] + ([dq_war[sl]] if sl in dq_war else []), inc="se")
                tq2 = P.op("v", lambda v, sl=sl: v.tensor_scalar(out=scr[:, sl:sl + 512], in0=scr[:, sl:sl + 512], scalar1=smv(C_HI), scalar2=smv(C_LO), op0=ALU.min, op1=ALU.max), waits=[tq1], inc="ve")
                tq3 = P.op("v", lambda v, hqb=hqb, ic=ic, sl=sl: v.tensor_scalar(out=wa[:, hqb + ic * 512:hqb + (ic + 1) * 512], in0=scr[:, sl:sl + 512], scalar1=smv(C_CB), scalar2=None, op0=ALU.subtract), waits=[tq2], inc="ve")
                dq_war[sl] = tq3
                dq_war_prev = dq_war
                hq_tok.append(tq3)
            for tp in range(2):
                tts = (2 * tp, 2 * tp + 1)
                gtoks = {(tt, eh): [] for tt in tts for eh in range(2)}
                for ic in range(IC):
                    for tt in tts:
                        dp = tt % 2
                        for eh in range(2):
                            pb = psum[4 + 2 * dp + eh]
                            def mmd(t, ic=ic, eh=eh, tt=tt, pb=pb, hqb=hqb):
                                return t.matmul(pb[:, 0:512], wa[:, hqb + ic * 512 + tt * 128:hqb + ic * 512 + tt * 128 + 128], wa[:, ic * 1024 + eh * 512:ic * 1024 + eh * 512 + 512], start=(ic == 0), stop=(ic == IC - 1))
                            w = [hq_tok[ic], wdT_tok[ic]]
                            if ic == 0 and dn_reader[2 * dp + eh] is not None:
                                w.append(dn_reader[2 * dp + eh])
                            gtoks[(tt, eh)].append(P.op("t", mmd, waits=w, inc="pe"))
                for tt in tts:
                    gt = g * 4 + tt
                    dp = tt % 2
                    mm_l = [gtoks[(tt, 0)][-1], gtoks[(tt, 1)][-1]]
                    hq_mm_last[par].append(mm_l[0])
                    hq_mm_last[par].append(mm_l[1])
                    x2s, wx2 = x2pool.acquire()
                    x2t = x0w[:, x2s * 1024:(x2s + 1) * 1024]
                    tlx = P.op("q", lambda q, gt=gt, x2t=x2t: q.dma_start(out=x2t, in_=spx2.ap()[gt * 128:(gt + 1) * 128, :]), waits=x2_all + [tis2] + wx2, inc="sX0" if x2s == 0 else "sX1", d=16)
                    x2pool.wrote(x2s, tlx)
                    t3t = []
                    for eh in range(2):
                        pb = psum[4 + 2 * dp + eh]
                        tok = P.op("v", lambda v, pb=pb, eh=eh, dp=dp, s_=s_: v.scalar_tensor_tensor(out=scr[:, 1024 + dp * 1024 + eh * 512:1024 + dp * 1024 + eh * 512 + 512], in0=pb[:, 0:512], scalar=smv(C_S2B), in1=repb[:, 4096 + s_ * 1024 + eh * 512:4096 + s_ * 1024 + eh * 512 + 512], op0=ALU.mult, op1=ALU.mult), waits=[mm_l[eh], ts2x, tis2] + rep_toks["scdgb"] + ([dn_reader[2 * dp + eh]] if dn_reader[2 * dp + eh] else []) + ([t3_war[dp]] if t3_war[dp] else []), inc="ve")
                        dn_reader[2 * dp + eh] = tok
                        t3t.append(tok)
                    osl, wo = opool.acquire()
                    ot = x0w[:, (2 + osl) * 1024:(3 + osl) * 1024]
                    tout = P.op("v", lambda v, x2t=x2t, ot=ot, dp=dp: v.tensor_tensor(out=ot, in0=x2t, in1=scr[:, 1024 + dp * 1024:1024 + dp * 1024 + 1024], op=ALU.add), waits=t3t + [tlx] + wo, inc="ve")
                    opool.wrote(osl, tout)
                    t3_war[dp] = tout
                    x2pool.read(x2s, tout)
                    tst = P.op("s", lambda s, gt=gt, ot=ot: s.dma_start(out=y_[gt * 128:(gt + 1) * 128, :], in_=ot), waits=[tout], inc="aO0" if osl == 0 else "aO1", d=16)
                    opool.read(osl, tst)
                    out_toks.append(tst)
            hq_mm_last[par] += t3t

        P.op("q", None, waits=out_toks)
        GROUP_SEMS = {"sS0", "sB0", "sS2", "sWT", "sWG", "sWU", "sWD", "sMB", "gBI"}

        @block.sync
        def _(q):
            P.emit("q", q, sems, GROUP_SEMS)

        @block.scalar
        def _(s):
            P.emit("s", s, sems, GROUP_SEMS)

        @block.vector
        def _(v):
            P.emit("v", v, sems, GROUP_SEMS)

        @block.gpsimd
        def _(g):
            P.emit("g", g, sems, GROUP_SEMS)

        @block.tensor
        def _(t):
            P.emit("t", t, sems, GROUP_SEMS)

    return nc


_NC_CACHE = None


def kernel(L0_attn_Z, L0_attn_decay, L0_attn_W, L0_attn_b, L0_attn_gamma,
           L0_attn_beta, L0_ffn_clip_x, L0_ffn_clip_down, L0_ffn_w_gate,
           L0_ffn_w_up, L0_ffn_w_down, L0_c, L0_wmod):
    global _NC_CACHE
    Zf = np.ascontiguousarray(np.asarray(L0_attn_Z, dtype=np.float32))
    ks = np.arange(16, 64)
    pw = (2.0 ** (-ks)).astype(np.float32).reshape(1, 48)
    cvals = (1.5 * 2.0 ** ks).astype(np.float32).reshape(1, 48)
    common = {
        "decay": np.asarray(L0_attn_decay, dtype=np.float32).reshape(1, 1),
        "W": np.ascontiguousarray(np.asarray(L0_attn_W, dtype=np.float32)),
        "bias": np.asarray(L0_attn_b, dtype=np.float32).reshape(1, D),
        "gamma": np.asarray(L0_attn_gamma, dtype=np.float32).reshape(1, D),
        "beta": np.asarray(L0_attn_beta, dtype=np.float32).reshape(1, D),
        "clip_x": np.asarray(L0_ffn_clip_x, dtype=np.float32).reshape(1, 1),
        "clip_d": np.asarray(L0_ffn_clip_down, dtype=np.float32).reshape(1, 1),
        "wg": np.ascontiguousarray(np.asarray(L0_ffn_w_gate, dtype=np.float32)),
        "wu": np.ascontiguousarray(np.asarray(L0_ffn_w_up, dtype=np.float32)),
        "wd": np.ascontiguousarray(np.asarray(L0_ffn_w_down, dtype=np.float32)),
        "wmod": np.ascontiguousarray(np.asarray(L0_wmod, dtype=np.float32)),
        "pw": pw,
        "cvals": cvals,
        "eye": np.eye(128, dtype=np.float32),
    }
    cc = np.asarray(L0_c, dtype=np.float32)
    in_maps = []
    for i in range(NCORES):
        m = dict(common)
        m["Z"] = np.ascontiguousarray(Zf[i * BL:(i + 1) * BL].reshape(R, D))
        m["cc_"] = np.ascontiguousarray(cc[i * BL:(i + 1) * BL])
        in_maps.append(m)
    if _NC_CACHE is None:
        _NC_CACHE = build_nc()
    res = run_bass_kernel_spmd(_NC_CACHE, in_maps, core_ids=list(range(NCORES)))
    out = np.empty((B, T, D), dtype=np.float32)
    for i in range(NCORES):
        out[i * BL:(i + 1) * BL] = res.results[i]["y"].reshape(BL, T, D)
    return out


# version 37
# speedup vs baseline: 1.0096x; 1.0096x over previous
import numpy as np
import concourse.bass as bass
import concourse.mybir as mybir
from concourse.bass_utils import run_bass_kernel_spmd

F32 = mybir.dt.float32
BF16 = mybir.dt.bfloat16
AF = mybir.ActivationFunctionType
ALU = mybir.AluOpType
AX = mybir.AxisListType

NCORES = 8
CC_WORLD = 8
B, T, D, I = 16, 2048, 1024, 2816
BL = B // NCORES
R = BL * T
SEG = 512
NH = T // SEG
TPH = SEG // 128
DC = D // 128
IC = I // 128
GT = 512
NG = R // GT
GPS = NG // BL
MD = 6 * D


class Sched:
    def __init__(self):
        self.ops = []
        self.cnt = {}

    def op(self, eng, fn, waits=(), inc=None, d=1):
        tok = None
        if inc is not None:
            self.cnt[inc] = self.cnt.get(inc, 0) + d
            tok = (inc, self.cnt[inc])
        self.ops.append((eng, fn, [w for w in waits if w is not None], inc, d))
        return tok

    def emit(self, name, eng, sems, group_sems=()):
        last = {}
        for e, fn, waits, inc, d in self.ops:
            if e != name:
                continue
            need = {}
            for sn, val in waits:
                if sn in group_sems:
                    val = self.cnt[sn]
                if val > need.get(sn, -1):
                    need[sn] = val
            for sn, val in need.items():
                if last.get(sn, -1) >= val:
                    continue
                eng.wait_ge(sems[sn], val)
                last[sn] = val
            inst = fn(eng) if fn is not None else None
            if inc is not None:
                inst.then_inc(sems[inc], d)


class Pool:
    def __init__(self, n, seed=None):
        self.n = n
        self.wr = [seed] * n
        self.rd = [[] for _ in range(n)]
        self.i = 0

    def acquire(self):
        s = self.i % self.n
        self.i += 1
        w = list(self.rd[s]) + ([self.wr[s]] if self.wr[s] else [])
        self.rd[s] = []
        return s, w

    def wrote(self, s, tok):
        self.wr[s] = tok

    def read(self, s, tok):
        if tok is not None:
            self.rd[s].append(tok)


def build_nc():
    nc = bass.Bass("TRN2", target_bir_lowering=False, debug=False)
    P = Sched()

    hZ = nc.dram_tensor("Z", [R, D], F32, kind="ExternalInput")
    hdec = nc.dram_tensor("decay", [1, 1], F32, kind="ExternalInput")
    hW = nc.dram_tensor("W", [D, D], F32, kind="ExternalInput")
    hb = nc.dram_tensor("bias", [1, D], F32, kind="ExternalInput")
    hg = nc.dram_tensor("gamma", [1, D], F32, kind="ExternalInput")
    hbe = nc.dram_tensor("beta", [1, D], F32, kind="ExternalInput")
    hcx = nc.dram_tensor("clip_x", [1, 1], F32, kind="ExternalInput")
    hcd = nc.dram_tensor("clip_d", [1, 1], F32, kind="ExternalInput")
    hwg = nc.dram_tensor("wg", [I, D], F32, kind="ExternalInput")
    hwu = nc.dram_tensor("wu", [I, D], F32, kind="ExternalInput")
    hwd = nc.dram_tensor("wd", [D, I], F32, kind="ExternalInput")
    hc = nc.dram_tensor("cc_", [BL, D], F32, kind="ExternalInput")
    hwm = nc.dram_tensor("wmod", [D, MD], F32, kind="ExternalInput")
    hpw = nc.dram_tensor("pw", [1, 48], F32, kind="ExternalInput")
    hcv = nc.dram_tensor("cvals", [1, 48], F32, kind="ExternalInput")
    hey = nc.dram_tensor("eye", [128, 128], F32, kind="ExternalInput")
    hy = nc.dram_tensor("y", [R, D], F32, kind="ExternalOutput")

    pkW = nc.dram_tensor("pkW", [D, D], BF16, kind="Internal")
    pkg = nc.dram_tensor("pkg", [I, D], BF16, kind="Internal")
    pku = nc.dram_tensor("pku", [I, D], BF16, kind="Internal")
    pkd = nc.dram_tensor("pkd", [D, I], BF16, kind="Internal")
    spx2 = nc.dram_tensor("spx2", [R, D], F32, kind="Internal")
    sph2 = nc.dram_tensor("sph2", [R, D], BF16, kind="Internal")
    sphh = nc.dram_tensor("sphh", [NG * IC, 128, GT], BF16, kind="Internal")
    hsf = nc.dram_tensor("hsf", [16, 1024], F32, kind="Internal")
    hsb = nc.dram_tensor("hsb", [96, 128], BF16, kind="Internal")
    hsc = nc.dram_tensor("hsc", [12, 1024], BF16, kind="Internal")
    cc1i = nc.dram_tensor("cc1i", [1, 1], F32, kind="Internal")
    cc1o = nc.dram_tensor("cc1o", [1, 1], F32, kind="Internal", addr_space="Shared" if CC_WORLD > 4 else "Local")
    cc2i = nc.dram_tensor("cc2i", [1, 1], F32, kind="Internal")
    cc2o = nc.dram_tensor("cc2o", [1, 1], F32, kind="Internal", addr_space="Shared" if CC_WORLD > 4 else "Local")
    pkuT = nc.dram_tensor("pkuT", [128, 8192], BF16, kind="Internal")
    cc1p = nc.dram_tensor("cc1p", [1, 1], F32, kind="Internal", addr_space="Shared" if CC_WORLD > 4 else "Local")
    cc2p = nc.dram_tensor("cc2p", [1, 1], F32, kind="Internal", addr_space="Shared" if CC_WORLD > 4 else "Local")

    Z, Wm, bia, gam, bet = hZ.ap(), hW.ap(), hb.ap(), hg.ap(), hbe.ap()
    dec, cx, cd = hdec.ap(), hcx.ap(), hcd.ap()
    wg_, wu_, wd_, c_, wm_ = hwg.ap(), hwu.ap(), hwd.ap(), hc.ap(), hwm.ap()
    pw_, cv_, y_, ey_ = hpw.ap(), hcv.ap(), hy.ap(), hey.ap()

    from contextlib import ExitStack
    with ExitStack() as stack:
        en = stack.enter_context
        wa = en(nc.sbuf_tensor("wa", [128, 45056], BF16))
        fmb = en(nc.sbuf_tensor("fmb", [128, 8192], BF16))
        sfm = en(nc.sbuf_tensor("sfm", [128, 8192], BF16))
        x0w = en(nc.sbuf_tensor("x0w", [128, 4 * 1024], F32))
        scr = en(nc.sbuf_tensor("scr", [128, 3072], F32))
        h1sb = en(nc.sbuf_tensor("h1sb", [128, 4 * 1024], BF16))
        xhsb = en(nc.sbuf_tensor("xhsb", [128, 4 * 1024], BF16))
        ynb2 = en(nc.sbuf_tensor("ynb2", [128, 2 * 1024], BF16))
        repb = en(nc.sbuf_tensor("repb", [128, 10 * 1024], BF16))
        sep = en(nc.sbuf_tensor("sep", [128, 5248], BF16))
        smf = en(nc.sbuf_tensor("smf", [128, 512], F32))
        hgb = en(nc.sbuf_tensor("hgb", [128, 2048], BF16))
        hho = en(nc.sbuf_tensor("hho", [128, 1024], BF16))
        ebuf = en(nc.sbuf_tensor("ebuf", [128, 128], BF16))
        p0 = en(nc.psum_tensor("p0", [128, 512], F32))
        p1 = en(nc.psum_tensor("p1", [128, 512], F32))
        p2 = en(nc.psum_tensor("p2", [128, 512], F32))
        p3 = en(nc.psum_tensor("p3", [128, 512], F32))
        p4 = en(nc.psum_tensor("p4", [128, 512], F32))
        p5 = en(nc.psum_tensor("p5", [128, 512], F32))
        p6 = en(nc.psum_tensor("p6", [128, 512], F32))
        p7 = en(nc.psum_tensor("p7", [128, 512], F32))
        SEM_NAMES = ["pe", "ve", "se", "ge", "cc",
                     "sZ0", "sZ1", "sE0", "sE1", "sT0", "sT1",
                     "sS0", "sB0", "sS2", "sS3", "sWT", "sWG", "sWU", "sWD",
                     "sCT", "sMB", "sMO", "sRP",
                     "sH0", "sH1", "sL0", "sL1", "sX0", "sX1", "sQ0", "sQ1", "sTA0", "sTA1",
                     "aX20", "aX21", "aH20", "aH21", "aHH0", "aHH1", "aP0", "aP1", "aO0", "aO1", "sBE", "sGR",
                     "gWP", "gWM0", "gWM1", "gWM2", "gWM3", "gWF0", "gWF1", "gCB", "gRB", "gBI", "sEY", "sGR2", "aPT0", "aPT1"]
        sems = {n: en(nc.semaphore(n)) for n in SEM_NAMES}
        block = en(nc.Block())
        psum = [p0, p1, p2, p3, p4, p5, p6, p7]
        wt = wa[:, 36864:45056]

        C_CB, C_HI, C_LO, C_A, C_SX, C_SD = 0, 1, 2, 3, 4, 5
        C_IS1, C_S2B, C_IS2 = 6, 7, 8
        C_AM, C_AN, C_T0, C_T1, C_EPS = 9, 10, 11, 12, 13
        O_CA1, O_CB1, O_SB = 16, 32, 48
        O_SCG, O_SCU, O_SCD = 64, 86, 108
        O_SGX, O_SUX = 128, 150
        O_TAIL = 176
        O_MX = 184
        O_ST = 192
        O_EPS, O_DLT, O_CV = 336, 384, 432
        O_SCL = 480
        O_CT = 488

        def smv(c):
            return smf[:, c:c + 1]

        S_S1, S_CT, S_MODT, S_CTL = 0, 3072, 3088, 5232
        SWB = [2048, 3184]
        S_CSIL = 3184
        ga_tm = x0w[0:2, 1024:2048]
        sgn = x0w[0:2, 3072:4096]
        repr_ = x0w[0:2, 0:1024]
        gb_tm = sep[0:2, 4208:5232]
        c_sil = sep[0:2, S_CSIL:S_CSIL + 1024]
        modT = sep[:, S_MODT:S_MODT + 96]
        gam2 = scr[0:2, 2048:3072]

        teps = P.op("v", lambda v: v.memset(smf[:, 13:14], 1e-5), inc="ve")
        t_pw = P.op("q", lambda q: q.dma_start(out=smf[0:1, O_EPS:O_EPS + 48], in_=pw_[:, :]), inc="sS0", d=16)
        t_cv = P.op("q", lambda q: q.dma_start(out=smf[0:1, O_CV:O_CV + 48], in_=cv_[:, :]), inc="sS0", d=16)
        t_dc = P.op("q", lambda q: q.dma_start(out=smf[0:1, O_SCL:O_SCL + 1], in_=dec[:, :]), inc="sS0", d=16)
        t_cx = P.op("q", lambda q: q.dma_start(out=smf[0:1, O_SCL + 1:O_SCL + 2], in_=cx[:, :]), inc="sS0", d=16)
        t_cd = P.op("q", lambda q: q.dma_start(out=smf[0:1, O_SCL + 2:O_SCL + 3], in_=cd[:, :]), inc="sS0", d=16)
        t_g2 = P.op("q", lambda q: q.dma_start(out=gam2, in_=bass.AP(hg, 0, [[0, 2], [1, 1024]])), inc="sS0", d=16)
        t_cl = P.op("q", lambda q: q.dma_start(out=x0w[0:2, 0:1024], in_=c_[:, :]), inc="sS0", d=16)

        tg1 = P.op("v", lambda v: v.tensor_scalar(out=smf[0:1, O_EPS:O_EPS + 48], in0=smf[0:1, O_EPS:O_EPS + 48], scalar1=1.0, scalar2=1.0, op0=ALU.add, op1=ALU.is_gt), waits=[t_pw], inc="ve")
        tg2 = P.op("v", lambda v: v.scalar_tensor_tensor(out=smf[0:1, O_DLT:O_DLT + 47], in0=smf[0:1, O_EPS + 1:O_EPS + 48], scalar=-1.0, in1=smf[0:1, O_EPS:O_EPS + 47], op0=ALU.mult, op1=ALU.add), waits=[tg1], inc="ve")
        tg3 = P.op("v", lambda v: v.tensor_tensor(out=smf[0:1, O_DLT:O_DLT + 47], in0=smf[0:1, O_DLT:O_DLT + 47], in1=smf[0:1, O_CV:O_CV + 47], op=ALU.mult), waits=[tg2, t_cv], inc="ve")
        tC = P.op("v", lambda v: v.tensor_reduce(out=smf[0:1, O_CT:O_CT + 1], in_=smf[0:1, O_DLT:O_DLT + 47], axis=AX.X, op=ALU.add), waits=[tg3], inc="ve")
        tsg = P.op("s", lambda s: s.activation(out=smf[0:1, O_SCL + 4:O_SCL + 7], in_=smf[0:1, O_SCL:O_SCL + 3], func=AF.Sigmoid), waits=[t_dc, t_cx, t_cd], inc="se")
        tb1 = P.op("q", lambda q: q.dma_start(out=hsf.ap()[0:1, 0:1], in_=smf[0:1, O_CT:O_CT + 1]), waits=[tC], inc="sB0", d=16)
        tb2 = P.op("q", lambda q: q.dma_start(out=hsf.ap()[0:1, 1:4], in_=smf[0:1, O_SCL + 4:O_SCL + 7]), waits=[tsg], inc="sB0", d=16)
        tCb = P.op("q", lambda q: q.dma_start(out=smv(C_CB), in_=bass.AP(hsf, 0, [[0, 128], [1, 1]])), waits=[tb1], inc="sS2", d=16)
        tac = P.op("q", lambda q: q.dma_start(out=smv(C_A), in_=bass.AP(hsf, 1, [[0, 128], [1, 1]])), waits=[tb2], inc="sS2", d=16)
        tsx = P.op("q", lambda q: q.dma_start(out=smv(C_SX), in_=bass.AP(hsf, 2, [[0, 128], [1, 1]])), waits=[tb2], inc="sS2", d=16)
        tsd = P.op("q", lambda q: q.dma_start(out=smv(C_SD), in_=bass.AP(hsf, 3, [[0, 128], [1, 1]])), waits=[tb2], inc="sS2", d=16)
        thi = P.op("v", lambda v: v.tensor_scalar(out=smv(C_HI), in0=smv(C_CB), scalar1=127.0, scalar2=None, op0=ALU.add), waits=[tCb], inc="ve")
        tlo = P.op("v", lambda v: v.tensor_scalar(out=smv(C_LO), in0=smv(C_CB), scalar1=-128.0, scalar2=None, op0=ALU.add), waits=[tCb], inc="ve")

        tones = P.op("v", lambda v: v.memset(scr[:, 0:2048], 1.0), inc="ve")
        tone2 = P.op("v", lambda v: v.memset(sep[0:2, S_S1:S_S1 + 2048], 1.0), inc="ve")
        ts1w = P.op("v", lambda v: v.tensor_tensor_scan(out=fmb[:, 0:2048], data0=bass.AP(smf, C_A, [[512, 128], [0, 2048]]), data1=scr[:, 0:2048], initial=0.0, op0=ALU.mult, op1=ALU.add), waits=[tones, tac], inc="ve")
        ts1 = P.op("v", lambda v: v.tensor_copy(out=sep[0:1, S_S1:S_S1 + 2048], in_=fmb[0:1, 0:2048]), waits=[ts1w, tone2], inc="ve")

        tWp = P.op("g", lambda g: g.dma_start(out=pkW.ap()[:, :], in_=Wm[:, :]), inc="gWP", d=16)
        tWt = []
        for dcc in range(DC):
            tWt.append(P.op("q", lambda q, dcc=dcc: q.dma_start_transpose(wt[:, dcc * 1024:(dcc + 1) * 1024], pkW.ap()[:, dcc * 128:(dcc + 1) * 128]), waits=[tWp], inc="sWT", d=16))

        c32 = x0w[0:2, 1024:2048]
        tcs = P.op("s", lambda s: s.activation(out=c32, in_=x0w[0:2, 0:1024], func=AF.Silu), waits=[t_cl], inc="se")
        tch = P.op("v", lambda v: v.tensor_copy(out=c_sil, in_=c32), waits=[tcs], inc="ve")
        c_lo = x0w[0:2, 0:1024]
        tclo = P.op("v", lambda v: v.tensor_tensor(out=c_lo, in0=c32, in1=c_sil, op=ALU.subtract), waits=[tch], inc="ve")
        tcb = P.op("g", lambda g: g.dma_start(out=hsc.ap()[0:2, :], in_=c_sil), waits=[tch], inc="gCB", d=16)
        tcb2 = P.op("g", lambda g: g.dma_start(out=hsc.ap()[8:10, :], in_=c_lo), waits=[tclo], inc="gRB", d=16)
        tct = P.op("q", lambda q: q.dma_start_transpose(sep[:, S_CT:S_CT + 16], bass.AP(hsc, 0, [[128, 16], [1, 128]])), waits=[tcb], inc="sCT", d=16)
        tct2 = P.op("q", lambda q: q.dma_start_transpose(sep[:, S_CTL:S_CTL + 16], bass.AP(hsc, 8 * 1024, [[128, 16], [1, 128]])), waits=[tcb2], inc="sCT", d=16)

        mod_tm = wa[0:2, 26624:26624 + MD]
        tmodev = [None, None]
        wm_slot_tok = [None, None, None, None]
        wm32war = [None, None]
        GAGB_N = {4, 5, 10, 11}
        for n in range(12):
            pb = psum[n % 2]
            mmtok = None
            for dcc in range(DC):
                slot = (n * DC + dcc) % 4
                hi = wa[:, 8192 + 1024 * slot:8192 + 1024 * slot + 512]
                lo = wa[:, 8192 + 1024 * slot + 512:8192 + 1024 * (slot + 1)]
                wld = [wm_slot_tok[slot]] if wm_slot_tok[slot] else []
                if n in GAGB_N:
                    f32s = slot % 2
                    wm32 = x0w[:, 2048 + 512 * f32s:2048 + 512 * (f32s + 1)]
                    w32 = [wm32war[f32s]] if wm32war[f32s] else []
                    tld = P.op("g", lambda g, n=n, dcc=dcc, wm32=wm32: g.dma_start(out=wm32, in_=wm_[dcc * 128:(dcc + 1) * 128, n * 512:(n + 1) * 512]), waits=w32, inc="gWF0" if f32s == 0 else "gWF1", d=16)
                    thh_ = P.op("v", lambda v, wm32=wm32, hi=hi: v.tensor_copy(out=hi, in_=wm32), waits=[tld] + wld, inc="ve")
                    tll = P.op("v", lambda v, wm32=wm32, hi=hi, lo=lo: v.tensor_tensor(out=lo, in0=wm32, in1=hi, op=ALU.subtract), waits=[thh_], inc="ve")
                    wm32war[f32s] = tll
                    def mm1(t, dcc=dcc, hi=hi, pb=pb):
                        return t.matmul(pb[0:2, 0:512], bass.AP(sep, S_CT + dcc, [[5248, 128], [8, 2]]), hi, start=(dcc == 0), stop=False)
                    def mm2(t, dcc=dcc, hi=hi, pb=pb):
                        return t.matmul(pb[0:2, 0:512], bass.AP(sep, S_CTL + dcc, [[5248, 128], [8, 2]]), hi, start=False, stop=False)
                    def mm3(t, dcc=dcc, lo=lo, pb=pb):
                        return t.matmul(pb[0:2, 0:512], bass.AP(sep, S_CT + dcc, [[5248, 128], [8, 2]]), lo, start=False, stop=(dcc == DC - 1))
                    w = [thh_, tct, tct2]
                    if dcc == 0 and tmodev[n % 2] is not None:
                        w.append(tmodev[n % 2])
                    P.op("t", mm1, waits=w, inc="pe")
                    P.op("t", mm2, waits=[thh_, tct2], inc="pe")
                    mmtok = P.op("t", mm3, waits=[tll], inc="pe")
                    wm_slot_tok[slot] = mmtok
                else:
                    tld = P.op("g", lambda g, n=n, dcc=dcc, hi=hi: g.dma_start(out=hi, in_=wm_[dcc * 128:(dcc + 1) * 128, n * 512:(n + 1) * 512]), waits=wld, inc="gWM%d" % slot, d=16)
                    def mmp(t, dcc=dcc, hi=hi, pb=pb):
                        return t.matmul(pb[0:2, 0:512], bass.AP(sep, S_CT + dcc, [[5248, 128], [8, 2]]), hi, start=(dcc == 0), stop=(dcc == DC - 1))
                    w = [tld, tct, tct2]
                    if dcc == 0 and tmodev[n % 2] is not None:
                        w.append(tmodev[n % 2])
                    mmtok = P.op("t", mmp, waits=w, inc="pe")
                    wm_slot_tok[slot] = mmtok
            tmodev[n % 2] = P.op("s", lambda s, n=n, pb=pb: s.activation(out=mod_tm[:, n * 512:(n + 1) * 512], in_=pb[0:2, 0:512], func=AF.Identity), waits=[mmtok], inc="se")
        tmodA, tmodB = tmodev

        tmb = []
        for s_ in range(2):
            tmb.append(P.op("q", lambda q, s_=s_: q.dma_start(out=bass.AP(hsb, s_ * 128, [[256, 48], [1, 128]]), in_=mod_tm[s_:s_ + 1, :]), waits=[tmodA, tmodB], inc="sMB", d=16))
        tmodT = P.op("q", lambda q: q.dma_start_transpose(modT, hsb.ap()[:, :]), waits=tmb, inc="sMO", d=16)
        tca1 = P.op("v", lambda v: v.tensor_scalar(out=smf[:, O_CA1:O_CA1 + 16], in0=modT[:, 16:32], scalar1=1.0, scalar2=None, op0=ALU.add), waits=[tmodT], inc="ve")
        tcb1 = P.op("v", lambda v: v.tensor_scalar(out=smf[:, O_CB1:O_CB1 + 16], in0=modT[:, 64:80], scalar1=1.0, scalar2=None, op0=ALU.add), waits=[tmodT], inc="ve")
        tsbf = P.op("v", lambda v: v.tensor_copy(out=smf[:, O_SB:O_SB + 16], in_=modT[:, 48:64]), waits=[tmodT], inc="ve")

        def eff(src_off, dst, prev):
            t1 = P.op("s", lambda s, o=src_off, dst=dst: s.activation(out=dst, in_=mod_tm[0:2, o:o + 1024], func=AF.Abs), waits=[tmodA, tmodB] + prev, inc="se")
            t2 = P.op("v", lambda v, dst=dst: v.tensor_scalar(out=dst, in0=dst, scalar1=0.25, scalar2=None, op0=ALU.max), waits=[t1], inc="ve")
            t3 = P.op("v", lambda v, o=src_off: v.tensor_scalar(out=sgn, in0=mod_tm[0:2, o:o + 1024], scalar1=0.0, scalar2=None, op0=ALU.is_ge), waits=[tmodA, tmodB] + prev, inc="ve")
            t4 = P.op("v", lambda v: v.tensor_scalar(out=sgn, in0=sgn, scalar1=2.0, scalar2=-1.0, op0=ALU.mult, op1=ALU.add), waits=[t3], inc="ve")
            t5 = P.op("v", lambda v, dst=dst: v.tensor_tensor(out=dst, in0=dst, in1=sgn, op=ALU.mult), waits=[t2, t4], inc="ve")
            return t5
        tga = eff(2048, ga_tm, [])
        tgb = eff(5120, gb_tm, [tga])

        rep_toks = {}

        def make_rep(name, rrow, mk, roff, srcap=None):
            tmk = mk()
            sap = repr_ if srcap is None else srcap
            tb = P.op("g", lambda g, rrow=rrow, sap=sap: g.dma_start(out=hsc.ap()[rrow:rrow + 2, :], in_=sap), waits=[tmk], inc="gRB", d=16)
            toks = []
            for s_ in range(2):
                dst = repb[:, (roff + s_) * 1024:(roff + s_ + 1) * 1024]
                toks.append(P.op("q", lambda q, s_=s_, rrow=rrow, dst=dst: q.dma_start(out=dst, in_=bass.AP(hsc, (rrow + s_) * 1024, [[0, 128], [1, 1024]])), waits=[tb], inc="sRP", d=16))
            rep_toks[name] = toks
            return tmk

        tgg = make_rep("gga", 0, lambda: P.op("v", lambda v: v.tensor_tensor(out=repr_, in0=gam2, in1=ga_tm, op=ALU.mult), waits=[tga, t_g2, tgb, tcs, tcb, tcb2], inc="ve"), 0)
        t_b2 = P.op("q", lambda q: q.dma_start(out=gam2, in_=bass.AP(hbe, 0, [[0, 2], [1, 1024]])), waits=[tgg] + rep_toks["gga"], inc="sBE", d=16)
        tbgm = make_rep("bga", 2, lambda: P.op("v", lambda v: v.tensor_tensor(out=repr_, in0=gam2, in1=ga_tm, op=ALU.mult), waits=[t_b2, tga, tgg] + rep_toks["gga"], inc="ve"), 2)
        tcb1r = make_rep("cb1", 6, lambda: P.op("v", lambda v: v.tensor_scalar(out=repr_, in0=mod_tm[0:2, 4096:5120], scalar1=1.0, scalar2=None, op0=ALU.add), waits=[tmodA, tmodB, tbgm] + rep_toks["bga"], inc="ve"), 6)
        tsbr = make_rep("sbr", 10, lambda: P.op("v", lambda v: v.tensor_copy(out=repr_, in_=mod_tm[0:2, 3072:4096]), waits=[tmodA, tmodB, tcb1r] + rep_toks["cb1"], inc="ve"), 8)


        swb_tok = []
        for s_ in range(2):
            for eh in range(2):
                pb = psum[4 + eh]
                toks = []
                for dcc in range(DC):
                    def mmsw(t, s_=s_, dcc=dcc, eh=eh, pb=pb):
                        return t.matmul(pb[0:1, 0:512], modT[:, 2 * dcc + s_:2 * dcc + s_ + 1], wt[:, dcc * 1024 + eh * 512:dcc * 1024 + eh * 512 + 512], start=(dcc == 0), stop=(dcc == DC - 1))
                    w = [tmodT, tWt[dcc]]
                    if dcc == 0 and s_ == 1:
                        w.append(swb_tok[eh])
                    toks.append(P.op("t", mmsw, waits=w, inc="pe"))
                swb_tok.append(P.op("s", lambda s, s_=s_, eh=eh, pb=pb: s.activation(out=sep[0:1, SWB[s_] + eh * 512:SWB[s_] + eh * 512 + 512], in_=pb[0:1, 0:512], func=AF.Identity), waits=[toks[-1]], inc="se"))
        tbias = []
        for s_ in range(2):
            tbias.append(P.op("g", lambda g, s_=s_: g.dma_start(out=sep[1:2, SWB[s_]:SWB[s_] + 1024], in_=bia[:, :]), waits=[tcs] if s_ == 0 else [], inc="gBI", d=16))

        wq_raw = Pool(2)
        nat_pool = Pool(2)
        park_done = {"wg": [], "wu": [], "wd": []}
        wgT_tok = [[] for _ in range(DC)]
        wuT_tok = [[] for _ in range(DC)]
        wTR_war = [[None] * 4, [None] * 4]
        wlast_bank = [None, None]
        stg_pool = Pool(2)
        stg_last = [None, None]

        def prep_gu_tile(name, src, park, scoff, ic):
                slot, w = wq_raw.acquire()
                raw = scr[:, slot * 1024:(slot + 1) * 1024]
                tl = P.op("g", lambda g, ic=ic, raw=raw, src=src: g.dma_start(out=raw, in_=src[ic * 128:(ic + 1) * 128, :]), waits=w + ([tones, ts1] if ic < 2 else []), inc="sQ0" if slot == 0 else "sQ1", d=16)
                wq_raw.wrote(slot, tl)
                tmx = P.op("v", lambda v, raw=raw: v.tensor_reduce(out=smv(C_T0), in_=raw, axis=AX.X, op=ALU.max, apply_absolute_value=True), waits=[tl], inc="ve")
                tsc = P.op("v", lambda v, ic=ic: v.tensor_scalar(out=smf[:, scoff + ic:scoff + ic + 1], in0=smv(C_T0), scalar1=1.0 / 127.0, scalar2=None, op0=ALU.mult), waits=[tmx], inc="ve")
                trc = P.op("v", lambda v, ic=ic: v.reciprocal(out=smv(C_T1), in_=smf[:, scoff + ic:scoff + ic + 1]), waits=[tsc], inc="ve")
                tq = P.op("s", lambda s, raw=raw: s.activation(out=raw, in_=raw, func=AF.Identity, scale=smv(C_T1), bias=smv(C_CB)), waits=[trc, tCb, tmx], inc="se")
                nslot, wn = nat_pool.acquire()
                nat = hgb[:, nslot * 1024:(nslot + 1) * 1024]
                tqq = P.op("v", lambda v, raw=raw, nat=nat: v.tensor_scalar(out=nat, in0=raw, scalar1=smv(C_CB), scalar2=None, op0=ALU.subtract), waits=[tq] + wn, inc="ve")
                nat_pool.wrote(nslot, tqq)
                wq_raw.read(slot, tqq)
                if name in ("wu", "wg"):
                    tpk = P.op("s", lambda s, ic=ic, nat=nat, park=park: s.dma_start(out=park.ap()[ic * 128:(ic + 1) * 128, :], in_=nat), waits=[tqq], inc="aP0" if nslot == 0 else "aP1", d=16)
                    nat_pool.read(nslot, tpk)
                    park_done[name].append(tpk)
                    return
                bank = psum[2 + nslot]
                rw = ([tmodA, tmodB] + [t for t in wm_slot_tok if t]) if name == "wg" else list(mod_tm_readers)
                for dcc in range(DC):
                    creg = dcc % 4
                    wtr = P.op("t", lambda t, bank=bank, creg=creg, nat=nat, dcc=dcc: t.matmul(bank[:, creg * 128:(creg + 1) * 128], nat[:, dcc * 128:(dcc + 1) * 128], ebuf[:, 0:128], start=True, stop=True), waits=[tqq, teye] + ([wTR_war[nslot][creg]] if wTR_war[nslot][creg] else []), inc="pe")
                    nat_pool.read(nslot, wtr)
                    dst = (dcc * 2816 + ic * 128) if name == "wg" else (22528 + dcc * 2816 + ic * 128)
                    if dst + 128 <= 36864:
                        tev = P.op("s", lambda s, bank=bank, creg=creg, dst=dst: s.activation(out=wa[:, dst:dst + 128], in_=bank[:, creg * 128:(creg + 1) * 128], func=AF.Identity), waits=[wtr] + rw, inc="se")
                        (wgT_tok if name == "wg" else wuT_tok)[dcc].append(tev)
                    else:
                        ssl, sw = stg_pool.acquire()
                        hst = hho[:, ssl * 128:(ssl + 1) * 128]
                        tev = P.op("s", lambda s, bank=bank, creg=creg, hst=hst: s.activation(out=hst, in_=bank[:, creg * 128:(creg + 1) * 128], func=AF.Identity), waits=[wtr] + sw, inc="se")
                        tpkT = P.op("s", lambda s, hst=hst, dst=dst: s.dma_start(out=pkuT.ap()[:, dst - 36864:dst - 36864 + 128], in_=hst), waits=[tev], inc="aPT0" if ssl == 0 else "aPT1", d=16)
                        stg_pool.wrote(ssl, tpkT)
                        stg_last[ssl] = tpkT
                    wTR_war[nslot][creg] = tev
                    wlast_bank[nslot] = tev
        PREP_JOBS = []
        for _ic in range(IC):
            PREP_JOBS.append(lambda ic=_ic: prep_gu_tile("wg", wg_, pkg, O_SCG, ic))
        for _ic in range(IC):
            PREP_JOBS.append(lambda ic=_ic: prep_gu_tile("wu", wu_, pku, O_SCU, ic))
        PREP_WD = []
        for _hcn in range(DC):
            PREP_WD.append(lambda hcn=_hcn: prep_wd_tile(hcn))

        def prep_wd_tile(hcn):
            mxp = []
            for j, (o, wdt) in enumerate([(0, 1024), (1024, 1024), (2048, 768)]):
                slot, w = wq_raw.acquire()
                raw = scr[:, slot * 1024:slot * 1024 + wdt]
                tl = P.op("g", lambda g, hcn=hcn, o=o, wdt=wdt, raw=raw: g.dma_start(out=raw, in_=wd_[hcn * 128:(hcn + 1) * 128, o:o + wdt]), waits=w, inc="sQ0" if slot == 0 else "sQ1", d=16)
                wq_raw.wrote(slot, tl)
                tmx = P.op("v", lambda v, raw=raw, j=j: v.tensor_reduce(out=smf[:, O_MX + j:O_MX + j + 1], in_=raw, axis=AX.X, op=ALU.max, apply_absolute_value=True), waits=[tl] + ([mxp[0]] if (j and False) else []), inc="ve")
                wq_raw.read(slot, tmx)
                mxp.append(tmx)
            tm2 = P.op("v", lambda v: v.tensor_tensor(out=smv(O_MX), in0=smf[:, O_MX:O_MX + 1], in1=smf[:, O_MX + 1:O_MX + 2], op=ALU.max), waits=mxp, inc="ve")
            tm3 = P.op("v", lambda v: v.tensor_tensor(out=smv(C_T0), in0=smv(O_MX), in1=smf[:, O_MX + 2:O_MX + 3], op=ALU.max), waits=[tm2], inc="ve")
            tsc = P.op("v", lambda v, hcn=hcn: v.tensor_scalar(out=smf[:, O_SCD + hcn:O_SCD + hcn + 1], in0=smv(C_T0), scalar1=1.0 / 127.0, scalar2=None, op0=ALU.mult), waits=[tm3], inc="ve")
            trc = P.op("v", lambda v: v.reciprocal(out=smv(C_T1), in_=smf[:, O_SCD + hcn:O_SCD + hcn + 1] if False else smv(C_T0)), waits=[tsc], inc="ve")
            trc = P.op("v", lambda v, hcn=hcn: v.reciprocal(out=smv(C_T1), in_=smf[:, O_SCD + hcn:O_SCD + hcn + 1]), waits=[tsc, trc], inc="ve")
            for j, (o, wdt) in enumerate([(0, 1024), (1024, 1024), (2048, 768)]):
                slot, w = wq_raw.acquire()
                raw = scr[:, slot * 1024:slot * 1024 + wdt]
                tl = P.op("g", lambda g, hcn=hcn, o=o, wdt=wdt, raw=raw: g.dma_start(out=raw, in_=wd_[hcn * 128:(hcn + 1) * 128, o:o + wdt]), waits=w + [trc], inc="sQ0" if slot == 0 else "sQ1", d=16)
                wq_raw.wrote(slot, tl)
                tq = P.op("s", lambda s, raw=raw: s.activation(out=raw, in_=raw, func=AF.Identity, scale=smv(C_T1), bias=smv(C_CB)), waits=[tl], inc="se")
                nslot, wn = nat_pool.acquire()
                nat = hgb[:, nslot * 1024:nslot * 1024 + wdt]
                tqq = P.op("v", lambda v, raw=raw, nat=nat: v.tensor_scalar(out=nat, in0=raw, scalar1=smv(C_CB), scalar2=None, op0=ALU.subtract), waits=[tq] + wn, inc="ve")
                nat_pool.wrote(nslot, tqq)
                wq_raw.read(slot, tqq)
                tpk = P.op("s", lambda s, hcn=hcn, o=o, wdt=wdt, nat=nat: s.dma_start(out=pkd.ap()[hcn * 128:(hcn + 1) * 128, o:o + wdt], in_=nat), waits=[tqq], inc="aP0" if nslot == 0 else "aP1", d=16)
                nat_pool.read(nslot, tpk)
                park_done["wd"].append(tpk)

        mod_tm_readers = [tga, tgb, tmodA, tmodB] + tmb
        teye = P.op("g", lambda g: g.dma_start(out=ebuf[:, 0:128], in_=ey_), inc="sEY", d=16)
        prev_evac = [[] for _ in range(DC)]
        prev_evac[0] = [swb_tok[0], swb_tok[2]]
        prev_evac[1] = [swb_tok[0], swb_tok[2]]
        prev_evac[2] = [swb_tok[1], swb_tok[3]]
        prev_evac[3] = [swb_tok[1], swb_tok[3]]
        tr_tok = [[None] * DC, [None] * DC]
        seedA = rep_toks["sbr"][1]
        zpool = Pool(2, seed=seedA)
        epool = Pool(2, seed=seedA)
        ynpool = Pool(2)
        h1pool = Pool(3)
        stpool = Pool(6)
        fm_hist = [[ts1], [ts1]]
        sfm_hist = [[], []]
        h2sp_done = []
        x2sp_done = {}
        h1war = []
        xhwar = []
        xhwar_prev = []
        ypar_reader = [tmodA, tmodB]
        scan_tail = [None] * DC
        amax_last = None
        tam0 = P.op("v", lambda v: v.memset(smv(C_AM), -3e38), inc="ve")
        tan0 = P.op("v", lambda v: v.memset(smv(C_AN), 3e38), waits=[tam0], inc="ve")
        amax_last = tan0

        def ln_stats(src, w, k, wb=None):
            o = O_ST + 24 * k
            t1 = P.op("v", lambda v, src=src, o=o: v.bn_stats(out=smf[:, o:o + 6], in_=src[:, 0:512]), waits=w, inc="ve")
            t2 = P.op("v", lambda v, src=src, o=o: v.bn_stats(out=smf[:, o + 6:o + 12], in_=src[:, 512:1024]), waits=(w if wb is None else wb), inc="ve")
            t3 = P.op("v", lambda v, o=o: v.bn_aggr(out=smf[:, o + 12:o + 14], in_=smf[:, o:o + 12]), waits=[t1, t2], inc="ve")
            t4 = P.op("s", lambda s, o=o: s.activation(out=smf[:, o + 14:o + 15], in_=smf[:, o + 13:o + 14], func=AF.Sqrt, bias=smv(C_EPS)), waits=[t3, teps], inc="se")
            t5 = P.op("v", lambda v, o=o: v.reciprocal(out=smf[:, o + 15:o + 16], in_=smf[:, o + 14:o + 15]), waits=[t4], inc="ve")
            t6 = P.op("v", lambda v, o=o: v.tensor_scalar(out=smf[:, o + 16:o + 17], in0=smf[:, o + 12:o + 13], scalar1=smf[:, o + 15:o + 16], scalar2=-1.0, op0=ALU.mult, op1=ALU.mult), waits=[t5], inc="ve")
            return smf[:, o + 15:o + 16], smf[:, o + 16:o + 17], t6

        for s_ in range(BL):
            for h in range(NH):
                par = (s_ * NH + h) % 2
                fo = par * 4096
                fmbwar = fm_hist[par]
                fm_hist[par] = []
                evac_half = [[None] * DC, [None] * DC]
                tx2_list = []
                for tt in range(TPH):
                    gt = s_ * (NH * TPH) + h * TPH + tt
                    zslot, w = zpool.acquire()
                    zt = x0w[:, zslot * 1024:(zslot + 1) * 1024]
                    extra = [tcs] if (zslot == 0 and gt < 2) else []
                    tz = P.op("q", lambda q, gt=gt, zt=zt: q.dma_start(out=zt, in_=Z[gt * 128:(gt + 1) * 128, :]), waits=w + extra, inc="sZ0" if zslot == 0 else "sZ1", d=16)
                    zpool.wrote(zslot, tz)
                    kst, wst = stpool.acquire()
                    r_ap, mr_ap, tstat = ln_stats(zt, [tz] + wst, kst)
                    zpool.read(zslot, tstat)
                    h1 = h1sb[:, tt * 1024:(tt + 1) * 1024]
                    th1 = P.op("s", lambda s, zt=zt, h1=h1, r_ap=r_ap, mr_ap=mr_ap: s.activation(out=h1, in_=zt, func=AF.Identity, scale=r_ap, bias=mr_ap), waits=[tstat] + h1war, inc="se")
                    zpool.read(zslot, th1)
                    stpool.wrote(kst, th1)
                    for dcc in range(DC):
                        pb = psum[4 + dcc // 2]
                        col = (dcc % 2) * 256 + (tt % 2) * 128
                        ttr = P.op("t", lambda t, pb=pb, col=col, h1=h1, dcc=dcc: t.matmul(pb[:, col:col + 128], h1[:, dcc * 128:(dcc + 1) * 128], ebuf[:, 0:128], start=True, stop=True), waits=[th1, teye] + prev_evac[dcc], inc="pe")
                        tr_tok[tt % 2][dcc] = ttr
                    if tt % 2 == 1:
                        p_ = tt // 2
                        for dcc in range(DC):
                            pb = psum[4 + dcc // 2]
                            tev = P.op("s", lambda s, pb=pb, dcc=dcc, fo=fo, p_=p_, s_=s_: s.activation(out=fmb[:, fo + dcc * SEG + p_ * 256:fo + dcc * SEG + p_ * 256 + 256], in_=pb[:, (dcc % 2) * 256:(dcc % 2) * 256 + 256], func=AF.Identity, scale=smf[:, O_CA1 + 2 * dcc + s_:O_CA1 + 2 * dcc + s_ + 1]), waits=[tr_tok[0][dcc], tr_tok[1][dcc], tca1] + fmbwar, inc="se")
                            prev_evac[dcc] = [tev]
                            evac_half[p_][dcc] = tev
                scans = []
                sfw = sfm_hist[par]
                sfm_hist[par] = []
                for dcc in range(DC):
                    init = 0.0 if h == 0 else smf[:, O_TAIL + dcc:O_TAIL + dcc + 1]
                    winit = [] if h == 0 else [scan_tail[dcc]]
                    tscan = P.op("v", lambda v, dcc=dcc, fo=fo, init=init: v.tensor_tensor_scan(out=sfm[:, fo + dcc * SEG:fo + (dcc + 1) * SEG], data0=bass.AP(smf, C_A, [[512, 128], [0, SEG]]), data1=fmb[:, fo + dcc * SEG:fo + (dcc + 1) * SEG], initial=init, op0=ALU.mult, op1=ALU.add), waits=[evac_half[0][dcc], evac_half[1][dcc], tac] + sfw + winit, inc="ve")
                    ttl = P.op("v", lambda v, dcc=dcc, fo=fo: v.tensor_copy(out=smf[:, O_TAIL + dcc:O_TAIL + dcc + 1], in_=sfm[:, fo + (dcc + 1) * SEG - 1:fo + (dcc + 1) * SEG]), waits=[tscan] + ([scan_tail[dcc]] if h > 0 else []), inc="ve")
                    scan_tail[dcc] = ttl
                    fm_hist[par].append(tscan)
                    sfm_hist[par].append(ttl)
                    scans.append(tscan)
                pending_post = None
                for tt in range(TPH):
                    gt = s_ * (NH * TPH) + h * TPH + tt
                    mm_l = []
                    for eh in range(2):
                        pb = psum[eh]
                        toks = []
                        for dcc in range(DC):
                            def mmy(t, dcc=dcc, eh=eh, tt=tt, pb=pb, fo=fo):
                                return t.matmul(pb[:, 0:512], sfm[:, fo + dcc * SEG + tt * 128:fo + dcc * SEG + tt * 128 + 128], wt[:, dcc * 1024 + eh * 512:dcc * 1024 + eh * 512 + 512], start=(dcc == 0), stop=False)
                            w = [scans[dcc], tWt[dcc]]
                            if dcc == 0 and ypar_reader[eh] is not None:
                                w.append(ypar_reader[eh])
                            toks.append(P.op("t", mmy, waits=w, inc="pe"))
                        def mmk2(t, eh=eh, pb=pb, s_=s_, gt=gt):
                            trel = (gt % (NH * TPH)) * 128
                            return t.matmul(pb[:, 0:512], sep[0:2, S_S1 + trel:S_S1 + trel + 128], sep[0:2, SWB[s_] + eh * 512:SWB[s_] + eh * 512 + 512], start=False, stop=True)
                        tk2 = P.op("t", mmk2, waits=[ts1, tone2, swb_tok[2 * s_ + eh], tbias[s_], toks[-1]], inc="pe")
                        last_y_mm = [tk2]
                        mm_l.append(tk2)
                        sfm_hist[par].append(tk2)
                    kst, wst = stpool.acquire()
                    oY = O_ST + 24 * kst
                    pbl, pbh = psum[0], psum[1]
                    ty1 = P.op("v", lambda v, pbl=pbl, oY=oY: v.bn_stats(out=smf[:, oY:oY + 6], in_=pbl[:, 0:512]), waits=[mm_l[0]] + wst, inc="ve")
                    ty2 = P.op("v", lambda v, pbh=pbh, oY=oY: v.bn_stats(out=smf[:, oY + 6:oY + 12], in_=pbh[:, 0:512]), waits=[mm_l[1]], inc="ve")
                    ty3 = P.op("v", lambda v, oY=oY: v.bn_aggr(out=smf[:, oY + 12:oY + 14], in_=smf[:, oY:oY + 12]), waits=[ty1, ty2], inc="ve")
                    ty4 = P.op("s", lambda s, oY=oY: s.activation(out=smf[:, oY + 14:oY + 15], in_=smf[:, oY + 13:oY + 14], func=AF.Sqrt, bias=smv(C_EPS)), waits=[ty3, teps], inc="se")
                    ty5 = P.op("v", lambda v, oY=oY: v.reciprocal(out=smf[:, oY + 15:oY + 16], in_=smf[:, oY + 14:oY + 15]), waits=[ty4], inc="ve")
                    ty6 = P.op("v", lambda v, oY=oY: v.tensor_scalar(out=smf[:, oY + 16:oY + 17], in0=smf[:, oY + 12:oY + 13], scalar1=smf[:, oY + 15:oY + 16], scalar2=-1.0, op0=ALU.mult, op1=ALU.mult), waits=[ty5], inc="ve")
                    ysl, wyn = ynpool.acquire()
                    yt = ynb2[:, ysl * 1024:(ysl + 1) * 1024]
                    tyn = []
                    for eh, pb in ((0, pbl), (1, pbh)):
                        tok = P.op("v", lambda v, pb=pb, eh=eh, yt=yt, oY=oY: v.tensor_scalar(out=yt[:, eh * 512:eh * 512 + 512], in0=pb[:, 0:512], scalar1=smf[:, oY + 15:oY + 16], scalar2=smf[:, oY + 16:oY + 17], op0=ALU.mult, op1=ALU.add), waits=[ty6] + wyn, inc="ve")
                        tyn.append(tok)
                        ypar_reader[eh] = tok
                    stpool.wrote(kst, ty6)
                    tt2 = P.op("v", lambda v, yt=yt, s_=s_: v.tensor_tensor(out=yt, in0=yt, in1=repb[:, s_ * 1024:(s_ + 1) * 1024], op=ALU.mult), waits=tyn + rep_toks["gga"], inc="ve")
                    tu2 = P.op("v", lambda v, yt=yt, s_=s_: v.tensor_tensor(out=yt, in0=yt, in1=repb[:, 2048 + s_ * 1024:2048 + (s_ + 1) * 1024], op=ALU.add), waits=[tt2] + rep_toks["bga"], inc="ve")
                    eslot, we = epool.acquire()
                    x2t = x0w[:, (2 + eslot) * 1024:(3 + eslot) * 1024]
                    tz2 = P.op("q", lambda q, gt=gt, x2t=x2t: q.dma_start(out=x2t, in_=Z[gt * 128:(gt + 1) * 128, :]), waits=we, inc="sE0" if eslot == 0 else "sE1", d=16)
                    tx2a = P.op("g", lambda g, x2t=x2t, yt=yt: g.tensor_tensor(out=x2t[:, 0:512], in0=x2t[:, 0:512], in1=yt[:, 0:512], op=ALU.add), waits=[tz2, tu2], inc="ge")
                    tx2b = P.op("g", lambda g, x2t=x2t, yt=yt: g.tensor_tensor(out=x2t[:, 512:1024], in0=x2t[:, 512:1024], in1=yt[:, 512:1024], op=ALU.add), inc="ge")
                    epool.wrote(eslot, tx2b)
                    epool.read(eslot, tx2a)
                    ynpool.wrote(ysl, tx2b)
                    ynpool.read(ysl, tx2a)
                    tx2_list.append(tx2b)
                    tsp = P.op("s", lambda s, gt=gt, x2t=x2t: s.dma_start(out=spx2.ap()[gt * 128:(gt + 1) * 128, :], in_=x2t), waits=[tx2a, tx2b], inc="aX20" if eslot == 0 else "aX21", d=16)
                    x2sp_done[gt] = tsp
                    epool.read(eslot, tsp)

                    def mk_post(gt=gt, tt=tt, x2t=x2t, eslot=eslot, tx2a=tx2a, tx2b=tx2b, s_=s_, par=par):
                        def post():
                            kst2, wst2 = stpool.acquire()
                            r2, mr2, tst2 = ln_stats(x2t, [tx2a] + wst2, kst2, wb=[tx2b] + wst2)
                            epool.read(eslot, tst2)
                            xh = xhsb[:, tt * 1024:(tt + 1) * 1024]
                            txh = P.op("v", lambda v, x2t=x2t, xh=xh, r2=r2, mr2=mr2: v.tensor_scalar(out=xh, in0=x2t, scalar1=r2, scalar2=mr2, op0=ALU.mult, op1=ALU.add), waits=[tst2] + xhwar_prev, inc="ve")
                            epool.read(eslot, txh)
                            stpool.wrote(kst2, txh)
                            tt4 = P.op("v", lambda v, xh=xh, s_=s_: v.tensor_tensor(out=xh, in0=xh, in1=repb[:, 6144 + s_ * 1024:6144 + (s_ + 1) * 1024], op=ALU.mult), waits=[txh] + rep_toks["cb1"], inc="ve")
                            th2 = P.op("v", lambda v, xh=xh, s_=s_: v.tensor_tensor(out=xh, in0=xh, in1=repb[:, 8192 + s_ * 1024:8192 + (s_ + 1) * 1024], op=ALU.add), waits=[tt4] + rep_toks["sbr"], inc="ve")
                            tamx = P.op("v", lambda v, xh=xh, gt=gt: v.tensor_reduce(out=smf[:, 384 + gt:385 + gt], in_=xh, axis=AX.X, op=ALU.max, apply_absolute_value=True), waits=[th2], inc="ve")
                            tspl = P.op("s", lambda s, gt=gt, xh=xh: s.dma_start(out=sph2.ap()[gt * 128:(gt + 1) * 128, :], in_=xh), waits=[th2], inc="aH20" if par == 0 else "aH21", d=16)
                            h2sp_done.append(tspl)
                            xhwar.append(tamx)
                            xhwar.append(tspl)
                            return tamx
                        return post

                    if pending_post is not None:
                        amax_last = pending_post()
                    pending_post = mk_post()
                    for _ in range(2):
                        if PREP_JOBS:
                            PREP_JOBS.pop(0)()
                if pending_post is not None:
                    amax_last = pending_post()
                    pending_post = None
                h1war = list(scans)
                xhwar_prev = list(xhwar)
                xhwar = []

        while PREP_JOBS:
            PREP_JOBS.pop(0)()
        wd_scr_guard = []
        for dcc in range(DC):
            wgT_tok[dcc].append(P.op("q", lambda q, dcc=dcc: q.dma_start_transpose(wa[:, dcc * 2816:(dcc + 1) * 2816], pkg.ap()[:, dcc * 128:(dcc + 1) * 128]), waits=park_done["wg"] + [tmodA, tmodB] + [t for t in wm_slot_tok if t], inc="sWG", d=16))
        for dcc in range(DC):
            wuT_tok[dcc].append(P.op("q", lambda q, dcc=dcc: q.dma_start_transpose(wa[:, 22528 + dcc * 2816:22528 + (dcc + 1) * 2816], pku.ap()[:, dcc * 128:(dcc + 1) * 128]), waits=park_done["wu"] + mod_tm_readers + last_y_mm, inc="sWU", d=16))

        def global_reduce(prev_tok, cci, cco, ccp, hrow, scol, sigcol, iscol, extra_q=(), extra_t8=()):
            t1 = P.op("v", lambda v: v.tensor_scalar(out=smv(C_T0), in0=smv(C_AN), scalar1=-1.0, scalar2=None, op0=ALU.mult), waits=[prev_tok], inc="ve")
            t2 = P.op("v", lambda v: v.tensor_tensor(out=smv(C_T0), in0=smv(C_T0), in1=smv(C_AM), op=ALU.max), waits=[t1], inc="ve")
            t3 = P.op("q", lambda q, hrow=hrow: q.dma_start(out=hsf.ap()[hrow:hrow + 1, 0:128], in_=smv(C_T0)), waits=[t2] + list(extra_q), inc="sGR", d=16)
            t4 = P.op("q", lambda q, hrow=hrow: q.dma_start(out=scr[0:1, 0:128], in_=hsf.ap()[hrow:hrow + 1, 0:128]), waits=[t3, tones, ts1], inc="sGR", d=16)
            t5 = P.op("v", lambda v: v.tensor_reduce(out=scr[0:1, 128:129], in_=scr[0:1, 0:128], axis=AX.X, op=ALU.max), waits=[t4], inc="ve")
            t6 = P.op("q", lambda q, cci=cci: q.dma_start(out=cci.ap()[:, :], in_=scr[0:1, 128:129]), waits=[t5], inc="sGR", d=16)
            t7 = P.op("g", lambda g, cci=cci, cco=cco: g.collective_compute("AllReduce", ALU.max, ins=[cci.ap()[:, :]], outs=[cco.ap()[:, :]], replica_groups=[list(range(CC_WORLD))]), waits=[t6], inc="cc")
            t7b = P.op("g", lambda g, cci=cci, ccp=ccp: g.collective_compute("AllReduce", ALU.max, ins=[cci.ap()[:, :]], outs=[ccp.ap()[:, :]], replica_groups=[list(range(CC_WORLD))]), waits=[t7], inc="cc")
            t8 = P.op("g", lambda g, cco=cco: g.dma_start(out=smv(scol), in_=bass.AP(cco, 0, [[0, 128], [1, 1]])), waits=[t7b] + list(extra_t8), inc="sGR2", d=16)
            t9 = P.op("v", lambda v: v.tensor_scalar(out=smv(scol), in0=smv(scol), scalar1=smv(sigcol), scalar2=1.0 / 127.0, op0=ALU.mult, op1=ALU.mult), waits=[t8, tsx, tsd], inc="ve")
            t9b = P.op("v", lambda v: v.tensor_scalar(out=smv(scol), in0=smv(scol), scalar1=1e-35, scalar2=None, op0=ALU.max), waits=[t9], inc="ve")
            t10 = P.op("v", lambda v: v.reciprocal(out=smv(iscol), in_=smv(scol)), waits=[t9b], inc="ve")
            return t9b, t10
        tcmb = P.op("v", lambda v: v.tensor_reduce(out=smv(C_AM), in_=smf[:, 384:416], axis=AX.X, op=ALU.max), waits=[amax_last], inc="ve")
        h2_all = [("aH20", P.cnt.get("aH20", 0)), ("aH21", P.cnt.get("aH21", 0))]
        x2_all = [("aX20", P.cnt.get("aX20", 0)), ("aX21", P.cnt.get("aX21", 0))]
        fmb_hist = [list(fm_hist[0]), list(fm_hist[1])]
        xq_hist = [list(sfm_hist[0]), list(sfm_hist[1])]
        last_mm_xq = [None, None]

        def emit_tl(g):
            par = g % 2
            base = par * 4096
            w_h2 = fmb_hist[par]
            fmb_hist[par] = []
            toks = []
            for dcc in range(DC):
                h2in = fmb[:, base + dcc * 512:base + (dcc + 1) * 512]
                tl = P.op("q", lambda q, g=g, dcc=dcc, h2in=h2in: q.dma_start_transpose(h2in, sph2.ap()[g * GT:(g + 1) * GT, dcc * 128:(dcc + 1) * 128]), waits=h2_all + w_h2 + ([last_mm_xq[par]] if last_mm_xq[par] else []), inc="sH0" if par == 0 else "sH1", d=16)
                toks.append(tl)
            return toks
        tl_pref = {0: emit_tl(0), 1: emit_tl(1)}
        ts1x, tis1 = global_reduce(tcmb, cc1i, cc1o, cc1p, 9, C_T1, C_SX, C_IS1, extra_t8=[tl_pref[1][-1]])
        tsgx = P.op("v", lambda v: v.tensor_scalar(out=smf[:, O_SGX:O_SGX + 22], in0=smf[:, O_SCG:O_SCG + 22], scalar1=smv(C_T1), scalar2=None, op0=ALU.mult), waits=[ts1x], inc="ve")
        tsux = P.op("v", lambda v: v.tensor_scalar(out=smf[:, O_SUX:O_SUX + 22], in0=smf[:, O_SCU:O_SCU + 22], scalar1=smv(C_T1), scalar2=None, op0=ALU.mult), waits=[ts1x], inc="ve")
        while PREP_WD:
            PREP_WD.pop(0)()
        wd_scr_guard += list(wq_raw.rd[0]) + list(wq_raw.rd[1]) + [t for t in (wq_raw.wr[0], wq_raw.wr[1]) if t]
        parks_all = [("aP0", P.cnt.get("aP0", 0)), ("aP1", P.cnt.get("aP1", 0))]
        tscdw = P.op("q", lambda q: q.dma_start(out=bass.AP(hsf, 8 * 1024, [[1, 128], [128, 8]]), in_=smf[:, O_SCD:O_SCD + 8], allow_slow_non_contiguous=True), waits=parks_all, inc="sS3", d=16)
        tscd2 = P.op("q", lambda q: q.dma_start(out=gam2, in_=bass.AP(hsf, 8 * 1024, [[0, 2], [1, 1024]])), waits=[tscdw, tbgm] + rep_toks["bga"], inc="sS3", d=16)
        make_rep("scdgb", 4, lambda: P.op("v", lambda v: v.tensor_tensor(out=gam2, in0=gam2, in1=gb_tm, op=ALU.mult), waits=[tscd2, tgb, tbgm] + rep_toks["sbr"], inc="ve"), 4, srcap=gam2)

        tam2 = P.op("v", lambda v: v.memset(smv(C_AM), -3e38), waits=[ts1x], inc="ve")
        tan2 = P.op("v", lambda v: v.memset(smv(C_AN), 3e38), waits=[tam2], inc="ve")
        amax_last = tan2
        hh_sp = []
        gu_reader = [None, None, wlast_bank[0], wlast_bank[1]]
        hg_war = [None, None]
        hu_war = [None, None]
        hh_war = [stg_last[0], stg_last[1]]
        tq1_last = [None, None]
        for g in range(NG):
            s_ = g // GPS
            par = g % 2
            base = par * 4096
            w_xq = xq_hist[par]
            xq_hist[par] = []
            xq_tok = []
            tl_toks = tl_pref.pop(g) if g in tl_pref else emit_tl(g)
            for dcc in range(DC):
                h2in = fmb[:, base + dcc * 512:base + (dcc + 1) * 512]
                tq1 = P.op("s", lambda s, h2in=h2in, par=par: s.activation(out=scr[:, par * 512:par * 512 + 512], in_=h2in, func=AF.Identity, scale=smv(C_IS1), bias=smv(C_CB)), waits=[tl_toks[-1], tis1] + ([tq1_last[par]] if tq1_last[par] else wd_scr_guard), inc="se")
                tq2 = P.op("v", lambda v, par=par: v.tensor_scalar(out=scr[:, par * 512:par * 512 + 512], in0=scr[:, par * 512:par * 512 + 512], scalar1=smv(C_HI), scalar2=smv(C_LO), op0=ALU.min, op1=ALU.max), waits=[tq1, thi, tlo], inc="ve")
                xq = sfm[:, base + dcc * 512:base + (dcc + 1) * 512]
                tq3 = P.op("v", lambda v, xq=xq, par=par: v.tensor_scalar(out=xq, in0=scr[:, par * 512:par * 512 + 512], scalar1=smv(C_CB), scalar2=None, op0=ALU.subtract), waits=[tq2] + w_xq, inc="ve")
                w_xq = []
                tq1_last[par] = tq3
                xq_tok.append(tq3)
                fmb_hist[par].append(tq1)
            for ic in range(IC):
                icp = ic % 2
                pg = psum[icp]
                pu = psum[2 + icp]
                gtoks = []
                for dcc in range(DC):
                    def mmg(t, ic=ic, dcc=dcc, pg=pg, base=base):
                        return t.matmul(pg[:, 0:512], wa[:, dcc * 2816 + ic * 128:dcc * 2816 + ic * 128 + 128], sfm[:, base + dcc * 512:base + (dcc + 1) * 512], start=(dcc == 0), stop=(dcc == DC - 1))
                    w = [xq_tok[dcc]] + wgT_tok[dcc]
                    if dcc == 0 and gu_reader[icp] is not None:
                        w.append(gu_reader[icp])
                    gtoks.append(P.op("t", mmg, waits=w, inc="pe"))
                utoks = []
                for dcc in range(DC):
                    def mmu(t, ic=ic, dcc=dcc, pu=pu, base=base):
                        return t.matmul(pu[:, 0:512], wa[:, 22528 + dcc * 2816 + ic * 128:22528 + dcc * 2816 + ic * 128 + 128], sfm[:, base + dcc * 512:base + (dcc + 1) * 512], start=(dcc == 0), stop=(dcc == DC - 1))
                    w = [xq_tok[dcc]] + wuT_tok[dcc]
                    if dcc == 0 and gu_reader[2 + icp] is not None:
                        w.append(gu_reader[2 + icp])
                    utoks.append(P.op("t", mmu, waits=w, inc="pe"))
                hgp = hgb[:, icp * 512:icp * 512 + 512]
                hup = hgb[:, 1024 + icp * 512:1024 + icp * 512 + 512]
                thg = P.op("s", lambda s, pg=pg, hgp=hgp, ic=ic: s.activation(out=hgp, in_=pg[:, 0:512], func=AF.Silu, scale=smf[:, O_SGX + ic:O_SGX + ic + 1]), waits=[gtoks[-1], tsgx] + ([hg_war[icp]] if hg_war[icp] else []), inc="se")
                gu_reader[icp] = thg
                thu = P.op("s", lambda s, pu=pu, hup=hup, ic=ic: s.activation(out=hup, in_=pu[:, 0:512], func=AF.Identity, scale=smf[:, O_SUX + ic:O_SUX + ic + 1]), waits=[utoks[-1], tsux] + ([hu_war[icp]] if hu_war[icp] else []), inc="se")
                gu_reader[2 + icp] = thu
                hh = hho[:, icp * 512:icp * 512 + 512]
                thh = P.op("v", lambda v, hgp=hgp, hup=hup, hh=hh: v.tensor_tensor(out=hh, in0=hgp, in1=hup, op=ALU.mult), waits=[thg, thu] + ([hh_war[icp]] if hh_war[icp] else []), inc="ve")
                hg_war[icp] = thh
                tm = P.op("v", lambda v, hh=hh, hup=hup: v.tensor_scalar(out=hup, in0=hh, scalar1=1.0, scalar2=smv(C_AM), op0=ALU.mult, op1=ALU.max, accum_out=smv(C_AM)), waits=[thh, amax_last], inc="ve")
                tn = P.op("v", lambda v, hh=hh, hup=hup: v.tensor_scalar(out=hup, in0=hh, scalar1=1.0, scalar2=smv(C_AN), op0=ALU.min, op1=ALU.min, accum_out=smv(C_AN)), waits=[tm], inc="ve")
                amax_last = tn
                hu_war[icp] = tn
                tsp = P.op("s", lambda s, g=g, ic=ic, hh=hh: s.dma_start(out=sphh.ap()[g * IC + ic, :, :], in_=hh), waits=[thh], inc="aHH0" if icp == 0 else "aHH1", d=16)
                hh_sp.append(tsp)
                hh_war[icp] = tsp
            last_mm_xq[par] = gtoks[-1]
            xq_hist[par] = [utoks[-1], gtoks[-1]]

        wdT_tok = []
        for icc in range(IC):
            wdT_tok.append(P.op("q", lambda q, icc=icc: q.dma_start_transpose(wa[:, icc * 1024:(icc + 1) * 1024], pkd.ap()[:, icc * 128:(icc + 1) * 128]), waits=parks_all + [amax_last, gu_reader[0], gu_reader[1]], inc="sWD", d=16))
        ts2x, tis2 = global_reduce(amax_last, cc2i, cc2o, cc2p, 10, C_S2B, C_SD, C_IS2, extra_t8=[wdT_tok[0]])

        dn_reader = [None] * 4
        t3_war = [None, None]
        dq_war_prev = {}
        hq_mm_last = [[], []]
        x2pool = Pool(2, seed=tis2)
        opool = Pool(2, seed=tis2)
        out_toks = []
        hh_all = [("aHH0", P.cnt.get("aHH0", 0)), ("aHH1", P.cnt.get("aHH1", 0))]
        for g in range(NG):
            s_ = g // GPS
            par = g % 2
            hqb = 22528 + par * 11264
            wclear = hq_mm_last[par]
            hq_mm_last[par] = []
            def mk_tlh(q, g=g, hqb=hqb):
                src = bass.AP(sphh, g * IC * 128 * GT, [[GT, 128], [128 * GT, IC], [1, GT]])
                return q.dma_start(out=wa[:, hqb:hqb + 11264].rearrange("p (i t) -> p i t", t=GT), in_=src)
            tlh = P.op("q", mk_tlh, waits=hh_all + [tis2] + wclear + (([t for l in wuT_tok for t in l]) if g < 2 else []), inc="sL0" if par == 0 else "sL1", d=16)
            hq_tok = []
            dq_war = dq_war_prev
            for ic in range(IC):
                sl = (ic % 2) * 512
                tq1 = P.op("s", lambda s, hqb=hqb, ic=ic, sl=sl: s.activation(out=scr[:, sl:sl + 512], in_=wa[:, hqb + ic * 512:hqb + (ic + 1) * 512], func=AF.Identity, scale=smv(C_IS2), bias=smv(C_CB)), waits=[tlh, tis2] + ([dq_war[sl]] if sl in dq_war else []), inc="se")
                tq2 = P.op("v", lambda v, sl=sl: v.tensor_scalar(out=scr[:, sl:sl + 512], in0=scr[:, sl:sl + 512], scalar1=smv(C_HI), scalar2=smv(C_LO), op0=ALU.min, op1=ALU.max), waits=[tq1], inc="ve")
                tq3 = P.op("v", lambda v, hqb=hqb, ic=ic, sl=sl: v.tensor_scalar(out=wa[:, hqb + ic * 512:hqb + (ic + 1) * 512], in0=scr[:, sl:sl + 512], scalar1=smv(C_CB), scalar2=None, op0=ALU.subtract), waits=[tq2], inc="ve")
                dq_war[sl] = tq3
                dq_war_prev = dq_war
                hq_tok.append(tq3)
            for tp in range(2):
                tts = (2 * tp, 2 * tp + 1)
                gtoks = {(tt, eh): [] for tt in tts for eh in range(2)}
                for ic in range(IC):
                    for tt in tts:
                        dp = tt % 2
                        for eh in range(2):
                            pb = psum[4 + 2 * dp + eh]
                            def mmd(t, ic=ic, eh=eh, tt=tt, pb=pb, hqb=hqb):
                                return t.matmul(pb[:, 0:512], wa[:, hqb + ic * 512 + tt * 128:hqb + ic * 512 + tt * 128 + 128], wa[:, ic * 1024 + eh * 512:ic * 1024 + eh * 512 + 512], start=(ic == 0), stop=(ic == IC - 1))
                            w = [hq_tok[ic], wdT_tok[ic]]
                            if ic == 0 and dn_reader[2 * dp + eh] is not None:
                                w.append(dn_reader[2 * dp + eh])
                            gtoks[(tt, eh)].append(P.op("t", mmd, waits=w, inc="pe"))
                for tt in tts:
                    gt = g * 4 + tt
                    dp = tt % 2
                    mm_l = [gtoks[(tt, 0)][-1], gtoks[(tt, 1)][-1]]
                    hq_mm_last[par].append(mm_l[0])
                    hq_mm_last[par].append(mm_l[1])
                    x2s, wx2 = x2pool.acquire()
                    x2t = x0w[:, x2s * 1024:(x2s + 1) * 1024]
                    tlx = P.op("q", lambda q, gt=gt, x2t=x2t: q.dma_start(out=x2t, in_=spx2.ap()[gt * 128:(gt + 1) * 128, :]), waits=x2_all + [tis2] + wx2, inc="sX0" if x2s == 0 else "sX1", d=16)
                    x2pool.wrote(x2s, tlx)
                    t3t = []
                    for eh in range(2):
                        pb = psum[4 + 2 * dp + eh]
                        tok = P.op("v", lambda v, pb=pb, eh=eh, dp=dp, s_=s_: v.scalar_tensor_tensor(out=scr[:, 1024 + dp * 1024 + eh * 512:1024 + dp * 1024 + eh * 512 + 512], in0=pb[:, 0:512], scalar=smv(C_S2B), in1=repb[:, 4096 + s_ * 1024 + eh * 512:4096 + s_ * 1024 + eh * 512 + 512], op0=ALU.mult, op1=ALU.mult), waits=[mm_l[eh], ts2x, tis2] + rep_toks["scdgb"] + ([dn_reader[2 * dp + eh]] if dn_reader[2 * dp + eh] else []) + ([t3_war[dp]] if t3_war[dp] else []), inc="ve")
                        dn_reader[2 * dp + eh] = tok
                        t3t.append(tok)
                    osl, wo = opool.acquire()
                    ot = x0w[:, (2 + osl) * 1024:(3 + osl) * 1024]
                    tout = P.op("v", lambda v, x2t=x2t, ot=ot, dp=dp: v.tensor_tensor(out=ot, in0=x2t, in1=scr[:, 1024 + dp * 1024:1024 + dp * 1024 + 1024], op=ALU.add), waits=t3t + [tlx] + wo, inc="ve")
                    opool.wrote(osl, tout)
                    t3_war[dp] = tout
                    x2pool.read(x2s, tout)
                    tst = P.op("s", lambda s, gt=gt, ot=ot: s.dma_start(out=y_[gt * 128:(gt + 1) * 128, :], in_=ot), waits=[tout], inc="aO0" if osl == 0 else "aO1", d=16)
                    opool.read(osl, tst)
                    out_toks.append(tst)
            hq_mm_last[par] += t3t

        P.op("q", None, waits=out_toks)
        GROUP_SEMS = {"sS0", "sB0", "sS2", "sWT", "sWG", "sWU", "sWD", "sMB", "gBI"}

        @block.sync
        def _(q):
            P.emit("q", q, sems, GROUP_SEMS)

        @block.scalar
        def _(s):
            P.emit("s", s, sems, GROUP_SEMS)

        @block.vector
        def _(v):
            P.emit("v", v, sems, GROUP_SEMS)

        @block.gpsimd
        def _(g):
            P.emit("g", g, sems, GROUP_SEMS)

        @block.tensor
        def _(t):
            P.emit("t", t, sems, GROUP_SEMS)

    return nc


_NC_CACHE = None


def kernel(L0_attn_Z, L0_attn_decay, L0_attn_W, L0_attn_b, L0_attn_gamma,
           L0_attn_beta, L0_ffn_clip_x, L0_ffn_clip_down, L0_ffn_w_gate,
           L0_ffn_w_up, L0_ffn_w_down, L0_c, L0_wmod):
    global _NC_CACHE
    Zf = np.ascontiguousarray(np.asarray(L0_attn_Z, dtype=np.float32))
    ks = np.arange(16, 64)
    pw = (2.0 ** (-ks)).astype(np.float32).reshape(1, 48)
    cvals = (1.5 * 2.0 ** ks).astype(np.float32).reshape(1, 48)
    common = {
        "decay": np.asarray(L0_attn_decay, dtype=np.float32).reshape(1, 1),
        "W": np.ascontiguousarray(np.asarray(L0_attn_W, dtype=np.float32)),
        "bias": np.asarray(L0_attn_b, dtype=np.float32).reshape(1, D),
        "gamma": np.asarray(L0_attn_gamma, dtype=np.float32).reshape(1, D),
        "beta": np.asarray(L0_attn_beta, dtype=np.float32).reshape(1, D),
        "clip_x": np.asarray(L0_ffn_clip_x, dtype=np.float32).reshape(1, 1),
        "clip_d": np.asarray(L0_ffn_clip_down, dtype=np.float32).reshape(1, 1),
        "wg": np.ascontiguousarray(np.asarray(L0_ffn_w_gate, dtype=np.float32)),
        "wu": np.ascontiguousarray(np.asarray(L0_ffn_w_up, dtype=np.float32)),
        "wd": np.ascontiguousarray(np.asarray(L0_ffn_w_down, dtype=np.float32)),
        "wmod": np.ascontiguousarray(np.asarray(L0_wmod, dtype=np.float32)),
        "pw": pw,
        "cvals": cvals,
        "eye": np.eye(128, dtype=np.float32),
    }
    cc = np.asarray(L0_c, dtype=np.float32)
    in_maps = []
    for i in range(NCORES):
        m = dict(common)
        m["Z"] = np.ascontiguousarray(Zf[i * BL:(i + 1) * BL].reshape(R, D))
        m["cc_"] = np.ascontiguousarray(cc[i * BL:(i + 1) * BL])
        in_maps.append(m)
    if _NC_CACHE is None:
        _NC_CACHE = build_nc()
    res = run_bass_kernel_spmd(_NC_CACHE, in_maps, core_ids=list(range(NCORES)))
    out = np.empty((B, T, D), dtype=np.float32)
    for i in range(NCORES):
        out[i * BL:(i + 1) * BL] = res.results[i]["y"].reshape(BL, T, D)
    return out


# version 38
# speedup vs baseline: 1.0115x; 1.0019x over previous
import numpy as np
import concourse.bass as bass
import concourse.mybir as mybir
from concourse.bass_utils import run_bass_kernel_spmd

F32 = mybir.dt.float32
BF16 = mybir.dt.bfloat16
AF = mybir.ActivationFunctionType
ALU = mybir.AluOpType
AX = mybir.AxisListType

NCORES = 8
CC_WORLD = 8
B, T, D, I = 16, 2048, 1024, 2816
BL = B // NCORES
R = BL * T
SEG = 512
NH = T // SEG
TPH = SEG // 128
DC = D // 128
IC = I // 128
GT = 512
NG = R // GT
GPS = NG // BL
MD = 6 * D


class Sched:
    def __init__(self):
        self.ops = []
        self.cnt = {}

    def op(self, eng, fn, waits=(), inc=None, d=1):
        tok = None
        if inc is not None:
            self.cnt[inc] = self.cnt.get(inc, 0) + d
            tok = (inc, self.cnt[inc])
        self.ops.append((eng, fn, [w for w in waits if w is not None], inc, d))
        return tok

    def emit(self, name, eng, sems, group_sems=()):
        last = {}
        for e, fn, waits, inc, d in self.ops:
            if e != name:
                continue
            need = {}
            for sn, val in waits:
                if sn in group_sems:
                    val = self.cnt[sn]
                if val > need.get(sn, -1):
                    need[sn] = val
            for sn, val in need.items():
                if last.get(sn, -1) >= val:
                    continue
                eng.wait_ge(sems[sn], val)
                last[sn] = val
            inst = fn(eng) if fn is not None else None
            if inc is not None:
                inst.then_inc(sems[inc], d)


class Pool:
    def __init__(self, n, seed=None):
        self.n = n
        self.wr = [seed] * n
        self.rd = [[] for _ in range(n)]
        self.i = 0

    def acquire(self):
        s = self.i % self.n
        self.i += 1
        w = list(self.rd[s]) + ([self.wr[s]] if self.wr[s] else [])
        self.rd[s] = []
        return s, w

    def wrote(self, s, tok):
        self.wr[s] = tok

    def read(self, s, tok):
        if tok is not None:
            self.rd[s].append(tok)


def build_nc():
    nc = bass.Bass("TRN2", target_bir_lowering=False, debug=False)
    P = Sched()

    hZ = nc.dram_tensor("Z", [R, D], F32, kind="ExternalInput")
    hdec = nc.dram_tensor("decay", [1, 1], F32, kind="ExternalInput")
    hW = nc.dram_tensor("W", [D, D], F32, kind="ExternalInput")
    hb = nc.dram_tensor("bias", [1, D], F32, kind="ExternalInput")
    hg = nc.dram_tensor("gamma", [1, D], F32, kind="ExternalInput")
    hbe = nc.dram_tensor("beta", [1, D], F32, kind="ExternalInput")
    hcx = nc.dram_tensor("clip_x", [1, 1], F32, kind="ExternalInput")
    hcd = nc.dram_tensor("clip_d", [1, 1], F32, kind="ExternalInput")
    hwg = nc.dram_tensor("wg", [I, D], F32, kind="ExternalInput")
    hwu = nc.dram_tensor("wu", [I, D], F32, kind="ExternalInput")
    hwd = nc.dram_tensor("wd", [D, I], F32, kind="ExternalInput")
    hc = nc.dram_tensor("cc_", [BL, D], F32, kind="ExternalInput")
    hwm = nc.dram_tensor("wmod", [D, MD], F32, kind="ExternalInput")
    hpw = nc.dram_tensor("pw", [1, 48], F32, kind="ExternalInput")
    hcv = nc.dram_tensor("cvals", [1, 48], F32, kind="ExternalInput")
    hey = nc.dram_tensor("eye", [128, 128], F32, kind="ExternalInput")
    hy = nc.dram_tensor("y", [R, D], F32, kind="ExternalOutput")

    pkW = nc.dram_tensor("pkW", [D, D], BF16, kind="Internal")
    pkg = nc.dram_tensor("pkg", [I, D], BF16, kind="Internal")
    pku = nc.dram_tensor("pku", [I, D], BF16, kind="Internal")
    pkd = nc.dram_tensor("pkd", [D, I], BF16, kind="Internal")
    spx2 = nc.dram_tensor("spx2", [R, D], F32, kind="Internal")
    sph2 = nc.dram_tensor("sph2", [R, D], BF16, kind="Internal")
    sphh = nc.dram_tensor("sphh", [NG * IC, 128, GT], BF16, kind="Internal")
    hsf = nc.dram_tensor("hsf", [16, 1024], F32, kind="Internal")
    hsb = nc.dram_tensor("hsb", [96, 128], BF16, kind="Internal")
    hsc = nc.dram_tensor("hsc", [12, 1024], BF16, kind="Internal")
    cc1i = nc.dram_tensor("cc1i", [1, 1], F32, kind="Internal")
    cc1o = nc.dram_tensor("cc1o", [1, 1], F32, kind="Internal", addr_space="Shared" if CC_WORLD > 4 else "Local")
    cc2i = nc.dram_tensor("cc2i", [1, 1], F32, kind="Internal")
    cc2o = nc.dram_tensor("cc2o", [1, 1], F32, kind="Internal", addr_space="Shared" if CC_WORLD > 4 else "Local")
    pkuT = nc.dram_tensor("pkuT", [128, 8192], BF16, kind="Internal")
    cc1p = nc.dram_tensor("cc1p", [1, 1], F32, kind="Internal", addr_space="Shared" if CC_WORLD > 4 else "Local")
    cc2p = nc.dram_tensor("cc2p", [1, 1], F32, kind="Internal", addr_space="Shared" if CC_WORLD > 4 else "Local")

    Z, Wm, bia, gam, bet = hZ.ap(), hW.ap(), hb.ap(), hg.ap(), hbe.ap()
    dec, cx, cd = hdec.ap(), hcx.ap(), hcd.ap()
    wg_, wu_, wd_, c_, wm_ = hwg.ap(), hwu.ap(), hwd.ap(), hc.ap(), hwm.ap()
    pw_, cv_, y_, ey_ = hpw.ap(), hcv.ap(), hy.ap(), hey.ap()

    from contextlib import ExitStack
    with ExitStack() as stack:
        en = stack.enter_context
        wa = en(nc.sbuf_tensor("wa", [128, 45056], BF16))
        fmb = en(nc.sbuf_tensor("fmb", [128, 8192], BF16))
        sfm = en(nc.sbuf_tensor("sfm", [128, 8192], BF16))
        x0w = en(nc.sbuf_tensor("x0w", [128, 4 * 1024], F32))
        scr = en(nc.sbuf_tensor("scr", [128, 3072], F32))
        h1sb = en(nc.sbuf_tensor("h1sb", [128, 4 * 1024], BF16))
        xhsb = en(nc.sbuf_tensor("xhsb", [128, 4 * 1024], BF16))
        ynb2 = en(nc.sbuf_tensor("ynb2", [128, 2 * 1024], BF16))
        repb = en(nc.sbuf_tensor("repb", [128, 10 * 1024], BF16))
        sep = en(nc.sbuf_tensor("sep", [128, 5248], BF16))
        smf = en(nc.sbuf_tensor("smf", [128, 512], F32))
        hgb = en(nc.sbuf_tensor("hgb", [128, 2048], BF16))
        hho = en(nc.sbuf_tensor("hho", [128, 1024], BF16))
        ebuf = en(nc.sbuf_tensor("ebuf", [128, 128], BF16))
        p0 = en(nc.psum_tensor("p0", [128, 512], F32))
        p1 = en(nc.psum_tensor("p1", [128, 512], F32))
        p2 = en(nc.psum_tensor("p2", [128, 512], F32))
        p3 = en(nc.psum_tensor("p3", [128, 512], F32))
        p4 = en(nc.psum_tensor("p4", [128, 512], F32))
        p5 = en(nc.psum_tensor("p5", [128, 512], F32))
        p6 = en(nc.psum_tensor("p6", [128, 512], F32))
        p7 = en(nc.psum_tensor("p7", [128, 512], F32))
        SEM_NAMES = ["pe", "ve", "se", "ge", "cc",
                     "sZ0", "sZ1", "sE0", "sE1", "sT0", "sT1",
                     "sS0", "sB0", "sS2", "sS3", "sWT", "sWG", "sWU", "sWD",
                     "sCT", "sMB", "sMO", "sRP",
                     "sH0", "sH1", "sL0", "sL1", "sX0", "sX1", "sQ0", "sQ1", "sTA0", "sTA1",
                     "aX20", "aX21", "aH20", "aH21", "aHH0", "aHH1", "aP0", "aP1", "aO0", "aO1", "sBE", "sGR",
                     "gWP", "gWM0", "gWM1", "gWM2", "gWM3", "gWF0", "gWF1", "gCB", "gRB", "gBI", "sEY", "sGR2", "aPT0", "aPT1"]
        sems = {n: en(nc.semaphore(n)) for n in SEM_NAMES}
        block = en(nc.Block())
        psum = [p0, p1, p2, p3, p4, p5, p6, p7]
        wt = wa[:, 36864:45056]

        C_CB, C_HI, C_LO, C_A, C_SX, C_SD = 0, 1, 2, 3, 4, 5
        C_IS1, C_S2B, C_IS2 = 6, 7, 8
        C_AM, C_AN, C_T0, C_T1, C_EPS = 9, 10, 11, 12, 13
        O_CA1, O_CB1, O_SB = 16, 32, 48
        O_SCG, O_SCU, O_SCD = 64, 86, 108
        O_SGX, O_SUX = 128, 150
        O_TAIL = 176
        O_MX = 184
        O_ST = 192
        O_EPS, O_DLT, O_CV = 336, 384, 432
        O_SCL = 480
        O_CT = 488

        def smv(c):
            return smf[:, c:c + 1]

        S_S1, S_CT, S_MODT, S_CTL = 0, 3072, 3088, 5232
        SWB = [2048, 3184]
        S_CSIL = 3184
        ga_tm = x0w[0:2, 1024:2048]
        sgn = x0w[0:2, 3072:4096]
        repr_ = x0w[0:2, 0:1024]
        gb_tm = sep[0:2, 4208:5232]
        c_sil = sep[0:2, S_CSIL:S_CSIL + 1024]
        modT = sep[:, S_MODT:S_MODT + 96]
        gam2 = scr[0:2, 2048:3072]

        teps = P.op("v", lambda v: v.memset(smf[:, 13:14], 1e-5), inc="ve")
        t_pw = P.op("q", lambda q: q.dma_start(out=smf[0:1, O_EPS:O_EPS + 48], in_=pw_[:, :]), inc="sS0", d=16)
        t_cv = P.op("q", lambda q: q.dma_start(out=smf[0:1, O_CV:O_CV + 48], in_=cv_[:, :]), inc="sS0", d=16)
        t_dc = P.op("q", lambda q: q.dma_start(out=smf[0:1, O_SCL:O_SCL + 1], in_=dec[:, :]), inc="sS0", d=16)
        t_cx = P.op("q", lambda q: q.dma_start(out=smf[0:1, O_SCL + 1:O_SCL + 2], in_=cx[:, :]), inc="sS0", d=16)
        t_cd = P.op("q", lambda q: q.dma_start(out=smf[0:1, O_SCL + 2:O_SCL + 3], in_=cd[:, :]), inc="sS0", d=16)
        t_g2 = P.op("q", lambda q: q.dma_start(out=gam2, in_=bass.AP(hg, 0, [[0, 2], [1, 1024]])), inc="sS0", d=16)
        t_cl = P.op("q", lambda q: q.dma_start(out=x0w[0:2, 0:1024], in_=c_[:, :]), inc="sS0", d=16)

        tg1 = P.op("v", lambda v: v.tensor_scalar(out=smf[0:1, O_EPS:O_EPS + 48], in0=smf[0:1, O_EPS:O_EPS + 48], scalar1=1.0, scalar2=1.0, op0=ALU.add, op1=ALU.is_gt), waits=[t_pw], inc="ve")
        tg2 = P.op("v", lambda v: v.scalar_tensor_tensor(out=smf[0:1, O_DLT:O_DLT + 47], in0=smf[0:1, O_EPS + 1:O_EPS + 48], scalar=-1.0, in1=smf[0:1, O_EPS:O_EPS + 47], op0=ALU.mult, op1=ALU.add), waits=[tg1], inc="ve")
        tg3 = P.op("v", lambda v: v.tensor_tensor(out=smf[0:1, O_DLT:O_DLT + 47], in0=smf[0:1, O_DLT:O_DLT + 47], in1=smf[0:1, O_CV:O_CV + 47], op=ALU.mult), waits=[tg2, t_cv], inc="ve")
        tC = P.op("v", lambda v: v.tensor_reduce(out=smf[0:1, O_CT:O_CT + 1], in_=smf[0:1, O_DLT:O_DLT + 47], axis=AX.X, op=ALU.add), waits=[tg3], inc="ve")
        tsg = P.op("s", lambda s: s.activation(out=smf[0:1, O_SCL + 4:O_SCL + 7], in_=smf[0:1, O_SCL:O_SCL + 3], func=AF.Sigmoid), waits=[t_dc, t_cx, t_cd], inc="se")
        tb1 = P.op("q", lambda q: q.dma_start(out=hsf.ap()[0:1, 0:1], in_=smf[0:1, O_CT:O_CT + 1]), waits=[tC], inc="sB0", d=16)
        tb2 = P.op("q", lambda q: q.dma_start(out=hsf.ap()[0:1, 1:4], in_=smf[0:1, O_SCL + 4:O_SCL + 7]), waits=[tsg], inc="sB0", d=16)
        tCb = P.op("q", lambda q: q.dma_start(out=smv(C_CB), in_=bass.AP(hsf, 0, [[0, 128], [1, 1]])), waits=[tb1], inc="sS2", d=16)
        tac = P.op("q", lambda q: q.dma_start(out=smv(C_A), in_=bass.AP(hsf, 1, [[0, 128], [1, 1]])), waits=[tb2], inc="sS2", d=16)
        tsx = P.op("q", lambda q: q.dma_start(out=smv(C_SX), in_=bass.AP(hsf, 2, [[0, 128], [1, 1]])), waits=[tb2], inc="sS2", d=16)
        tsd = P.op("q", lambda q: q.dma_start(out=smv(C_SD), in_=bass.AP(hsf, 3, [[0, 128], [1, 1]])), waits=[tb2], inc="sS2", d=16)
        thi = P.op("v", lambda v: v.tensor_scalar(out=smv(C_HI), in0=smv(C_CB), scalar1=127.0, scalar2=None, op0=ALU.add), waits=[tCb], inc="ve")
        tNCB = P.op("v", lambda v: v.tensor_scalar(out=smf[:, 14:15], in0=smv(C_CB), scalar1=-1.0, scalar2=None, op0=ALU.mult), waits=[tCb], inc="ve")
        tlo = P.op("v", lambda v: v.tensor_scalar(out=smv(C_LO), in0=smv(C_CB), scalar1=-128.0, scalar2=None, op0=ALU.add), waits=[tCb], inc="ve")

        tones = P.op("v", lambda v: v.memset(scr[:, 0:2048], 1.0), inc="ve")
        tone2 = P.op("v", lambda v: v.memset(sep[0:2, S_S1:S_S1 + 2048], 1.0), inc="ve")
        ts1w = P.op("v", lambda v: v.tensor_tensor_scan(out=fmb[:, 0:2048], data0=bass.AP(smf, C_A, [[512, 128], [0, 2048]]), data1=scr[:, 0:2048], initial=0.0, op0=ALU.mult, op1=ALU.add), waits=[tones, tac], inc="ve")
        ts1 = P.op("v", lambda v: v.tensor_copy(out=sep[0:1, S_S1:S_S1 + 2048], in_=fmb[0:1, 0:2048]), waits=[ts1w, tone2], inc="ve")

        tWp = P.op("g", lambda g: g.dma_start(out=pkW.ap()[:, :], in_=Wm[:, :]), inc="gWP", d=16)
        tWt = []
        for dcc in range(DC):
            tWt.append(P.op("q", lambda q, dcc=dcc: q.dma_start_transpose(wt[:, dcc * 1024:(dcc + 1) * 1024], pkW.ap()[:, dcc * 128:(dcc + 1) * 128]), waits=[tWp], inc="sWT", d=16))

        c32 = x0w[0:2, 1024:2048]
        tcs = P.op("s", lambda s: s.activation(out=c32, in_=x0w[0:2, 0:1024], func=AF.Silu), waits=[t_cl], inc="se")
        tch = P.op("v", lambda v: v.tensor_copy(out=c_sil, in_=c32), waits=[tcs], inc="ve")
        c_lo = x0w[0:2, 0:1024]
        tclo = P.op("v", lambda v: v.tensor_tensor(out=c_lo, in0=c32, in1=c_sil, op=ALU.subtract), waits=[tch], inc="ve")
        tcb = P.op("g", lambda g: g.dma_start(out=hsc.ap()[0:2, :], in_=c_sil), waits=[tch], inc="gCB", d=16)
        tcb2 = P.op("g", lambda g: g.dma_start(out=hsc.ap()[8:10, :], in_=c_lo), waits=[tclo], inc="gRB", d=16)
        tct = P.op("q", lambda q: q.dma_start_transpose(sep[:, S_CT:S_CT + 16], bass.AP(hsc, 0, [[128, 16], [1, 128]])), waits=[tcb], inc="sCT", d=16)
        tct2 = P.op("q", lambda q: q.dma_start_transpose(sep[:, S_CTL:S_CTL + 16], bass.AP(hsc, 8 * 1024, [[128, 16], [1, 128]])), waits=[tcb2], inc="sCT", d=16)

        mod_tm = wa[0:2, 26624:26624 + MD]
        tmodev = [None, None]
        wm_slot_tok = [None, None, None, None]
        wm32war = [None, None]
        GAGB_N = {4, 5, 10, 11}
        for n in range(12):
            pb = psum[n % 2]
            mmtok = None
            for dcc in range(DC):
                slot = (n * DC + dcc) % 4
                hi = wa[:, 8192 + 1024 * slot:8192 + 1024 * slot + 512]
                lo = wa[:, 8192 + 1024 * slot + 512:8192 + 1024 * (slot + 1)]
                wld = [wm_slot_tok[slot]] if wm_slot_tok[slot] else []
                if n in GAGB_N:
                    f32s = slot % 2
                    wm32 = x0w[:, 2048 + 512 * f32s:2048 + 512 * (f32s + 1)]
                    w32 = [wm32war[f32s]] if wm32war[f32s] else []
                    tld = P.op("g", lambda g, n=n, dcc=dcc, wm32=wm32: g.dma_start(out=wm32, in_=wm_[dcc * 128:(dcc + 1) * 128, n * 512:(n + 1) * 512]), waits=w32, inc="gWF0" if f32s == 0 else "gWF1", d=16)
                    thh_ = P.op("v", lambda v, wm32=wm32, hi=hi: v.tensor_copy(out=hi, in_=wm32), waits=[tld] + wld, inc="ve")
                    tll = P.op("v", lambda v, wm32=wm32, hi=hi, lo=lo: v.tensor_tensor(out=lo, in0=wm32, in1=hi, op=ALU.subtract), waits=[thh_], inc="ve")
                    wm32war[f32s] = tll
                    def mm1(t, dcc=dcc, hi=hi, pb=pb):
                        return t.matmul(pb[0:2, 0:512], bass.AP(sep, S_CT + dcc, [[5248, 128], [8, 2]]), hi, start=(dcc == 0), stop=False)
                    def mm2(t, dcc=dcc, hi=hi, pb=pb):
                        return t.matmul(pb[0:2, 0:512], bass.AP(sep, S_CTL + dcc, [[5248, 128], [8, 2]]), hi, start=False, stop=False)
                    def mm3(t, dcc=dcc, lo=lo, pb=pb):
                        return t.matmul(pb[0:2, 0:512], bass.AP(sep, S_CT + dcc, [[5248, 128], [8, 2]]), lo, start=False, stop=(dcc == DC - 1))
                    w = [thh_, tct, tct2]
                    if dcc == 0 and tmodev[n % 2] is not None:
                        w.append(tmodev[n % 2])
                    P.op("t", mm1, waits=w, inc="pe")
                    P.op("t", mm2, waits=[thh_, tct2], inc="pe")
                    mmtok = P.op("t", mm3, waits=[tll], inc="pe")
                    wm_slot_tok[slot] = mmtok
                else:
                    tld = P.op("g", lambda g, n=n, dcc=dcc, hi=hi: g.dma_start(out=hi, in_=wm_[dcc * 128:(dcc + 1) * 128, n * 512:(n + 1) * 512]), waits=wld, inc="gWM%d" % slot, d=16)
                    def mmp(t, dcc=dcc, hi=hi, pb=pb):
                        return t.matmul(pb[0:2, 0:512], bass.AP(sep, S_CT + dcc, [[5248, 128], [8, 2]]), hi, start=(dcc == 0), stop=(dcc == DC - 1))
                    w = [tld, tct, tct2]
                    if dcc == 0 and tmodev[n % 2] is not None:
                        w.append(tmodev[n % 2])
                    mmtok = P.op("t", mmp, waits=w, inc="pe")
                    wm_slot_tok[slot] = mmtok
            tmodev[n % 2] = P.op("s", lambda s, n=n, pb=pb: s.activation(out=mod_tm[:, n * 512:(n + 1) * 512], in_=pb[0:2, 0:512], func=AF.Identity), waits=[mmtok], inc="se")
        tmodA, tmodB = tmodev

        tmb = []
        for s_ in range(2):
            tmb.append(P.op("q", lambda q, s_=s_: q.dma_start(out=bass.AP(hsb, s_ * 128, [[256, 48], [1, 128]]), in_=mod_tm[s_:s_ + 1, :]), waits=[tmodA, tmodB], inc="sMB", d=16))
        tmodT = P.op("q", lambda q: q.dma_start_transpose(modT, hsb.ap()[:, :]), waits=tmb, inc="sMO", d=16)
        tca1 = P.op("v", lambda v: v.tensor_scalar(out=smf[:, O_CA1:O_CA1 + 16], in0=modT[:, 16:32], scalar1=1.0, scalar2=None, op0=ALU.add), waits=[tmodT], inc="ve")
        tcb1 = P.op("v", lambda v: v.tensor_scalar(out=smf[:, O_CB1:O_CB1 + 16], in0=modT[:, 64:80], scalar1=1.0, scalar2=None, op0=ALU.add), waits=[tmodT], inc="ve")
        tsbf = P.op("v", lambda v: v.tensor_copy(out=smf[:, O_SB:O_SB + 16], in_=modT[:, 48:64]), waits=[tmodT], inc="ve")

        def eff(src_off, dst, prev):
            t1 = P.op("s", lambda s, o=src_off, dst=dst: s.activation(out=dst, in_=mod_tm[0:2, o:o + 1024], func=AF.Abs), waits=[tmodA, tmodB] + prev, inc="se")
            t2 = P.op("v", lambda v, dst=dst: v.tensor_scalar(out=dst, in0=dst, scalar1=0.25, scalar2=None, op0=ALU.max), waits=[t1], inc="ve")
            t3 = P.op("v", lambda v, o=src_off: v.tensor_scalar(out=sgn, in0=mod_tm[0:2, o:o + 1024], scalar1=0.0, scalar2=None, op0=ALU.is_ge), waits=[tmodA, tmodB] + prev, inc="ve")
            t4 = P.op("v", lambda v: v.tensor_scalar(out=sgn, in0=sgn, scalar1=2.0, scalar2=-1.0, op0=ALU.mult, op1=ALU.add), waits=[t3], inc="ve")
            t5 = P.op("v", lambda v, dst=dst: v.tensor_tensor(out=dst, in0=dst, in1=sgn, op=ALU.mult), waits=[t2, t4], inc="ve")
            return t5
        tga = eff(2048, ga_tm, [])
        tgb = eff(5120, gb_tm, [tga])

        rep_toks = {}

        def make_rep(name, rrow, mk, roff, srcap=None):
            tmk = mk()
            sap = repr_ if srcap is None else srcap
            tb = P.op("g", lambda g, rrow=rrow, sap=sap: g.dma_start(out=hsc.ap()[rrow:rrow + 2, :], in_=sap), waits=[tmk], inc="gRB", d=16)
            toks = []
            for s_ in range(2):
                dst = repb[:, (roff + s_) * 1024:(roff + s_ + 1) * 1024]
                toks.append(P.op("q", lambda q, s_=s_, rrow=rrow, dst=dst: q.dma_start(out=dst, in_=bass.AP(hsc, (rrow + s_) * 1024, [[0, 128], [1, 1024]])), waits=[tb], inc="sRP", d=16))
            rep_toks[name] = toks
            return tmk

        tgg = make_rep("gga", 0, lambda: P.op("v", lambda v: v.tensor_tensor(out=repr_, in0=gam2, in1=ga_tm, op=ALU.mult), waits=[tga, t_g2, tgb, tcs, tcb, tcb2], inc="ve"), 0)
        t_b2 = P.op("q", lambda q: q.dma_start(out=gam2, in_=bass.AP(hbe, 0, [[0, 2], [1, 1024]])), waits=[tgg] + rep_toks["gga"], inc="sBE", d=16)
        tbgm = make_rep("bga", 2, lambda: P.op("v", lambda v: v.tensor_tensor(out=repr_, in0=gam2, in1=ga_tm, op=ALU.mult), waits=[t_b2, tga, tgg] + rep_toks["gga"], inc="ve"), 2)
        tcb1r = make_rep("cb1", 6, lambda: P.op("v", lambda v: v.tensor_scalar(out=repr_, in0=mod_tm[0:2, 4096:5120], scalar1=1.0, scalar2=None, op0=ALU.add), waits=[tmodA, tmodB, tbgm] + rep_toks["bga"], inc="ve"), 6)
        tsbr = make_rep("sbr", 10, lambda: P.op("v", lambda v: v.tensor_copy(out=repr_, in_=mod_tm[0:2, 3072:4096]), waits=[tmodA, tmodB, tcb1r] + rep_toks["cb1"], inc="ve"), 8)


        swb_tok = []
        for s_ in range(2):
            for eh in range(2):
                pb = psum[4 + eh]
                toks = []
                for dcc in range(DC):
                    def mmsw(t, s_=s_, dcc=dcc, eh=eh, pb=pb):
                        return t.matmul(pb[0:1, 0:512], modT[:, 2 * dcc + s_:2 * dcc + s_ + 1], wt[:, dcc * 1024 + eh * 512:dcc * 1024 + eh * 512 + 512], start=(dcc == 0), stop=(dcc == DC - 1))
                    w = [tmodT, tWt[dcc]]
                    if dcc == 0 and s_ == 1:
                        w.append(swb_tok[eh])
                    toks.append(P.op("t", mmsw, waits=w, inc="pe"))
                swb_tok.append(P.op("s", lambda s, s_=s_, eh=eh, pb=pb: s.activation(out=sep[0:1, SWB[s_] + eh * 512:SWB[s_] + eh * 512 + 512], in_=pb[0:1, 0:512], func=AF.Identity), waits=[toks[-1]], inc="se"))
        tbias = []
        for s_ in range(2):
            tbias.append(P.op("g", lambda g, s_=s_: g.dma_start(out=sep[1:2, SWB[s_]:SWB[s_] + 1024], in_=bia[:, :]), waits=[tcs] if s_ == 0 else [], inc="gBI", d=16))

        wq_raw = Pool(2)
        nat_pool = Pool(2)
        tq_prev = [None]
        park_done = {"wg": [], "wu": [], "wd": []}
        wgT_tok = [[] for _ in range(DC)]
        wuT_tok = [[] for _ in range(DC)]
        wTR_war = [[None] * 4, [None] * 4]
        wlast_bank = [None, None]
        stg_pool = Pool(2)
        stg_last = [None, None]

        def prep_gu_tile(name, src, park, scoff, ic):
                slot, w = wq_raw.acquire()
                raw = scr[:, slot * 1024:(slot + 1) * 1024]
                tl = P.op("g", lambda g, ic=ic, raw=raw, src=src: g.dma_start(out=raw, in_=src[ic * 128:(ic + 1) * 128, :]), waits=w + ([tones, ts1] if ic < 2 else []), inc="sQ0" if slot == 0 else "sQ1", d=16)
                wq_raw.wrote(slot, tl)
                tmx = P.op("v", lambda v, raw=raw: v.tensor_reduce(out=smv(C_T0), in_=raw, axis=AX.X, op=ALU.max, apply_absolute_value=True), waits=[tl], inc="ve")
                tsc = P.op("v", lambda v, ic=ic: v.tensor_scalar(out=smf[:, scoff + ic:scoff + ic + 1], in0=smv(C_T0), scalar1=1.0 / 127.0, scalar2=None, op0=ALU.mult), waits=[tmx], inc="ve")
                trc = P.op("v", lambda v, ic=ic: v.reciprocal(out=smv(C_T1), in_=smf[:, scoff + ic:scoff + ic + 1]), waits=[tsc] + ([tq_prev[0]] if tq_prev[0] else []), inc="ve")
                tq = P.op("s", lambda s, raw=raw: s.activation(out=raw, in_=raw, func=AF.Identity, scale=smv(C_T1), bias=smv(C_CB)), waits=[trc, tCb, tmx], inc="se")
                tq_prev[0] = tq
                nslot, wn = nat_pool.acquire()
                nat = hgb[:, nslot * 1024:(nslot + 1) * 1024]
                tqq = P.op("s", lambda s, raw=raw, nat=nat: s.activation(out=nat, in_=raw, func=AF.Identity, bias=smf[:, 14:15]), waits=[tq, tNCB] + wn, inc="se")
                nat_pool.wrote(nslot, tqq)
                wq_raw.read(slot, tqq)
                if name in ("wu", "wg"):
                    tpk = P.op("s", lambda s, ic=ic, nat=nat, park=park: s.dma_start(out=park.ap()[ic * 128:(ic + 1) * 128, :], in_=nat), waits=[tqq], inc="aP0" if nslot == 0 else "aP1", d=16)
                    nat_pool.read(nslot, tpk)
                    park_done[name].append(tpk)
                    return
                bank = psum[2 + nslot]
                rw = ([tmodA, tmodB] + [t for t in wm_slot_tok if t]) if name == "wg" else list(mod_tm_readers)
                for dcc in range(DC):
                    creg = dcc % 4
                    wtr = P.op("t", lambda t, bank=bank, creg=creg, nat=nat, dcc=dcc: t.matmul(bank[:, creg * 128:(creg + 1) * 128], nat[:, dcc * 128:(dcc + 1) * 128], ebuf[:, 0:128], start=True, stop=True), waits=[tqq, teye] + ([wTR_war[nslot][creg]] if wTR_war[nslot][creg] else []), inc="pe")
                    nat_pool.read(nslot, wtr)
                    dst = (dcc * 2816 + ic * 128) if name == "wg" else (22528 + dcc * 2816 + ic * 128)
                    if dst + 128 <= 36864:
                        tev = P.op("s", lambda s, bank=bank, creg=creg, dst=dst: s.activation(out=wa[:, dst:dst + 128], in_=bank[:, creg * 128:(creg + 1) * 128], func=AF.Identity), waits=[wtr] + rw, inc="se")
                        (wgT_tok if name == "wg" else wuT_tok)[dcc].append(tev)
                    else:
                        ssl, sw = stg_pool.acquire()
                        hst = hho[:, ssl * 128:(ssl + 1) * 128]
                        tev = P.op("s", lambda s, bank=bank, creg=creg, hst=hst: s.activation(out=hst, in_=bank[:, creg * 128:(creg + 1) * 128], func=AF.Identity), waits=[wtr] + sw, inc="se")
                        tpkT = P.op("s", lambda s, hst=hst, dst=dst: s.dma_start(out=pkuT.ap()[:, dst - 36864:dst - 36864 + 128], in_=hst), waits=[tev], inc="aPT0" if ssl == 0 else "aPT1", d=16)
                        stg_pool.wrote(ssl, tpkT)
                        stg_last[ssl] = tpkT
                    wTR_war[nslot][creg] = tev
                    wlast_bank[nslot] = tev
        PREP_JOBS = []
        for _ic in range(IC):
            PREP_JOBS.append(lambda ic=_ic: prep_gu_tile("wg", wg_, pkg, O_SCG, ic))
        for _ic in range(IC):
            PREP_JOBS.append(lambda ic=_ic: prep_gu_tile("wu", wu_, pku, O_SCU, ic))
        for _hcn in range(DC):
            PREP_JOBS.append(lambda hcn=_hcn: prep_wd_tile(hcn))

        def prep_wd_tile(hcn):
            mxp = []
            for j, (o, wdt) in enumerate([(0, 1024), (1024, 1024), (2048, 768)]):
                slot, w = wq_raw.acquire()
                raw = scr[:, slot * 1024:slot * 1024 + wdt]
                tl = P.op("g", lambda g, hcn=hcn, o=o, wdt=wdt, raw=raw: g.dma_start(out=raw, in_=wd_[hcn * 128:(hcn + 1) * 128, o:o + wdt]), waits=w, inc="sQ0" if slot == 0 else "sQ1", d=16)
                wq_raw.wrote(slot, tl)
                tmx = P.op("v", lambda v, raw=raw, j=j: v.tensor_reduce(out=smf[:, O_MX + j:O_MX + j + 1], in_=raw, axis=AX.X, op=ALU.max, apply_absolute_value=True), waits=[tl] + ([mxp[0]] if (j and False) else []), inc="ve")
                wq_raw.read(slot, tmx)
                mxp.append(tmx)
            tm2 = P.op("v", lambda v: v.tensor_tensor(out=smv(O_MX), in0=smf[:, O_MX:O_MX + 1], in1=smf[:, O_MX + 1:O_MX + 2], op=ALU.max), waits=mxp, inc="ve")
            tm3 = P.op("v", lambda v: v.tensor_tensor(out=smv(C_T0), in0=smv(O_MX), in1=smf[:, O_MX + 2:O_MX + 3], op=ALU.max), waits=[tm2], inc="ve")
            tsc = P.op("v", lambda v, hcn=hcn: v.tensor_scalar(out=smf[:, O_SCD + hcn:O_SCD + hcn + 1], in0=smv(C_T0), scalar1=1.0 / 127.0, scalar2=None, op0=ALU.mult), waits=[tm3], inc="ve")
            trc = P.op("v", lambda v: v.reciprocal(out=smv(C_T1), in_=smf[:, O_SCD + hcn:O_SCD + hcn + 1] if False else smv(C_T0)), waits=[tsc], inc="ve")
            trc = P.op("v", lambda v, hcn=hcn: v.reciprocal(out=smv(C_T1), in_=smf[:, O_SCD + hcn:O_SCD + hcn + 1]), waits=[tsc, trc] + ([tq_prev[0]] if tq_prev[0] else []), inc="ve")
            for j, (o, wdt) in enumerate([(0, 1024), (1024, 1024), (2048, 768)]):
                slot, w = wq_raw.acquire()
                raw = scr[:, slot * 1024:slot * 1024 + wdt]
                tl = P.op("g", lambda g, hcn=hcn, o=o, wdt=wdt, raw=raw: g.dma_start(out=raw, in_=wd_[hcn * 128:(hcn + 1) * 128, o:o + wdt]), waits=w + [trc], inc="sQ0" if slot == 0 else "sQ1", d=16)
                wq_raw.wrote(slot, tl)
                tq = P.op("s", lambda s, raw=raw: s.activation(out=raw, in_=raw, func=AF.Identity, scale=smv(C_T1), bias=smv(C_CB)), waits=[tl], inc="se")
                tq_prev[0] = tq
                nslot, wn = nat_pool.acquire()
                nat = hgb[:, nslot * 1024:nslot * 1024 + wdt]
                tqq = P.op("s", lambda s, raw=raw, nat=nat: s.activation(out=nat, in_=raw, func=AF.Identity, bias=smf[:, 14:15]), waits=[tq, tNCB] + wn, inc="se")
                nat_pool.wrote(nslot, tqq)
                wq_raw.read(slot, tqq)
                tpk = P.op("s", lambda s, hcn=hcn, o=o, wdt=wdt, nat=nat: s.dma_start(out=pkd.ap()[hcn * 128:(hcn + 1) * 128, o:o + wdt], in_=nat), waits=[tqq], inc="aP0" if nslot == 0 else "aP1", d=16)
                nat_pool.read(nslot, tpk)
                park_done["wd"].append(tpk)

        mod_tm_readers = [tga, tgb, tmodA, tmodB] + tmb
        teye = P.op("g", lambda g: g.dma_start(out=ebuf[:, 0:128], in_=ey_), inc="sEY", d=16)
        prev_evac = [[] for _ in range(DC)]
        prev_evac[0] = [swb_tok[0], swb_tok[2]]
        prev_evac[1] = [swb_tok[0], swb_tok[2]]
        prev_evac[2] = [swb_tok[1], swb_tok[3]]
        prev_evac[3] = [swb_tok[1], swb_tok[3]]
        tr_tok = [[None] * DC, [None] * DC]
        seedA = rep_toks["sbr"][1]
        zpool = Pool(2, seed=seedA)
        epool = Pool(2, seed=seedA)
        ynpool = Pool(2)
        h1pool = Pool(3)
        stpool = Pool(6)
        fm_hist = [[ts1], [ts1]]
        sfm_hist = [[], []]
        h2sp_done = []
        x2sp_done = {}
        h1war = []
        xhwar = []
        xhwar_prev = []
        ypar_reader = [tmodA, tmodB]
        scan_tail = [None] * DC
        amax_last = None
        tam0 = P.op("v", lambda v: v.memset(smv(C_AM), -3e38), inc="ve")
        tan0 = P.op("v", lambda v: v.memset(smv(C_AN), 3e38), waits=[tam0], inc="ve")
        amax_last = tan0

        def ln_stats(src, w, k, wb=None):
            o = O_ST + 24 * k
            t1 = P.op("v", lambda v, src=src, o=o: v.bn_stats(out=smf[:, o:o + 6], in_=src[:, 0:512]), waits=w, inc="ve")
            t2 = P.op("v", lambda v, src=src, o=o: v.bn_stats(out=smf[:, o + 6:o + 12], in_=src[:, 512:1024]), waits=(w if wb is None else wb), inc="ve")
            t3 = P.op("v", lambda v, o=o: v.bn_aggr(out=smf[:, o + 12:o + 14], in_=smf[:, o:o + 12]), waits=[t1, t2], inc="ve")
            t4 = P.op("s", lambda s, o=o: s.activation(out=smf[:, o + 14:o + 15], in_=smf[:, o + 13:o + 14], func=AF.Sqrt, bias=smv(C_EPS)), waits=[t3, teps], inc="se")
            t5 = P.op("v", lambda v, o=o: v.reciprocal(out=smf[:, o + 15:o + 16], in_=smf[:, o + 14:o + 15]), waits=[t4], inc="ve")
            t6 = P.op("v", lambda v, o=o: v.tensor_scalar(out=smf[:, o + 16:o + 17], in0=smf[:, o + 12:o + 13], scalar1=smf[:, o + 15:o + 16], scalar2=-1.0, op0=ALU.mult, op1=ALU.mult), waits=[t5], inc="ve")
            return smf[:, o + 15:o + 16], smf[:, o + 16:o + 17], t6

        for s_ in range(BL):
            for h in range(NH):
                par = (s_ * NH + h) % 2
                fo = par * 4096
                fmbwar = fm_hist[par]
                fm_hist[par] = []
                evac_half = [[None] * DC, [None] * DC]
                tx2_list = []
                for tt in range(TPH):
                    gt = s_ * (NH * TPH) + h * TPH + tt
                    zslot, w = zpool.acquire()
                    zt = x0w[:, zslot * 1024:(zslot + 1) * 1024]
                    extra = [tcs] if (zslot == 0 and gt < 2) else []
                    tz = P.op("q", lambda q, gt=gt, zt=zt: q.dma_start(out=zt, in_=Z[gt * 128:(gt + 1) * 128, :]), waits=w + extra, inc="sZ0" if zslot == 0 else "sZ1", d=16)
                    zpool.wrote(zslot, tz)
                    kst, wst = stpool.acquire()
                    r_ap, mr_ap, tstat = ln_stats(zt, [tz] + wst, kst)
                    zpool.read(zslot, tstat)
                    h1 = h1sb[:, tt * 1024:(tt + 1) * 1024]
                    th1 = P.op("s", lambda s, zt=zt, h1=h1, r_ap=r_ap, mr_ap=mr_ap: s.activation(out=h1, in_=zt, func=AF.Identity, scale=r_ap, bias=mr_ap), waits=[tstat] + h1war, inc="se")
                    zpool.read(zslot, th1)
                    stpool.wrote(kst, th1)
                    for dcc in range(DC):
                        pb = psum[4 + dcc // 2]
                        col = (dcc % 2) * 256 + (tt % 2) * 128
                        ttr = P.op("t", lambda t, pb=pb, col=col, h1=h1, dcc=dcc: t.matmul(pb[:, col:col + 128], h1[:, dcc * 128:(dcc + 1) * 128], ebuf[:, 0:128], start=True, stop=True), waits=[th1, teye] + prev_evac[dcc], inc="pe")
                        tr_tok[tt % 2][dcc] = ttr
                    if tt % 2 == 1:
                        p_ = tt // 2
                        for dcc in range(DC):
                            pb = psum[4 + dcc // 2]
                            tev = P.op("s", lambda s, pb=pb, dcc=dcc, fo=fo, p_=p_, s_=s_: s.activation(out=fmb[:, fo + dcc * SEG + p_ * 256:fo + dcc * SEG + p_ * 256 + 256], in_=pb[:, (dcc % 2) * 256:(dcc % 2) * 256 + 256], func=AF.Identity, scale=smf[:, O_CA1 + 2 * dcc + s_:O_CA1 + 2 * dcc + s_ + 1]), waits=[tr_tok[0][dcc], tr_tok[1][dcc], tca1] + fmbwar, inc="se")
                            prev_evac[dcc] = [tev]
                            evac_half[p_][dcc] = tev
                scans = []
                sfw = sfm_hist[par]
                sfm_hist[par] = []
                for dcc in range(DC):
                    init = 0.0 if h == 0 else smf[:, O_TAIL + dcc:O_TAIL + dcc + 1]
                    winit = [] if h == 0 else [scan_tail[dcc]]
                    tscan = P.op("v", lambda v, dcc=dcc, fo=fo, init=init: v.tensor_tensor_scan(out=sfm[:, fo + dcc * SEG:fo + (dcc + 1) * SEG], data0=bass.AP(smf, C_A, [[512, 128], [0, SEG]]), data1=fmb[:, fo + dcc * SEG:fo + (dcc + 1) * SEG], initial=init, op0=ALU.mult, op1=ALU.add), waits=[evac_half[0][dcc], evac_half[1][dcc], tac] + sfw + winit, inc="ve")
                    ttl = P.op("v", lambda v, dcc=dcc, fo=fo: v.tensor_copy(out=smf[:, O_TAIL + dcc:O_TAIL + dcc + 1], in_=sfm[:, fo + (dcc + 1) * SEG - 1:fo + (dcc + 1) * SEG]), waits=[tscan] + ([scan_tail[dcc]] if h > 0 else []), inc="ve")
                    scan_tail[dcc] = ttl
                    fm_hist[par].append(tscan)
                    sfm_hist[par].append(ttl)
                    scans.append(tscan)
                pending_post = None
                for tt in range(TPH):
                    gt = s_ * (NH * TPH) + h * TPH + tt
                    mm_l = []
                    for eh in range(2):
                        pb = psum[eh]
                        toks = []
                        for dcc in range(DC):
                            def mmy(t, dcc=dcc, eh=eh, tt=tt, pb=pb, fo=fo):
                                return t.matmul(pb[:, 0:512], sfm[:, fo + dcc * SEG + tt * 128:fo + dcc * SEG + tt * 128 + 128], wt[:, dcc * 1024 + eh * 512:dcc * 1024 + eh * 512 + 512], start=(dcc == 0), stop=False)
                            w = [scans[dcc], tWt[dcc]]
                            if dcc == 0 and ypar_reader[eh] is not None:
                                w.append(ypar_reader[eh])
                            toks.append(P.op("t", mmy, waits=w, inc="pe"))
                        def mmk2(t, eh=eh, pb=pb, s_=s_, gt=gt):
                            trel = (gt % (NH * TPH)) * 128
                            return t.matmul(pb[:, 0:512], sep[0:2, S_S1 + trel:S_S1 + trel + 128], sep[0:2, SWB[s_] + eh * 512:SWB[s_] + eh * 512 + 512], start=False, stop=True)
                        tk2 = P.op("t", mmk2, waits=[ts1, tone2, swb_tok[2 * s_ + eh], tbias[s_], toks[-1]], inc="pe")
                        last_y_mm = [tk2]
                        mm_l.append(tk2)
                        sfm_hist[par].append(tk2)
                    kst, wst = stpool.acquire()
                    oY = O_ST + 24 * kst
                    pbl, pbh = psum[0], psum[1]
                    ty1 = P.op("v", lambda v, pbl=pbl, oY=oY: v.bn_stats(out=smf[:, oY:oY + 6], in_=pbl[:, 0:512]), waits=[mm_l[0]] + wst, inc="ve")
                    ty2 = P.op("v", lambda v, pbh=pbh, oY=oY: v.bn_stats(out=smf[:, oY + 6:oY + 12], in_=pbh[:, 0:512]), waits=[mm_l[1]], inc="ve")
                    ty3 = P.op("v", lambda v, oY=oY: v.bn_aggr(out=smf[:, oY + 12:oY + 14], in_=smf[:, oY:oY + 12]), waits=[ty1, ty2], inc="ve")
                    ty4 = P.op("s", lambda s, oY=oY: s.activation(out=smf[:, oY + 14:oY + 15], in_=smf[:, oY + 13:oY + 14], func=AF.Sqrt, bias=smv(C_EPS)), waits=[ty3, teps], inc="se")
                    ty5 = P.op("v", lambda v, oY=oY: v.reciprocal(out=smf[:, oY + 15:oY + 16], in_=smf[:, oY + 14:oY + 15]), waits=[ty4], inc="ve")
                    ty6 = P.op("v", lambda v, oY=oY: v.tensor_scalar(out=smf[:, oY + 16:oY + 17], in0=smf[:, oY + 12:oY + 13], scalar1=smf[:, oY + 15:oY + 16], scalar2=-1.0, op0=ALU.mult, op1=ALU.mult), waits=[ty5], inc="ve")
                    ysl, wyn = ynpool.acquire()
                    yt = ynb2[:, ysl * 1024:(ysl + 1) * 1024]
                    tyn = []
                    for eh, pb in ((0, pbl), (1, pbh)):
                        tok = P.op("v", lambda v, pb=pb, eh=eh, yt=yt, oY=oY: v.tensor_scalar(out=yt[:, eh * 512:eh * 512 + 512], in0=pb[:, 0:512], scalar1=smf[:, oY + 15:oY + 16], scalar2=smf[:, oY + 16:oY + 17], op0=ALU.mult, op1=ALU.add), waits=[ty6] + wyn, inc="ve")
                        tyn.append(tok)
                        ypar_reader[eh] = tok
                    stpool.wrote(kst, ty6)
                    tt2 = P.op("v", lambda v, yt=yt, s_=s_: v.tensor_tensor(out=yt, in0=yt, in1=repb[:, s_ * 1024:(s_ + 1) * 1024], op=ALU.mult), waits=tyn + rep_toks["gga"], inc="ve")
                    tu2 = P.op("v", lambda v, yt=yt, s_=s_: v.tensor_tensor(out=yt, in0=yt, in1=repb[:, 2048 + s_ * 1024:2048 + (s_ + 1) * 1024], op=ALU.add), waits=[tt2] + rep_toks["bga"], inc="ve")
                    eslot, we = epool.acquire()
                    x2t = x0w[:, (2 + eslot) * 1024:(3 + eslot) * 1024]
                    tz2 = P.op("q", lambda q, gt=gt, x2t=x2t: q.dma_start(out=x2t, in_=Z[gt * 128:(gt + 1) * 128, :]), waits=we, inc="sE0" if eslot == 0 else "sE1", d=16)
                    tx2a = P.op("g", lambda g, x2t=x2t, yt=yt: g.tensor_tensor(out=x2t[:, 0:512], in0=x2t[:, 0:512], in1=yt[:, 0:512], op=ALU.add), waits=[tz2, tu2], inc="ge")
                    tx2b = P.op("g", lambda g, x2t=x2t, yt=yt: g.tensor_tensor(out=x2t[:, 512:1024], in0=x2t[:, 512:1024], in1=yt[:, 512:1024], op=ALU.add), inc="ge")
                    epool.wrote(eslot, tx2b)
                    epool.read(eslot, tx2a)
                    ynpool.wrote(ysl, tx2b)
                    ynpool.read(ysl, tx2a)
                    tx2_list.append(tx2b)
                    tsp = P.op("s", lambda s, gt=gt, x2t=x2t: s.dma_start(out=spx2.ap()[gt * 128:(gt + 1) * 128, :], in_=x2t), waits=[tx2a, tx2b], inc="aX20" if eslot == 0 else "aX21", d=16)
                    x2sp_done[gt] = tsp
                    epool.read(eslot, tsp)

                    def mk_post(gt=gt, tt=tt, x2t=x2t, eslot=eslot, tx2a=tx2a, tx2b=tx2b, s_=s_, par=par):
                        def post():
                            kst2, wst2 = stpool.acquire()
                            r2, mr2, tst2 = ln_stats(x2t, [tx2a] + wst2, kst2, wb=[tx2b] + wst2)
                            epool.read(eslot, tst2)
                            xh = xhsb[:, tt * 1024:(tt + 1) * 1024]
                            txh = P.op("v", lambda v, x2t=x2t, xh=xh, r2=r2, mr2=mr2: v.tensor_scalar(out=xh, in0=x2t, scalar1=r2, scalar2=mr2, op0=ALU.mult, op1=ALU.add), waits=[tst2] + xhwar_prev, inc="ve")
                            epool.read(eslot, txh)
                            stpool.wrote(kst2, txh)
                            tt4 = P.op("v", lambda v, xh=xh, s_=s_: v.tensor_tensor(out=xh, in0=xh, in1=repb[:, 6144 + s_ * 1024:6144 + (s_ + 1) * 1024], op=ALU.mult), waits=[txh] + rep_toks["cb1"], inc="ve")
                            th2 = P.op("v", lambda v, xh=xh, s_=s_: v.tensor_tensor(out=xh, in0=xh, in1=repb[:, 8192 + s_ * 1024:8192 + (s_ + 1) * 1024], op=ALU.add), waits=[tt4] + rep_toks["sbr"], inc="ve")
                            tamx = P.op("v", lambda v, xh=xh, gt=gt: v.tensor_reduce(out=smf[:, 384 + gt:385 + gt], in_=xh, axis=AX.X, op=ALU.max, apply_absolute_value=True), waits=[th2], inc="ve")
                            tspl = P.op("s", lambda s, gt=gt, xh=xh: s.dma_start(out=sph2.ap()[gt * 128:(gt + 1) * 128, :], in_=xh), waits=[th2], inc="aH20" if par == 0 else "aH21", d=16)
                            h2sp_done.append(tspl)
                            xhwar.append(tamx)
                            xhwar.append(tspl)
                            return tamx
                        return post

                    if pending_post is not None:
                        amax_last = pending_post()
                    pending_post = mk_post()
                    for _ in range(2):
                        if PREP_JOBS:
                            PREP_JOBS.pop(0)()
                if pending_post is not None:
                    amax_last = pending_post()
                    pending_post = None
                h1war = list(scans)
                xhwar_prev = list(xhwar)
                xhwar = []

        while PREP_JOBS:
            PREP_JOBS.pop(0)()
        parks_all = [("aP0", P.cnt.get("aP0", 0)), ("aP1", P.cnt.get("aP1", 0))]
        tscdw = P.op("q", lambda q: q.dma_start(out=bass.AP(hsf, 8 * 1024, [[1, 128], [128, 8]]), in_=smf[:, O_SCD:O_SCD + 8], allow_slow_non_contiguous=True), waits=parks_all, inc="sS3", d=16)
        tscd2 = P.op("q", lambda q: q.dma_start(out=gam2, in_=bass.AP(hsf, 8 * 1024, [[0, 2], [1, 1024]])), waits=[tscdw, tbgm] + rep_toks["bga"], inc="sS3", d=16)
        make_rep("scdgb", 4, lambda: P.op("v", lambda v: v.tensor_tensor(out=gam2, in0=gam2, in1=gb_tm, op=ALU.mult), waits=[tscd2, tgb, tbgm] + rep_toks["sbr"], inc="ve"), 4, srcap=gam2)

        for dcc in range(DC):
            wgT_tok[dcc].append(P.op("q", lambda q, dcc=dcc: q.dma_start_transpose(wa[:, dcc * 2816:(dcc + 1) * 2816], pkg.ap()[:, dcc * 128:(dcc + 1) * 128]), waits=park_done["wg"] + [tmodA, tmodB] + [t for t in wm_slot_tok if t], inc="sWG", d=16))
        for dcc in range(DC):
            wuT_tok[dcc].append(P.op("q", lambda q, dcc=dcc: q.dma_start_transpose(wa[:, 22528 + dcc * 2816:22528 + (dcc + 1) * 2816], pku.ap()[:, dcc * 128:(dcc + 1) * 128]), waits=park_done["wu"] + mod_tm_readers + last_y_mm, inc="sWU", d=16))

        def global_reduce(prev_tok, cci, cco, ccp, hrow, scol, sigcol, iscol, extra_q=(), extra_t8=()):
            t1 = P.op("v", lambda v: v.tensor_scalar(out=smv(C_T0), in0=smv(C_AN), scalar1=-1.0, scalar2=None, op0=ALU.mult), waits=[prev_tok], inc="ve")
            t2 = P.op("v", lambda v: v.tensor_tensor(out=smv(C_T0), in0=smv(C_T0), in1=smv(C_AM), op=ALU.max), waits=[t1], inc="ve")
            t3 = P.op("q", lambda q, hrow=hrow: q.dma_start(out=hsf.ap()[hrow:hrow + 1, 0:128], in_=smv(C_T0)), waits=[t2] + list(extra_q), inc="sGR", d=16)
            t4 = P.op("q", lambda q, hrow=hrow: q.dma_start(out=scr[0:1, 0:128], in_=hsf.ap()[hrow:hrow + 1, 0:128]), waits=[t3, tones, ts1], inc="sGR", d=16)
            t5 = P.op("v", lambda v: v.tensor_reduce(out=scr[0:1, 128:129], in_=scr[0:1, 0:128], axis=AX.X, op=ALU.max), waits=[t4], inc="ve")
            t6 = P.op("q", lambda q, cci=cci: q.dma_start(out=cci.ap()[:, :], in_=scr[0:1, 128:129]), waits=[t5], inc="sGR", d=16)
            t7 = P.op("g", lambda g, cci=cci, cco=cco: g.collective_compute("AllReduce", ALU.max, ins=[cci.ap()[:, :]], outs=[cco.ap()[:, :]], replica_groups=[list(range(CC_WORLD))]), waits=[t6], inc="cc")
            t7b = P.op("g", lambda g, cci=cci, ccp=ccp: g.collective_compute("AllReduce", ALU.max, ins=[cci.ap()[:, :]], outs=[ccp.ap()[:, :]], replica_groups=[list(range(CC_WORLD))]), waits=[t7], inc="cc")
            t8 = P.op("g", lambda g, cco=cco: g.dma_start(out=smv(scol), in_=bass.AP(cco, 0, [[0, 128], [1, 1]])), waits=[t7b] + list(extra_t8), inc="sGR2", d=16)
            t9 = P.op("v", lambda v: v.tensor_scalar(out=smv(scol), in0=smv(scol), scalar1=smv(sigcol), scalar2=1.0 / 127.0, op0=ALU.mult, op1=ALU.mult), waits=[t8, tsx, tsd], inc="ve")
            t9b = P.op("v", lambda v: v.tensor_scalar(out=smv(scol), in0=smv(scol), scalar1=1e-35, scalar2=None, op0=ALU.max), waits=[t9], inc="ve")
            t10 = P.op("v", lambda v: v.reciprocal(out=smv(iscol), in_=smv(scol)), waits=[t9b], inc="ve")
            return t9b, t10
        tcmb = P.op("v", lambda v: v.tensor_reduce(out=smv(C_AM), in_=smf[:, 384:416], axis=AX.X, op=ALU.max), waits=[amax_last], inc="ve")
        h2_all = [("aH20", P.cnt.get("aH20", 0)), ("aH21", P.cnt.get("aH21", 0))]
        x2_all = [("aX20", P.cnt.get("aX20", 0)), ("aX21", P.cnt.get("aX21", 0))]
        fmb_hist = [list(fm_hist[0]), list(fm_hist[1])]
        xq_hist = [list(sfm_hist[0]), list(sfm_hist[1])]
        last_mm_xq = [None, None]

        def emit_tl(g):
            par = g % 2
            base = par * 4096
            w_h2 = fmb_hist[par]
            fmb_hist[par] = []
            toks = []
            for dcc in range(DC):
                h2in = fmb[:, base + dcc * 512:base + (dcc + 1) * 512]
                tl = P.op("q", lambda q, g=g, dcc=dcc, h2in=h2in: q.dma_start_transpose(h2in, sph2.ap()[g * GT:(g + 1) * GT, dcc * 128:(dcc + 1) * 128]), waits=h2_all + w_h2 + ([last_mm_xq[par]] if last_mm_xq[par] else []), inc="sH0" if par == 0 else "sH1", d=16)
                toks.append(tl)
            return toks
        tl_pref = {0: emit_tl(0), 1: emit_tl(1)}
        ts1x, tis1 = global_reduce(tcmb, cc1i, cc1o, cc1p, 9, C_T1, C_SX, C_IS1, extra_t8=[tl_pref[1][-1]])
        tsgx = P.op("v", lambda v: v.tensor_scalar(out=smf[:, O_SGX:O_SGX + 22], in0=smf[:, O_SCG:O_SCG + 22], scalar1=smv(C_T1), scalar2=None, op0=ALU.mult), waits=[ts1x], inc="ve")
        tsux = P.op("v", lambda v: v.tensor_scalar(out=smf[:, O_SUX:O_SUX + 22], in0=smf[:, O_SCU:O_SCU + 22], scalar1=smv(C_T1), scalar2=None, op0=ALU.mult), waits=[ts1x], inc="ve")

        tam2 = P.op("v", lambda v: v.memset(smv(C_AM), -3e38), waits=[ts1x], inc="ve")
        tan2 = P.op("v", lambda v: v.memset(smv(C_AN), 3e38), waits=[tam2], inc="ve")
        amax_last = tan2
        hh_sp = []
        gu_reader = [None, None, wlast_bank[0], wlast_bank[1]]
        hg_war = [None, None]
        hu_war = [None, None]
        hh_war = [stg_last[0], stg_last[1]]
        tq1_last = [None, None]
        for g in range(NG):
            s_ = g // GPS
            par = g % 2
            base = par * 4096
            w_xq = xq_hist[par]
            xq_hist[par] = []
            xq_tok = []
            tl_toks = tl_pref.pop(g) if g in tl_pref else emit_tl(g)
            for dcc in range(DC):
                h2in = fmb[:, base + dcc * 512:base + (dcc + 1) * 512]
                tq1 = P.op("s", lambda s, h2in=h2in, par=par: s.activation(out=scr[:, par * 512:par * 512 + 512], in_=h2in, func=AF.Identity, scale=smv(C_IS1), bias=smv(C_CB)), waits=[tl_toks[-1], tis1] + ([tq1_last[par]] if tq1_last[par] else []), inc="se")
                tq2 = P.op("v", lambda v, par=par: v.tensor_scalar(out=scr[:, par * 512:par * 512 + 512], in0=scr[:, par * 512:par * 512 + 512], scalar1=smv(C_HI), scalar2=smv(C_LO), op0=ALU.min, op1=ALU.max), waits=[tq1, thi, tlo], inc="ve")
                xq = sfm[:, base + dcc * 512:base + (dcc + 1) * 512]
                tq3 = P.op("v", lambda v, xq=xq, par=par: v.tensor_scalar(out=xq, in0=scr[:, par * 512:par * 512 + 512], scalar1=smv(C_CB), scalar2=None, op0=ALU.subtract), waits=[tq2] + w_xq, inc="ve")
                w_xq = []
                tq1_last[par] = tq3
                xq_tok.append(tq3)
                fmb_hist[par].append(tq1)
            for ic in range(IC):
                icp = ic % 2
                pg = psum[icp]
                pu = psum[2 + icp]
                gtoks = []
                for dcc in range(DC):
                    def mmg(t, ic=ic, dcc=dcc, pg=pg, base=base):
                        return t.matmul(pg[:, 0:512], wa[:, dcc * 2816 + ic * 128:dcc * 2816 + ic * 128 + 128], sfm[:, base + dcc * 512:base + (dcc + 1) * 512], start=(dcc == 0), stop=(dcc == DC - 1))
                    w = [xq_tok[dcc]] + wgT_tok[dcc]
                    if dcc == 0 and gu_reader[icp] is not None:
                        w.append(gu_reader[icp])
                    gtoks.append(P.op("t", mmg, waits=w, inc="pe"))
                utoks = []
                for dcc in range(DC):
                    def mmu(t, ic=ic, dcc=dcc, pu=pu, base=base):
                        return t.matmul(pu[:, 0:512], wa[:, 22528 + dcc * 2816 + ic * 128:22528 + dcc * 2816 + ic * 128 + 128], sfm[:, base + dcc * 512:base + (dcc + 1) * 512], start=(dcc == 0), stop=(dcc == DC - 1))
                    w = [xq_tok[dcc]] + wuT_tok[dcc]
                    if dcc == 0 and gu_reader[2 + icp] is not None:
                        w.append(gu_reader[2 + icp])
                    utoks.append(P.op("t", mmu, waits=w, inc="pe"))
                hgp = hgb[:, icp * 512:icp * 512 + 512]
                hup = hgb[:, 1024 + icp * 512:1024 + icp * 512 + 512]
                thg = P.op("s", lambda s, pg=pg, hgp=hgp, ic=ic: s.activation(out=hgp, in_=pg[:, 0:512], func=AF.Silu, scale=smf[:, O_SGX + ic:O_SGX + ic + 1]), waits=[gtoks[-1], tsgx] + ([hg_war[icp]] if hg_war[icp] else []), inc="se")
                gu_reader[icp] = thg
                thu = P.op("s", lambda s, pu=pu, hup=hup, ic=ic: s.activation(out=hup, in_=pu[:, 0:512], func=AF.Identity, scale=smf[:, O_SUX + ic:O_SUX + ic + 1]), waits=[utoks[-1], tsux] + ([hu_war[icp]] if hu_war[icp] else []), inc="se")
                gu_reader[2 + icp] = thu
                hh = hho[:, icp * 512:icp * 512 + 512]
                thh = P.op("v", lambda v, hgp=hgp, hup=hup, hh=hh: v.tensor_tensor(out=hh, in0=hgp, in1=hup, op=ALU.mult), waits=[thg, thu] + ([hh_war[icp]] if hh_war[icp] else []), inc="ve")
                hg_war[icp] = thh
                tm = P.op("v", lambda v, hh=hh, hup=hup: v.tensor_scalar(out=hup, in0=hh, scalar1=1.0, scalar2=smv(C_AM), op0=ALU.mult, op1=ALU.max, accum_out=smv(C_AM)), waits=[thh, amax_last], inc="ve")
                tn = P.op("v", lambda v, hh=hh, hup=hup: v.tensor_scalar(out=hup, in0=hh, scalar1=1.0, scalar2=smv(C_AN), op0=ALU.min, op1=ALU.min, accum_out=smv(C_AN)), waits=[tm], inc="ve")
                amax_last = tn
                hu_war[icp] = tn
                tsp = P.op("s", lambda s, g=g, ic=ic, hh=hh: s.dma_start(out=sphh.ap()[g * IC + ic, :, :], in_=hh), waits=[thh], inc="aHH0" if icp == 0 else "aHH1", d=16)
                hh_sp.append(tsp)
                hh_war[icp] = tsp
            last_mm_xq[par] = gtoks[-1]
            xq_hist[par] = [utoks[-1], gtoks[-1]]

        wdT_tok = []
        for icc in range(IC):
            wdT_tok.append(P.op("q", lambda q, icc=icc: q.dma_start_transpose(wa[:, icc * 1024:(icc + 1) * 1024], pkd.ap()[:, icc * 128:(icc + 1) * 128]), waits=parks_all + [amax_last, gu_reader[0], gu_reader[1]], inc="sWD", d=16))
        ts2x, tis2 = global_reduce(amax_last, cc2i, cc2o, cc2p, 10, C_S2B, C_SD, C_IS2, extra_t8=[wdT_tok[0]])

        dn_reader = [None] * 4
        t3_war = [None, None]
        dq_war_prev = {}
        hq_mm_last = [[], []]
        x2pool = Pool(2, seed=tis2)
        opool = Pool(2, seed=tis2)
        out_toks = []
        hh_all = [("aHH0", P.cnt.get("aHH0", 0)), ("aHH1", P.cnt.get("aHH1", 0))]
        for g in range(NG):
            s_ = g // GPS
            par = g % 2
            hqb = 22528 + par * 11264
            wclear = hq_mm_last[par]
            hq_mm_last[par] = []
            def mk_tlh(q, g=g, hqb=hqb):
                src = bass.AP(sphh, g * IC * 128 * GT, [[GT, 128], [128 * GT, IC], [1, GT]])
                return q.dma_start(out=wa[:, hqb:hqb + 11264].rearrange("p (i t) -> p i t", t=GT), in_=src)
            tlh = P.op("q", mk_tlh, waits=hh_all + [tis2] + wclear + (([t for l in wuT_tok for t in l]) if g < 2 else []), inc="sL0" if par == 0 else "sL1", d=16)
            hq_tok = []
            dq_war = dq_war_prev
            for ic in range(IC):
                sl = (ic % 2) * 512
                tq1 = P.op("s", lambda s, hqb=hqb, ic=ic, sl=sl: s.activation(out=scr[:, sl:sl + 512], in_=wa[:, hqb + ic * 512:hqb + (ic + 1) * 512], func=AF.Identity, scale=smv(C_IS2), bias=smv(C_CB)), waits=[tlh, tis2] + ([dq_war[sl]] if sl in dq_war else []), inc="se")
                tq2 = P.op("v", lambda v, sl=sl: v.tensor_scalar(out=scr[:, sl:sl + 512], in0=scr[:, sl:sl + 512], scalar1=smv(C_HI), scalar2=smv(C_LO), op0=ALU.min, op1=ALU.max), waits=[tq1], inc="ve")
                tq3 = P.op("v", lambda v, hqb=hqb, ic=ic, sl=sl: v.tensor_scalar(out=wa[:, hqb + ic * 512:hqb + (ic + 1) * 512], in0=scr[:, sl:sl + 512], scalar1=smv(C_CB), scalar2=None, op0=ALU.subtract), waits=[tq2], inc="ve")
                dq_war[sl] = tq3
                dq_war_prev = dq_war
                hq_tok.append(tq3)
            for tp in range(2):
                tts = (2 * tp, 2 * tp + 1)
                gtoks = {(tt, eh): [] for tt in tts for eh in range(2)}
                for ic in range(IC):
                    for tt in tts:
                        dp = tt % 2
                        for eh in range(2):
                            pb = psum[4 + 2 * dp + eh]
                            def mmd(t, ic=ic, eh=eh, tt=tt, pb=pb, hqb=hqb):
                                return t.matmul(pb[:, 0:512], wa[:, hqb + ic * 512 + tt * 128:hqb + ic * 512 + tt * 128 + 128], wa[:, ic * 1024 + eh * 512:ic * 1024 + eh * 512 + 512], start=(ic == 0), stop=(ic == IC - 1))
                            w = [hq_tok[ic], wdT_tok[ic]]
                            if ic == 0 and dn_reader[2 * dp + eh] is not None:
                                w.append(dn_reader[2 * dp + eh])
                            gtoks[(tt, eh)].append(P.op("t", mmd, waits=w, inc="pe"))
                for tt in tts:
                    gt = g * 4 + tt
                    dp = tt % 2
                    mm_l = [gtoks[(tt, 0)][-1], gtoks[(tt, 1)][-1]]
                    hq_mm_last[par].append(mm_l[0])
                    hq_mm_last[par].append(mm_l[1])
                    x2s, wx2 = x2pool.acquire()
                    x2t = x0w[:, x2s * 1024:(x2s + 1) * 1024]
                    tlx = P.op("q", lambda q, gt=gt, x2t=x2t: q.dma_start(out=x2t, in_=spx2.ap()[gt * 128:(gt + 1) * 128, :]), waits=x2_all + [tis2] + wx2, inc="sX0" if x2s == 0 else "sX1", d=16)
                    x2pool.wrote(x2s, tlx)
                    t3t = []
                    for eh in range(2):
                        pb = psum[4 + 2 * dp + eh]
                        tok = P.op("v", lambda v, pb=pb, eh=eh, dp=dp, s_=s_: v.scalar_tensor_tensor(out=scr[:, 1024 + dp * 1024 + eh * 512:1024 + dp * 1024 + eh * 512 + 512], in0=pb[:, 0:512], scalar=smv(C_S2B), in1=repb[:, 4096 + s_ * 1024 + eh * 512:4096 + s_ * 1024 + eh * 512 + 512], op0=ALU.mult, op1=ALU.mult), waits=[mm_l[eh], ts2x, tis2] + rep_toks["scdgb"] + ([dn_reader[2 * dp + eh]] if dn_reader[2 * dp + eh] else []) + ([t3_war[dp]] if t3_war[dp] else []), inc="ve")
                        dn_reader[2 * dp + eh] = tok
                        t3t.append(tok)
                    osl, wo = opool.acquire()
                    ot = x0w[:, (2 + osl) * 1024:(3 + osl) * 1024]
                    tout = P.op("v", lambda v, x2t=x2t, ot=ot, dp=dp: v.tensor_tensor(out=ot, in0=x2t, in1=scr[:, 1024 + dp * 1024:1024 + dp * 1024 + 1024], op=ALU.add), waits=t3t + [tlx] + wo, inc="ve")
                    opool.wrote(osl, tout)
                    t3_war[dp] = tout
                    x2pool.read(x2s, tout)
                    tst = P.op("s", lambda s, gt=gt, ot=ot: s.dma_start(out=y_[gt * 128:(gt + 1) * 128, :], in_=ot), waits=[tout], inc="aO0" if osl == 0 else "aO1", d=16)
                    opool.read(osl, tst)
                    out_toks.append(tst)
            hq_mm_last[par] += t3t

        P.op("q", None, waits=out_toks)
        GROUP_SEMS = {"sS0", "sB0", "sS2", "sWT", "sWG", "sWU", "sWD", "sMB", "gBI"}

        @block.sync
        def _(q):
            P.emit("q", q, sems, GROUP_SEMS)

        @block.scalar
        def _(s):
            P.emit("s", s, sems, GROUP_SEMS)

        @block.vector
        def _(v):
            P.emit("v", v, sems, GROUP_SEMS)

        @block.gpsimd
        def _(g):
            P.emit("g", g, sems, GROUP_SEMS)

        @block.tensor
        def _(t):
            P.emit("t", t, sems, GROUP_SEMS)

    return nc


_NC_CACHE = None


def kernel(L0_attn_Z, L0_attn_decay, L0_attn_W, L0_attn_b, L0_attn_gamma,
           L0_attn_beta, L0_ffn_clip_x, L0_ffn_clip_down, L0_ffn_w_gate,
           L0_ffn_w_up, L0_ffn_w_down, L0_c, L0_wmod):
    global _NC_CACHE
    Zf = np.ascontiguousarray(np.asarray(L0_attn_Z, dtype=np.float32))
    ks = np.arange(16, 64)
    pw = (2.0 ** (-ks)).astype(np.float32).reshape(1, 48)
    cvals = (1.5 * 2.0 ** ks).astype(np.float32).reshape(1, 48)
    common = {
        "decay": np.asarray(L0_attn_decay, dtype=np.float32).reshape(1, 1),
        "W": np.ascontiguousarray(np.asarray(L0_attn_W, dtype=np.float32)),
        "bias": np.asarray(L0_attn_b, dtype=np.float32).reshape(1, D),
        "gamma": np.asarray(L0_attn_gamma, dtype=np.float32).reshape(1, D),
        "beta": np.asarray(L0_attn_beta, dtype=np.float32).reshape(1, D),
        "clip_x": np.asarray(L0_ffn_clip_x, dtype=np.float32).reshape(1, 1),
        "clip_d": np.asarray(L0_ffn_clip_down, dtype=np.float32).reshape(1, 1),
        "wg": np.ascontiguousarray(np.asarray(L0_ffn_w_gate, dtype=np.float32)),
        "wu": np.ascontiguousarray(np.asarray(L0_ffn_w_up, dtype=np.float32)),
        "wd": np.ascontiguousarray(np.asarray(L0_ffn_w_down, dtype=np.float32)),
        "wmod": np.ascontiguousarray(np.asarray(L0_wmod, dtype=np.float32)),
        "pw": pw,
        "cvals": cvals,
        "eye": np.eye(128, dtype=np.float32),
    }
    cc = np.asarray(L0_c, dtype=np.float32)
    in_maps = []
    for i in range(NCORES):
        m = dict(common)
        m["Z"] = np.ascontiguousarray(Zf[i * BL:(i + 1) * BL].reshape(R, D))
        m["cc_"] = np.ascontiguousarray(cc[i * BL:(i + 1) * BL])
        in_maps.append(m)
    if _NC_CACHE is None:
        _NC_CACHE = build_nc()
    res = run_bass_kernel_spmd(_NC_CACHE, in_maps, core_ids=list(range(NCORES)))
    out = np.empty((B, T, D), dtype=np.float32)
    for i in range(NCORES):
        out[i * BL:(i + 1) * BL] = res.results[i]["y"].reshape(BL, T, D)
    return out


# version 40 (final)
# speedup vs baseline: 1.0125x; 1.0010x over previous
import numpy as np
import concourse.bass as bass
import concourse.mybir as mybir
from concourse.bass_utils import run_bass_kernel_spmd

F32 = mybir.dt.float32
BF16 = mybir.dt.bfloat16
AF = mybir.ActivationFunctionType
ALU = mybir.AluOpType
AX = mybir.AxisListType

NCORES = 8
CC_WORLD = 8
B, T, D, I = 16, 2048, 1024, 2816
BL = B // NCORES
R = BL * T
SEG = 512
NH = T // SEG
TPH = SEG // 128
DC = D // 128
IC = I // 128
GT = 512
NG = R // GT
GPS = NG // BL
MD = 6 * D


class Sched:
    def __init__(self):
        self.ops = []
        self.cnt = {}

    def op(self, eng, fn, waits=(), inc=None, d=1):
        tok = None
        if inc is not None:
            self.cnt[inc] = self.cnt.get(inc, 0) + d
            tok = (inc, self.cnt[inc])
        self.ops.append((eng, fn, [w for w in waits if w is not None], inc, d))
        return tok

    def emit(self, name, eng, sems, group_sems=()):
        last = {}
        for e, fn, waits, inc, d in self.ops:
            if e != name:
                continue
            need = {}
            for sn, val in waits:
                if sn in group_sems:
                    val = self.cnt[sn]
                if val > need.get(sn, -1):
                    need[sn] = val
            for sn, val in need.items():
                if last.get(sn, -1) >= val:
                    continue
                eng.wait_ge(sems[sn], val)
                last[sn] = val
            inst = fn(eng) if fn is not None else None
            if inc is not None:
                inst.then_inc(sems[inc], d)


class Pool:
    def __init__(self, n, seed=None):
        self.n = n
        self.wr = [seed] * n
        self.rd = [[] for _ in range(n)]
        self.i = 0

    def acquire(self):
        s = self.i % self.n
        self.i += 1
        w = list(self.rd[s]) + ([self.wr[s]] if self.wr[s] else [])
        self.rd[s] = []
        return s, w

    def wrote(self, s, tok):
        self.wr[s] = tok

    def read(self, s, tok):
        if tok is not None:
            self.rd[s].append(tok)


def build_nc():
    nc = bass.Bass("TRN2", target_bir_lowering=False, debug=False)
    P = Sched()

    hZ = nc.dram_tensor("Z", [R, D], F32, kind="ExternalInput")
    hdec = nc.dram_tensor("decay", [1, 1], F32, kind="ExternalInput")
    hW = nc.dram_tensor("W", [D, D], F32, kind="ExternalInput")
    hb = nc.dram_tensor("bias", [1, D], F32, kind="ExternalInput")
    hg = nc.dram_tensor("gamma", [1, D], F32, kind="ExternalInput")
    hbe = nc.dram_tensor("beta", [1, D], F32, kind="ExternalInput")
    hcx = nc.dram_tensor("clip_x", [1, 1], F32, kind="ExternalInput")
    hcd = nc.dram_tensor("clip_d", [1, 1], F32, kind="ExternalInput")
    hwg = nc.dram_tensor("wg", [I, D], F32, kind="ExternalInput")
    hwu = nc.dram_tensor("wu", [I, D], F32, kind="ExternalInput")
    hwd = nc.dram_tensor("wd", [D, I], F32, kind="ExternalInput")
    hc = nc.dram_tensor("cc_", [BL, D], F32, kind="ExternalInput")
    hwm = nc.dram_tensor("wmod", [D, MD], F32, kind="ExternalInput")
    hpw = nc.dram_tensor("pw", [1, 48], F32, kind="ExternalInput")
    hcv = nc.dram_tensor("cvals", [1, 48], F32, kind="ExternalInput")
    hey = nc.dram_tensor("eye", [128, 128], F32, kind="ExternalInput")
    hy = nc.dram_tensor("y", [R, D], F32, kind="ExternalOutput")

    pkW = nc.dram_tensor("pkW", [D, D], BF16, kind="Internal")
    pkg = nc.dram_tensor("pkg", [I, D], BF16, kind="Internal")
    pku = nc.dram_tensor("pku", [I, D], BF16, kind="Internal")
    pkd = nc.dram_tensor("pkd", [D, I], BF16, kind="Internal")
    spx2 = nc.dram_tensor("spx2", [R, D], F32, kind="Internal")
    sph2 = nc.dram_tensor("sph2", [R, D], BF16, kind="Internal")
    sphh = nc.dram_tensor("sphh", [NG * IC, 128, GT], BF16, kind="Internal")
    hsf = nc.dram_tensor("hsf", [16, 1024], F32, kind="Internal")
    hsb = nc.dram_tensor("hsb", [96, 128], BF16, kind="Internal")
    hsc = nc.dram_tensor("hsc", [12, 1024], BF16, kind="Internal")
    cc1i = nc.dram_tensor("cc1i", [1, 1], F32, kind="Internal")
    cc1o = nc.dram_tensor("cc1o", [1, 1], F32, kind="Internal", addr_space="Shared" if CC_WORLD > 4 else "Local")
    cc2i = nc.dram_tensor("cc2i", [1, 1], F32, kind="Internal")
    cc2o = nc.dram_tensor("cc2o", [1, 1], F32, kind="Internal", addr_space="Shared" if CC_WORLD > 4 else "Local")
    pkuT = nc.dram_tensor("pkuT", [128, 8192], BF16, kind="Internal")
    cc1p = nc.dram_tensor("cc1p", [1, 1], F32, kind="Internal", addr_space="Shared" if CC_WORLD > 4 else "Local")
    cc2p = nc.dram_tensor("cc2p", [1, 1], F32, kind="Internal", addr_space="Shared" if CC_WORLD > 4 else "Local")

    Z, Wm, bia, gam, bet = hZ.ap(), hW.ap(), hb.ap(), hg.ap(), hbe.ap()
    dec, cx, cd = hdec.ap(), hcx.ap(), hcd.ap()
    wg_, wu_, wd_, c_, wm_ = hwg.ap(), hwu.ap(), hwd.ap(), hc.ap(), hwm.ap()
    pw_, cv_, y_, ey_ = hpw.ap(), hcv.ap(), hy.ap(), hey.ap()

    from contextlib import ExitStack
    with ExitStack() as stack:
        en = stack.enter_context
        wa = en(nc.sbuf_tensor("wa", [128, 45056], BF16))
        fmb = en(nc.sbuf_tensor("fmb", [128, 8192], BF16))
        sfm = en(nc.sbuf_tensor("sfm", [128, 8192], BF16))
        x0w = en(nc.sbuf_tensor("x0w", [128, 4 * 1024], F32))
        scr = en(nc.sbuf_tensor("scr", [128, 3072], F32))
        h1sb = en(nc.sbuf_tensor("h1sb", [128, 4 * 1024], BF16))
        xhsb = en(nc.sbuf_tensor("xhsb", [128, 4 * 1024], BF16))
        ynb2 = en(nc.sbuf_tensor("ynb2", [128, 2 * 1024], BF16))
        repb = en(nc.sbuf_tensor("repb", [128, 10 * 1024], BF16))
        sep = en(nc.sbuf_tensor("sep", [128, 5248], BF16))
        smf = en(nc.sbuf_tensor("smf", [128, 512], F32))
        hgb = en(nc.sbuf_tensor("hgb", [128, 2048], BF16))
        hho = en(nc.sbuf_tensor("hho", [128, 1024], BF16))
        ebuf = en(nc.sbuf_tensor("ebuf", [128, 128], BF16))
        p0 = en(nc.psum_tensor("p0", [128, 512], F32))
        p1 = en(nc.psum_tensor("p1", [128, 512], F32))
        p2 = en(nc.psum_tensor("p2", [128, 512], F32))
        p3 = en(nc.psum_tensor("p3", [128, 512], F32))
        p4 = en(nc.psum_tensor("p4", [128, 512], F32))
        p5 = en(nc.psum_tensor("p5", [128, 512], F32))
        p6 = en(nc.psum_tensor("p6", [128, 512], F32))
        p7 = en(nc.psum_tensor("p7", [128, 512], F32))
        SEM_NAMES = ["pe", "ve", "se", "ge", "cc",
                     "sZ0", "sZ1", "sE0", "sE1", "sT0", "sT1",
                     "sS0", "sB0", "sS2", "sS3", "sWT", "sWG", "sWU", "sWD",
                     "sCT", "sMB", "sMO", "sRP",
                     "sH0", "sH1", "sL0", "sL1", "sX0", "sX1", "sQ0", "sQ1", "sTA0", "sTA1",
                     "aX20", "aX21", "aH20", "aH21", "aHH0", "aHH1", "aP0", "aP1", "aO0", "aO1", "sBE", "sGR",
                     "gWP", "gWM0", "gWM1", "gWM2", "gWM3", "gWF0", "gWF1", "gCB", "gRB", "gBI", "sEY", "sGR2", "aPT0", "aPT1"]
        sems = {n: en(nc.semaphore(n)) for n in SEM_NAMES}
        block = en(nc.Block())
        psum = [p0, p1, p2, p3, p4, p5, p6, p7]
        wt = wa[:, 36864:45056]

        C_CB, C_HI, C_LO, C_A, C_SX, C_SD = 0, 1, 2, 3, 4, 5
        C_IS1, C_S2B, C_IS2 = 6, 7, 8
        C_AM, C_AN, C_T0, C_T1, C_EPS = 9, 10, 11, 12, 13
        O_CA1, O_CB1, O_SB = 16, 32, 48
        O_SCG, O_SCU, O_SCD = 64, 86, 108
        O_SGX, O_SUX = 128, 150
        O_TAIL = 176
        O_MX = 184
        O_ST = 192
        O_EPS, O_DLT, O_CV = 336, 384, 432
        O_SCL = 480
        O_CT = 488

        def smv(c):
            return smf[:, c:c + 1]

        S_S1, S_CT, S_MODT, S_CTL = 0, 3072, 3088, 5232
        SWB = [2048, 3184]
        S_CSIL = 3184
        ga_tm = x0w[0:2, 1024:2048]
        sgn = x0w[0:2, 3072:4096]
        repr_ = x0w[0:2, 0:1024]
        gb_tm = sep[0:2, 4208:5232]
        c_sil = sep[0:2, S_CSIL:S_CSIL + 1024]
        modT = sep[:, S_MODT:S_MODT + 96]
        gam2 = scr[0:2, 2048:3072]

        teps = P.op("v", lambda v: v.memset(smf[:, 13:14], 1e-5), inc="ve")
        t_pw = P.op("q", lambda q: q.dma_start(out=smf[0:1, O_EPS:O_EPS + 48], in_=pw_[:, :]), inc="sS0", d=16)
        t_cv = P.op("q", lambda q: q.dma_start(out=smf[0:1, O_CV:O_CV + 48], in_=cv_[:, :]), inc="sS0", d=16)
        t_dc = P.op("q", lambda q: q.dma_start(out=smf[0:1, O_SCL:O_SCL + 1], in_=dec[:, :]), inc="sS0", d=16)
        t_cx = P.op("q", lambda q: q.dma_start(out=smf[0:1, O_SCL + 1:O_SCL + 2], in_=cx[:, :]), inc="sS0", d=16)
        t_cd = P.op("q", lambda q: q.dma_start(out=smf[0:1, O_SCL + 2:O_SCL + 3], in_=cd[:, :]), inc="sS0", d=16)
        t_g2 = P.op("q", lambda q: q.dma_start(out=gam2, in_=bass.AP(hg, 0, [[0, 2], [1, 1024]])), inc="sS0", d=16)
        t_cl = P.op("q", lambda q: q.dma_start(out=x0w[0:2, 0:1024], in_=c_[:, :]), inc="sS0", d=16)

        tg1 = P.op("v", lambda v: v.tensor_scalar(out=smf[0:1, O_EPS:O_EPS + 48], in0=smf[0:1, O_EPS:O_EPS + 48], scalar1=1.0, scalar2=1.0, op0=ALU.add, op1=ALU.is_gt), waits=[t_pw], inc="ve")
        tg2 = P.op("v", lambda v: v.scalar_tensor_tensor(out=smf[0:1, O_DLT:O_DLT + 47], in0=smf[0:1, O_EPS + 1:O_EPS + 48], scalar=-1.0, in1=smf[0:1, O_EPS:O_EPS + 47], op0=ALU.mult, op1=ALU.add), waits=[tg1], inc="ve")
        tg3 = P.op("v", lambda v: v.tensor_tensor(out=smf[0:1, O_DLT:O_DLT + 47], in0=smf[0:1, O_DLT:O_DLT + 47], in1=smf[0:1, O_CV:O_CV + 47], op=ALU.mult), waits=[tg2, t_cv], inc="ve")
        tC = P.op("v", lambda v: v.tensor_reduce(out=smf[0:1, O_CT:O_CT + 1], in_=smf[0:1, O_DLT:O_DLT + 47], axis=AX.X, op=ALU.add), waits=[tg3], inc="ve")
        tsg = P.op("s", lambda s: s.activation(out=smf[0:1, O_SCL + 4:O_SCL + 7], in_=smf[0:1, O_SCL:O_SCL + 3], func=AF.Sigmoid), waits=[t_dc, t_cx, t_cd], inc="se")
        tb1 = P.op("q", lambda q: q.dma_start(out=hsf.ap()[0:1, 0:1], in_=smf[0:1, O_CT:O_CT + 1]), waits=[tC], inc="sB0", d=16)
        tb2 = P.op("q", lambda q: q.dma_start(out=hsf.ap()[0:1, 1:4], in_=smf[0:1, O_SCL + 4:O_SCL + 7]), waits=[tsg], inc="sB0", d=16)
        tCb = P.op("q", lambda q: q.dma_start(out=smv(C_CB), in_=bass.AP(hsf, 0, [[0, 128], [1, 1]])), waits=[tb1], inc="sS2", d=16)
        tac = P.op("q", lambda q: q.dma_start(out=smv(C_A), in_=bass.AP(hsf, 1, [[0, 128], [1, 1]])), waits=[tb2], inc="sS2", d=16)
        tsx = P.op("q", lambda q: q.dma_start(out=smv(C_SX), in_=bass.AP(hsf, 2, [[0, 128], [1, 1]])), waits=[tb2], inc="sS2", d=16)
        tsd = P.op("q", lambda q: q.dma_start(out=smv(C_SD), in_=bass.AP(hsf, 3, [[0, 128], [1, 1]])), waits=[tb2], inc="sS2", d=16)
        thi = P.op("v", lambda v: v.tensor_scalar(out=smv(C_HI), in0=smv(C_CB), scalar1=127.0, scalar2=None, op0=ALU.add), waits=[tCb], inc="ve")
        tNCB = P.op("v", lambda v: v.tensor_scalar(out=smf[:, 14:15], in0=smv(C_CB), scalar1=-1.0, scalar2=None, op0=ALU.mult), waits=[tCb], inc="ve")
        tlo = P.op("v", lambda v: v.tensor_scalar(out=smv(C_LO), in0=smv(C_CB), scalar1=-128.0, scalar2=None, op0=ALU.add), waits=[tCb], inc="ve")

        tones = P.op("v", lambda v: v.memset(scr[:, 0:2048], 1.0), inc="ve")
        tone2 = P.op("v", lambda v: v.memset(sep[0:2, S_S1:S_S1 + 2048], 1.0), inc="ve")
        ts1w = P.op("v", lambda v: v.tensor_tensor_scan(out=fmb[:, 0:2048], data0=bass.AP(smf, C_A, [[512, 128], [0, 2048]]), data1=scr[:, 0:2048], initial=0.0, op0=ALU.mult, op1=ALU.add), waits=[tones, tac], inc="ve")
        ts1 = P.op("v", lambda v: v.tensor_copy(out=sep[0:1, S_S1:S_S1 + 2048], in_=fmb[0:1, 0:2048]), waits=[ts1w, tone2], inc="ve")

        tWp = P.op("g", lambda g: g.dma_start(out=pkW.ap()[:, :], in_=Wm[:, :]), inc="gWP", d=16)
        tWt = []
        for dcc in range(DC):
            tWt.append(P.op("q", lambda q, dcc=dcc: q.dma_start_transpose(wt[:, dcc * 1024:(dcc + 1) * 1024], pkW.ap()[:, dcc * 128:(dcc + 1) * 128]), waits=[tWp], inc="sWT", d=16))

        c32 = x0w[0:2, 1024:2048]
        tcs = P.op("s", lambda s: s.activation(out=c32, in_=x0w[0:2, 0:1024], func=AF.Silu), waits=[t_cl], inc="se")
        tch = P.op("v", lambda v: v.tensor_copy(out=c_sil, in_=c32), waits=[tcs], inc="ve")
        c_lo = x0w[0:2, 0:1024]
        tclo = P.op("v", lambda v: v.tensor_tensor(out=c_lo, in0=c32, in1=c_sil, op=ALU.subtract), waits=[tch], inc="ve")
        tcb = P.op("g", lambda g: g.dma_start(out=hsc.ap()[0:2, :], in_=c_sil), waits=[tch], inc="gCB", d=16)
        tcb2 = P.op("g", lambda g: g.dma_start(out=hsc.ap()[8:10, :], in_=c_lo), waits=[tclo], inc="gRB", d=16)
        tct = P.op("q", lambda q: q.dma_start_transpose(sep[:, S_CT:S_CT + 16], bass.AP(hsc, 0, [[128, 16], [1, 128]])), waits=[tcb], inc="sCT", d=16)
        tct2 = P.op("q", lambda q: q.dma_start_transpose(sep[:, S_CTL:S_CTL + 16], bass.AP(hsc, 8 * 1024, [[128, 16], [1, 128]])), waits=[tcb2], inc="sCT", d=16)

        mod_tm = wa[0:2, 26624:26624 + MD]
        tmodev = [None, None]
        wm_slot_tok = [None, None, None, None]
        wm32war = [None, None]
        GAGB_N = {4, 5, 10, 11}
        for n in range(12):
            pb = psum[n % 2]
            mmtok = None
            for dcc in range(DC):
                slot = (n * DC + dcc) % 4
                hi = wa[:, 8192 + 1024 * slot:8192 + 1024 * slot + 512]
                lo = wa[:, 8192 + 1024 * slot + 512:8192 + 1024 * (slot + 1)]
                wld = [wm_slot_tok[slot]] if wm_slot_tok[slot] else []
                if n in GAGB_N:
                    f32s = slot % 2
                    wm32 = x0w[:, 2048 + 512 * f32s:2048 + 512 * (f32s + 1)]
                    w32 = [wm32war[f32s]] if wm32war[f32s] else []
                    tld = P.op("g", lambda g, n=n, dcc=dcc, wm32=wm32: g.dma_start(out=wm32, in_=wm_[dcc * 128:(dcc + 1) * 128, n * 512:(n + 1) * 512]), waits=w32, inc="gWF0" if f32s == 0 else "gWF1", d=16)
                    thh_ = P.op("v", lambda v, wm32=wm32, hi=hi: v.tensor_copy(out=hi, in_=wm32), waits=[tld] + wld, inc="ve")
                    tll = P.op("v", lambda v, wm32=wm32, hi=hi, lo=lo: v.tensor_tensor(out=lo, in0=wm32, in1=hi, op=ALU.subtract), waits=[thh_], inc="ve")
                    wm32war[f32s] = tll
                    def mm1(t, dcc=dcc, hi=hi, pb=pb):
                        return t.matmul(pb[0:2, 0:512], bass.AP(sep, S_CT + dcc, [[5248, 128], [8, 2]]), hi, start=(dcc == 0), stop=False)
                    def mm2(t, dcc=dcc, hi=hi, pb=pb):
                        return t.matmul(pb[0:2, 0:512], bass.AP(sep, S_CTL + dcc, [[5248, 128], [8, 2]]), hi, start=False, stop=False)
                    def mm3(t, dcc=dcc, lo=lo, pb=pb):
                        return t.matmul(pb[0:2, 0:512], bass.AP(sep, S_CT + dcc, [[5248, 128], [8, 2]]), lo, start=False, stop=(dcc == DC - 1))
                    w = [thh_, tct, tct2]
                    if dcc == 0 and tmodev[n % 2] is not None:
                        w.append(tmodev[n % 2])
                    P.op("t", mm1, waits=w, inc="pe")
                    P.op("t", mm2, waits=[thh_, tct2], inc="pe")
                    mmtok = P.op("t", mm3, waits=[tll], inc="pe")
                    wm_slot_tok[slot] = mmtok
                else:
                    tld = P.op("g", lambda g, n=n, dcc=dcc, hi=hi: g.dma_start(out=hi, in_=wm_[dcc * 128:(dcc + 1) * 128, n * 512:(n + 1) * 512]), waits=wld, inc="gWM%d" % slot, d=16)
                    def mmp(t, dcc=dcc, hi=hi, pb=pb):
                        return t.matmul(pb[0:2, 0:512], bass.AP(sep, S_CT + dcc, [[5248, 128], [8, 2]]), hi, start=(dcc == 0), stop=(dcc == DC - 1))
                    w = [tld, tct, tct2]
                    if dcc == 0 and tmodev[n % 2] is not None:
                        w.append(tmodev[n % 2])
                    mmtok = P.op("t", mmp, waits=w, inc="pe")
                    wm_slot_tok[slot] = mmtok
            tmodev[n % 2] = P.op("s", lambda s, n=n, pb=pb: s.activation(out=mod_tm[:, n * 512:(n + 1) * 512], in_=pb[0:2, 0:512], func=AF.Identity), waits=[mmtok], inc="se")
        tmodA, tmodB = tmodev

        tmb = []
        for s_ in range(2):
            tmb.append(P.op("q", lambda q, s_=s_: q.dma_start(out=bass.AP(hsb, s_ * 128, [[256, 48], [1, 128]]), in_=mod_tm[s_:s_ + 1, :]), waits=[tmodA, tmodB], inc="sMB", d=16))
        tmodT = P.op("q", lambda q: q.dma_start_transpose(modT, hsb.ap()[:, :]), waits=tmb, inc="sMO", d=16)
        tca1 = P.op("v", lambda v: v.tensor_scalar(out=smf[:, O_CA1:O_CA1 + 16], in0=modT[:, 16:32], scalar1=1.0, scalar2=None, op0=ALU.add), waits=[tmodT], inc="ve")
        tcb1 = P.op("v", lambda v: v.tensor_scalar(out=smf[:, O_CB1:O_CB1 + 16], in0=modT[:, 64:80], scalar1=1.0, scalar2=None, op0=ALU.add), waits=[tmodT], inc="ve")
        tsbf = P.op("v", lambda v: v.tensor_copy(out=smf[:, O_SB:O_SB + 16], in_=modT[:, 48:64]), waits=[tmodT], inc="ve")

        def eff(src_off, dst, prev):
            t1 = P.op("s", lambda s, o=src_off, dst=dst: s.activation(out=dst, in_=mod_tm[0:2, o:o + 1024], func=AF.Abs), waits=[tmodA, tmodB] + prev, inc="se")
            t2 = P.op("v", lambda v, dst=dst: v.tensor_scalar(out=dst, in0=dst, scalar1=0.25, scalar2=None, op0=ALU.max), waits=[t1], inc="ve")
            t3 = P.op("v", lambda v, o=src_off: v.tensor_scalar(out=sgn, in0=mod_tm[0:2, o:o + 1024], scalar1=0.0, scalar2=None, op0=ALU.is_ge), waits=[tmodA, tmodB] + prev, inc="ve")
            t4 = P.op("v", lambda v: v.tensor_scalar(out=sgn, in0=sgn, scalar1=2.0, scalar2=-1.0, op0=ALU.mult, op1=ALU.add), waits=[t3], inc="ve")
            t5 = P.op("v", lambda v, dst=dst: v.tensor_tensor(out=dst, in0=dst, in1=sgn, op=ALU.mult), waits=[t2, t4], inc="ve")
            return t5
        tga = eff(2048, ga_tm, [])
        tgb = eff(5120, gb_tm, [tga])

        rep_toks = {}

        def make_rep(name, rrow, mk, roff, srcap=None):
            tmk = mk()
            sap = repr_ if srcap is None else srcap
            tb = P.op("g", lambda g, rrow=rrow, sap=sap: g.dma_start(out=hsc.ap()[rrow:rrow + 2, :], in_=sap), waits=[tmk], inc="gRB", d=16)
            toks = []
            for s_ in range(2):
                dst = repb[:, (roff + s_) * 1024:(roff + s_ + 1) * 1024]
                toks.append(P.op("q", lambda q, s_=s_, rrow=rrow, dst=dst: q.dma_start(out=dst, in_=bass.AP(hsc, (rrow + s_) * 1024, [[0, 128], [1, 1024]])), waits=[tb], inc="sRP", d=16))
            rep_toks[name] = toks
            return tmk

        tgg = make_rep("gga", 0, lambda: P.op("v", lambda v: v.tensor_tensor(out=repr_, in0=gam2, in1=ga_tm, op=ALU.mult), waits=[tga, t_g2, tgb, tcs, tcb, tcb2], inc="ve"), 0)
        t_b2 = P.op("q", lambda q: q.dma_start(out=gam2, in_=bass.AP(hbe, 0, [[0, 2], [1, 1024]])), waits=[tgg] + rep_toks["gga"], inc="sBE", d=16)
        tbgm = make_rep("bga", 2, lambda: P.op("v", lambda v: v.tensor_tensor(out=repr_, in0=gam2, in1=ga_tm, op=ALU.mult), waits=[t_b2, tga, tgg] + rep_toks["gga"], inc="ve"), 2)
        tcb1r = make_rep("cb1", 6, lambda: P.op("v", lambda v: v.tensor_scalar(out=repr_, in0=mod_tm[0:2, 4096:5120], scalar1=1.0, scalar2=None, op0=ALU.add), waits=[tmodA, tmodB, tbgm] + rep_toks["bga"], inc="ve"), 6)
        tsbr = make_rep("sbr", 10, lambda: P.op("v", lambda v: v.tensor_copy(out=repr_, in_=mod_tm[0:2, 3072:4096]), waits=[tmodA, tmodB, tcb1r] + rep_toks["cb1"], inc="ve"), 8)


        swb_tok = []
        for s_ in range(2):
            for eh in range(2):
                pb = psum[4 + eh]
                toks = []
                for dcc in range(DC):
                    def mmsw(t, s_=s_, dcc=dcc, eh=eh, pb=pb):
                        return t.matmul(pb[0:1, 0:512], modT[:, 2 * dcc + s_:2 * dcc + s_ + 1], wt[:, dcc * 1024 + eh * 512:dcc * 1024 + eh * 512 + 512], start=(dcc == 0), stop=(dcc == DC - 1))
                    w = [tmodT, tWt[dcc]]
                    if dcc == 0 and s_ == 1:
                        w.append(swb_tok[eh])
                    toks.append(P.op("t", mmsw, waits=w, inc="pe"))
                swb_tok.append(P.op("s", lambda s, s_=s_, eh=eh, pb=pb: s.activation(out=sep[0:1, SWB[s_] + eh * 512:SWB[s_] + eh * 512 + 512], in_=pb[0:1, 0:512], func=AF.Identity), waits=[toks[-1]], inc="se"))
        tbias = []
        for s_ in range(2):
            tbias.append(P.op("g", lambda g, s_=s_: g.dma_start(out=sep[1:2, SWB[s_]:SWB[s_] + 1024], in_=bia[:, :]), waits=[tcs] if s_ == 0 else [], inc="gBI", d=16))

        wq_raw = Pool(2)
        nat_pool = Pool(2)
        tq_prev = [None]
        park_done = {"wg": [], "wu": [], "wd": []}
        wgT_tok = [[] for _ in range(DC)]
        wuT_tok = [[] for _ in range(DC)]
        wTR_war = [[None] * 4, [None] * 4]
        wlast_bank = [None, None]
        stg_pool = Pool(2)
        stg_last = [None, None]

        def prep_gu_tile(name, src, park, scoff, ic):
                slot, w = wq_raw.acquire()
                raw = scr[:, slot * 1024:(slot + 1) * 1024]
                tl = P.op("g", lambda g, ic=ic, raw=raw, src=src: g.dma_start(out=raw, in_=src[ic * 128:(ic + 1) * 128, :]), waits=w + ([tones, ts1] if ic < 2 else []), inc="sQ0" if slot == 0 else "sQ1", d=16)
                wq_raw.wrote(slot, tl)
                tmx = P.op("v", lambda v, raw=raw: v.tensor_reduce(out=smv(C_T0), in_=raw, axis=AX.X, op=ALU.max, apply_absolute_value=True), waits=[tl], inc="ve")
                tsc = P.op("v", lambda v, ic=ic: v.tensor_scalar(out=smf[:, scoff + ic:scoff + ic + 1], in0=smv(C_T0), scalar1=1.0 / 127.0, scalar2=None, op0=ALU.mult), waits=[tmx], inc="ve")
                trc = P.op("v", lambda v, ic=ic: v.reciprocal(out=smv(C_T1), in_=smf[:, scoff + ic:scoff + ic + 1]), waits=[tsc] + ([tq_prev[0]] if tq_prev[0] else []), inc="ve")
                tq = P.op("s", lambda s, raw=raw: s.activation(out=raw, in_=raw, func=AF.Identity, scale=smv(C_T1), bias=smv(C_CB)), waits=[trc, tCb, tmx], inc="se")
                tq_prev[0] = tq
                nslot, wn = nat_pool.acquire()
                nat = hgb[:, nslot * 1024:(nslot + 1) * 1024]
                tqq = P.op("s", lambda s, raw=raw, nat=nat: s.activation(out=nat, in_=raw, func=AF.Identity, bias=smf[:, 14:15]), waits=[tq, tNCB] + wn, inc="se")
                nat_pool.wrote(nslot, tqq)
                wq_raw.read(slot, tqq)
                if name in ("wu", "wg"):
                    tpk = P.op("s", lambda s, ic=ic, nat=nat, park=park: s.dma_start(out=park.ap()[ic * 128:(ic + 1) * 128, :], in_=nat), waits=[tqq], inc="aP0" if nslot == 0 else "aP1", d=16)
                    nat_pool.read(nslot, tpk)
                    park_done[name].append(tpk)
                    return
                bank = psum[2 + nslot]
                rw = ([tmodA, tmodB] + [t for t in wm_slot_tok if t]) if name == "wg" else list(mod_tm_readers)
                for dcc in range(DC):
                    creg = dcc % 4
                    wtr = P.op("t", lambda t, bank=bank, creg=creg, nat=nat, dcc=dcc: t.matmul(bank[:, creg * 128:(creg + 1) * 128], nat[:, dcc * 128:(dcc + 1) * 128], ebuf[:, 0:128], start=True, stop=True), waits=[tqq, teye] + ([wTR_war[nslot][creg]] if wTR_war[nslot][creg] else []), inc="pe")
                    nat_pool.read(nslot, wtr)
                    dst = (dcc * 2816 + ic * 128) if name == "wg" else (22528 + dcc * 2816 + ic * 128)
                    if dst + 128 <= 36864:
                        tev = P.op("s", lambda s, bank=bank, creg=creg, dst=dst: s.activation(out=wa[:, dst:dst + 128], in_=bank[:, creg * 128:(creg + 1) * 128], func=AF.Identity), waits=[wtr] + rw, inc="se")
                        (wgT_tok if name == "wg" else wuT_tok)[dcc].append(tev)
                    else:
                        ssl, sw = stg_pool.acquire()
                        hst = hho[:, ssl * 128:(ssl + 1) * 128]
                        tev = P.op("s", lambda s, bank=bank, creg=creg, hst=hst: s.activation(out=hst, in_=bank[:, creg * 128:(creg + 1) * 128], func=AF.Identity), waits=[wtr] + sw, inc="se")
                        tpkT = P.op("s", lambda s, hst=hst, dst=dst: s.dma_start(out=pkuT.ap()[:, dst - 36864:dst - 36864 + 128], in_=hst), waits=[tev], inc="aPT0" if ssl == 0 else "aPT1", d=16)
                        stg_pool.wrote(ssl, tpkT)
                        stg_last[ssl] = tpkT
                    wTR_war[nslot][creg] = tev
                    wlast_bank[nslot] = tev
        PREP_JOBS = []
        for _ic in range(IC):
            PREP_JOBS.append(lambda ic=_ic: prep_gu_tile("wg", wg_, pkg, O_SCG, ic))
        for _ic in range(IC):
            PREP_JOBS.append(lambda ic=_ic: prep_gu_tile("wu", wu_, pku, O_SCU, ic))
        for _hcn in range(DC):
            PREP_JOBS.append(lambda hcn=_hcn: prep_wd_tile(hcn))

        def prep_wd_tile(hcn):
            mxp = []
            for j, (o, wdt) in enumerate([(0, 1024), (1024, 1024), (2048, 768)]):
                slot, w = wq_raw.acquire()
                raw = scr[:, slot * 1024:slot * 1024 + wdt]
                tl = P.op("g", lambda g, hcn=hcn, o=o, wdt=wdt, raw=raw: g.dma_start(out=raw, in_=wd_[hcn * 128:(hcn + 1) * 128, o:o + wdt]), waits=w, inc="sQ0" if slot == 0 else "sQ1", d=16)
                wq_raw.wrote(slot, tl)
                tmx = P.op("v", lambda v, raw=raw, j=j: v.tensor_reduce(out=smf[:, O_MX + j:O_MX + j + 1], in_=raw, axis=AX.X, op=ALU.max, apply_absolute_value=True), waits=[tl] + ([mxp[0]] if (j and False) else []), inc="ve")
                wq_raw.read(slot, tmx)
                mxp.append(tmx)
            tm2 = P.op("v", lambda v: v.tensor_tensor(out=smv(O_MX), in0=smf[:, O_MX:O_MX + 1], in1=smf[:, O_MX + 1:O_MX + 2], op=ALU.max), waits=mxp, inc="ve")
            tm3 = P.op("v", lambda v: v.tensor_tensor(out=smv(C_T0), in0=smv(O_MX), in1=smf[:, O_MX + 2:O_MX + 3], op=ALU.max), waits=[tm2], inc="ve")
            tsc = P.op("v", lambda v, hcn=hcn: v.tensor_scalar(out=smf[:, O_SCD + hcn:O_SCD + hcn + 1], in0=smv(C_T0), scalar1=1.0 / 127.0, scalar2=None, op0=ALU.mult), waits=[tm3], inc="ve")
            trc = P.op("v", lambda v: v.reciprocal(out=smv(C_T1), in_=smf[:, O_SCD + hcn:O_SCD + hcn + 1] if False else smv(C_T0)), waits=[tsc], inc="ve")
            trc = P.op("v", lambda v, hcn=hcn: v.reciprocal(out=smv(C_T1), in_=smf[:, O_SCD + hcn:O_SCD + hcn + 1]), waits=[tsc, trc] + ([tq_prev[0]] if tq_prev[0] else []), inc="ve")
            for j, (o, wdt) in enumerate([(0, 1024), (1024, 1024), (2048, 768)]):
                slot, w = wq_raw.acquire()
                raw = scr[:, slot * 1024:slot * 1024 + wdt]
                tl = P.op("g", lambda g, hcn=hcn, o=o, wdt=wdt, raw=raw: g.dma_start(out=raw, in_=wd_[hcn * 128:(hcn + 1) * 128, o:o + wdt]), waits=w + [trc], inc="sQ0" if slot == 0 else "sQ1", d=16)
                wq_raw.wrote(slot, tl)
                tq = P.op("s", lambda s, raw=raw: s.activation(out=raw, in_=raw, func=AF.Identity, scale=smv(C_T1), bias=smv(C_CB)), waits=[tl], inc="se")
                tq_prev[0] = tq
                nslot, wn = nat_pool.acquire()
                nat = hgb[:, nslot * 1024:nslot * 1024 + wdt]
                tqq = P.op("s", lambda s, raw=raw, nat=nat: s.activation(out=nat, in_=raw, func=AF.Identity, bias=smf[:, 14:15]), waits=[tq, tNCB] + wn, inc="se")
                nat_pool.wrote(nslot, tqq)
                wq_raw.read(slot, tqq)
                tpk = P.op("s", lambda s, hcn=hcn, o=o, wdt=wdt, nat=nat: s.dma_start(out=pkd.ap()[hcn * 128:(hcn + 1) * 128, o:o + wdt], in_=nat), waits=[tqq], inc="aP0" if nslot == 0 else "aP1", d=16)
                nat_pool.read(nslot, tpk)
                park_done["wd"].append(tpk)

        mod_tm_readers = [tga, tgb, tmodA, tmodB] + tmb
        teye = P.op("g", lambda g: g.dma_start(out=ebuf[:, 0:128], in_=ey_), inc="sEY", d=16)
        prev_evac = [[] for _ in range(DC)]
        prev_evac[0] = [swb_tok[0], swb_tok[2]]
        prev_evac[1] = [swb_tok[0], swb_tok[2]]
        prev_evac[2] = [swb_tok[1], swb_tok[3]]
        prev_evac[3] = [swb_tok[1], swb_tok[3]]
        tr_tok = [[None] * DC, [None] * DC]
        seedA = rep_toks["sbr"][1]
        zpool = Pool(2, seed=seedA)
        epool = Pool(2, seed=seedA)
        ynpool = Pool(2)
        h1pool = Pool(3)
        stpool = Pool(6)
        fm_hist = [[ts1], [ts1]]
        sfm_hist = [[], []]
        h2sp_done = []
        x2sp_done = {}
        h1war = []
        xhwar = []
        xhwar_prev = []
        ypar_reader = [tmodA, tmodB]
        scan_tail = [None] * DC
        amax_last = None
        tam0 = P.op("v", lambda v: v.memset(smv(C_AM), -3e38), inc="ve")
        tan0 = P.op("v", lambda v: v.memset(smv(C_AN), 3e38), waits=[tam0], inc="ve")
        amax_last = tan0

        def ln_stats(src, w, k, wb=None):
            o = O_ST + 24 * k
            t1 = P.op("v", lambda v, src=src, o=o: v.bn_stats(out=smf[:, o:o + 6], in_=src[:, 0:512]), waits=w, inc="ve")
            t2 = P.op("v", lambda v, src=src, o=o: v.bn_stats(out=smf[:, o + 6:o + 12], in_=src[:, 512:1024]), waits=(w if wb is None else wb), inc="ve")
            t3 = P.op("v", lambda v, o=o: v.bn_aggr(out=smf[:, o + 12:o + 14], in_=smf[:, o:o + 12]), waits=[t1, t2], inc="ve")
            t4 = P.op("s", lambda s, o=o: s.activation(out=smf[:, o + 14:o + 15], in_=smf[:, o + 13:o + 14], func=AF.Sqrt, bias=smv(C_EPS)), waits=[t3, teps], inc="se")
            t5 = P.op("v", lambda v, o=o: v.reciprocal(out=smf[:, o + 15:o + 16], in_=smf[:, o + 14:o + 15]), waits=[t4], inc="ve")
            t6 = P.op("v", lambda v, o=o: v.tensor_scalar(out=smf[:, o + 16:o + 17], in0=smf[:, o + 12:o + 13], scalar1=smf[:, o + 15:o + 16], scalar2=-1.0, op0=ALU.mult, op1=ALU.mult), waits=[t5], inc="ve")
            return smf[:, o + 15:o + 16], smf[:, o + 16:o + 17], t6

        for s_ in range(BL):
            for h in range(NH):
                par = (s_ * NH + h) % 2
                fo = par * 4096
                fmbwar = fm_hist[par]
                fm_hist[par] = []
                evac_half = [[None] * DC, [None] * DC]
                tx2_list = []
                for tt in range(TPH):
                    gt = s_ * (NH * TPH) + h * TPH + tt
                    zslot, w = zpool.acquire()
                    zt = x0w[:, zslot * 1024:(zslot + 1) * 1024]
                    extra = [tcs] if (zslot == 0 and gt < 2) else []
                    tz = P.op("q", lambda q, gt=gt, zt=zt: q.dma_start(out=zt, in_=Z[gt * 128:(gt + 1) * 128, :]), waits=w + extra, inc="sZ0" if zslot == 0 else "sZ1", d=16)
                    zpool.wrote(zslot, tz)
                    kst, wst = stpool.acquire()
                    r_ap, mr_ap, tstat = ln_stats(zt, [tz] + wst, kst)
                    zpool.read(zslot, tstat)
                    h1 = h1sb[:, tt * 1024:(tt + 1) * 1024]
                    th1 = P.op("s", lambda s, zt=zt, h1=h1, r_ap=r_ap, mr_ap=mr_ap: s.activation(out=h1, in_=zt, func=AF.Identity, scale=r_ap, bias=mr_ap), waits=[tstat] + h1war, inc="se")
                    zpool.read(zslot, th1)
                    stpool.wrote(kst, th1)
                    for dcc in range(DC):
                        pb = psum[4 + dcc // 2]
                        col = (dcc % 2) * 256 + (tt % 2) * 128
                        ttr = P.op("t", lambda t, pb=pb, col=col, h1=h1, dcc=dcc: t.matmul(pb[:, col:col + 128], h1[:, dcc * 128:(dcc + 1) * 128], ebuf[:, 0:128], start=True, stop=True), waits=[th1, teye] + prev_evac[dcc], inc="pe")
                        tr_tok[tt % 2][dcc] = ttr
                    if tt % 2 == 1:
                        p_ = tt // 2
                        for dcc in range(DC):
                            pb = psum[4 + dcc // 2]
                            tev = P.op("s", lambda s, pb=pb, dcc=dcc, fo=fo, p_=p_, s_=s_: s.activation(out=fmb[:, fo + dcc * SEG + p_ * 256:fo + dcc * SEG + p_ * 256 + 256], in_=pb[:, (dcc % 2) * 256:(dcc % 2) * 256 + 256], func=AF.Identity, scale=smf[:, O_CA1 + 2 * dcc + s_:O_CA1 + 2 * dcc + s_ + 1]), waits=[tr_tok[0][dcc], tr_tok[1][dcc], tca1] + fmbwar, inc="se")
                            prev_evac[dcc] = [tev]
                            evac_half[p_][dcc] = tev
                scans = []
                sfw = sfm_hist[par]
                sfm_hist[par] = []
                for dcc in range(DC):
                    init = 0.0 if h == 0 else smf[:, O_TAIL + dcc:O_TAIL + dcc + 1]
                    winit = [] if h == 0 else [scan_tail[dcc]]
                    tscan = P.op("v", lambda v, dcc=dcc, fo=fo, init=init: v.tensor_tensor_scan(out=sfm[:, fo + dcc * SEG:fo + (dcc + 1) * SEG], data0=bass.AP(smf, C_A, [[512, 128], [0, SEG]]), data1=fmb[:, fo + dcc * SEG:fo + (dcc + 1) * SEG], initial=init, op0=ALU.mult, op1=ALU.add), waits=[evac_half[0][dcc], evac_half[1][dcc], tac] + sfw + winit, inc="ve")
                    ttl = P.op("v", lambda v, dcc=dcc, fo=fo: v.tensor_copy(out=smf[:, O_TAIL + dcc:O_TAIL + dcc + 1], in_=sfm[:, fo + (dcc + 1) * SEG - 1:fo + (dcc + 1) * SEG]), waits=[tscan] + ([scan_tail[dcc]] if h > 0 else []), inc="ve")
                    scan_tail[dcc] = ttl
                    fm_hist[par].append(tscan)
                    sfm_hist[par].append(ttl)
                    scans.append(tscan)
                pending_post = None
                for tt in range(TPH):
                    gt = s_ * (NH * TPH) + h * TPH + tt
                    mm_l = []
                    for eh in range(2):
                        pb = psum[eh]
                        toks = []
                        for dcc in range(DC):
                            def mmy(t, dcc=dcc, eh=eh, tt=tt, pb=pb, fo=fo):
                                return t.matmul(pb[:, 0:512], sfm[:, fo + dcc * SEG + tt * 128:fo + dcc * SEG + tt * 128 + 128], wt[:, dcc * 1024 + eh * 512:dcc * 1024 + eh * 512 + 512], start=(dcc == 0), stop=False)
                            w = [scans[dcc], tWt[dcc]]
                            if dcc == 0 and ypar_reader[eh] is not None:
                                w.append(ypar_reader[eh])
                            toks.append(P.op("t", mmy, waits=w, inc="pe"))
                        def mmk2(t, eh=eh, pb=pb, s_=s_, gt=gt):
                            trel = (gt % (NH * TPH)) * 128
                            return t.matmul(pb[:, 0:512], sep[0:2, S_S1 + trel:S_S1 + trel + 128], sep[0:2, SWB[s_] + eh * 512:SWB[s_] + eh * 512 + 512], start=False, stop=True)
                        tk2 = P.op("t", mmk2, waits=[ts1, tone2, swb_tok[2 * s_ + eh], tbias[s_], toks[-1]], inc="pe")
                        last_y_mm = [tk2]
                        mm_l.append(tk2)
                        sfm_hist[par].append(tk2)
                    kst, wst = stpool.acquire()
                    oY = O_ST + 24 * kst
                    pbl, pbh = psum[0], psum[1]
                    ty1 = P.op("v", lambda v, pbl=pbl, oY=oY: v.bn_stats(out=smf[:, oY:oY + 6], in_=pbl[:, 0:512]), waits=[mm_l[0]] + wst, inc="ve")
                    ty2 = P.op("v", lambda v, pbh=pbh, oY=oY: v.bn_stats(out=smf[:, oY + 6:oY + 12], in_=pbh[:, 0:512]), waits=[mm_l[1]], inc="ve")
                    ty3 = P.op("v", lambda v, oY=oY: v.bn_aggr(out=smf[:, oY + 12:oY + 14], in_=smf[:, oY:oY + 12]), waits=[ty1, ty2], inc="ve")
                    ty4 = P.op("s", lambda s, oY=oY: s.activation(out=smf[:, oY + 14:oY + 15], in_=smf[:, oY + 13:oY + 14], func=AF.Sqrt, bias=smv(C_EPS)), waits=[ty3, teps], inc="se")
                    ty5 = P.op("v", lambda v, oY=oY: v.reciprocal(out=smf[:, oY + 15:oY + 16], in_=smf[:, oY + 14:oY + 15]), waits=[ty4], inc="ve")
                    ty6 = P.op("v", lambda v, oY=oY: v.tensor_scalar(out=smf[:, oY + 16:oY + 17], in0=smf[:, oY + 12:oY + 13], scalar1=smf[:, oY + 15:oY + 16], scalar2=-1.0, op0=ALU.mult, op1=ALU.mult), waits=[ty5], inc="ve")
                    ysl, wyn = ynpool.acquire()
                    yt = ynb2[:, ysl * 1024:(ysl + 1) * 1024]
                    tyn = []
                    for eh, pb in ((0, pbl), (1, pbh)):
                        tok = P.op("s", lambda s, pb=pb, eh=eh, yt=yt, oY=oY: s.activation(out=yt[:, eh * 512:eh * 512 + 512], in_=pb[:, 0:512], func=AF.Identity, scale=smf[:, oY + 15:oY + 16], bias=smf[:, oY + 16:oY + 17]), waits=[ty6] + wyn, inc="se")
                        tyn.append(tok)
                        ypar_reader[eh] = tok
                    stpool.wrote(kst, ty6)
                    tt2 = P.op("v", lambda v, yt=yt, s_=s_: v.tensor_tensor(out=yt, in0=yt, in1=repb[:, s_ * 1024:(s_ + 1) * 1024], op=ALU.mult), waits=tyn + rep_toks["gga"], inc="ve")
                    tu2 = P.op("v", lambda v, yt=yt, s_=s_: v.tensor_tensor(out=yt, in0=yt, in1=repb[:, 2048 + s_ * 1024:2048 + (s_ + 1) * 1024], op=ALU.add), waits=[tt2] + rep_toks["bga"], inc="ve")
                    eslot, we = epool.acquire()
                    x2t = x0w[:, (2 + eslot) * 1024:(3 + eslot) * 1024]
                    tz2 = P.op("q", lambda q, gt=gt, x2t=x2t: q.dma_start(out=x2t, in_=Z[gt * 128:(gt + 1) * 128, :]), waits=we, inc="sE0" if eslot == 0 else "sE1", d=16)
                    tx2a = P.op("g", lambda g, x2t=x2t, yt=yt: g.tensor_tensor(out=x2t[:, 0:512], in0=x2t[:, 0:512], in1=yt[:, 0:512], op=ALU.add), waits=[tz2, tu2], inc="ge")
                    tx2b = P.op("g", lambda g, x2t=x2t, yt=yt: g.tensor_tensor(out=x2t[:, 512:1024], in0=x2t[:, 512:1024], in1=yt[:, 512:1024], op=ALU.add), inc="ge")
                    epool.wrote(eslot, tx2b)
                    epool.read(eslot, tx2a)
                    ynpool.wrote(ysl, tx2b)
                    ynpool.read(ysl, tx2a)
                    tx2_list.append(tx2b)
                    tsp = P.op("s", lambda s, gt=gt, x2t=x2t: s.dma_start(out=spx2.ap()[gt * 128:(gt + 1) * 128, :], in_=x2t), waits=[tx2a, tx2b], inc="aX20" if eslot == 0 else "aX21", d=16)
                    x2sp_done[gt] = tsp
                    epool.read(eslot, tsp)

                    def mk_post(gt=gt, tt=tt, x2t=x2t, eslot=eslot, tx2a=tx2a, tx2b=tx2b, s_=s_, par=par):
                        def post():
                            kst2, wst2 = stpool.acquire()
                            r2, mr2, tst2 = ln_stats(x2t, [tx2a] + wst2, kst2, wb=[tx2b] + wst2)
                            epool.read(eslot, tst2)
                            xh = xhsb[:, tt * 1024:(tt + 1) * 1024]
                            txh = P.op("v", lambda v, x2t=x2t, xh=xh, r2=r2, mr2=mr2: v.tensor_scalar(out=xh, in0=x2t, scalar1=r2, scalar2=mr2, op0=ALU.mult, op1=ALU.add), waits=[tst2] + xhwar_prev, inc="ve")
                            epool.read(eslot, txh)
                            stpool.wrote(kst2, txh)
                            tt4 = P.op("v", lambda v, xh=xh, s_=s_: v.tensor_tensor(out=xh, in0=xh, in1=repb[:, 6144 + s_ * 1024:6144 + (s_ + 1) * 1024], op=ALU.mult), waits=[txh] + rep_toks["cb1"], inc="ve")
                            th2 = P.op("v", lambda v, xh=xh, s_=s_: v.tensor_tensor(out=xh, in0=xh, in1=repb[:, 8192 + s_ * 1024:8192 + (s_ + 1) * 1024], op=ALU.add), waits=[tt4] + rep_toks["sbr"], inc="ve")
                            tamx = P.op("v", lambda v, xh=xh, gt=gt: v.tensor_reduce(out=smf[:, 384 + gt:385 + gt], in_=xh, axis=AX.X, op=ALU.max, apply_absolute_value=True), waits=[th2], inc="ve")
                            tspl = P.op("s", lambda s, gt=gt, xh=xh: s.dma_start(out=sph2.ap()[gt * 128:(gt + 1) * 128, :], in_=xh), waits=[th2], inc="aH20" if par == 0 else "aH21", d=16)
                            h2sp_done.append(tspl)
                            xhwar.append(tamx)
                            xhwar.append(tspl)
                            return tamx
                        return post

                    if pending_post is not None:
                        amax_last = pending_post()
                    pending_post = mk_post()
                    for _ in range(2):
                        if PREP_JOBS:
                            PREP_JOBS.pop(0)()
                if pending_post is not None:
                    amax_last = pending_post()
                    pending_post = None
                h1war = list(scans)
                xhwar_prev = list(xhwar)
                xhwar = []

        while PREP_JOBS:
            PREP_JOBS.pop(0)()
        parks_all = [("aP0", P.cnt.get("aP0", 0)), ("aP1", P.cnt.get("aP1", 0))]
        tscdw = P.op("q", lambda q: q.dma_start(out=bass.AP(hsf, 8 * 1024, [[1, 128], [128, 8]]), in_=smf[:, O_SCD:O_SCD + 8], allow_slow_non_contiguous=True), waits=parks_all, inc="sS3", d=16)
        tscd2 = P.op("q", lambda q: q.dma_start(out=gam2, in_=bass.AP(hsf, 8 * 1024, [[0, 2], [1, 1024]])), waits=[tscdw, tbgm] + rep_toks["bga"], inc="sS3", d=16)
        make_rep("scdgb", 4, lambda: P.op("v", lambda v: v.tensor_tensor(out=gam2, in0=gam2, in1=gb_tm, op=ALU.mult), waits=[tscd2, tgb, tbgm] + rep_toks["sbr"], inc="ve"), 4, srcap=gam2)

        for dcc in range(DC):
            wgT_tok[dcc].append(P.op("q", lambda q, dcc=dcc: q.dma_start_transpose(wa[:, dcc * 2816:(dcc + 1) * 2816], pkg.ap()[:, dcc * 128:(dcc + 1) * 128]), waits=park_done["wg"] + [tmodA, tmodB] + [t for t in wm_slot_tok if t], inc="sWG", d=16))
        for dcc in range(DC):
            wuT_tok[dcc].append(P.op("q", lambda q, dcc=dcc: q.dma_start_transpose(wa[:, 22528 + dcc * 2816:22528 + (dcc + 1) * 2816], pku.ap()[:, dcc * 128:(dcc + 1) * 128]), waits=park_done["wu"] + mod_tm_readers + last_y_mm, inc="sWU", d=16))

        def global_reduce(prev_tok, cci, cco, ccp, hrow, scol, sigcol, iscol, extra_q=(), extra_t8=()):
            t1 = P.op("v", lambda v: v.tensor_scalar(out=smv(C_T0), in0=smv(C_AN), scalar1=-1.0, scalar2=None, op0=ALU.mult), waits=[prev_tok], inc="ve")
            t2 = P.op("v", lambda v: v.tensor_tensor(out=smv(C_T0), in0=smv(C_T0), in1=smv(C_AM), op=ALU.max), waits=[t1], inc="ve")
            t3 = P.op("q", lambda q, hrow=hrow: q.dma_start(out=hsf.ap()[hrow:hrow + 1, 0:128], in_=smv(C_T0)), waits=[t2] + list(extra_q), inc="sGR", d=16)
            t4 = P.op("q", lambda q, hrow=hrow: q.dma_start(out=scr[0:1, 0:128], in_=hsf.ap()[hrow:hrow + 1, 0:128]), waits=[t3, tones, ts1], inc="sGR", d=16)
            t5 = P.op("v", lambda v: v.tensor_reduce(out=scr[0:1, 128:129], in_=scr[0:1, 0:128], axis=AX.X, op=ALU.max), waits=[t4], inc="ve")
            t6 = P.op("q", lambda q, cci=cci: q.dma_start(out=cci.ap()[:, :], in_=scr[0:1, 128:129]), waits=[t5], inc="sGR", d=16)
            t7 = P.op("g", lambda g, cci=cci, cco=cco: g.collective_compute("AllReduce", ALU.max, ins=[cci.ap()[:, :]], outs=[cco.ap()[:, :]], replica_groups=[list(range(CC_WORLD))]), waits=[t6], inc="cc")
            t7b = P.op("g", lambda g, cci=cci, ccp=ccp: g.collective_compute("AllReduce", ALU.max, ins=[cci.ap()[:, :]], outs=[ccp.ap()[:, :]], replica_groups=[list(range(CC_WORLD))]), waits=[t7], inc="cc")
            t8 = P.op("g", lambda g, cco=cco: g.dma_start(out=smv(scol), in_=bass.AP(cco, 0, [[0, 128], [1, 1]])), waits=[t7b] + list(extra_t8), inc="sGR2", d=16)
            t9 = P.op("v", lambda v: v.tensor_scalar(out=smv(scol), in0=smv(scol), scalar1=smv(sigcol), scalar2=1.0 / 127.0, op0=ALU.mult, op1=ALU.mult), waits=[t8, tsx, tsd], inc="ve")
            t9b = P.op("v", lambda v: v.tensor_scalar(out=smv(scol), in0=smv(scol), scalar1=1e-35, scalar2=None, op0=ALU.max), waits=[t9], inc="ve")
            t10 = P.op("v", lambda v: v.reciprocal(out=smv(iscol), in_=smv(scol)), waits=[t9b], inc="ve")
            return t9b, t10
        tcmb = P.op("v", lambda v: v.tensor_reduce(out=smv(C_AM), in_=smf[:, 384:416], axis=AX.X, op=ALU.max), waits=[amax_last], inc="ve")
        h2_all = [("aH20", P.cnt.get("aH20", 0)), ("aH21", P.cnt.get("aH21", 0))]
        x2_all = [("aX20", P.cnt.get("aX20", 0)), ("aX21", P.cnt.get("aX21", 0))]
        fmb_hist = [list(fm_hist[0]), list(fm_hist[1])]
        xq_hist = [list(sfm_hist[0]), list(sfm_hist[1])]
        last_mm_xq = [None, None]

        def emit_tl(g):
            par = g % 2
            base = par * 4096
            w_h2 = fmb_hist[par]
            fmb_hist[par] = []
            toks = []
            for dcc in range(DC):
                h2in = fmb[:, base + dcc * 512:base + (dcc + 1) * 512]
                tl = P.op("q", lambda q, g=g, dcc=dcc, h2in=h2in: q.dma_start_transpose(h2in, sph2.ap()[g * GT:(g + 1) * GT, dcc * 128:(dcc + 1) * 128]), waits=h2_all + w_h2 + ([last_mm_xq[par]] if last_mm_xq[par] else []), inc="sH0" if par == 0 else "sH1", d=16)
                toks.append(tl)
            return toks
        tl_pref = {0: emit_tl(0), 1: emit_tl(1)}
        ts1x, tis1 = global_reduce(tcmb, cc1i, cc1o, cc1p, 9, C_T1, C_SX, C_IS1, extra_t8=[tl_pref[1][-1]])
        tsgx = P.op("v", lambda v: v.tensor_scalar(out=smf[:, O_SGX:O_SGX + 22], in0=smf[:, O_SCG:O_SCG + 22], scalar1=smv(C_T1), scalar2=None, op0=ALU.mult), waits=[ts1x], inc="ve")
        tsux = P.op("v", lambda v: v.tensor_scalar(out=smf[:, O_SUX:O_SUX + 22], in0=smf[:, O_SCU:O_SCU + 22], scalar1=smv(C_T1), scalar2=None, op0=ALU.mult), waits=[ts1x], inc="ve")

        tam2 = P.op("v", lambda v: v.memset(smv(C_AM), -3e38), waits=[ts1x], inc="ve")
        tan2 = P.op("v", lambda v: v.memset(smv(C_AN), 3e38), waits=[tam2], inc="ve")
        amax_last = tan2
        hh_sp = []
        gu_reader = [None, None, wlast_bank[0], wlast_bank[1]]
        hg_war = [None, None]
        hu_war = [None, None]
        hh_war = [stg_last[0], stg_last[1]]
        tq1_last = [None, None]
        for g in range(NG):
            s_ = g // GPS
            par = g % 2
            base = par * 4096
            w_xq = xq_hist[par]
            xq_hist[par] = []
            xq_tok = []
            tl_toks = tl_pref.pop(g) if g in tl_pref else emit_tl(g)
            for dcc in range(DC):
                h2in = fmb[:, base + dcc * 512:base + (dcc + 1) * 512]
                tq1 = P.op("s", lambda s, h2in=h2in, par=par: s.activation(out=scr[:, par * 512:par * 512 + 512], in_=h2in, func=AF.Identity, scale=smv(C_IS1), bias=smv(C_CB)), waits=[tl_toks[-1], tis1] + ([tq1_last[par]] if tq1_last[par] else []), inc="se")
                tq2 = P.op("v", lambda v, par=par: v.tensor_scalar(out=scr[:, par * 512:par * 512 + 512], in0=scr[:, par * 512:par * 512 + 512], scalar1=smv(C_HI), scalar2=smv(C_LO), op0=ALU.min, op1=ALU.max), waits=[tq1, thi, tlo], inc="ve")
                xq = sfm[:, base + dcc * 512:base + (dcc + 1) * 512]
                tq3 = P.op("v", lambda v, xq=xq, par=par: v.tensor_scalar(out=xq, in0=scr[:, par * 512:par * 512 + 512], scalar1=smv(C_CB), scalar2=None, op0=ALU.subtract), waits=[tq2] + w_xq, inc="ve")
                w_xq = []
                tq1_last[par] = tq3
                xq_tok.append(tq3)
                fmb_hist[par].append(tq1)
            for ic in range(IC):
                icp = ic % 2
                pg = psum[icp]
                pu = psum[2 + icp]
                gtoks = []
                for dcc in range(DC):
                    def mmg(t, ic=ic, dcc=dcc, pg=pg, base=base):
                        return t.matmul(pg[:, 0:512], wa[:, dcc * 2816 + ic * 128:dcc * 2816 + ic * 128 + 128], sfm[:, base + dcc * 512:base + (dcc + 1) * 512], start=(dcc == 0), stop=(dcc == DC - 1))
                    w = [xq_tok[dcc]] + wgT_tok[dcc]
                    if dcc == 0 and gu_reader[icp] is not None:
                        w.append(gu_reader[icp])
                    gtoks.append(P.op("t", mmg, waits=w, inc="pe"))
                utoks = []
                for dcc in range(DC):
                    def mmu(t, ic=ic, dcc=dcc, pu=pu, base=base):
                        return t.matmul(pu[:, 0:512], wa[:, 22528 + dcc * 2816 + ic * 128:22528 + dcc * 2816 + ic * 128 + 128], sfm[:, base + dcc * 512:base + (dcc + 1) * 512], start=(dcc == 0), stop=(dcc == DC - 1))
                    w = [xq_tok[dcc]] + wuT_tok[dcc]
                    if dcc == 0 and gu_reader[2 + icp] is not None:
                        w.append(gu_reader[2 + icp])
                    utoks.append(P.op("t", mmu, waits=w, inc="pe"))
                hgp = hgb[:, icp * 512:icp * 512 + 512]
                hup = hgb[:, 1024 + icp * 512:1024 + icp * 512 + 512]
                thg = P.op("s", lambda s, pg=pg, hgp=hgp, ic=ic: s.activation(out=hgp, in_=pg[:, 0:512], func=AF.Silu, scale=smf[:, O_SGX + ic:O_SGX + ic + 1]), waits=[gtoks[-1], tsgx] + ([hg_war[icp]] if hg_war[icp] else []), inc="se")
                gu_reader[icp] = thg
                thu = P.op("s", lambda s, pu=pu, hup=hup, ic=ic: s.activation(out=hup, in_=pu[:, 0:512], func=AF.Identity, scale=smf[:, O_SUX + ic:O_SUX + ic + 1]), waits=[utoks[-1], tsux] + ([hu_war[icp]] if hu_war[icp] else []), inc="se")
                gu_reader[2 + icp] = thu
                hh = hho[:, icp * 512:icp * 512 + 512]
                thh = P.op("v", lambda v, hgp=hgp, hup=hup, hh=hh: v.tensor_tensor(out=hh, in0=hgp, in1=hup, op=ALU.mult), waits=[thg, thu] + ([hh_war[icp]] if hh_war[icp] else []), inc="ve")
                hg_war[icp] = thh
                tm = P.op("v", lambda v, hh=hh, hup=hup: v.tensor_scalar(out=hup, in0=hh, scalar1=1.0, scalar2=smv(C_AM), op0=ALU.mult, op1=ALU.max, accum_out=smv(C_AM)), waits=[thh, amax_last], inc="ve")
                tn = P.op("v", lambda v, hh=hh, hup=hup: v.tensor_scalar(out=hup, in0=hh, scalar1=1.0, scalar2=smv(C_AN), op0=ALU.min, op1=ALU.min, accum_out=smv(C_AN)), waits=[tm], inc="ve")
                amax_last = tn
                hu_war[icp] = tn
                tsp = P.op("s", lambda s, g=g, ic=ic, hh=hh: s.dma_start(out=sphh.ap()[g * IC + ic, :, :], in_=hh), waits=[thh], inc="aHH0" if icp == 0 else "aHH1", d=16)
                hh_sp.append(tsp)
                hh_war[icp] = tsp
            last_mm_xq[par] = gtoks[-1]
            xq_hist[par] = [utoks[-1], gtoks[-1]]

        wdT_tok = []
        for icc in range(IC):
            wdT_tok.append(P.op("q", lambda q, icc=icc: q.dma_start_transpose(wa[:, icc * 1024:(icc + 1) * 1024], pkd.ap()[:, icc * 128:(icc + 1) * 128]), waits=parks_all + [amax_last, gu_reader[0], gu_reader[1]], inc="sWD", d=16))
        ts2x, tis2 = global_reduce(amax_last, cc2i, cc2o, cc2p, 10, C_S2B, C_SD, C_IS2, extra_t8=[wdT_tok[0]])

        dn_reader = [None] * 4
        t3_war = [None, None]
        dq_war_prev = {}
        hq_mm_last = [[], []]
        x2pool = Pool(2, seed=tis2)
        opool = Pool(2, seed=tis2)
        out_toks = []
        hh_all = [("aHH0", P.cnt.get("aHH0", 0)), ("aHH1", P.cnt.get("aHH1", 0))]
        for g in range(NG):
            s_ = g // GPS
            par = g % 2
            hqb = 22528 + par * 11264
            wclear = hq_mm_last[par]
            hq_mm_last[par] = []
            def mk_tlh(q, g=g, hqb=hqb):
                src = bass.AP(sphh, g * IC * 128 * GT, [[GT, 128], [128 * GT, IC], [1, GT]])
                return q.dma_start(out=wa[:, hqb:hqb + 11264].rearrange("p (i t) -> p i t", t=GT), in_=src)
            tlh = P.op("q", mk_tlh, waits=hh_all + [tis2] + wclear + (([t for l in wuT_tok for t in l]) if g < 2 else []), inc="sL0" if par == 0 else "sL1", d=16)
            hq_tok = []
            dq_war = dq_war_prev
            for ic in range(IC):
                sl = (ic % 2) * 512
                tq1 = P.op("s", lambda s, hqb=hqb, ic=ic, sl=sl: s.activation(out=scr[:, sl:sl + 512], in_=wa[:, hqb + ic * 512:hqb + (ic + 1) * 512], func=AF.Identity, scale=smv(C_IS2), bias=smv(C_CB)), waits=[tlh, tis2] + ([dq_war[sl]] if sl in dq_war else []), inc="se")
                tq2 = P.op("v", lambda v, sl=sl: v.tensor_scalar(out=scr[:, sl:sl + 512], in0=scr[:, sl:sl + 512], scalar1=smv(C_HI), scalar2=smv(C_LO), op0=ALU.min, op1=ALU.max), waits=[tq1], inc="ve")
                tq3 = P.op("v", lambda v, hqb=hqb, ic=ic, sl=sl: v.tensor_scalar(out=wa[:, hqb + ic * 512:hqb + (ic + 1) * 512], in0=scr[:, sl:sl + 512], scalar1=smv(C_CB), scalar2=None, op0=ALU.subtract), waits=[tq2], inc="ve")
                dq_war[sl] = tq3
                dq_war_prev = dq_war
                hq_tok.append(tq3)
            for tp in range(2):
                tts = (2 * tp, 2 * tp + 1)
                gtoks = {(tt, eh): [] for tt in tts for eh in range(2)}
                for ic in range(IC):
                    for tt in tts:
                        dp = tt % 2
                        for eh in range(2):
                            pb = psum[4 + 2 * dp + eh]
                            def mmd(t, ic=ic, eh=eh, tt=tt, pb=pb, hqb=hqb):
                                return t.matmul(pb[:, 0:512], wa[:, hqb + ic * 512 + tt * 128:hqb + ic * 512 + tt * 128 + 128], wa[:, ic * 1024 + eh * 512:ic * 1024 + eh * 512 + 512], start=(ic == 0), stop=(ic == IC - 1))
                            w = [hq_tok[ic], wdT_tok[ic]]
                            if ic == 0 and dn_reader[2 * dp + eh] is not None:
                                w.append(dn_reader[2 * dp + eh])
                            gtoks[(tt, eh)].append(P.op("t", mmd, waits=w, inc="pe"))
                for tt in tts:
                    gt = g * 4 + tt
                    dp = tt % 2
                    mm_l = [gtoks[(tt, 0)][-1], gtoks[(tt, 1)][-1]]
                    hq_mm_last[par].append(mm_l[0])
                    hq_mm_last[par].append(mm_l[1])
                    x2s, wx2 = x2pool.acquire()
                    x2t = x0w[:, x2s * 1024:(x2s + 1) * 1024]
                    tlx = P.op("q", lambda q, gt=gt, x2t=x2t: q.dma_start(out=x2t, in_=spx2.ap()[gt * 128:(gt + 1) * 128, :]), waits=x2_all + [tis2] + wx2, inc="sX0" if x2s == 0 else "sX1", d=16)
                    x2pool.wrote(x2s, tlx)
                    t3t = []
                    for eh in range(2):
                        pb = psum[4 + 2 * dp + eh]
                        tok = P.op("v", lambda v, pb=pb, eh=eh, dp=dp, s_=s_: v.scalar_tensor_tensor(out=scr[:, 1024 + dp * 1024 + eh * 512:1024 + dp * 1024 + eh * 512 + 512], in0=pb[:, 0:512], scalar=smv(C_S2B), in1=repb[:, 4096 + s_ * 1024 + eh * 512:4096 + s_ * 1024 + eh * 512 + 512], op0=ALU.mult, op1=ALU.mult), waits=[mm_l[eh], ts2x, tis2] + rep_toks["scdgb"] + ([dn_reader[2 * dp + eh]] if dn_reader[2 * dp + eh] else []) + ([t3_war[dp]] if t3_war[dp] else []), inc="ve")
                        dn_reader[2 * dp + eh] = tok
                        t3t.append(tok)
                    osl, wo = opool.acquire()
                    ot = x0w[:, (2 + osl) * 1024:(3 + osl) * 1024]
                    tout = P.op("v", lambda v, x2t=x2t, ot=ot, dp=dp: v.tensor_tensor(out=ot, in0=x2t, in1=scr[:, 1024 + dp * 1024:1024 + dp * 1024 + 1024], op=ALU.add), waits=t3t + [tlx] + wo, inc="ve")
                    opool.wrote(osl, tout)
                    t3_war[dp] = tout
                    x2pool.read(x2s, tout)
                    tst = P.op("s", lambda s, gt=gt, ot=ot: s.dma_start(out=y_[gt * 128:(gt + 1) * 128, :], in_=ot), waits=[tout], inc="aO0" if osl == 0 else "aO1", d=16)
                    opool.read(osl, tst)
                    out_toks.append(tst)
            hq_mm_last[par] += t3t

        P.op("q", None, waits=out_toks)
        GROUP_SEMS = {"sS0", "sB0", "sS2", "sWT", "sWG", "sWU", "sWD", "sMB", "gBI"}

        @block.sync
        def _(q):
            P.emit("q", q, sems, GROUP_SEMS)

        @block.scalar
        def _(s):
            P.emit("s", s, sems, GROUP_SEMS)

        @block.vector
        def _(v):
            P.emit("v", v, sems, GROUP_SEMS)

        @block.gpsimd
        def _(g):
            P.emit("g", g, sems, GROUP_SEMS)

        @block.tensor
        def _(t):
            P.emit("t", t, sems, GROUP_SEMS)

    return nc


_NC_CACHE = None


def kernel(L0_attn_Z, L0_attn_decay, L0_attn_W, L0_attn_b, L0_attn_gamma,
           L0_attn_beta, L0_ffn_clip_x, L0_ffn_clip_down, L0_ffn_w_gate,
           L0_ffn_w_up, L0_ffn_w_down, L0_c, L0_wmod):
    global _NC_CACHE
    Zf = np.ascontiguousarray(np.asarray(L0_attn_Z, dtype=np.float32))
    ks = np.arange(16, 64)
    pw = (2.0 ** (-ks)).astype(np.float32).reshape(1, 48)
    cvals = (1.5 * 2.0 ** ks).astype(np.float32).reshape(1, 48)
    common = {
        "decay": np.asarray(L0_attn_decay, dtype=np.float32).reshape(1, 1),
        "W": np.ascontiguousarray(np.asarray(L0_attn_W, dtype=np.float32)),
        "bias": np.asarray(L0_attn_b, dtype=np.float32).reshape(1, D),
        "gamma": np.asarray(L0_attn_gamma, dtype=np.float32).reshape(1, D),
        "beta": np.asarray(L0_attn_beta, dtype=np.float32).reshape(1, D),
        "clip_x": np.asarray(L0_ffn_clip_x, dtype=np.float32).reshape(1, 1),
        "clip_d": np.asarray(L0_ffn_clip_down, dtype=np.float32).reshape(1, 1),
        "wg": np.ascontiguousarray(np.asarray(L0_ffn_w_gate, dtype=np.float32)),
        "wu": np.ascontiguousarray(np.asarray(L0_ffn_w_up, dtype=np.float32)),
        "wd": np.ascontiguousarray(np.asarray(L0_ffn_w_down, dtype=np.float32)),
        "wmod": np.ascontiguousarray(np.asarray(L0_wmod, dtype=np.float32)),
        "pw": pw,
        "cvals": cvals,
        "eye": np.eye(128, dtype=np.float32),
    }
    cc = np.asarray(L0_c, dtype=np.float32)
    in_maps = []
    for i in range(NCORES):
        m = dict(common)
        m["Z"] = np.ascontiguousarray(Zf[i * BL:(i + 1) * BL].reshape(R, D))
        m["cc_"] = np.ascontiguousarray(cc[i * BL:(i + 1) * BL])
        in_maps.append(m)
    if _NC_CACHE is None:
        _NC_CACHE = build_nc()
    res = run_bass_kernel_spmd(_NC_CACHE, in_maps, core_ids=list(range(NCORES)))
    out = np.empty((B, T, D), dtype=np.float32)
    for i in range(NCORES):
        out[i * BL:(i + 1) * BL] = res.results[i]["y"].reshape(BL, T, D)
    return out


# version 41
# speedup vs baseline: 1.0142x; 1.0017x over previous
import numpy as np
import concourse.bass as bass
import concourse.mybir as mybir
from concourse.bass_utils import run_bass_kernel_spmd

F32 = mybir.dt.float32
BF16 = mybir.dt.bfloat16
AF = mybir.ActivationFunctionType
ALU = mybir.AluOpType
AX = mybir.AxisListType

NCORES = 8
CC_WORLD = 8
B, T, D, I = 16, 2048, 1024, 2816
BL = B // NCORES
R = BL * T
SEG = 512
NH = T // SEG
TPH = SEG // 128
DC = D // 128
IC = I // 128
GT = 512
NG = R // GT
GPS = NG // BL
MD = 6 * D


class Sched:
    def __init__(self):
        self.ops = []
        self.cnt = {}

    def op(self, eng, fn, waits=(), inc=None, d=1):
        tok = None
        if inc is not None:
            self.cnt[inc] = self.cnt.get(inc, 0) + d
            tok = (inc, self.cnt[inc])
        self.ops.append((eng, fn, [w for w in waits if w is not None], inc, d))
        return tok

    def emit(self, name, eng, sems, group_sems=()):
        last = {}
        for e, fn, waits, inc, d in self.ops:
            if e != name:
                continue
            need = {}
            for sn, val in waits:
                if sn in group_sems:
                    val = self.cnt[sn]
                if val > need.get(sn, -1):
                    need[sn] = val
            for sn, val in need.items():
                if last.get(sn, -1) >= val:
                    continue
                eng.wait_ge(sems[sn], val)
                last[sn] = val
            inst = fn(eng) if fn is not None else None
            if inc is not None:
                inst.then_inc(sems[inc], d)


class Pool:
    def __init__(self, n, seed=None):
        self.n = n
        self.wr = [seed] * n
        self.rd = [[] for _ in range(n)]
        self.i = 0

    def acquire(self):
        s = self.i % self.n
        self.i += 1
        w = list(self.rd[s]) + ([self.wr[s]] if self.wr[s] else [])
        self.rd[s] = []
        return s, w

    def wrote(self, s, tok):
        self.wr[s] = tok

    def read(self, s, tok):
        if tok is not None:
            self.rd[s].append(tok)


def build_nc():
    nc = bass.Bass("TRN2", target_bir_lowering=False, debug=False)
    P = Sched()

    hZ = nc.dram_tensor("Z", [R, D], F32, kind="ExternalInput")
    hdec = nc.dram_tensor("decay", [1, 1], F32, kind="ExternalInput")
    hW = nc.dram_tensor("W", [D, D], F32, kind="ExternalInput")
    hb = nc.dram_tensor("bias", [1, D], F32, kind="ExternalInput")
    hg = nc.dram_tensor("gamma", [1, D], F32, kind="ExternalInput")
    hbe = nc.dram_tensor("beta", [1, D], F32, kind="ExternalInput")
    hcx = nc.dram_tensor("clip_x", [1, 1], F32, kind="ExternalInput")
    hcd = nc.dram_tensor("clip_d", [1, 1], F32, kind="ExternalInput")
    hwg = nc.dram_tensor("wg", [I, D], F32, kind="ExternalInput")
    hwu = nc.dram_tensor("wu", [I, D], F32, kind="ExternalInput")
    hwd = nc.dram_tensor("wd", [D, I], F32, kind="ExternalInput")
    hc = nc.dram_tensor("cc_", [BL, D], F32, kind="ExternalInput")
    hwm = nc.dram_tensor("wmod", [D, MD], F32, kind="ExternalInput")
    hpw = nc.dram_tensor("pw", [1, 48], F32, kind="ExternalInput")
    hcv = nc.dram_tensor("cvals", [1, 48], F32, kind="ExternalInput")
    hey = nc.dram_tensor("eye", [128, 128], F32, kind="ExternalInput")
    hy = nc.dram_tensor("y", [R, D], F32, kind="ExternalOutput")

    pkW = nc.dram_tensor("pkW", [D, D], BF16, kind="Internal")
    pkg = nc.dram_tensor("pkg", [I, D], BF16, kind="Internal")
    pku = nc.dram_tensor("pku", [I, D], BF16, kind="Internal")
    pkd = nc.dram_tensor("pkd", [D, I], BF16, kind="Internal")
    spx2 = nc.dram_tensor("spx2", [R, D], F32, kind="Internal")
    sph2 = nc.dram_tensor("sph2", [R, D], BF16, kind="Internal")
    sphh = nc.dram_tensor("sphh", [NG * IC, 128, GT], BF16, kind="Internal")
    hsf = nc.dram_tensor("hsf", [16, 1024], F32, kind="Internal")
    hsb = nc.dram_tensor("hsb", [96, 128], BF16, kind="Internal")
    hsc = nc.dram_tensor("hsc", [12, 1024], BF16, kind="Internal")
    cc1i = nc.dram_tensor("cc1i", [1, 1], F32, kind="Internal")
    cc1o = nc.dram_tensor("cc1o", [1, 1], F32, kind="Internal", addr_space="Shared" if CC_WORLD > 4 else "Local")
    cc2i = nc.dram_tensor("cc2i", [1, 1], F32, kind="Internal")
    cc2o = nc.dram_tensor("cc2o", [1, 1], F32, kind="Internal", addr_space="Shared" if CC_WORLD > 4 else "Local")
    pkuT = nc.dram_tensor("pkuT", [128, 8192], BF16, kind="Internal")
    cc1p = nc.dram_tensor("cc1p", [1, 1], F32, kind="Internal", addr_space="Shared" if CC_WORLD > 4 else "Local")
    cc2p = nc.dram_tensor("cc2p", [1, 1], F32, kind="Internal", addr_space="Shared" if CC_WORLD > 4 else "Local")

    Z, Wm, bia, gam, bet = hZ.ap(), hW.ap(), hb.ap(), hg.ap(), hbe.ap()
    dec, cx, cd = hdec.ap(), hcx.ap(), hcd.ap()
    wg_, wu_, wd_, c_, wm_ = hwg.ap(), hwu.ap(), hwd.ap(), hc.ap(), hwm.ap()
    pw_, cv_, y_, ey_ = hpw.ap(), hcv.ap(), hy.ap(), hey.ap()

    from contextlib import ExitStack
    with ExitStack() as stack:
        en = stack.enter_context
        wa = en(nc.sbuf_tensor("wa", [128, 45056], BF16))
        fmb = en(nc.sbuf_tensor("fmb", [128, 8192], BF16))
        sfm = en(nc.sbuf_tensor("sfm", [128, 8192], BF16))
        x0w = en(nc.sbuf_tensor("x0w", [128, 4 * 1024], F32))
        scr = en(nc.sbuf_tensor("scr", [128, 3072], F32))
        h1sb = en(nc.sbuf_tensor("h1sb", [128, 4 * 1024], BF16))
        xhsb = en(nc.sbuf_tensor("xhsb", [128, 4 * 1024], BF16))
        ynb2 = en(nc.sbuf_tensor("ynb2", [128, 2 * 1024], BF16))
        repb = en(nc.sbuf_tensor("repb", [128, 10 * 1024], BF16))
        sep = en(nc.sbuf_tensor("sep", [128, 5248], BF16))
        smf = en(nc.sbuf_tensor("smf", [128, 512], F32))
        hgb = en(nc.sbuf_tensor("hgb", [128, 2048], BF16))
        hho = en(nc.sbuf_tensor("hho", [128, 1024], BF16))
        ebuf = en(nc.sbuf_tensor("ebuf", [128, 128], BF16))
        p0 = en(nc.psum_tensor("p0", [128, 512], F32))
        p1 = en(nc.psum_tensor("p1", [128, 512], F32))
        p2 = en(nc.psum_tensor("p2", [128, 512], F32))
        p3 = en(nc.psum_tensor("p3", [128, 512], F32))
        p4 = en(nc.psum_tensor("p4", [128, 512], F32))
        p5 = en(nc.psum_tensor("p5", [128, 512], F32))
        p6 = en(nc.psum_tensor("p6", [128, 512], F32))
        p7 = en(nc.psum_tensor("p7", [128, 512], F32))
        SEM_NAMES = ["pe", "ve", "se", "ge", "cc",
                     "sZ0", "sZ1", "sE0", "sE1", "sT0", "sT1",
                     "sS0", "sB0", "sS2", "sS3", "sWT", "sWG", "sWU", "sWD",
                     "sCT", "sMB", "sMO", "sRP",
                     "sH0", "sH1", "sL0", "sL1", "sX0", "sX1", "sQ0", "sQ1", "sTA0", "sTA1",
                     "aX20", "aX21", "aH20", "aH21", "aHH0", "aHH1", "aP0", "aP1", "aO0", "aO1", "sBE", "sGR",
                     "gWP", "gWM0", "gWM1", "gWM2", "gWM3", "gWF0", "gWF1", "gCB", "gRB", "gBI", "sEY", "sGR2", "aPT0", "aPT1"]
        sems = {n: en(nc.semaphore(n)) for n in SEM_NAMES}
        block = en(nc.Block())
        psum = [p0, p1, p2, p3, p4, p5, p6, p7]
        wt = wa[:, 36864:45056]

        C_CB, C_HI, C_LO, C_A, C_SX, C_SD = 0, 1, 2, 3, 4, 5
        C_IS1, C_S2B, C_IS2 = 6, 7, 8
        C_AM, C_AN, C_T0, C_T1, C_EPS = 9, 10, 11, 12, 13
        O_CA1, O_CB1, O_SB = 16, 32, 48
        O_SCG, O_SCU, O_SCD = 64, 86, 108
        O_SGX, O_SUX = 128, 150
        O_TAIL = 176
        O_MX = 184
        O_ST = 192
        O_EPS, O_DLT, O_CV = 336, 384, 432
        O_SCL = 480
        O_CT = 488

        def smv(c):
            return smf[:, c:c + 1]

        S_S1, S_CT, S_MODT, S_CTL = 0, 3072, 3088, 5232
        SWB = [2048, 3184]
        S_CSIL = 3184
        ga_tm = x0w[0:2, 1024:2048]
        sgn = x0w[0:2, 3072:4096]
        repr_ = x0w[0:2, 0:1024]
        gb_tm = sep[0:2, 4208:5232]
        c_sil = sep[0:2, S_CSIL:S_CSIL + 1024]
        modT = sep[:, S_MODT:S_MODT + 96]
        gam2 = scr[0:2, 2048:3072]

        teps = P.op("v", lambda v: v.memset(smf[:, 13:14], 1e-5), inc="ve")
        t_pw = P.op("q", lambda q: q.dma_start(out=smf[0:1, O_EPS:O_EPS + 48], in_=pw_[:, :]), inc="sS0", d=16)
        t_cv = P.op("q", lambda q: q.dma_start(out=smf[0:1, O_CV:O_CV + 48], in_=cv_[:, :]), inc="sS0", d=16)
        t_dc = P.op("q", lambda q: q.dma_start(out=smf[0:1, O_SCL:O_SCL + 1], in_=dec[:, :]), inc="sS0", d=16)
        t_cx = P.op("q", lambda q: q.dma_start(out=smf[0:1, O_SCL + 1:O_SCL + 2], in_=cx[:, :]), inc="sS0", d=16)
        t_cd = P.op("q", lambda q: q.dma_start(out=smf[0:1, O_SCL + 2:O_SCL + 3], in_=cd[:, :]), inc="sS0", d=16)
        t_g2 = P.op("q", lambda q: q.dma_start(out=gam2, in_=bass.AP(hg, 0, [[0, 2], [1, 1024]])), inc="sS0", d=16)
        t_cl = P.op("q", lambda q: q.dma_start(out=x0w[0:2, 0:1024], in_=c_[:, :]), inc="sS0", d=16)

        tg1 = P.op("v", lambda v: v.tensor_scalar(out=smf[0:1, O_EPS:O_EPS + 48], in0=smf[0:1, O_EPS:O_EPS + 48], scalar1=1.0, scalar2=1.0, op0=ALU.add, op1=ALU.is_gt), waits=[t_pw], inc="ve")
        tg2 = P.op("v", lambda v: v.scalar_tensor_tensor(out=smf[0:1, O_DLT:O_DLT + 47], in0=smf[0:1, O_EPS + 1:O_EPS + 48], scalar=-1.0, in1=smf[0:1, O_EPS:O_EPS + 47], op0=ALU.mult, op1=ALU.add), waits=[tg1], inc="ve")
        tg3 = P.op("v", lambda v: v.tensor_tensor(out=smf[0:1, O_DLT:O_DLT + 47], in0=smf[0:1, O_DLT:O_DLT + 47], in1=smf[0:1, O_CV:O_CV + 47], op=ALU.mult), waits=[tg2, t_cv], inc="ve")
        tC = P.op("v", lambda v: v.tensor_reduce(out=smf[0:1, O_CT:O_CT + 1], in_=smf[0:1, O_DLT:O_DLT + 47], axis=AX.X, op=ALU.add), waits=[tg3], inc="ve")
        tsg = P.op("s", lambda s: s.activation(out=smf[0:1, O_SCL + 4:O_SCL + 7], in_=smf[0:1, O_SCL:O_SCL + 3], func=AF.Sigmoid), waits=[t_dc, t_cx, t_cd], inc="se")
        tb1 = P.op("q", lambda q: q.dma_start(out=hsf.ap()[0:1, 0:1], in_=smf[0:1, O_CT:O_CT + 1]), waits=[tC], inc="sB0", d=16)
        tb2 = P.op("q", lambda q: q.dma_start(out=hsf.ap()[0:1, 1:4], in_=smf[0:1, O_SCL + 4:O_SCL + 7]), waits=[tsg], inc="sB0", d=16)
        tCb = P.op("q", lambda q: q.dma_start(out=smv(C_CB), in_=bass.AP(hsf, 0, [[0, 128], [1, 1]])), waits=[tb1], inc="sS2", d=16)
        tac = P.op("q", lambda q: q.dma_start(out=smv(C_A), in_=bass.AP(hsf, 1, [[0, 128], [1, 1]])), waits=[tb2], inc="sS2", d=16)
        tsx = P.op("q", lambda q: q.dma_start(out=smv(C_SX), in_=bass.AP(hsf, 2, [[0, 128], [1, 1]])), waits=[tb2], inc="sS2", d=16)
        tsd = P.op("q", lambda q: q.dma_start(out=smv(C_SD), in_=bass.AP(hsf, 3, [[0, 128], [1, 1]])), waits=[tb2], inc="sS2", d=16)
        thi = P.op("v", lambda v: v.tensor_scalar(out=smv(C_HI), in0=smv(C_CB), scalar1=127.0, scalar2=None, op0=ALU.add), waits=[tCb], inc="ve")
        tNCB = P.op("v", lambda v: v.tensor_scalar(out=smf[:, 14:15], in0=smv(C_CB), scalar1=-1.0, scalar2=None, op0=ALU.mult), waits=[tCb], inc="ve")
        tlo = P.op("v", lambda v: v.tensor_scalar(out=smv(C_LO), in0=smv(C_CB), scalar1=-128.0, scalar2=None, op0=ALU.add), waits=[tCb], inc="ve")

        tones = P.op("v", lambda v: v.memset(scr[:, 0:2048], 1.0), inc="ve")
        tone2 = P.op("v", lambda v: v.memset(sep[0:2, S_S1:S_S1 + 2048], 1.0), inc="ve")
        ts1w = P.op("v", lambda v: v.tensor_tensor_scan(out=fmb[:, 0:2048], data0=bass.AP(smf, C_A, [[512, 128], [0, 2048]]), data1=scr[:, 0:2048], initial=0.0, op0=ALU.mult, op1=ALU.add), waits=[tones, tac], inc="ve")
        ts1 = P.op("v", lambda v: v.tensor_copy(out=sep[0:1, S_S1:S_S1 + 2048], in_=fmb[0:1, 0:2048]), waits=[ts1w, tone2], inc="ve")

        tWp = P.op("g", lambda g: g.dma_start(out=pkW.ap()[:, :], in_=Wm[:, :]), inc="gWP", d=16)
        tWt = []
        for dcc in range(DC):
            tWt.append(P.op("q", lambda q, dcc=dcc: q.dma_start_transpose(wt[:, dcc * 1024:(dcc + 1) * 1024], pkW.ap()[:, dcc * 128:(dcc + 1) * 128]), waits=[tWp], inc="sWT", d=16))

        c32 = x0w[0:2, 1024:2048]
        tcs = P.op("s", lambda s: s.activation(out=c32, in_=x0w[0:2, 0:1024], func=AF.Silu), waits=[t_cl], inc="se")
        tch = P.op("v", lambda v: v.tensor_copy(out=c_sil, in_=c32), waits=[tcs], inc="ve")
        c_lo = x0w[0:2, 0:1024]
        tclo = P.op("v", lambda v: v.tensor_tensor(out=c_lo, in0=c32, in1=c_sil, op=ALU.subtract), waits=[tch], inc="ve")
        tcb = P.op("g", lambda g: g.dma_start(out=hsc.ap()[0:2, :], in_=c_sil), waits=[tch], inc="gCB", d=16)
        tcb2 = P.op("g", lambda g: g.dma_start(out=hsc.ap()[8:10, :], in_=c_lo), waits=[tclo], inc="gRB", d=16)
        tct = P.op("q", lambda q: q.dma_start_transpose(sep[:, S_CT:S_CT + 16], bass.AP(hsc, 0, [[128, 16], [1, 128]])), waits=[tcb], inc="sCT", d=16)
        tct2 = P.op("q", lambda q: q.dma_start_transpose(sep[:, S_CTL:S_CTL + 16], bass.AP(hsc, 8 * 1024, [[128, 16], [1, 128]])), waits=[tcb2], inc="sCT", d=16)

        mod_tm = wa[0:2, 26624:26624 + MD]
        tmodev = [None, None]
        wm_slot_tok = [None, None, None, None]
        wm32war = [None, None]
        GAGB_N = {4, 5, 10, 11}
        for n in range(12):
            pb = psum[n % 2]
            mmtok = None
            for dcc in range(DC):
                slot = (n * DC + dcc) % 4
                hi = wa[:, 8192 + 1024 * slot:8192 + 1024 * slot + 512]
                lo = wa[:, 8192 + 1024 * slot + 512:8192 + 1024 * (slot + 1)]
                wld = [wm_slot_tok[slot]] if wm_slot_tok[slot] else []
                if n in GAGB_N:
                    f32s = slot % 2
                    wm32 = x0w[:, 2048 + 512 * f32s:2048 + 512 * (f32s + 1)]
                    w32 = [wm32war[f32s]] if wm32war[f32s] else []
                    tld = P.op("g", lambda g, n=n, dcc=dcc, wm32=wm32: g.dma_start(out=wm32, in_=wm_[dcc * 128:(dcc + 1) * 128, n * 512:(n + 1) * 512]), waits=w32, inc="gWF0" if f32s == 0 else "gWF1", d=16)
                    thh_ = P.op("v", lambda v, wm32=wm32, hi=hi: v.tensor_copy(out=hi, in_=wm32), waits=[tld] + wld, inc="ve")
                    tll = P.op("v", lambda v, wm32=wm32, hi=hi, lo=lo: v.tensor_tensor(out=lo, in0=wm32, in1=hi, op=ALU.subtract), waits=[thh_], inc="ve")
                    wm32war[f32s] = tll
                    def mm1(t, dcc=dcc, hi=hi, pb=pb):
                        return t.matmul(pb[0:2, 0:512], bass.AP(sep, S_CT + dcc, [[5248, 128], [8, 2]]), hi, start=(dcc == 0), stop=False)
                    def mm2(t, dcc=dcc, hi=hi, pb=pb):
                        return t.matmul(pb[0:2, 0:512], bass.AP(sep, S_CTL + dcc, [[5248, 128], [8, 2]]), hi, start=False, stop=False)
                    def mm3(t, dcc=dcc, lo=lo, pb=pb):
                        return t.matmul(pb[0:2, 0:512], bass.AP(sep, S_CT + dcc, [[5248, 128], [8, 2]]), lo, start=False, stop=(dcc == DC - 1))
                    w = [thh_, tct, tct2]
                    if dcc == 0 and tmodev[n % 2] is not None:
                        w.append(tmodev[n % 2])
                    P.op("t", mm1, waits=w, inc="pe")
                    P.op("t", mm2, waits=[thh_, tct2], inc="pe")
                    mmtok = P.op("t", mm3, waits=[tll], inc="pe")
                    wm_slot_tok[slot] = mmtok
                else:
                    tld = P.op("g", lambda g, n=n, dcc=dcc, hi=hi: g.dma_start(out=hi, in_=wm_[dcc * 128:(dcc + 1) * 128, n * 512:(n + 1) * 512]), waits=wld, inc="gWM%d" % slot, d=16)
                    def mmp(t, dcc=dcc, hi=hi, pb=pb):
                        return t.matmul(pb[0:2, 0:512], bass.AP(sep, S_CT + dcc, [[5248, 128], [8, 2]]), hi, start=(dcc == 0), stop=(dcc == DC - 1))
                    w = [tld, tct, tct2]
                    if dcc == 0 and tmodev[n % 2] is not None:
                        w.append(tmodev[n % 2])
                    mmtok = P.op("t", mmp, waits=w, inc="pe")
                    wm_slot_tok[slot] = mmtok
            tmodev[n % 2] = P.op("s", lambda s, n=n, pb=pb: s.activation(out=mod_tm[:, n * 512:(n + 1) * 512], in_=pb[0:2, 0:512], func=AF.Identity), waits=[mmtok], inc="se")
        tmodA, tmodB = tmodev

        tmb = []
        for s_ in range(2):
            tmb.append(P.op("q", lambda q, s_=s_: q.dma_start(out=bass.AP(hsb, s_ * 128, [[256, 48], [1, 128]]), in_=mod_tm[s_:s_ + 1, :]), waits=[tmodA, tmodB], inc="sMB", d=16))
        tmodT = P.op("q", lambda q: q.dma_start_transpose(modT, hsb.ap()[:, :]), waits=tmb, inc="sMO", d=16)
        tca1 = P.op("v", lambda v: v.tensor_scalar(out=smf[:, O_CA1:O_CA1 + 16], in0=modT[:, 16:32], scalar1=1.0, scalar2=None, op0=ALU.add), waits=[tmodT], inc="ve")
        tcb1 = P.op("v", lambda v: v.tensor_scalar(out=smf[:, O_CB1:O_CB1 + 16], in0=modT[:, 64:80], scalar1=1.0, scalar2=None, op0=ALU.add), waits=[tmodT], inc="ve")
        tsbf = P.op("v", lambda v: v.tensor_copy(out=smf[:, O_SB:O_SB + 16], in_=modT[:, 48:64]), waits=[tmodT], inc="ve")

        def eff(src_off, dst, prev):
            t1 = P.op("s", lambda s, o=src_off, dst=dst: s.activation(out=dst, in_=mod_tm[0:2, o:o + 1024], func=AF.Abs), waits=[tmodA, tmodB] + prev, inc="se")
            t2 = P.op("v", lambda v, dst=dst: v.tensor_scalar(out=dst, in0=dst, scalar1=0.25, scalar2=None, op0=ALU.max), waits=[t1], inc="ve")
            t3 = P.op("v", lambda v, o=src_off: v.tensor_scalar(out=sgn, in0=mod_tm[0:2, o:o + 1024], scalar1=0.0, scalar2=None, op0=ALU.is_ge), waits=[tmodA, tmodB] + prev, inc="ve")
            t4 = P.op("v", lambda v: v.tensor_scalar(out=sgn, in0=sgn, scalar1=2.0, scalar2=-1.0, op0=ALU.mult, op1=ALU.add), waits=[t3], inc="ve")
            t5 = P.op("v", lambda v, dst=dst: v.tensor_tensor(out=dst, in0=dst, in1=sgn, op=ALU.mult), waits=[t2, t4], inc="ve")
            return t5
        tga = eff(2048, ga_tm, [])
        tgb = eff(5120, gb_tm, [tga])

        rep_toks = {}

        def make_rep(name, rrow, mk, roff, srcap=None):
            tmk = mk()
            sap = repr_ if srcap is None else srcap
            tb = P.op("g", lambda g, rrow=rrow, sap=sap: g.dma_start(out=hsc.ap()[rrow:rrow + 2, :], in_=sap), waits=[tmk], inc="gRB", d=16)
            toks = []
            for s_ in range(2):
                dst = repb[:, (roff + s_) * 1024:(roff + s_ + 1) * 1024]
                toks.append(P.op("q", lambda q, s_=s_, rrow=rrow, dst=dst: q.dma_start(out=dst, in_=bass.AP(hsc, (rrow + s_) * 1024, [[0, 128], [1, 1024]])), waits=[tb], inc="sRP", d=16))
            rep_toks[name] = toks
            return tmk

        tgg = make_rep("gga", 0, lambda: P.op("v", lambda v: v.tensor_tensor(out=repr_, in0=gam2, in1=ga_tm, op=ALU.mult), waits=[tga, t_g2, tgb, tcs, tcb, tcb2], inc="ve"), 0)
        t_b2 = P.op("q", lambda q: q.dma_start(out=gam2, in_=bass.AP(hbe, 0, [[0, 2], [1, 1024]])), waits=[tgg] + rep_toks["gga"], inc="sBE", d=16)
        tbgm = make_rep("bga", 2, lambda: P.op("v", lambda v: v.tensor_tensor(out=repr_, in0=gam2, in1=ga_tm, op=ALU.mult), waits=[t_b2, tga, tgg] + rep_toks["gga"], inc="ve"), 2)
        tcb1r = make_rep("cb1", 6, lambda: P.op("v", lambda v: v.tensor_scalar(out=repr_, in0=mod_tm[0:2, 4096:5120], scalar1=1.0, scalar2=None, op0=ALU.add), waits=[tmodA, tmodB, tbgm] + rep_toks["bga"], inc="ve"), 6)
        tsbr = make_rep("sbr", 10, lambda: P.op("v", lambda v: v.tensor_copy(out=repr_, in_=mod_tm[0:2, 3072:4096]), waits=[tmodA, tmodB, tcb1r] + rep_toks["cb1"], inc="ve"), 8)


        swb_tok = []
        for s_ in range(2):
            for eh in range(2):
                pb = psum[4 + eh]
                toks = []
                for dcc in range(DC):
                    def mmsw(t, s_=s_, dcc=dcc, eh=eh, pb=pb):
                        return t.matmul(pb[0:1, 0:512], modT[:, 2 * dcc + s_:2 * dcc + s_ + 1], wt[:, dcc * 1024 + eh * 512:dcc * 1024 + eh * 512 + 512], start=(dcc == 0), stop=(dcc == DC - 1))
                    w = [tmodT, tWt[dcc]]
                    if dcc == 0 and s_ == 1:
                        w.append(swb_tok[eh])
                    toks.append(P.op("t", mmsw, waits=w, inc="pe"))
                swb_tok.append(P.op("s", lambda s, s_=s_, eh=eh, pb=pb: s.activation(out=sep[0:1, SWB[s_] + eh * 512:SWB[s_] + eh * 512 + 512], in_=pb[0:1, 0:512], func=AF.Identity), waits=[toks[-1]], inc="se"))
        tbias = []
        for s_ in range(2):
            tbias.append(P.op("g", lambda g, s_=s_: g.dma_start(out=sep[1:2, SWB[s_]:SWB[s_] + 1024], in_=bia[:, :]), waits=[tcs] if s_ == 0 else [], inc="gBI", d=16))

        wq_raw = Pool(2)
        nat_pool = Pool(2)
        tq_prev = [None]
        park_done = {"wg": [], "wu": [], "wd": []}
        wgT_tok = [[] for _ in range(DC)]
        wuT_tok = [[] for _ in range(DC)]
        wTR_war = [[None] * 4, [None] * 4]
        wlast_bank = [None, None]
        stg_pool = Pool(2)
        stg_last = [None, None]

        def prep_gu_tile(name, src, park, scoff, ic):
                slot, w = wq_raw.acquire()
                raw = scr[:, slot * 1024:(slot + 1) * 1024]
                tl = P.op("g", lambda g, ic=ic, raw=raw, src=src: g.dma_start(out=raw, in_=src[ic * 128:(ic + 1) * 128, :]), waits=w + ([tones, ts1] if ic < 2 else []), inc="sQ0" if slot == 0 else "sQ1", d=16)
                wq_raw.wrote(slot, tl)
                tmx = P.op("v", lambda v, raw=raw: v.tensor_reduce(out=smv(C_T0), in_=raw, axis=AX.X, op=ALU.max, apply_absolute_value=True), waits=[tl], inc="ve")
                tsc = P.op("v", lambda v, ic=ic: v.tensor_scalar(out=smf[:, scoff + ic:scoff + ic + 1], in0=smv(C_T0), scalar1=1.0 / 127.0, scalar2=None, op0=ALU.mult), waits=[tmx], inc="ve")
                trc = P.op("v", lambda v, ic=ic: v.reciprocal(out=smv(C_T1), in_=smf[:, scoff + ic:scoff + ic + 1]), waits=[tsc] + ([tq_prev[0]] if tq_prev[0] else []), inc="ve")
                tq = P.op("s", lambda s, raw=raw: s.activation(out=raw, in_=raw, func=AF.Identity, scale=smv(C_T1), bias=smv(C_CB)), waits=[trc, tCb, tmx], inc="se")
                tq_prev[0] = tq
                nslot, wn = nat_pool.acquire()
                nat = hgb[:, nslot * 1024:(nslot + 1) * 1024]
                tqq = P.op("s", lambda s, raw=raw, nat=nat: s.activation(out=nat, in_=raw, func=AF.Identity, bias=smf[:, 14:15]), waits=[tq, tNCB] + wn, inc="se")
                nat_pool.wrote(nslot, tqq)
                wq_raw.read(slot, tqq)
                if name in ("wu", "wg"):
                    tpk = P.op("s", lambda s, ic=ic, nat=nat, park=park: s.dma_start(out=park.ap()[ic * 128:(ic + 1) * 128, :], in_=nat), waits=[tqq], inc="aP0" if nslot == 0 else "aP1", d=16)
                    nat_pool.read(nslot, tpk)
                    park_done[name].append(tpk)
                    return
                bank = psum[2 + nslot]
                rw = ([tmodA, tmodB] + [t for t in wm_slot_tok if t]) if name == "wg" else list(mod_tm_readers)
                for dcc in range(DC):
                    creg = dcc % 4
                    wtr = P.op("t", lambda t, bank=bank, creg=creg, nat=nat, dcc=dcc: t.matmul(bank[:, creg * 128:(creg + 1) * 128], nat[:, dcc * 128:(dcc + 1) * 128], ebuf[:, 0:128], start=True, stop=True), waits=[tqq, teye] + ([wTR_war[nslot][creg]] if wTR_war[nslot][creg] else []), inc="pe")
                    nat_pool.read(nslot, wtr)
                    dst = (dcc * 2816 + ic * 128) if name == "wg" else (22528 + dcc * 2816 + ic * 128)
                    if dst + 128 <= 36864:
                        tev = P.op("s", lambda s, bank=bank, creg=creg, dst=dst: s.activation(out=wa[:, dst:dst + 128], in_=bank[:, creg * 128:(creg + 1) * 128], func=AF.Identity), waits=[wtr] + rw, inc="se")
                        (wgT_tok if name == "wg" else wuT_tok)[dcc].append(tev)
                    else:
                        ssl, sw = stg_pool.acquire()
                        hst = hho[:, ssl * 128:(ssl + 1) * 128]
                        tev = P.op("s", lambda s, bank=bank, creg=creg, hst=hst: s.activation(out=hst, in_=bank[:, creg * 128:(creg + 1) * 128], func=AF.Identity), waits=[wtr] + sw, inc="se")
                        tpkT = P.op("s", lambda s, hst=hst, dst=dst: s.dma_start(out=pkuT.ap()[:, dst - 36864:dst - 36864 + 128], in_=hst), waits=[tev], inc="aPT0" if ssl == 0 else "aPT1", d=16)
                        stg_pool.wrote(ssl, tpkT)
                        stg_last[ssl] = tpkT
                    wTR_war[nslot][creg] = tev
                    wlast_bank[nslot] = tev
        PREP_JOBS = []
        for _ic in range(IC):
            PREP_JOBS.append(lambda ic=_ic: prep_gu_tile("wg", wg_, pkg, O_SCG, ic))
        for _ic in range(IC):
            PREP_JOBS.append(lambda ic=_ic: prep_gu_tile("wu", wu_, pku, O_SCU, ic))
        for _hcn in range(DC):
            PREP_JOBS.append(lambda hcn=_hcn: prep_wd_tile(hcn))

        def prep_wd_tile(hcn):
            mxp = []
            for j, (o, wdt) in enumerate([(0, 1024), (1024, 1024), (2048, 768)]):
                slot, w = wq_raw.acquire()
                raw = scr[:, slot * 1024:slot * 1024 + wdt]
                tl = P.op("g", lambda g, hcn=hcn, o=o, wdt=wdt, raw=raw: g.dma_start(out=raw, in_=wd_[hcn * 128:(hcn + 1) * 128, o:o + wdt]), waits=w, inc="sQ0" if slot == 0 else "sQ1", d=16)
                wq_raw.wrote(slot, tl)
                tmx = P.op("v", lambda v, raw=raw, j=j: v.tensor_reduce(out=smf[:, O_MX + j:O_MX + j + 1], in_=raw, axis=AX.X, op=ALU.max, apply_absolute_value=True), waits=[tl] + ([mxp[0]] if (j and False) else []), inc="ve")
                wq_raw.read(slot, tmx)
                mxp.append(tmx)
            tm2 = P.op("v", lambda v: v.tensor_tensor(out=smv(O_MX), in0=smf[:, O_MX:O_MX + 1], in1=smf[:, O_MX + 1:O_MX + 2], op=ALU.max), waits=mxp, inc="ve")
            tm3 = P.op("v", lambda v: v.tensor_tensor(out=smv(C_T0), in0=smv(O_MX), in1=smf[:, O_MX + 2:O_MX + 3], op=ALU.max), waits=[tm2], inc="ve")
            tsc = P.op("v", lambda v, hcn=hcn: v.tensor_scalar(out=smf[:, O_SCD + hcn:O_SCD + hcn + 1], in0=smv(C_T0), scalar1=1.0 / 127.0, scalar2=None, op0=ALU.mult), waits=[tm3], inc="ve")
            trc = P.op("v", lambda v: v.reciprocal(out=smv(C_T1), in_=smf[:, O_SCD + hcn:O_SCD + hcn + 1] if False else smv(C_T0)), waits=[tsc], inc="ve")
            trc = P.op("v", lambda v, hcn=hcn: v.reciprocal(out=smv(C_T1), in_=smf[:, O_SCD + hcn:O_SCD + hcn + 1]), waits=[tsc, trc] + ([tq_prev[0]] if tq_prev[0] else []), inc="ve")
            for j, (o, wdt) in enumerate([(0, 1024), (1024, 1024), (2048, 768)]):
                slot, w = wq_raw.acquire()
                raw = scr[:, slot * 1024:slot * 1024 + wdt]
                tl = P.op("g", lambda g, hcn=hcn, o=o, wdt=wdt, raw=raw: g.dma_start(out=raw, in_=wd_[hcn * 128:(hcn + 1) * 128, o:o + wdt]), waits=w + [trc], inc="sQ0" if slot == 0 else "sQ1", d=16)
                wq_raw.wrote(slot, tl)
                tq = P.op("s", lambda s, raw=raw: s.activation(out=raw, in_=raw, func=AF.Identity, scale=smv(C_T1), bias=smv(C_CB)), waits=[tl], inc="se")
                tq_prev[0] = tq
                nslot, wn = nat_pool.acquire()
                nat = hgb[:, nslot * 1024:nslot * 1024 + wdt]
                tqq = P.op("s", lambda s, raw=raw, nat=nat: s.activation(out=nat, in_=raw, func=AF.Identity, bias=smf[:, 14:15]), waits=[tq, tNCB] + wn, inc="se")
                nat_pool.wrote(nslot, tqq)
                wq_raw.read(slot, tqq)
                tpk = P.op("s", lambda s, hcn=hcn, o=o, wdt=wdt, nat=nat: s.dma_start(out=pkd.ap()[hcn * 128:(hcn + 1) * 128, o:o + wdt], in_=nat), waits=[tqq], inc="aP0" if nslot == 0 else "aP1", d=16)
                nat_pool.read(nslot, tpk)
                park_done["wd"].append(tpk)

        mod_tm_readers = [tga, tgb, tmodA, tmodB] + tmb
        teye = P.op("g", lambda g: g.dma_start(out=ebuf[:, 0:128], in_=ey_), inc="sEY", d=16)
        prev_evac = [[] for _ in range(DC)]
        prev_evac[0] = [swb_tok[0], swb_tok[2]]
        prev_evac[1] = [swb_tok[0], swb_tok[2]]
        prev_evac[2] = [swb_tok[1], swb_tok[3]]
        prev_evac[3] = [swb_tok[1], swb_tok[3]]
        tr_tok = [[None] * DC, [None] * DC]
        seedA = rep_toks["sbr"][1]
        zpool = Pool(2, seed=seedA)
        epool = Pool(2, seed=seedA)
        ynpool = Pool(2)
        h1pool = Pool(3)
        stpool = Pool(6)
        fm_hist = [[ts1], [ts1]]
        sfm_hist = [[], []]
        h2sp_done = []
        x2sp_done = {}
        h1war = []
        xhwar = []
        xhwar_prev = []
        ypar_reader = [tmodA, tmodB]
        scan_tail = [None] * DC
        amax_last = None
        tam0 = P.op("v", lambda v: v.memset(smv(C_AM), -3e38), inc="ve")
        tan0 = P.op("v", lambda v: v.memset(smv(C_AN), 3e38), waits=[tam0], inc="ve")
        amax_last = tan0

        def ln_stats(src, w, k, wb=None):
            o = O_ST + 24 * k
            t1 = P.op("v", lambda v, src=src, o=o: v.bn_stats(out=smf[:, o:o + 6], in_=src[:, 0:512]), waits=w, inc="ve")
            t2 = P.op("v", lambda v, src=src, o=o: v.bn_stats(out=smf[:, o + 6:o + 12], in_=src[:, 512:1024]), waits=(w if wb is None else wb), inc="ve")
            t3 = P.op("v", lambda v, o=o: v.bn_aggr(out=smf[:, o + 12:o + 14], in_=smf[:, o:o + 12]), waits=[t1, t2], inc="ve")
            t4 = P.op("s", lambda s, o=o: s.activation(out=smf[:, o + 14:o + 15], in_=smf[:, o + 13:o + 14], func=AF.Sqrt, bias=smv(C_EPS)), waits=[t3, teps], inc="se")
            t5 = P.op("v", lambda v, o=o: v.reciprocal(out=smf[:, o + 15:o + 16], in_=smf[:, o + 14:o + 15]), waits=[t4], inc="ve")
            t6 = P.op("v", lambda v, o=o: v.tensor_scalar(out=smf[:, o + 16:o + 17], in0=smf[:, o + 12:o + 13], scalar1=smf[:, o + 15:o + 16], scalar2=-1.0, op0=ALU.mult, op1=ALU.mult), waits=[t5], inc="ve")
            return smf[:, o + 15:o + 16], smf[:, o + 16:o + 17], t6

        for s_ in range(BL):
            for h in range(NH):
                par = (s_ * NH + h) % 2
                fo = par * 4096
                fmbwar = fm_hist[par]
                fm_hist[par] = []
                evac_half = [[None] * DC, [None] * DC]
                tx2_list = []
                for tt in range(TPH):
                    gt = s_ * (NH * TPH) + h * TPH + tt
                    zslot, w = zpool.acquire()
                    zt = x0w[:, zslot * 1024:(zslot + 1) * 1024]
                    extra = [tcs] if (zslot == 0 and gt < 2) else []
                    tz = P.op("q", lambda q, gt=gt, zt=zt: q.dma_start(out=zt, in_=Z[gt * 128:(gt + 1) * 128, :]), waits=w + extra, inc="sZ0" if zslot == 0 else "sZ1", d=16)
                    zpool.wrote(zslot, tz)
                    kst, wst = stpool.acquire()
                    r_ap, mr_ap, tstat = ln_stats(zt, [tz] + wst, kst)
                    zpool.read(zslot, tstat)
                    h1 = h1sb[:, tt * 1024:(tt + 1) * 1024]
                    th1 = P.op("s", lambda s, zt=zt, h1=h1, r_ap=r_ap, mr_ap=mr_ap: s.activation(out=h1, in_=zt, func=AF.Identity, scale=r_ap, bias=mr_ap), waits=[tstat] + h1war, inc="se")
                    zpool.read(zslot, th1)
                    stpool.wrote(kst, th1)
                    for dcc in range(DC):
                        pb = psum[4 + dcc // 2]
                        col = (dcc % 2) * 256 + (tt % 2) * 128
                        ttr = P.op("t", lambda t, pb=pb, col=col, h1=h1, dcc=dcc: t.matmul(pb[:, col:col + 128], h1[:, dcc * 128:(dcc + 1) * 128], ebuf[:, 0:128], start=True, stop=True), waits=[th1, teye] + prev_evac[dcc], inc="pe")
                        tr_tok[tt % 2][dcc] = ttr
                    if tt % 2 == 1:
                        p_ = tt // 2
                        for dcc in range(DC):
                            pb = psum[4 + dcc // 2]
                            tev = P.op("s", lambda s, pb=pb, dcc=dcc, fo=fo, p_=p_, s_=s_: s.activation(out=fmb[:, fo + dcc * SEG + p_ * 256:fo + dcc * SEG + p_ * 256 + 256], in_=pb[:, (dcc % 2) * 256:(dcc % 2) * 256 + 256], func=AF.Identity, scale=smf[:, O_CA1 + 2 * dcc + s_:O_CA1 + 2 * dcc + s_ + 1]), waits=[tr_tok[0][dcc], tr_tok[1][dcc], tca1] + fmbwar, inc="se")
                            prev_evac[dcc] = [tev]
                            evac_half[p_][dcc] = tev
                scans = []
                sfw = sfm_hist[par]
                sfm_hist[par] = []
                for dcc in range(DC):
                    init = 0.0 if h == 0 else smf[:, O_TAIL + dcc:O_TAIL + dcc + 1]
                    winit = [] if h == 0 else [scan_tail[dcc]]
                    tscan = P.op("v", lambda v, dcc=dcc, fo=fo, init=init: v.tensor_tensor_scan(out=sfm[:, fo + dcc * SEG:fo + (dcc + 1) * SEG], data0=bass.AP(smf, C_A, [[512, 128], [0, SEG]]), data1=fmb[:, fo + dcc * SEG:fo + (dcc + 1) * SEG], initial=init, op0=ALU.mult, op1=ALU.add), waits=[evac_half[0][dcc], evac_half[1][dcc], tac] + sfw + winit, inc="ve")
                    ttl = P.op("v", lambda v, dcc=dcc, fo=fo: v.tensor_copy(out=smf[:, O_TAIL + dcc:O_TAIL + dcc + 1], in_=sfm[:, fo + (dcc + 1) * SEG - 1:fo + (dcc + 1) * SEG]), waits=[tscan] + ([scan_tail[dcc]] if h > 0 else []), inc="ve")
                    scan_tail[dcc] = ttl
                    fm_hist[par].append(tscan)
                    sfm_hist[par].append(ttl)
                    scans.append(tscan)
                pending_post = None
                for tt in range(TPH):
                    gt = s_ * (NH * TPH) + h * TPH + tt
                    mm_l = []
                    for eh in range(2):
                        pb = psum[eh]
                        toks = []
                        for dcc in range(DC):
                            def mmy(t, dcc=dcc, eh=eh, tt=tt, pb=pb, fo=fo):
                                return t.matmul(pb[:, 0:512], sfm[:, fo + dcc * SEG + tt * 128:fo + dcc * SEG + tt * 128 + 128], wt[:, dcc * 1024 + eh * 512:dcc * 1024 + eh * 512 + 512], start=(dcc == 0), stop=False)
                            w = [scans[dcc], tWt[dcc]]
                            if dcc == 0 and ypar_reader[eh] is not None:
                                w.append(ypar_reader[eh])
                            toks.append(P.op("t", mmy, waits=w, inc="pe"))
                        def mmk2(t, eh=eh, pb=pb, s_=s_, gt=gt):
                            trel = (gt % (NH * TPH)) * 128
                            return t.matmul(pb[:, 0:512], sep[0:2, S_S1 + trel:S_S1 + trel + 128], sep[0:2, SWB[s_] + eh * 512:SWB[s_] + eh * 512 + 512], start=False, stop=True)
                        tk2 = P.op("t", mmk2, waits=[ts1, tone2, swb_tok[2 * s_ + eh], tbias[s_], toks[-1]], inc="pe")
                        last_y_mm = [tk2]
                        mm_l.append(tk2)
                        sfm_hist[par].append(tk2)
                    kst, wst = stpool.acquire()
                    oY = O_ST + 24 * kst
                    pbl, pbh = psum[0], psum[1]
                    ty1 = P.op("v", lambda v, pbl=pbl, oY=oY: v.bn_stats(out=smf[:, oY:oY + 6], in_=pbl[:, 0:512]), waits=[mm_l[0]] + wst, inc="ve")
                    ty2 = P.op("v", lambda v, pbh=pbh, oY=oY: v.bn_stats(out=smf[:, oY + 6:oY + 12], in_=pbh[:, 0:512]), waits=[mm_l[1]], inc="ve")
                    ty3 = P.op("v", lambda v, oY=oY: v.bn_aggr(out=smf[:, oY + 12:oY + 14], in_=smf[:, oY:oY + 12]), waits=[ty1, ty2], inc="ve")
                    ty4 = P.op("s", lambda s, oY=oY: s.activation(out=smf[:, oY + 14:oY + 15], in_=smf[:, oY + 13:oY + 14], func=AF.Sqrt, bias=smv(C_EPS)), waits=[ty3, teps], inc="se")
                    ty5 = P.op("v", lambda v, oY=oY: v.reciprocal(out=smf[:, oY + 15:oY + 16], in_=smf[:, oY + 14:oY + 15]), waits=[ty4], inc="ve")
                    ty6 = P.op("v", lambda v, oY=oY: v.tensor_scalar(out=smf[:, oY + 16:oY + 17], in0=smf[:, oY + 12:oY + 13], scalar1=smf[:, oY + 15:oY + 16], scalar2=-1.0, op0=ALU.mult, op1=ALU.mult), waits=[ty5], inc="ve")
                    ysl, wyn = ynpool.acquire()
                    yt = ynb2[:, ysl * 1024:(ysl + 1) * 1024]
                    tyn = []
                    for eh, pb in ((0, pbl), (1, pbh)):
                        tok = P.op("s", lambda s, pb=pb, eh=eh, yt=yt, oY=oY: s.activation(out=yt[:, eh * 512:eh * 512 + 512], in_=pb[:, 0:512], func=AF.Identity, scale=smf[:, oY + 15:oY + 16], bias=smf[:, oY + 16:oY + 17]), waits=[ty6] + wyn, inc="se")
                        tyn.append(tok)
                        ypar_reader[eh] = tok
                    stpool.wrote(kst, ty6)
                    tt2 = P.op("v", lambda v, yt=yt, s_=s_: v.tensor_tensor(out=yt, in0=yt, in1=repb[:, s_ * 1024:(s_ + 1) * 1024], op=ALU.mult), waits=tyn + rep_toks["gga"], inc="ve")
                    tu2 = P.op("v", lambda v, yt=yt, s_=s_: v.tensor_tensor(out=yt, in0=yt, in1=repb[:, 2048 + s_ * 1024:2048 + (s_ + 1) * 1024], op=ALU.add), waits=[tt2] + rep_toks["bga"], inc="ve")
                    eslot, we = epool.acquire()
                    x2t = x0w[:, (2 + eslot) * 1024:(3 + eslot) * 1024]
                    tz2 = P.op("q", lambda q, gt=gt, x2t=x2t: q.dma_start(out=x2t, in_=Z[gt * 128:(gt + 1) * 128, :]), waits=we, inc="sE0" if eslot == 0 else "sE1", d=16)
                    tx2a = P.op("g", lambda g, x2t=x2t, yt=yt: g.tensor_tensor(out=x2t[:, 0:512], in0=x2t[:, 0:512], in1=yt[:, 0:512], op=ALU.add), waits=[tz2, tu2], inc="ge")
                    tx2b = P.op("g", lambda g, x2t=x2t, yt=yt: g.tensor_tensor(out=x2t[:, 512:1024], in0=x2t[:, 512:1024], in1=yt[:, 512:1024], op=ALU.add), inc="ge")
                    epool.wrote(eslot, tx2b)
                    epool.read(eslot, tx2a)
                    ynpool.wrote(ysl, tx2b)
                    ynpool.read(ysl, tx2a)
                    tx2_list.append(tx2b)
                    tsp = P.op("s", lambda s, gt=gt, x2t=x2t: s.dma_start(out=spx2.ap()[gt * 128:(gt + 1) * 128, :], in_=x2t), waits=[tx2a, tx2b], inc="aX20" if eslot == 0 else "aX21", d=16)
                    x2sp_done[gt] = tsp
                    epool.read(eslot, tsp)

                    def mk_post(gt=gt, tt=tt, x2t=x2t, eslot=eslot, tx2a=tx2a, tx2b=tx2b, s_=s_, par=par):
                        def post():
                            kst2, wst2 = stpool.acquire()
                            r2, mr2, tst2 = ln_stats(x2t, [tx2a] + wst2, kst2, wb=[tx2b] + wst2)
                            epool.read(eslot, tst2)
                            xh = xhsb[:, tt * 1024:(tt + 1) * 1024]
                            txh = P.op("s", lambda s, x2t=x2t, xh=xh, r2=r2, mr2=mr2: s.activation(out=xh, in_=x2t, func=AF.Identity, scale=r2, bias=mr2), waits=[tst2] + xhwar_prev, inc="se")
                            epool.read(eslot, txh)
                            stpool.wrote(kst2, txh)
                            tt4 = P.op("v", lambda v, xh=xh, s_=s_: v.tensor_tensor(out=xh, in0=xh, in1=repb[:, 6144 + s_ * 1024:6144 + (s_ + 1) * 1024], op=ALU.mult), waits=[txh] + rep_toks["cb1"], inc="ve")
                            th2 = P.op("v", lambda v, xh=xh, s_=s_: v.tensor_tensor(out=xh, in0=xh, in1=repb[:, 8192 + s_ * 1024:8192 + (s_ + 1) * 1024], op=ALU.add), waits=[tt4] + rep_toks["sbr"], inc="ve")
                            tamx = P.op("v", lambda v, xh=xh, gt=gt: v.tensor_reduce(out=smf[:, 384 + gt:385 + gt], in_=xh, axis=AX.X, op=ALU.max, apply_absolute_value=True), waits=[th2], inc="ve")
                            tspl = P.op("s", lambda s, gt=gt, xh=xh: s.dma_start(out=sph2.ap()[gt * 128:(gt + 1) * 128, :], in_=xh), waits=[th2], inc="aH20" if par == 0 else "aH21", d=16)
                            h2sp_done.append(tspl)
                            xhwar.append(tamx)
                            xhwar.append(tspl)
                            return tamx
                        return post

                    if pending_post is not None:
                        amax_last = pending_post()
                    pending_post = mk_post()
                    for _ in range(2):
                        if PREP_JOBS:
                            PREP_JOBS.pop(0)()
                if pending_post is not None:
                    amax_last = pending_post()
                    pending_post = None
                h1war = list(scans)
                xhwar_prev = list(xhwar)
                xhwar = []

        while PREP_JOBS:
            PREP_JOBS.pop(0)()
        parks_all = [("aP0", P.cnt.get("aP0", 0)), ("aP1", P.cnt.get("aP1", 0))]
        tscdw = P.op("q", lambda q: q.dma_start(out=bass.AP(hsf, 8 * 1024, [[1, 128], [128, 8]]), in_=smf[:, O_SCD:O_SCD + 8], allow_slow_non_contiguous=True), waits=parks_all, inc="sS3", d=16)
        tscd2 = P.op("q", lambda q: q.dma_start(out=gam2, in_=bass.AP(hsf, 8 * 1024, [[0, 2], [1, 1024]])), waits=[tscdw, tbgm] + rep_toks["bga"], inc="sS3", d=16)
        make_rep("scdgb", 4, lambda: P.op("v", lambda v: v.tensor_tensor(out=gam2, in0=gam2, in1=gb_tm, op=ALU.mult), waits=[tscd2, tgb, tbgm] + rep_toks["sbr"], inc="ve"), 4, srcap=gam2)

        for dcc in range(DC):
            wgT_tok[dcc].append(P.op("q", lambda q, dcc=dcc: q.dma_start_transpose(wa[:, dcc * 2816:(dcc + 1) * 2816], pkg.ap()[:, dcc * 128:(dcc + 1) * 128]), waits=park_done["wg"] + [tmodA, tmodB] + [t for t in wm_slot_tok if t], inc="sWG", d=16))
        for dcc in range(DC):
            wuT_tok[dcc].append(P.op("q", lambda q, dcc=dcc: q.dma_start_transpose(wa[:, 22528 + dcc * 2816:22528 + (dcc + 1) * 2816], pku.ap()[:, dcc * 128:(dcc + 1) * 128]), waits=park_done["wu"] + mod_tm_readers + last_y_mm, inc="sWU", d=16))

        def global_reduce(prev_tok, cci, cco, ccp, hrow, scol, sigcol, iscol, extra_q=(), extra_t8=()):
            t1 = P.op("v", lambda v: v.tensor_scalar(out=smv(C_T0), in0=smv(C_AN), scalar1=-1.0, scalar2=None, op0=ALU.mult), waits=[prev_tok], inc="ve")
            t2 = P.op("v", lambda v: v.tensor_tensor(out=smv(C_T0), in0=smv(C_T0), in1=smv(C_AM), op=ALU.max), waits=[t1], inc="ve")
            t3 = P.op("q", lambda q, hrow=hrow: q.dma_start(out=hsf.ap()[hrow:hrow + 1, 0:128], in_=smv(C_T0)), waits=[t2] + list(extra_q), inc="sGR", d=16)
            t4 = P.op("q", lambda q, hrow=hrow: q.dma_start(out=scr[0:1, 0:128], in_=hsf.ap()[hrow:hrow + 1, 0:128]), waits=[t3, tones, ts1], inc="sGR", d=16)
            t5 = P.op("v", lambda v: v.tensor_reduce(out=scr[0:1, 128:129], in_=scr[0:1, 0:128], axis=AX.X, op=ALU.max), waits=[t4], inc="ve")
            t6 = P.op("q", lambda q, cci=cci: q.dma_start(out=cci.ap()[:, :], in_=scr[0:1, 128:129]), waits=[t5], inc="sGR", d=16)
            t7 = P.op("g", lambda g, cci=cci, cco=cco: g.collective_compute("AllReduce", ALU.max, ins=[cci.ap()[:, :]], outs=[cco.ap()[:, :]], replica_groups=[list(range(CC_WORLD))]), waits=[t6], inc="cc")
            t7b = P.op("g", lambda g, cci=cci, ccp=ccp: g.collective_compute("AllReduce", ALU.max, ins=[cci.ap()[:, :]], outs=[ccp.ap()[:, :]], replica_groups=[list(range(CC_WORLD))]), waits=[t7], inc="cc")
            t8 = P.op("g", lambda g, cco=cco: g.dma_start(out=smv(scol), in_=bass.AP(cco, 0, [[0, 128], [1, 1]])), waits=[t7b] + list(extra_t8), inc="sGR2", d=16)
            t9 = P.op("v", lambda v: v.tensor_scalar(out=smv(scol), in0=smv(scol), scalar1=smv(sigcol), scalar2=1.0 / 127.0, op0=ALU.mult, op1=ALU.mult), waits=[t8, tsx, tsd], inc="ve")
            t9b = P.op("v", lambda v: v.tensor_scalar(out=smv(scol), in0=smv(scol), scalar1=1e-35, scalar2=None, op0=ALU.max), waits=[t9], inc="ve")
            t10 = P.op("v", lambda v: v.reciprocal(out=smv(iscol), in_=smv(scol)), waits=[t9b], inc="ve")
            return t9b, t10
        tcmb = P.op("v", lambda v: v.tensor_reduce(out=smv(C_AM), in_=smf[:, 384:416], axis=AX.X, op=ALU.max), waits=[amax_last], inc="ve")
        h2_all = [("aH20", P.cnt.get("aH20", 0)), ("aH21", P.cnt.get("aH21", 0))]
        x2_all = [("aX20", P.cnt.get("aX20", 0)), ("aX21", P.cnt.get("aX21", 0))]
        fmb_hist = [list(fm_hist[0]), list(fm_hist[1])]
        xq_hist = [list(sfm_hist[0]), list(sfm_hist[1])]
        last_mm_xq = [None, None]

        def emit_tl(g):
            par = g % 2
            base = par * 4096
            w_h2 = fmb_hist[par]
            fmb_hist[par] = []
            toks = []
            for dcc in range(DC):
                h2in = fmb[:, base + dcc * 512:base + (dcc + 1) * 512]
                tl = P.op("q", lambda q, g=g, dcc=dcc, h2in=h2in: q.dma_start_transpose(h2in, sph2.ap()[g * GT:(g + 1) * GT, dcc * 128:(dcc + 1) * 128]), waits=h2_all + w_h2 + ([last_mm_xq[par]] if last_mm_xq[par] else []), inc="sH0" if par == 0 else "sH1", d=16)
                toks.append(tl)
            return toks
        tl_pref = {0: emit_tl(0), 1: emit_tl(1)}
        ts1x, tis1 = global_reduce(tcmb, cc1i, cc1o, cc1p, 9, C_T1, C_SX, C_IS1, extra_t8=[tl_pref[1][-1]])
        tsgx = P.op("v", lambda v: v.tensor_scalar(out=smf[:, O_SGX:O_SGX + 22], in0=smf[:, O_SCG:O_SCG + 22], scalar1=smv(C_T1), scalar2=None, op0=ALU.mult), waits=[ts1x], inc="ve")
        tsux = P.op("v", lambda v: v.tensor_scalar(out=smf[:, O_SUX:O_SUX + 22], in0=smf[:, O_SCU:O_SCU + 22], scalar1=smv(C_T1), scalar2=None, op0=ALU.mult), waits=[ts1x], inc="ve")

        tam2 = P.op("v", lambda v: v.memset(smv(C_AM), -3e38), waits=[ts1x], inc="ve")
        tan2 = P.op("v", lambda v: v.memset(smv(C_AN), 3e38), waits=[tam2], inc="ve")
        amax_last = tan2
        hh_sp = []
        gu_reader = [None, None, wlast_bank[0], wlast_bank[1]]
        hg_war = [None, None]
        hu_war = [None, None]
        hh_war = [stg_last[0], stg_last[1]]
        tq1_last = [None, None]
        for g in range(NG):
            s_ = g // GPS
            par = g % 2
            base = par * 4096
            w_xq = xq_hist[par]
            xq_hist[par] = []
            xq_tok = []
            tl_toks = tl_pref.pop(g) if g in tl_pref else emit_tl(g)
            for dcc in range(DC):
                h2in = fmb[:, base + dcc * 512:base + (dcc + 1) * 512]
                tq1 = P.op("s", lambda s, h2in=h2in, par=par: s.activation(out=scr[:, par * 512:par * 512 + 512], in_=h2in, func=AF.Identity, scale=smv(C_IS1), bias=smv(C_CB)), waits=[tl_toks[-1], tis1] + ([tq1_last[par]] if tq1_last[par] else []), inc="se")
                tq2 = P.op("v", lambda v, par=par: v.tensor_scalar(out=scr[:, par * 512:par * 512 + 512], in0=scr[:, par * 512:par * 512 + 512], scalar1=smv(C_HI), scalar2=smv(C_LO), op0=ALU.min, op1=ALU.max), waits=[tq1, thi, tlo], inc="ve")
                xq = sfm[:, base + dcc * 512:base + (dcc + 1) * 512]
                tq3 = P.op("v", lambda v, xq=xq, par=par: v.tensor_scalar(out=xq, in0=scr[:, par * 512:par * 512 + 512], scalar1=smv(C_CB), scalar2=None, op0=ALU.subtract), waits=[tq2] + w_xq, inc="ve")
                w_xq = []
                tq1_last[par] = tq3
                xq_tok.append(tq3)
                fmb_hist[par].append(tq1)
            for ic in range(IC):
                icp = ic % 2
                pg = psum[icp]
                pu = psum[2 + icp]
                gtoks = []
                for dcc in range(DC):
                    def mmg(t, ic=ic, dcc=dcc, pg=pg, base=base):
                        return t.matmul(pg[:, 0:512], wa[:, dcc * 2816 + ic * 128:dcc * 2816 + ic * 128 + 128], sfm[:, base + dcc * 512:base + (dcc + 1) * 512], start=(dcc == 0), stop=(dcc == DC - 1))
                    w = [xq_tok[dcc]] + wgT_tok[dcc]
                    if dcc == 0 and gu_reader[icp] is not None:
                        w.append(gu_reader[icp])
                    gtoks.append(P.op("t", mmg, waits=w, inc="pe"))
                utoks = []
                for dcc in range(DC):
                    def mmu(t, ic=ic, dcc=dcc, pu=pu, base=base):
                        return t.matmul(pu[:, 0:512], wa[:, 22528 + dcc * 2816 + ic * 128:22528 + dcc * 2816 + ic * 128 + 128], sfm[:, base + dcc * 512:base + (dcc + 1) * 512], start=(dcc == 0), stop=(dcc == DC - 1))
                    w = [xq_tok[dcc]] + wuT_tok[dcc]
                    if dcc == 0 and gu_reader[2 + icp] is not None:
                        w.append(gu_reader[2 + icp])
                    utoks.append(P.op("t", mmu, waits=w, inc="pe"))
                hgp = hgb[:, icp * 512:icp * 512 + 512]
                hup = hgb[:, 1024 + icp * 512:1024 + icp * 512 + 512]
                thg = P.op("s", lambda s, pg=pg, hgp=hgp, ic=ic: s.activation(out=hgp, in_=pg[:, 0:512], func=AF.Silu, scale=smf[:, O_SGX + ic:O_SGX + ic + 1]), waits=[gtoks[-1], tsgx] + ([hg_war[icp]] if hg_war[icp] else []), inc="se")
                gu_reader[icp] = thg
                thu = P.op("s", lambda s, pu=pu, hup=hup, ic=ic: s.activation(out=hup, in_=pu[:, 0:512], func=AF.Identity, scale=smf[:, O_SUX + ic:O_SUX + ic + 1]), waits=[utoks[-1], tsux] + ([hu_war[icp]] if hu_war[icp] else []), inc="se")
                gu_reader[2 + icp] = thu
                hh = hho[:, icp * 512:icp * 512 + 512]
                thh = P.op("v", lambda v, hgp=hgp, hup=hup, hh=hh: v.tensor_tensor(out=hh, in0=hgp, in1=hup, op=ALU.mult), waits=[thg, thu] + ([hh_war[icp]] if hh_war[icp] else []), inc="ve")
                hg_war[icp] = thh
                tm = P.op("v", lambda v, hh=hh, hup=hup: v.tensor_scalar(out=hup, in0=hh, scalar1=1.0, scalar2=smv(C_AM), op0=ALU.mult, op1=ALU.max, accum_out=smv(C_AM)), waits=[thh, amax_last], inc="ve")
                tn = P.op("v", lambda v, hh=hh, hup=hup: v.tensor_scalar(out=hup, in0=hh, scalar1=1.0, scalar2=smv(C_AN), op0=ALU.min, op1=ALU.min, accum_out=smv(C_AN)), waits=[tm], inc="ve")
                amax_last = tn
                hu_war[icp] = tn
                tsp = P.op("s", lambda s, g=g, ic=ic, hh=hh: s.dma_start(out=sphh.ap()[g * IC + ic, :, :], in_=hh), waits=[thh], inc="aHH0" if icp == 0 else "aHH1", d=16)
                hh_sp.append(tsp)
                hh_war[icp] = tsp
            last_mm_xq[par] = gtoks[-1]
            xq_hist[par] = [utoks[-1], gtoks[-1]]

        wdT_tok = []
        for icc in range(IC):
            wdT_tok.append(P.op("q", lambda q, icc=icc: q.dma_start_transpose(wa[:, icc * 1024:(icc + 1) * 1024], pkd.ap()[:, icc * 128:(icc + 1) * 128]), waits=parks_all + [amax_last, gu_reader[0], gu_reader[1]], inc="sWD", d=16))
        ts2x, tis2 = global_reduce(amax_last, cc2i, cc2o, cc2p, 10, C_S2B, C_SD, C_IS2, extra_t8=[wdT_tok[0]])

        dn_reader = [None] * 4
        t3_war = [None, None]
        dq_war_prev = {}
        hq_mm_last = [[], []]
        x2pool = Pool(2, seed=tis2)
        opool = Pool(2, seed=tis2)
        out_toks = []
        hh_all = [("aHH0", P.cnt.get("aHH0", 0)), ("aHH1", P.cnt.get("aHH1", 0))]
        for g in range(NG):
            s_ = g // GPS
            par = g % 2
            hqb = 22528 + par * 11264
            wclear = hq_mm_last[par]
            hq_mm_last[par] = []
            def mk_tlh(q, g=g, hqb=hqb):
                src = bass.AP(sphh, g * IC * 128 * GT, [[GT, 128], [128 * GT, IC], [1, GT]])
                return q.dma_start(out=wa[:, hqb:hqb + 11264].rearrange("p (i t) -> p i t", t=GT), in_=src)
            tlh = P.op("q", mk_tlh, waits=hh_all + [tis2] + wclear + (([t for l in wuT_tok for t in l]) if g < 2 else []), inc="sL0" if par == 0 else "sL1", d=16)
            hq_tok = []
            dq_war = dq_war_prev
            for ic in range(IC):
                sl = (ic % 2) * 512
                tq1 = P.op("s", lambda s, hqb=hqb, ic=ic, sl=sl: s.activation(out=scr[:, sl:sl + 512], in_=wa[:, hqb + ic * 512:hqb + (ic + 1) * 512], func=AF.Identity, scale=smv(C_IS2), bias=smv(C_CB)), waits=[tlh, tis2] + ([dq_war[sl]] if sl in dq_war else []), inc="se")
                tq2 = P.op("v", lambda v, sl=sl: v.tensor_scalar(out=scr[:, sl:sl + 512], in0=scr[:, sl:sl + 512], scalar1=smv(C_HI), scalar2=smv(C_LO), op0=ALU.min, op1=ALU.max), waits=[tq1], inc="ve")
                tq3 = P.op("v", lambda v, hqb=hqb, ic=ic, sl=sl: v.tensor_scalar(out=wa[:, hqb + ic * 512:hqb + (ic + 1) * 512], in0=scr[:, sl:sl + 512], scalar1=smv(C_CB), scalar2=None, op0=ALU.subtract), waits=[tq2], inc="ve")
                dq_war[sl] = tq3
                dq_war_prev = dq_war
                hq_tok.append(tq3)
            for tp in range(2):
                tts = (2 * tp, 2 * tp + 1)
                gtoks = {(tt, eh): [] for tt in tts for eh in range(2)}
                for ic in range(IC):
                    for tt in tts:
                        dp = tt % 2
                        for eh in range(2):
                            pb = psum[4 + 2 * dp + eh]
                            def mmd(t, ic=ic, eh=eh, tt=tt, pb=pb, hqb=hqb):
                                return t.matmul(pb[:, 0:512], wa[:, hqb + ic * 512 + tt * 128:hqb + ic * 512 + tt * 128 + 128], wa[:, ic * 1024 + eh * 512:ic * 1024 + eh * 512 + 512], start=(ic == 0), stop=(ic == IC - 1))
                            w = [hq_tok[ic], wdT_tok[ic]]
                            if ic == 0 and dn_reader[2 * dp + eh] is not None:
                                w.append(dn_reader[2 * dp + eh])
                            gtoks[(tt, eh)].append(P.op("t", mmd, waits=w, inc="pe"))
                for tt in tts:
                    gt = g * 4 + tt
                    dp = tt % 2
                    mm_l = [gtoks[(tt, 0)][-1], gtoks[(tt, 1)][-1]]
                    hq_mm_last[par].append(mm_l[0])
                    hq_mm_last[par].append(mm_l[1])
                    x2s, wx2 = x2pool.acquire()
                    x2t = x0w[:, x2s * 1024:(x2s + 1) * 1024]
                    tlx = P.op("q", lambda q, gt=gt, x2t=x2t: q.dma_start(out=x2t, in_=spx2.ap()[gt * 128:(gt + 1) * 128, :]), waits=x2_all + [tis2] + wx2, inc="sX0" if x2s == 0 else "sX1", d=16)
                    x2pool.wrote(x2s, tlx)
                    t3t = []
                    for eh in range(2):
                        pb = psum[4 + 2 * dp + eh]
                        tok = P.op("v", lambda v, pb=pb, eh=eh, dp=dp, s_=s_: v.scalar_tensor_tensor(out=scr[:, 1024 + dp * 1024 + eh * 512:1024 + dp * 1024 + eh * 512 + 512], in0=pb[:, 0:512], scalar=smv(C_S2B), in1=repb[:, 4096 + s_ * 1024 + eh * 512:4096 + s_ * 1024 + eh * 512 + 512], op0=ALU.mult, op1=ALU.mult), waits=[mm_l[eh], ts2x, tis2] + rep_toks["scdgb"] + ([dn_reader[2 * dp + eh]] if dn_reader[2 * dp + eh] else []) + ([t3_war[dp]] if t3_war[dp] else []), inc="ve")
                        dn_reader[2 * dp + eh] = tok
                        t3t.append(tok)
                    osl, wo = opool.acquire()
                    ot = x0w[:, (2 + osl) * 1024:(3 + osl) * 1024]
                    tout = P.op("v", lambda v, x2t=x2t, ot=ot, dp=dp: v.tensor_tensor(out=ot, in0=x2t, in1=scr[:, 1024 + dp * 1024:1024 + dp * 1024 + 1024], op=ALU.add), waits=t3t + [tlx] + wo, inc="ve")
                    opool.wrote(osl, tout)
                    t3_war[dp] = tout
                    x2pool.read(x2s, tout)
                    tst = P.op("s", lambda s, gt=gt, ot=ot: s.dma_start(out=y_[gt * 128:(gt + 1) * 128, :], in_=ot), waits=[tout], inc="aO0" if osl == 0 else "aO1", d=16)
                    opool.read(osl, tst)
                    out_toks.append(tst)
            hq_mm_last[par] += t3t

        P.op("q", None, waits=out_toks)
        GROUP_SEMS = {"sS0", "sB0", "sS2", "sWT", "sWG", "sWU", "sWD", "sMB", "gBI"}

        @block.sync
        def _(q):
            P.emit("q", q, sems, GROUP_SEMS)

        @block.scalar
        def _(s):
            P.emit("s", s, sems, GROUP_SEMS)

        @block.vector
        def _(v):
            P.emit("v", v, sems, GROUP_SEMS)

        @block.gpsimd
        def _(g):
            P.emit("g", g, sems, GROUP_SEMS)

        @block.tensor
        def _(t):
            P.emit("t", t, sems, GROUP_SEMS)

    return nc


_NC_CACHE = None


def kernel(L0_attn_Z, L0_attn_decay, L0_attn_W, L0_attn_b, L0_attn_gamma,
           L0_attn_beta, L0_ffn_clip_x, L0_ffn_clip_down, L0_ffn_w_gate,
           L0_ffn_w_up, L0_ffn_w_down, L0_c, L0_wmod):
    global _NC_CACHE
    Zf = np.ascontiguousarray(np.asarray(L0_attn_Z, dtype=np.float32))
    ks = np.arange(16, 64)
    pw = (2.0 ** (-ks)).astype(np.float32).reshape(1, 48)
    cvals = (1.5 * 2.0 ** ks).astype(np.float32).reshape(1, 48)
    common = {
        "decay": np.asarray(L0_attn_decay, dtype=np.float32).reshape(1, 1),
        "W": np.ascontiguousarray(np.asarray(L0_attn_W, dtype=np.float32)),
        "bias": np.asarray(L0_attn_b, dtype=np.float32).reshape(1, D),
        "gamma": np.asarray(L0_attn_gamma, dtype=np.float32).reshape(1, D),
        "beta": np.asarray(L0_attn_beta, dtype=np.float32).reshape(1, D),
        "clip_x": np.asarray(L0_ffn_clip_x, dtype=np.float32).reshape(1, 1),
        "clip_d": np.asarray(L0_ffn_clip_down, dtype=np.float32).reshape(1, 1),
        "wg": np.ascontiguousarray(np.asarray(L0_ffn_w_gate, dtype=np.float32)),
        "wu": np.ascontiguousarray(np.asarray(L0_ffn_w_up, dtype=np.float32)),
        "wd": np.ascontiguousarray(np.asarray(L0_ffn_w_down, dtype=np.float32)),
        "wmod": np.ascontiguousarray(np.asarray(L0_wmod, dtype=np.float32)),
        "pw": pw,
        "cvals": cvals,
        "eye": np.eye(128, dtype=np.float32),
    }
    cc = np.asarray(L0_c, dtype=np.float32)
    in_maps = []
    for i in range(NCORES):
        m = dict(common)
        m["Z"] = np.ascontiguousarray(Zf[i * BL:(i + 1) * BL].reshape(R, D))
        m["cc_"] = np.ascontiguousarray(cc[i * BL:(i + 1) * BL])
        in_maps.append(m)
    if _NC_CACHE is None:
        _NC_CACHE = build_nc()
    res = run_bass_kernel_spmd(_NC_CACHE, in_maps, core_ids=list(range(NCORES)))
    out = np.empty((B, T, D), dtype=np.float32)
    for i in range(NCORES):
        out[i * BL:(i + 1) * BL] = res.results[i]["y"].reshape(BL, T, D)
    return out


# version 42
# speedup vs baseline: 1.0177x; 1.0034x over previous
import numpy as np
import concourse.bass as bass
import concourse.mybir as mybir
from concourse.bass_utils import run_bass_kernel_spmd

F32 = mybir.dt.float32
BF16 = mybir.dt.bfloat16
AF = mybir.ActivationFunctionType
ALU = mybir.AluOpType
AX = mybir.AxisListType

NCORES = 8
CC_WORLD = 8
B, T, D, I = 16, 2048, 1024, 2816
BL = B // NCORES
R = BL * T
SEG = 512
NH = T // SEG
TPH = SEG // 128
DC = D // 128
IC = I // 128
GT = 512
NG = R // GT
GPS = NG // BL
MD = 6 * D


class Sched:
    def __init__(self):
        self.ops = []
        self.cnt = {}

    def op(self, eng, fn, waits=(), inc=None, d=1):
        tok = None
        if inc is not None:
            self.cnt[inc] = self.cnt.get(inc, 0) + d
            tok = (inc, self.cnt[inc])
        self.ops.append((eng, fn, [w for w in waits if w is not None], inc, d))
        return tok

    def emit(self, name, eng, sems, group_sems=()):
        last = {}
        for e, fn, waits, inc, d in self.ops:
            if e != name:
                continue
            need = {}
            for sn, val in waits:
                if sn in group_sems:
                    val = self.cnt[sn]
                if val > need.get(sn, -1):
                    need[sn] = val
            for sn, val in need.items():
                if last.get(sn, -1) >= val:
                    continue
                eng.wait_ge(sems[sn], val)
                last[sn] = val
            inst = fn(eng) if fn is not None else None
            if inc is not None:
                inst.then_inc(sems[inc], d)


class Pool:
    def __init__(self, n, seed=None):
        self.n = n
        self.wr = [seed] * n
        self.rd = [[] for _ in range(n)]
        self.i = 0

    def acquire(self):
        s = self.i % self.n
        self.i += 1
        w = list(self.rd[s]) + ([self.wr[s]] if self.wr[s] else [])
        self.rd[s] = []
        return s, w

    def wrote(self, s, tok):
        self.wr[s] = tok

    def read(self, s, tok):
        if tok is not None:
            self.rd[s].append(tok)


def build_nc():
    nc = bass.Bass("TRN2", target_bir_lowering=False, debug=False)
    P = Sched()

    hZ = nc.dram_tensor("Z", [R, D], F32, kind="ExternalInput")
    hdec = nc.dram_tensor("decay", [1, 1], F32, kind="ExternalInput")
    hW = nc.dram_tensor("W", [D, D], F32, kind="ExternalInput")
    hb = nc.dram_tensor("bias", [1, D], F32, kind="ExternalInput")
    hg = nc.dram_tensor("gamma", [1, D], F32, kind="ExternalInput")
    hbe = nc.dram_tensor("beta", [1, D], F32, kind="ExternalInput")
    hcx = nc.dram_tensor("clip_x", [1, 1], F32, kind="ExternalInput")
    hcd = nc.dram_tensor("clip_d", [1, 1], F32, kind="ExternalInput")
    hwg = nc.dram_tensor("wg", [I, D], F32, kind="ExternalInput")
    hwu = nc.dram_tensor("wu", [I, D], F32, kind="ExternalInput")
    hwd = nc.dram_tensor("wd", [D, I], F32, kind="ExternalInput")
    hc = nc.dram_tensor("cc_", [BL, D], F32, kind="ExternalInput")
    hwm = nc.dram_tensor("wmod", [D, MD], F32, kind="ExternalInput")
    hpw = nc.dram_tensor("pw", [1, 48], F32, kind="ExternalInput")
    hcv = nc.dram_tensor("cvals", [1, 48], F32, kind="ExternalInput")
    hey = nc.dram_tensor("eye", [128, 128], F32, kind="ExternalInput")
    hy = nc.dram_tensor("y", [R, D], F32, kind="ExternalOutput")

    pkW = nc.dram_tensor("pkW", [D, D], BF16, kind="Internal")
    pkg = nc.dram_tensor("pkg", [I, D], BF16, kind="Internal")
    pku = nc.dram_tensor("pku", [I, D], BF16, kind="Internal")
    pkd = nc.dram_tensor("pkd", [D, I], BF16, kind="Internal")
    spx2 = nc.dram_tensor("spx2", [R, D], F32, kind="Internal")
    sph2 = nc.dram_tensor("sph2", [R, D], BF16, kind="Internal")
    sphh = nc.dram_tensor("sphh", [NG * IC, 128, GT], BF16, kind="Internal")
    hsf = nc.dram_tensor("hsf", [16, 1024], F32, kind="Internal")
    hsb = nc.dram_tensor("hsb", [96, 128], BF16, kind="Internal")
    hsc = nc.dram_tensor("hsc", [12, 1024], BF16, kind="Internal")
    cc1i = nc.dram_tensor("cc1i", [1, 1], F32, kind="Internal")
    cc1o = nc.dram_tensor("cc1o", [1, 1], F32, kind="Internal", addr_space="Shared" if CC_WORLD > 4 else "Local")
    cc2i = nc.dram_tensor("cc2i", [1, 1], F32, kind="Internal")
    cc2o = nc.dram_tensor("cc2o", [1, 1], F32, kind="Internal", addr_space="Shared" if CC_WORLD > 4 else "Local")
    pkuT = nc.dram_tensor("pkuT", [128, 8192], BF16, kind="Internal")
    cc1p = nc.dram_tensor("cc1p", [1, 1], F32, kind="Internal", addr_space="Shared" if CC_WORLD > 4 else "Local")
    cc2p = nc.dram_tensor("cc2p", [1, 1], F32, kind="Internal", addr_space="Shared" if CC_WORLD > 4 else "Local")

    Z, Wm, bia, gam, bet = hZ.ap(), hW.ap(), hb.ap(), hg.ap(), hbe.ap()
    dec, cx, cd = hdec.ap(), hcx.ap(), hcd.ap()
    wg_, wu_, wd_, c_, wm_ = hwg.ap(), hwu.ap(), hwd.ap(), hc.ap(), hwm.ap()
    pw_, cv_, y_, ey_ = hpw.ap(), hcv.ap(), hy.ap(), hey.ap()

    from contextlib import ExitStack
    with ExitStack() as stack:
        en = stack.enter_context
        wa = en(nc.sbuf_tensor("wa", [128, 45056], BF16))
        fmb = en(nc.sbuf_tensor("fmb", [128, 8192], BF16))
        sfm = en(nc.sbuf_tensor("sfm", [128, 8192], BF16))
        x0w = en(nc.sbuf_tensor("x0w", [128, 4 * 1024], F32))
        scr = en(nc.sbuf_tensor("scr", [128, 3072], F32))
        h1sb = en(nc.sbuf_tensor("h1sb", [128, 4 * 1024], BF16))
        xhsb = en(nc.sbuf_tensor("xhsb", [128, 4 * 1024], BF16))
        ynb2 = en(nc.sbuf_tensor("ynb2", [128, 2 * 1024], BF16))
        repb = en(nc.sbuf_tensor("repb", [128, 10 * 1024], BF16))
        sep = en(nc.sbuf_tensor("sep", [128, 5248], BF16))
        smf = en(nc.sbuf_tensor("smf", [128, 512], F32))
        hgb = en(nc.sbuf_tensor("hgb", [128, 2048], BF16))
        hho = en(nc.sbuf_tensor("hho", [128, 1024], BF16))
        ebuf = en(nc.sbuf_tensor("ebuf", [128, 128], BF16))
        p0 = en(nc.psum_tensor("p0", [128, 512], F32))
        p1 = en(nc.psum_tensor("p1", [128, 512], F32))
        p2 = en(nc.psum_tensor("p2", [128, 512], F32))
        p3 = en(nc.psum_tensor("p3", [128, 512], F32))
        p4 = en(nc.psum_tensor("p4", [128, 512], F32))
        p5 = en(nc.psum_tensor("p5", [128, 512], F32))
        p6 = en(nc.psum_tensor("p6", [128, 512], F32))
        p7 = en(nc.psum_tensor("p7", [128, 512], F32))
        SEM_NAMES = ["pe", "ve", "se", "ge", "cc",
                     "sZ0", "sZ1", "sE0", "sE1", "sT0", "sT1",
                     "sS0", "sB0", "sS2", "sS3", "sWT", "sWG", "sWU", "sWD",
                     "sCT", "sMB", "sMO", "sRP",
                     "sH0", "sH1", "sL0", "sL1", "sX0", "sX1", "sQ0", "sQ1", "sTA0", "sTA1",
                     "aX20", "aX21", "aH20", "aH21", "aHH0", "aHH1", "aP0", "aP1", "aO0", "aO1", "sBE", "sGR",
                     "gWP", "gWM0", "gWM1", "gWM2", "gWM3", "gWF0", "gWF1", "gCB", "gRB", "gBI", "sEY", "sGR2", "aPT0", "aPT1"]
        sems = {n: en(nc.semaphore(n)) for n in SEM_NAMES}
        block = en(nc.Block())
        psum = [p0, p1, p2, p3, p4, p5, p6, p7]
        wt = wa[:, 36864:45056]

        C_CB, C_HI, C_LO, C_A, C_SX, C_SD = 0, 1, 2, 3, 4, 5
        C_IS1, C_S2B, C_IS2 = 6, 7, 8
        C_AM, C_AN, C_T0, C_T1, C_EPS = 9, 10, 11, 12, 13
        O_CA1, O_CB1, O_SB = 16, 32, 48
        O_SCG, O_SCU, O_SCD = 64, 86, 108
        O_SGX, O_SUX = 128, 150
        O_TAIL = 176
        O_MX = 184
        O_ST = 192
        O_EPS, O_DLT, O_CV = 336, 384, 432
        O_SCL = 480
        O_CT = 488

        def smv(c):
            return smf[:, c:c + 1]

        S_S1, S_CT, S_MODT, S_CTL = 0, 3072, 3088, 5232
        SWB = [2048, 3184]
        S_CSIL = 3184
        ga_tm = x0w[0:2, 1024:2048]
        sgn = x0w[0:2, 3072:4096]
        repr_ = x0w[0:2, 0:1024]
        gb_tm = sep[0:2, 4208:5232]
        c_sil = sep[0:2, S_CSIL:S_CSIL + 1024]
        modT = sep[:, S_MODT:S_MODT + 96]
        gam2 = scr[0:2, 2048:3072]

        teps = P.op("v", lambda v: v.memset(smf[:, 13:14], 1e-5), inc="ve")
        t_pw = P.op("q", lambda q: q.dma_start(out=smf[0:1, O_EPS:O_EPS + 48], in_=pw_[:, :]), inc="sS0", d=16)
        t_cv = P.op("q", lambda q: q.dma_start(out=smf[0:1, O_CV:O_CV + 48], in_=cv_[:, :]), inc="sS0", d=16)
        t_dc = P.op("q", lambda q: q.dma_start(out=smf[0:1, O_SCL:O_SCL + 1], in_=dec[:, :]), inc="sS0", d=16)
        t_cx = P.op("q", lambda q: q.dma_start(out=smf[0:1, O_SCL + 1:O_SCL + 2], in_=cx[:, :]), inc="sS0", d=16)
        t_cd = P.op("q", lambda q: q.dma_start(out=smf[0:1, O_SCL + 2:O_SCL + 3], in_=cd[:, :]), inc="sS0", d=16)
        t_g2 = P.op("q", lambda q: q.dma_start(out=gam2, in_=bass.AP(hg, 0, [[0, 2], [1, 1024]])), inc="sS0", d=16)
        t_cl = P.op("q", lambda q: q.dma_start(out=x0w[0:2, 0:1024], in_=c_[:, :]), inc="sS0", d=16)

        tg1 = P.op("v", lambda v: v.tensor_scalar(out=smf[0:1, O_EPS:O_EPS + 48], in0=smf[0:1, O_EPS:O_EPS + 48], scalar1=1.0, scalar2=1.0, op0=ALU.add, op1=ALU.is_gt), waits=[t_pw], inc="ve")
        tg2 = P.op("v", lambda v: v.scalar_tensor_tensor(out=smf[0:1, O_DLT:O_DLT + 47], in0=smf[0:1, O_EPS + 1:O_EPS + 48], scalar=-1.0, in1=smf[0:1, O_EPS:O_EPS + 47], op0=ALU.mult, op1=ALU.add), waits=[tg1], inc="ve")
        tg3 = P.op("v", lambda v: v.tensor_tensor(out=smf[0:1, O_DLT:O_DLT + 47], in0=smf[0:1, O_DLT:O_DLT + 47], in1=smf[0:1, O_CV:O_CV + 47], op=ALU.mult), waits=[tg2, t_cv], inc="ve")
        tC = P.op("v", lambda v: v.tensor_reduce(out=smf[0:1, O_CT:O_CT + 1], in_=smf[0:1, O_DLT:O_DLT + 47], axis=AX.X, op=ALU.add), waits=[tg3], inc="ve")
        tsg = P.op("s", lambda s: s.activation(out=smf[0:1, O_SCL + 4:O_SCL + 7], in_=smf[0:1, O_SCL:O_SCL + 3], func=AF.Sigmoid), waits=[t_dc, t_cx, t_cd], inc="se")
        tb1 = P.op("q", lambda q: q.dma_start(out=hsf.ap()[0:1, 0:1], in_=smf[0:1, O_CT:O_CT + 1]), waits=[tC], inc="sB0", d=16)
        tb2 = P.op("q", lambda q: q.dma_start(out=hsf.ap()[0:1, 1:4], in_=smf[0:1, O_SCL + 4:O_SCL + 7]), waits=[tsg], inc="sB0", d=16)
        tCb = P.op("q", lambda q: q.dma_start(out=smv(C_CB), in_=bass.AP(hsf, 0, [[0, 128], [1, 1]])), waits=[tb1], inc="sS2", d=16)
        tac = P.op("q", lambda q: q.dma_start(out=smv(C_A), in_=bass.AP(hsf, 1, [[0, 128], [1, 1]])), waits=[tb2], inc="sS2", d=16)
        tsx = P.op("q", lambda q: q.dma_start(out=smv(C_SX), in_=bass.AP(hsf, 2, [[0, 128], [1, 1]])), waits=[tb2], inc="sS2", d=16)
        tsd = P.op("q", lambda q: q.dma_start(out=smv(C_SD), in_=bass.AP(hsf, 3, [[0, 128], [1, 1]])), waits=[tb2], inc="sS2", d=16)
        thi = P.op("v", lambda v: v.tensor_scalar(out=smv(C_HI), in0=smv(C_CB), scalar1=127.0, scalar2=None, op0=ALU.add), waits=[tCb], inc="ve")
        tNCB = P.op("v", lambda v: v.tensor_scalar(out=smf[:, 14:15], in0=smv(C_CB), scalar1=-1.0, scalar2=None, op0=ALU.mult), waits=[tCb], inc="ve")
        tlo = P.op("v", lambda v: v.tensor_scalar(out=smv(C_LO), in0=smv(C_CB), scalar1=-128.0, scalar2=None, op0=ALU.add), waits=[tCb], inc="ve")

        tones = P.op("v", lambda v: v.memset(scr[:, 0:2048], 1.0), inc="ve")
        tone2 = P.op("v", lambda v: v.memset(sep[0:2, S_S1:S_S1 + 2048], 1.0), inc="ve")
        ts1w = P.op("v", lambda v: v.tensor_tensor_scan(out=fmb[:, 0:2048], data0=bass.AP(smf, C_A, [[512, 128], [0, 2048]]), data1=scr[:, 0:2048], initial=0.0, op0=ALU.mult, op1=ALU.add), waits=[tones, tac], inc="ve")
        ts1 = P.op("v", lambda v: v.tensor_copy(out=sep[0:1, S_S1:S_S1 + 2048], in_=fmb[0:1, 0:2048]), waits=[ts1w, tone2], inc="ve")

        tWp = P.op("g", lambda g: g.dma_start(out=pkW.ap()[:, :], in_=Wm[:, :]), inc="gWP", d=16)
        tWt = []
        for dcc in range(DC):
            tWt.append(P.op("q", lambda q, dcc=dcc: q.dma_start_transpose(wt[:, dcc * 1024:(dcc + 1) * 1024], pkW.ap()[:, dcc * 128:(dcc + 1) * 128]), waits=[tWp], inc="sWT", d=16))

        c32 = x0w[0:2, 1024:2048]
        tcs = P.op("s", lambda s: s.activation(out=c32, in_=x0w[0:2, 0:1024], func=AF.Silu), waits=[t_cl], inc="se")
        tch = P.op("v", lambda v: v.tensor_copy(out=c_sil, in_=c32), waits=[tcs], inc="ve")
        c_lo = x0w[0:2, 0:1024]
        tclo = P.op("v", lambda v: v.tensor_tensor(out=c_lo, in0=c32, in1=c_sil, op=ALU.subtract), waits=[tch], inc="ve")
        tcb = P.op("g", lambda g: g.dma_start(out=hsc.ap()[0:2, :], in_=c_sil), waits=[tch], inc="gCB", d=16)
        tcb2 = P.op("g", lambda g: g.dma_start(out=hsc.ap()[8:10, :], in_=c_lo), waits=[tclo], inc="gRB", d=16)
        tct = P.op("q", lambda q: q.dma_start_transpose(sep[:, S_CT:S_CT + 16], bass.AP(hsc, 0, [[128, 16], [1, 128]])), waits=[tcb], inc="sCT", d=16)
        tct2 = P.op("q", lambda q: q.dma_start_transpose(sep[:, S_CTL:S_CTL + 16], bass.AP(hsc, 8 * 1024, [[128, 16], [1, 128]])), waits=[tcb2], inc="sCT", d=16)

        mod_tm = wa[0:2, 26624:26624 + MD]
        tmodev = [None, None]
        wm_slot_tok = [None, None, None, None]
        wm32war = [None, None]
        GAGB_N = {4, 5, 10, 11}
        for n in range(12):
            pb = psum[n % 2]
            mmtok = None
            for dcc in range(DC):
                slot = (n * DC + dcc) % 4
                hi = wa[:, 8192 + 1024 * slot:8192 + 1024 * slot + 512]
                lo = wa[:, 8192 + 1024 * slot + 512:8192 + 1024 * (slot + 1)]
                wld = [wm_slot_tok[slot]] if wm_slot_tok[slot] else []
                if n in GAGB_N:
                    f32s = slot % 2
                    wm32 = x0w[:, 2048 + 512 * f32s:2048 + 512 * (f32s + 1)]
                    w32 = [wm32war[f32s]] if wm32war[f32s] else []
                    tld = P.op("g", lambda g, n=n, dcc=dcc, wm32=wm32: g.dma_start(out=wm32, in_=wm_[dcc * 128:(dcc + 1) * 128, n * 512:(n + 1) * 512]), waits=w32, inc="gWF0" if f32s == 0 else "gWF1", d=16)
                    thh_ = P.op("v", lambda v, wm32=wm32, hi=hi: v.tensor_copy(out=hi, in_=wm32), waits=[tld] + wld, inc="ve")
                    tll = P.op("v", lambda v, wm32=wm32, hi=hi, lo=lo: v.tensor_tensor(out=lo, in0=wm32, in1=hi, op=ALU.subtract), waits=[thh_], inc="ve")
                    wm32war[f32s] = tll
                    def mm1(t, dcc=dcc, hi=hi, pb=pb):
                        return t.matmul(pb[0:2, 0:512], bass.AP(sep, S_CT + dcc, [[5248, 128], [8, 2]]), hi, start=(dcc == 0), stop=False)
                    def mm2(t, dcc=dcc, hi=hi, pb=pb):
                        return t.matmul(pb[0:2, 0:512], bass.AP(sep, S_CTL + dcc, [[5248, 128], [8, 2]]), hi, start=False, stop=False)
                    def mm3(t, dcc=dcc, lo=lo, pb=pb):
                        return t.matmul(pb[0:2, 0:512], bass.AP(sep, S_CT + dcc, [[5248, 128], [8, 2]]), lo, start=False, stop=(dcc == DC - 1))
                    w = [thh_, tct, tct2]
                    if dcc == 0 and tmodev[n % 2] is not None:
                        w.append(tmodev[n % 2])
                    P.op("t", mm1, waits=w, inc="pe")
                    P.op("t", mm2, waits=[thh_, tct2], inc="pe")
                    mmtok = P.op("t", mm3, waits=[tll], inc="pe")
                    wm_slot_tok[slot] = mmtok
                else:
                    tld = P.op("g", lambda g, n=n, dcc=dcc, hi=hi: g.dma_start(out=hi, in_=wm_[dcc * 128:(dcc + 1) * 128, n * 512:(n + 1) * 512]), waits=wld, inc="gWM%d" % slot, d=16)
                    def mmp(t, dcc=dcc, hi=hi, pb=pb):
                        return t.matmul(pb[0:2, 0:512], bass.AP(sep, S_CT + dcc, [[5248, 128], [8, 2]]), hi, start=(dcc == 0), stop=(dcc == DC - 1))
                    w = [tld, tct, tct2]
                    if dcc == 0 and tmodev[n % 2] is not None:
                        w.append(tmodev[n % 2])
                    mmtok = P.op("t", mmp, waits=w, inc="pe")
                    wm_slot_tok[slot] = mmtok
            tmodev[n % 2] = P.op("s", lambda s, n=n, pb=pb: s.activation(out=mod_tm[:, n * 512:(n + 1) * 512], in_=pb[0:2, 0:512], func=AF.Identity), waits=[mmtok], inc="se")
        tmodA, tmodB = tmodev

        tmb = []
        for s_ in range(2):
            tmb.append(P.op("q", lambda q, s_=s_: q.dma_start(out=bass.AP(hsb, s_ * 128, [[256, 48], [1, 128]]), in_=mod_tm[s_:s_ + 1, :]), waits=[tmodA, tmodB], inc="sMB", d=16))
        tmodT = P.op("q", lambda q: q.dma_start_transpose(modT, hsb.ap()[:, :]), waits=tmb, inc="sMO", d=16)
        tca1 = P.op("v", lambda v: v.tensor_scalar(out=smf[:, O_CA1:O_CA1 + 16], in0=modT[:, 16:32], scalar1=1.0, scalar2=None, op0=ALU.add), waits=[tmodT], inc="ve")
        tcb1 = P.op("v", lambda v: v.tensor_scalar(out=smf[:, O_CB1:O_CB1 + 16], in0=modT[:, 64:80], scalar1=1.0, scalar2=None, op0=ALU.add), waits=[tmodT], inc="ve")
        tsbf = P.op("v", lambda v: v.tensor_copy(out=smf[:, O_SB:O_SB + 16], in_=modT[:, 48:64]), waits=[tmodT], inc="ve")

        def eff(src_off, dst, prev):
            t1 = P.op("s", lambda s, o=src_off, dst=dst: s.activation(out=dst, in_=mod_tm[0:2, o:o + 1024], func=AF.Abs), waits=[tmodA, tmodB] + prev, inc="se")
            t2 = P.op("v", lambda v, dst=dst: v.tensor_scalar(out=dst, in0=dst, scalar1=0.25, scalar2=None, op0=ALU.max), waits=[t1], inc="ve")
            t3 = P.op("v", lambda v, o=src_off: v.tensor_scalar(out=sgn, in0=mod_tm[0:2, o:o + 1024], scalar1=0.0, scalar2=None, op0=ALU.is_ge), waits=[tmodA, tmodB] + prev, inc="ve")
            t4 = P.op("v", lambda v: v.tensor_scalar(out=sgn, in0=sgn, scalar1=2.0, scalar2=-1.0, op0=ALU.mult, op1=ALU.add), waits=[t3], inc="ve")
            t5 = P.op("v", lambda v, dst=dst: v.tensor_tensor(out=dst, in0=dst, in1=sgn, op=ALU.mult), waits=[t2, t4], inc="ve")
            return t5
        tga = eff(2048, ga_tm, [])
        tgb = eff(5120, gb_tm, [tga])

        rep_toks = {}

        def make_rep(name, rrow, mk, roff, srcap=None):
            tmk = mk()
            sap = repr_ if srcap is None else srcap
            tb = P.op("g", lambda g, rrow=rrow, sap=sap: g.dma_start(out=hsc.ap()[rrow:rrow + 2, :], in_=sap), waits=[tmk], inc="gRB", d=16)
            toks = []
            for s_ in range(2):
                dst = repb[:, (roff + s_) * 1024:(roff + s_ + 1) * 1024]
                toks.append(P.op("q", lambda q, s_=s_, rrow=rrow, dst=dst: q.dma_start(out=dst, in_=bass.AP(hsc, (rrow + s_) * 1024, [[0, 128], [1, 1024]])), waits=[tb], inc="sRP", d=16))
            rep_toks[name] = toks
            return tmk

        tgg = make_rep("gga", 0, lambda: P.op("v", lambda v: v.tensor_tensor(out=repr_, in0=gam2, in1=ga_tm, op=ALU.mult), waits=[tga, t_g2, tgb, tcs, tcb, tcb2], inc="ve"), 0)
        t_b2 = P.op("q", lambda q: q.dma_start(out=gam2, in_=bass.AP(hbe, 0, [[0, 2], [1, 1024]])), waits=[tgg] + rep_toks["gga"], inc="sBE", d=16)
        tbgm = make_rep("bga", 2, lambda: P.op("v", lambda v: v.tensor_tensor(out=repr_, in0=gam2, in1=ga_tm, op=ALU.mult), waits=[t_b2, tga, tgg] + rep_toks["gga"], inc="ve"), 2)
        tcb1r = make_rep("cb1", 6, lambda: P.op("v", lambda v: v.tensor_scalar(out=repr_, in0=mod_tm[0:2, 4096:5120], scalar1=1.0, scalar2=None, op0=ALU.add), waits=[tmodA, tmodB, tbgm] + rep_toks["bga"], inc="ve"), 6)
        tsbr = make_rep("sbr", 10, lambda: P.op("v", lambda v: v.tensor_copy(out=repr_, in_=mod_tm[0:2, 3072:4096]), waits=[tmodA, tmodB, tcb1r] + rep_toks["cb1"], inc="ve"), 8)


        swb_tok = []
        for s_ in range(2):
            for eh in range(2):
                pb = psum[4 + eh]
                toks = []
                for dcc in range(DC):
                    def mmsw(t, s_=s_, dcc=dcc, eh=eh, pb=pb):
                        return t.matmul(pb[0:1, 0:512], modT[:, 2 * dcc + s_:2 * dcc + s_ + 1], wt[:, dcc * 1024 + eh * 512:dcc * 1024 + eh * 512 + 512], start=(dcc == 0), stop=(dcc == DC - 1))
                    w = [tmodT, tWt[dcc]]
                    if dcc == 0 and s_ == 1:
                        w.append(swb_tok[eh])
                    toks.append(P.op("t", mmsw, waits=w, inc="pe"))
                swb_tok.append(P.op("s", lambda s, s_=s_, eh=eh, pb=pb: s.activation(out=sep[0:1, SWB[s_] + eh * 512:SWB[s_] + eh * 512 + 512], in_=pb[0:1, 0:512], func=AF.Identity), waits=[toks[-1]], inc="se"))
        tbias = []
        for s_ in range(2):
            tbias.append(P.op("g", lambda g, s_=s_: g.dma_start(out=sep[1:2, SWB[s_]:SWB[s_] + 1024], in_=bia[:, :]), waits=[tcs] if s_ == 0 else [], inc="gBI", d=16))

        wq_raw = Pool(2)
        nat_pool = Pool(2)
        tq_prev = [None]
        park_done = {"wg": [], "wu": [], "wd": []}
        wgT_tok = [[] for _ in range(DC)]
        wuT_tok = [[] for _ in range(DC)]
        wTR_war = [[None] * 4, [None] * 4]
        wlast_bank = [None, None]
        stg_pool = Pool(2)
        stg_last = [None, None]

        def prep_gu_tile(name, src, park, scoff, ic):
                slot, w = wq_raw.acquire()
                raw = scr[:, slot * 1024:(slot + 1) * 1024]
                tl = P.op("g", lambda g, ic=ic, raw=raw, src=src: g.dma_start(out=raw, in_=src[ic * 128:(ic + 1) * 128, :]), waits=w + ([tones, ts1] if ic < 2 else []), inc="sQ0" if slot == 0 else "sQ1", d=16)
                wq_raw.wrote(slot, tl)
                tmx = P.op("v", lambda v, raw=raw: v.tensor_reduce(out=smv(C_T0), in_=raw, axis=AX.X, op=ALU.max, apply_absolute_value=True), waits=[tl], inc="ve")
                tsc = P.op("v", lambda v, ic=ic: v.tensor_scalar(out=smf[:, scoff + ic:scoff + ic + 1], in0=smv(C_T0), scalar1=1.0 / 127.0, scalar2=None, op0=ALU.mult), waits=[tmx], inc="ve")
                trc = P.op("v", lambda v, ic=ic: v.reciprocal(out=smv(C_T1), in_=smf[:, scoff + ic:scoff + ic + 1]), waits=[tsc] + ([tq_prev[0]] if tq_prev[0] else []), inc="ve")
                tq = P.op("s", lambda s, raw=raw: s.activation(out=raw, in_=raw, func=AF.Identity, scale=smv(C_T1), bias=smv(C_CB)), waits=[trc, tCb, tmx], inc="se")
                tq_prev[0] = tq
                nslot, wn = nat_pool.acquire()
                nat = hgb[:, nslot * 1024:(nslot + 1) * 1024]
                tqq = P.op("s", lambda s, raw=raw, nat=nat: s.activation(out=nat, in_=raw, func=AF.Identity, bias=smf[:, 14:15]), waits=[tq, tNCB] + wn, inc="se")
                nat_pool.wrote(nslot, tqq)
                wq_raw.read(slot, tqq)
                if name in ("wu", "wg"):
                    tpk = P.op("s", lambda s, ic=ic, nat=nat, park=park: s.dma_start(out=park.ap()[ic * 128:(ic + 1) * 128, :], in_=nat), waits=[tqq], inc="aP0" if nslot == 0 else "aP1", d=16)
                    nat_pool.read(nslot, tpk)
                    park_done[name].append(tpk)
                    return
                bank = psum[2 + nslot]
                rw = ([tmodA, tmodB] + [t for t in wm_slot_tok if t]) if name == "wg" else list(mod_tm_readers)
                for dcc in range(DC):
                    creg = dcc % 4
                    wtr = P.op("t", lambda t, bank=bank, creg=creg, nat=nat, dcc=dcc: t.matmul(bank[:, creg * 128:(creg + 1) * 128], nat[:, dcc * 128:(dcc + 1) * 128], ebuf[:, 0:128], start=True, stop=True), waits=[tqq, teye] + ([wTR_war[nslot][creg]] if wTR_war[nslot][creg] else []), inc="pe")
                    nat_pool.read(nslot, wtr)
                    dst = (dcc * 2816 + ic * 128) if name == "wg" else (22528 + dcc * 2816 + ic * 128)
                    if dst + 128 <= 36864:
                        tev = P.op("s", lambda s, bank=bank, creg=creg, dst=dst: s.activation(out=wa[:, dst:dst + 128], in_=bank[:, creg * 128:(creg + 1) * 128], func=AF.Identity), waits=[wtr] + rw, inc="se")
                        (wgT_tok if name == "wg" else wuT_tok)[dcc].append(tev)
                    else:
                        ssl, sw = stg_pool.acquire()
                        hst = hho[:, ssl * 128:(ssl + 1) * 128]
                        tev = P.op("s", lambda s, bank=bank, creg=creg, hst=hst: s.activation(out=hst, in_=bank[:, creg * 128:(creg + 1) * 128], func=AF.Identity), waits=[wtr] + sw, inc="se")
                        tpkT = P.op("s", lambda s, hst=hst, dst=dst: s.dma_start(out=pkuT.ap()[:, dst - 36864:dst - 36864 + 128], in_=hst), waits=[tev], inc="aPT0" if ssl == 0 else "aPT1", d=16)
                        stg_pool.wrote(ssl, tpkT)
                        stg_last[ssl] = tpkT
                    wTR_war[nslot][creg] = tev
                    wlast_bank[nslot] = tev
        PREP_JOBS = []
        for _ic in range(IC):
            PREP_JOBS.append(lambda ic=_ic: prep_gu_tile("wg", wg_, pkg, O_SCG, ic))
        for _ic in range(IC):
            PREP_JOBS.append(lambda ic=_ic: prep_gu_tile("wu", wu_, pku, O_SCU, ic))
        for _hcn in range(DC):
            PREP_JOBS.append(lambda hcn=_hcn: prep_wd_tile(hcn))

        def prep_wd_tile(hcn):
            mxp = []
            for j, (o, wdt) in enumerate([(0, 1024), (1024, 1024), (2048, 768)]):
                slot, w = wq_raw.acquire()
                raw = scr[:, slot * 1024:slot * 1024 + wdt]
                tl = P.op("g", lambda g, hcn=hcn, o=o, wdt=wdt, raw=raw: g.dma_start(out=raw, in_=wd_[hcn * 128:(hcn + 1) * 128, o:o + wdt]), waits=w, inc="sQ0" if slot == 0 else "sQ1", d=16)
                wq_raw.wrote(slot, tl)
                tmx = P.op("v", lambda v, raw=raw, j=j: v.tensor_reduce(out=smf[:, O_MX + j:O_MX + j + 1], in_=raw, axis=AX.X, op=ALU.max, apply_absolute_value=True), waits=[tl] + ([mxp[0]] if (j and False) else []), inc="ve")
                wq_raw.read(slot, tmx)
                mxp.append(tmx)
            tm2 = P.op("v", lambda v: v.tensor_tensor(out=smv(O_MX), in0=smf[:, O_MX:O_MX + 1], in1=smf[:, O_MX + 1:O_MX + 2], op=ALU.max), waits=mxp, inc="ve")
            tm3 = P.op("v", lambda v: v.tensor_tensor(out=smv(C_T0), in0=smv(O_MX), in1=smf[:, O_MX + 2:O_MX + 3], op=ALU.max), waits=[tm2], inc="ve")
            tsc = P.op("v", lambda v, hcn=hcn: v.tensor_scalar(out=smf[:, O_SCD + hcn:O_SCD + hcn + 1], in0=smv(C_T0), scalar1=1.0 / 127.0, scalar2=None, op0=ALU.mult), waits=[tm3], inc="ve")
            trc = P.op("v", lambda v: v.reciprocal(out=smv(C_T1), in_=smf[:, O_SCD + hcn:O_SCD + hcn + 1] if False else smv(C_T0)), waits=[tsc], inc="ve")
            trc = P.op("v", lambda v, hcn=hcn: v.reciprocal(out=smv(C_T1), in_=smf[:, O_SCD + hcn:O_SCD + hcn + 1]), waits=[tsc, trc] + ([tq_prev[0]] if tq_prev[0] else []), inc="ve")
            for j, (o, wdt) in enumerate([(0, 1024), (1024, 1024), (2048, 768)]):
                slot, w = wq_raw.acquire()
                raw = scr[:, slot * 1024:slot * 1024 + wdt]
                tl = P.op("g", lambda g, hcn=hcn, o=o, wdt=wdt, raw=raw: g.dma_start(out=raw, in_=wd_[hcn * 128:(hcn + 1) * 128, o:o + wdt]), waits=w + [trc], inc="sQ0" if slot == 0 else "sQ1", d=16)
                wq_raw.wrote(slot, tl)
                tq = P.op("s", lambda s, raw=raw: s.activation(out=raw, in_=raw, func=AF.Identity, scale=smv(C_T1), bias=smv(C_CB)), waits=[tl], inc="se")
                tq_prev[0] = tq
                nslot, wn = nat_pool.acquire()
                nat = hgb[:, nslot * 1024:nslot * 1024 + wdt]
                tqq = P.op("s", lambda s, raw=raw, nat=nat: s.activation(out=nat, in_=raw, func=AF.Identity, bias=smf[:, 14:15]), waits=[tq, tNCB] + wn, inc="se")
                nat_pool.wrote(nslot, tqq)
                wq_raw.read(slot, tqq)
                tpk = P.op("s", lambda s, hcn=hcn, o=o, wdt=wdt, nat=nat: s.dma_start(out=pkd.ap()[hcn * 128:(hcn + 1) * 128, o:o + wdt], in_=nat), waits=[tqq], inc="aP0" if nslot == 0 else "aP1", d=16)
                nat_pool.read(nslot, tpk)
                park_done["wd"].append(tpk)

        mod_tm_readers = [tga, tgb, tmodA, tmodB] + tmb
        teye = P.op("g", lambda g: g.dma_start(out=ebuf[:, 0:128], in_=ey_), inc="sEY", d=16)
        prev_evac = [[] for _ in range(DC)]
        prev_evac[0] = [swb_tok[0], swb_tok[2]]
        prev_evac[1] = [swb_tok[0], swb_tok[2]]
        prev_evac[2] = [swb_tok[1], swb_tok[3]]
        prev_evac[3] = [swb_tok[1], swb_tok[3]]
        tr_tok = [[None] * DC, [None] * DC]
        seedA = rep_toks["sbr"][1]
        zpool = Pool(2, seed=seedA)
        epool = Pool(2, seed=seedA)
        ynpool = Pool(2)
        h1pool = Pool(3)
        stpool = Pool(6)
        fm_hist = [[ts1], [ts1]]
        sfm_hist = [[], []]
        h2sp_done = []
        x2sp_done = {}
        h1war = []
        xhwar = []
        xhwar_prev = []
        ypar_reader = [tmodA, tmodB]
        scan_tail = [None] * DC
        amax_last = None
        tam0 = P.op("v", lambda v: v.memset(smv(C_AM), -3e38), inc="ve")
        tan0 = P.op("v", lambda v: v.memset(smv(C_AN), 3e38), waits=[tam0], inc="ve")
        amax_last = tan0

        def ln_stats(src, w, k, wb=None):
            o = O_ST + 24 * k
            t1 = P.op("v", lambda v, src=src, o=o: v.bn_stats(out=smf[:, o:o + 6], in_=src[:, 0:512]), waits=w, inc="ve")
            t2 = P.op("v", lambda v, src=src, o=o: v.bn_stats(out=smf[:, o + 6:o + 12], in_=src[:, 512:1024]), waits=(w if wb is None else wb), inc="ve")
            t3 = P.op("v", lambda v, o=o: v.bn_aggr(out=smf[:, o + 12:o + 14], in_=smf[:, o:o + 12]), waits=[t1, t2], inc="ve")
            t4 = P.op("s", lambda s, o=o: s.activation(out=smf[:, o + 14:o + 15], in_=smf[:, o + 13:o + 14], func=AF.Sqrt, bias=smv(C_EPS)), waits=[t3, teps], inc="se")
            t5 = P.op("v", lambda v, o=o: v.reciprocal(out=smf[:, o + 15:o + 16], in_=smf[:, o + 14:o + 15]), waits=[t4], inc="ve")
            t6 = P.op("v", lambda v, o=o: v.tensor_scalar(out=smf[:, o + 16:o + 17], in0=smf[:, o + 12:o + 13], scalar1=smf[:, o + 15:o + 16], scalar2=-1.0, op0=ALU.mult, op1=ALU.mult), waits=[t5], inc="ve")
            return smf[:, o + 15:o + 16], smf[:, o + 16:o + 17], t6

        for s_ in range(BL):
            for h in range(NH):
                par = (s_ * NH + h) % 2
                fo = par * 4096
                fmbwar = fm_hist[par]
                fm_hist[par] = []
                evac_half = [[None] * DC, [None] * DC]
                tx2_list = []
                for tt in range(TPH):
                    gt = s_ * (NH * TPH) + h * TPH + tt
                    zslot, w = zpool.acquire()
                    zt = x0w[:, zslot * 1024:(zslot + 1) * 1024]
                    extra = [tcs] if (zslot == 0 and gt < 2) else []
                    tz = P.op("q", lambda q, gt=gt, zt=zt: q.dma_start(out=zt, in_=Z[gt * 128:(gt + 1) * 128, :]), waits=w + extra, inc="sZ0" if zslot == 0 else "sZ1", d=16)
                    zpool.wrote(zslot, tz)
                    kst, wst = stpool.acquire()
                    r_ap, mr_ap, tstat = ln_stats(zt, [tz] + wst, kst)
                    zpool.read(zslot, tstat)
                    h1 = h1sb[:, tt * 1024:(tt + 1) * 1024]
                    th1 = P.op("s", lambda s, zt=zt, h1=h1, r_ap=r_ap, mr_ap=mr_ap: s.activation(out=h1, in_=zt, func=AF.Identity, scale=r_ap, bias=mr_ap), waits=[tstat] + h1war, inc="se")
                    zpool.read(zslot, th1)
                    stpool.wrote(kst, th1)
                    for dcc in range(DC):
                        pb = psum[4 + dcc // 2]
                        col = (dcc % 2) * 256 + (tt % 2) * 128
                        ttr = P.op("t", lambda t, pb=pb, col=col, h1=h1, dcc=dcc: t.matmul(pb[:, col:col + 128], h1[:, dcc * 128:(dcc + 1) * 128], ebuf[:, 0:128], start=True, stop=True), waits=[th1, teye] + prev_evac[dcc], inc="pe")
                        tr_tok[tt % 2][dcc] = ttr
                    if tt % 2 == 1:
                        p_ = tt // 2
                        for dcc in range(DC):
                            pb = psum[4 + dcc // 2]
                            tev = P.op("s", lambda s, pb=pb, dcc=dcc, fo=fo, p_=p_, s_=s_: s.activation(out=fmb[:, fo + dcc * SEG + p_ * 256:fo + dcc * SEG + p_ * 256 + 256], in_=pb[:, (dcc % 2) * 256:(dcc % 2) * 256 + 256], func=AF.Identity, scale=smf[:, O_CA1 + 2 * dcc + s_:O_CA1 + 2 * dcc + s_ + 1]), waits=[tr_tok[0][dcc], tr_tok[1][dcc], tca1] + fmbwar, inc="se")
                            prev_evac[dcc] = [tev]
                            evac_half[p_][dcc] = tev
                scans = []
                sfw = sfm_hist[par]
                sfm_hist[par] = []
                for dcc in range(DC):
                    init = 0.0 if h == 0 else smf[:, O_TAIL + dcc:O_TAIL + dcc + 1]
                    winit = [] if h == 0 else [scan_tail[dcc]]
                    tscan = P.op("v", lambda v, dcc=dcc, fo=fo, init=init: v.tensor_tensor_scan(out=sfm[:, fo + dcc * SEG:fo + (dcc + 1) * SEG], data0=bass.AP(smf, C_A, [[512, 128], [0, SEG]]), data1=fmb[:, fo + dcc * SEG:fo + (dcc + 1) * SEG], initial=init, op0=ALU.mult, op1=ALU.add), waits=[evac_half[0][dcc], evac_half[1][dcc], tac] + sfw + winit, inc="ve")
                    ttl = P.op("v", lambda v, dcc=dcc, fo=fo: v.tensor_copy(out=smf[:, O_TAIL + dcc:O_TAIL + dcc + 1], in_=sfm[:, fo + (dcc + 1) * SEG - 1:fo + (dcc + 1) * SEG]), waits=[tscan] + ([scan_tail[dcc]] if h > 0 else []), inc="ve")
                    scan_tail[dcc] = ttl
                    fm_hist[par].append(tscan)
                    sfm_hist[par].append(ttl)
                    scans.append(tscan)
                pending_post = None
                for tt in range(TPH):
                    gt = s_ * (NH * TPH) + h * TPH + tt
                    mm_l = []
                    for eh in range(2):
                        pb = psum[eh]
                        toks = []
                        for dcc in range(DC):
                            def mmy(t, dcc=dcc, eh=eh, tt=tt, pb=pb, fo=fo):
                                return t.matmul(pb[:, 0:512], sfm[:, fo + dcc * SEG + tt * 128:fo + dcc * SEG + tt * 128 + 128], wt[:, dcc * 1024 + eh * 512:dcc * 1024 + eh * 512 + 512], start=(dcc == 0), stop=False)
                            w = [scans[dcc], tWt[dcc]]
                            if dcc == 0 and ypar_reader[eh] is not None:
                                w.append(ypar_reader[eh])
                            toks.append(P.op("t", mmy, waits=w, inc="pe"))
                        def mmk2(t, eh=eh, pb=pb, s_=s_, gt=gt):
                            trel = (gt % (NH * TPH)) * 128
                            return t.matmul(pb[:, 0:512], sep[0:2, S_S1 + trel:S_S1 + trel + 128], sep[0:2, SWB[s_] + eh * 512:SWB[s_] + eh * 512 + 512], start=False, stop=True)
                        tk2 = P.op("t", mmk2, waits=[ts1, tone2, swb_tok[2 * s_ + eh], tbias[s_], toks[-1]], inc="pe")
                        last_y_mm = [tk2]
                        mm_l.append(tk2)
                        sfm_hist[par].append(tk2)
                    kst, wst = stpool.acquire()
                    oY = O_ST + 24 * kst
                    pbl, pbh = psum[0], psum[1]
                    ty1 = P.op("v", lambda v, pbl=pbl, oY=oY: v.bn_stats(out=smf[:, oY:oY + 6], in_=pbl[:, 0:512]), waits=[mm_l[0]] + wst, inc="ve")
                    ty2 = P.op("v", lambda v, pbh=pbh, oY=oY: v.bn_stats(out=smf[:, oY + 6:oY + 12], in_=pbh[:, 0:512]), waits=[mm_l[1]], inc="ve")
                    ty3 = P.op("v", lambda v, oY=oY: v.bn_aggr(out=smf[:, oY + 12:oY + 14], in_=smf[:, oY:oY + 12]), waits=[ty1, ty2], inc="ve")
                    ty4 = P.op("s", lambda s, oY=oY: s.activation(out=smf[:, oY + 14:oY + 15], in_=smf[:, oY + 13:oY + 14], func=AF.Sqrt, bias=smv(C_EPS)), waits=[ty3, teps], inc="se")
                    ty5 = P.op("v", lambda v, oY=oY: v.reciprocal(out=smf[:, oY + 15:oY + 16], in_=smf[:, oY + 14:oY + 15]), waits=[ty4], inc="ve")
                    ty6 = P.op("v", lambda v, oY=oY: v.tensor_scalar(out=smf[:, oY + 16:oY + 17], in0=smf[:, oY + 12:oY + 13], scalar1=smf[:, oY + 15:oY + 16], scalar2=-1.0, op0=ALU.mult, op1=ALU.mult), waits=[ty5], inc="ve")
                    ysl, wyn = ynpool.acquire()
                    yt = ynb2[:, ysl * 1024:(ysl + 1) * 1024]
                    tyn = []
                    for eh, pb in ((0, pbl), (1, pbh)):
                        tok = P.op("s", lambda s, pb=pb, eh=eh, yt=yt, oY=oY: s.activation(out=yt[:, eh * 512:eh * 512 + 512], in_=pb[:, 0:512], func=AF.Identity, scale=smf[:, oY + 15:oY + 16], bias=smf[:, oY + 16:oY + 17]), waits=[ty6] + wyn, inc="se")
                        tyn.append(tok)
                        ypar_reader[eh] = tok
                    stpool.wrote(kst, ty6)
                    tt2 = P.op("v", lambda v, yt=yt, s_=s_: v.tensor_tensor(out=yt, in0=yt, in1=repb[:, s_ * 1024:(s_ + 1) * 1024], op=ALU.mult), waits=tyn + rep_toks["gga"], inc="ve")
                    tu2 = P.op("v", lambda v, yt=yt, s_=s_: v.tensor_tensor(out=yt, in0=yt, in1=repb[:, 2048 + s_ * 1024:2048 + (s_ + 1) * 1024], op=ALU.add), waits=[tt2] + rep_toks["bga"], inc="ve")
                    eslot, we = epool.acquire()
                    x2t = x0w[:, (2 + eslot) * 1024:(3 + eslot) * 1024]
                    tz2 = P.op("q", lambda q, gt=gt, x2t=x2t: q.dma_start(out=x2t, in_=Z[gt * 128:(gt + 1) * 128, :]), waits=we, inc="sE0" if eslot == 0 else "sE1", d=16)
                    tx2a = P.op("g", lambda g, x2t=x2t, yt=yt: g.tensor_tensor(out=x2t[:, 0:512], in0=x2t[:, 0:512], in1=yt[:, 0:512], op=ALU.add), waits=[tz2, tu2], inc="ge")
                    tx2b = P.op("g", lambda g, x2t=x2t, yt=yt: g.tensor_tensor(out=x2t[:, 512:1024], in0=x2t[:, 512:1024], in1=yt[:, 512:1024], op=ALU.add), inc="ge")
                    epool.wrote(eslot, tx2b)
                    epool.read(eslot, tx2a)
                    ynpool.wrote(ysl, tx2b)
                    ynpool.read(ysl, tx2a)
                    tx2_list.append(tx2b)
                    tsp = P.op("s", lambda s, gt=gt, x2t=x2t: s.dma_start(out=spx2.ap()[gt * 128:(gt + 1) * 128, :], in_=x2t), waits=[tx2a, tx2b], inc="aX20" if eslot == 0 else "aX21", d=16)
                    x2sp_done[gt] = tsp
                    epool.read(eslot, tsp)

                    def mk_post(gt=gt, tt=tt, x2t=x2t, eslot=eslot, tx2a=tx2a, tx2b=tx2b, s_=s_, par=par):
                        def post():
                            kst2, wst2 = stpool.acquire()
                            r2, mr2, tst2 = ln_stats(x2t, [tx2a] + wst2, kst2, wb=[tx2b] + wst2)
                            epool.read(eslot, tst2)
                            xh = xhsb[:, tt * 1024:(tt + 1) * 1024]
                            txh = P.op("v", lambda v, x2t=x2t, xh=xh, r2=r2, mr2=mr2: v.tensor_scalar(out=xh, in0=x2t, scalar1=r2, scalar2=mr2, op0=ALU.mult, op1=ALU.add), waits=[tst2] + xhwar_prev, inc="ve")
                            epool.read(eslot, txh)
                            stpool.wrote(kst2, txh)
                            tt4 = P.op("v", lambda v, xh=xh, s_=s_: v.tensor_tensor(out=xh, in0=xh, in1=repb[:, 6144 + s_ * 1024:6144 + (s_ + 1) * 1024], op=ALU.mult), waits=[txh] + rep_toks["cb1"], inc="ve")
                            th2 = P.op("v", lambda v, xh=xh, s_=s_: v.tensor_tensor(out=xh, in0=xh, in1=repb[:, 8192 + s_ * 1024:8192 + (s_ + 1) * 1024], op=ALU.add), waits=[tt4] + rep_toks["sbr"], inc="ve")
                            tamx = P.op("v", lambda v, xh=xh, gt=gt: v.tensor_reduce(out=smf[:, 384 + gt:385 + gt], in_=xh, axis=AX.X, op=ALU.max, apply_absolute_value=True), waits=[th2], inc="ve")
                            tspl = P.op("s", lambda s, gt=gt, xh=xh: s.dma_start(out=sph2.ap()[gt * 128:(gt + 1) * 128, :], in_=xh), waits=[th2], inc="aH20" if par == 0 else "aH21", d=16)
                            h2sp_done.append(tspl)
                            xhwar.append(tamx)
                            xhwar.append(tspl)
                            return tamx
                        return post

                    if pending_post is not None:
                        amax_last = pending_post()
                    pending_post = mk_post()
                    for _ in range(2):
                        if PREP_JOBS:
                            PREP_JOBS.pop(0)()
                if pending_post is not None:
                    amax_last = pending_post()
                    pending_post = None
                h1war = list(scans)
                xhwar_prev = list(xhwar)
                xhwar = []

        while PREP_JOBS:
            PREP_JOBS.pop(0)()
        parks_all = [("aP0", P.cnt.get("aP0", 0)), ("aP1", P.cnt.get("aP1", 0))]
        tscdw = P.op("q", lambda q: q.dma_start(out=bass.AP(hsf, 8 * 1024, [[1, 128], [128, 8]]), in_=smf[:, O_SCD:O_SCD + 8], allow_slow_non_contiguous=True), waits=parks_all, inc="sS3", d=16)
        tscd2 = P.op("q", lambda q: q.dma_start(out=gam2, in_=bass.AP(hsf, 8 * 1024, [[0, 2], [1, 1024]])), waits=[tscdw, tbgm] + rep_toks["bga"], inc="sS3", d=16)
        make_rep("scdgb", 4, lambda: P.op("v", lambda v: v.tensor_tensor(out=gam2, in0=gam2, in1=gb_tm, op=ALU.mult), waits=[tscd2, tgb, tbgm] + rep_toks["sbr"], inc="ve"), 4, srcap=gam2)

        for dcc in range(DC):
            wgT_tok[dcc].append(P.op("q", lambda q, dcc=dcc: q.dma_start_transpose(wa[:, dcc * 2816:(dcc + 1) * 2816], pkg.ap()[:, dcc * 128:(dcc + 1) * 128]), waits=park_done["wg"] + [tmodA, tmodB] + [t for t in wm_slot_tok if t], inc="sWG", d=16))
        for dcc in range(DC):
            wuT_tok[dcc].append(P.op("q", lambda q, dcc=dcc: q.dma_start_transpose(wa[:, 22528 + dcc * 2816:22528 + (dcc + 1) * 2816], pku.ap()[:, dcc * 128:(dcc + 1) * 128]), waits=park_done["wu"] + mod_tm_readers + last_y_mm, inc="sWU", d=16))

        def global_reduce(prev_tok, cci, cco, ccp, hrow, scol, sigcol, iscol, extra_q=(), extra_t8=()):
            t1 = P.op("v", lambda v: v.tensor_scalar(out=smv(C_T0), in0=smv(C_AN), scalar1=-1.0, scalar2=None, op0=ALU.mult), waits=[prev_tok], inc="ve")
            t2 = P.op("v", lambda v: v.tensor_tensor(out=smv(C_T0), in0=smv(C_T0), in1=smv(C_AM), op=ALU.max), waits=[t1], inc="ve")
            t3 = P.op("q", lambda q, hrow=hrow: q.dma_start(out=hsf.ap()[hrow:hrow + 1, 0:128], in_=smv(C_T0)), waits=[t2] + list(extra_q), inc="sGR", d=16)
            t4 = P.op("q", lambda q, hrow=hrow: q.dma_start(out=scr[0:1, 0:128], in_=hsf.ap()[hrow:hrow + 1, 0:128]), waits=[t3, tones, ts1], inc="sGR", d=16)
            t5 = P.op("v", lambda v: v.tensor_reduce(out=scr[0:1, 128:129], in_=scr[0:1, 0:128], axis=AX.X, op=ALU.max), waits=[t4], inc="ve")
            t6 = P.op("q", lambda q, cci=cci: q.dma_start(out=cci.ap()[:, :], in_=scr[0:1, 128:129]), waits=[t5], inc="sGR", d=16)
            t7 = P.op("g", lambda g, cci=cci, cco=cco: g.collective_compute("AllReduce", ALU.max, ins=[cci.ap()[:, :]], outs=[cco.ap()[:, :]], replica_groups=[list(range(CC_WORLD))]), waits=[t6], inc="cc")
            t7b = P.op("g", lambda g, cci=cci, ccp=ccp: g.collective_compute("AllReduce", ALU.max, ins=[cci.ap()[:, :]], outs=[ccp.ap()[:, :]], replica_groups=[list(range(CC_WORLD))]), waits=[t7], inc="cc")
            t8 = P.op("g", lambda g, cco=cco: g.dma_start(out=smv(scol), in_=bass.AP(cco, 0, [[0, 128], [1, 1]])), waits=[t7b] + list(extra_t8), inc="sGR2", d=16)
            t9 = P.op("v", lambda v: v.tensor_scalar(out=smv(scol), in0=smv(scol), scalar1=smv(sigcol), scalar2=1.0 / 127.0, op0=ALU.mult, op1=ALU.mult), waits=[t8, tsx, tsd], inc="ve")
            t9b = P.op("v", lambda v: v.tensor_scalar(out=smv(scol), in0=smv(scol), scalar1=1e-35, scalar2=None, op0=ALU.max), waits=[t9], inc="ve")
            t10 = P.op("v", lambda v: v.reciprocal(out=smv(iscol), in_=smv(scol)), waits=[t9b], inc="ve")
            return t9b, t10
        tcmb = P.op("v", lambda v: v.tensor_reduce(out=smv(C_AM), in_=smf[:, 384:416], axis=AX.X, op=ALU.max), waits=[amax_last], inc="ve")
        h2_all = [("aH20", P.cnt.get("aH20", 0)), ("aH21", P.cnt.get("aH21", 0))]
        x2_all = [("aX20", P.cnt.get("aX20", 0)), ("aX21", P.cnt.get("aX21", 0))]
        fmb_hist = [list(fm_hist[0]), list(fm_hist[1])]
        xq_hist = [list(sfm_hist[0]), list(sfm_hist[1])]
        last_mm_xq = [None, None]

        def emit_tl(g):
            par = g % 2
            base = par * 4096
            w_h2 = fmb_hist[par]
            fmb_hist[par] = []
            toks = []
            for dcc in range(DC):
                h2in = fmb[:, base + dcc * 512:base + (dcc + 1) * 512]
                tl = P.op("q", lambda q, g=g, dcc=dcc, h2in=h2in: q.dma_start_transpose(h2in, sph2.ap()[g * GT:(g + 1) * GT, dcc * 128:(dcc + 1) * 128]), waits=h2_all + w_h2, inc="sH0" if par == 0 else "sH1", d=16)
                toks.append(tl)
            return toks
        tl_pref = {0: emit_tl(0), 1: emit_tl(1)}
        ts1x, tis1 = global_reduce(tcmb, cc1i, cc1o, cc1p, 9, C_T1, C_SX, C_IS1, extra_t8=[tl_pref[1][-1]])
        tsgx = P.op("v", lambda v: v.tensor_scalar(out=smf[:, O_SGX:O_SGX + 22], in0=smf[:, O_SCG:O_SCG + 22], scalar1=smv(C_T1), scalar2=None, op0=ALU.mult), waits=[ts1x], inc="ve")
        tsux = P.op("v", lambda v: v.tensor_scalar(out=smf[:, O_SUX:O_SUX + 22], in0=smf[:, O_SCU:O_SCU + 22], scalar1=smv(C_T1), scalar2=None, op0=ALU.mult), waits=[ts1x], inc="ve")

        tam2 = P.op("v", lambda v: v.memset(smv(C_AM), -3e38), waits=[ts1x], inc="ve")
        tan2 = P.op("v", lambda v: v.memset(smv(C_AN), 3e38), waits=[tam2], inc="ve")
        amax_last = tan2
        hh_sp = []
        gu_reader = [None, None, wlast_bank[0], wlast_bank[1]]
        hg_war = [None, None]
        hu_war = [None, None]
        hh_war = [stg_last[0], stg_last[1]]
        tq1_last = [None, None]
        for g in range(NG):
            s_ = g // GPS
            par = g % 2
            base = par * 4096
            w_xq = xq_hist[par]
            xq_hist[par] = []
            xq_tok = []
            tl_toks = tl_pref.pop(g) if g in tl_pref else emit_tl(g)
            for dcc in range(DC):
                h2in = fmb[:, base + dcc * 512:base + (dcc + 1) * 512]
                tq1 = P.op("s", lambda s, h2in=h2in, par=par: s.activation(out=scr[:, par * 512:par * 512 + 512], in_=h2in, func=AF.Identity, scale=smv(C_IS1), bias=smv(C_CB)), waits=[tl_toks[-1], tis1] + ([tq1_last[par]] if tq1_last[par] else []), inc="se")
                tq2 = P.op("v", lambda v, par=par: v.tensor_scalar(out=scr[:, par * 512:par * 512 + 512], in0=scr[:, par * 512:par * 512 + 512], scalar1=smv(C_HI), scalar2=smv(C_LO), op0=ALU.min, op1=ALU.max), waits=[tq1, thi, tlo], inc="ve")
                xq = sfm[:, base + dcc * 512:base + (dcc + 1) * 512]
                tq3 = P.op("v", lambda v, xq=xq, par=par: v.tensor_scalar(out=xq, in0=scr[:, par * 512:par * 512 + 512], scalar1=smv(C_CB), scalar2=None, op0=ALU.subtract), waits=[tq2] + w_xq, inc="ve")
                w_xq = []
                tq1_last[par] = tq3
                xq_tok.append(tq3)
                fmb_hist[par].append(tq1)
            for ic in range(IC):
                icp = ic % 2
                pg = psum[icp]
                pu = psum[2 + icp]
                gtoks = []
                for dcc in range(DC):
                    def mmg(t, ic=ic, dcc=dcc, pg=pg, base=base):
                        return t.matmul(pg[:, 0:512], wa[:, dcc * 2816 + ic * 128:dcc * 2816 + ic * 128 + 128], sfm[:, base + dcc * 512:base + (dcc + 1) * 512], start=(dcc == 0), stop=(dcc == DC - 1))
                    w = [xq_tok[dcc]] + wgT_tok[dcc]
                    if dcc == 0 and gu_reader[icp] is not None:
                        w.append(gu_reader[icp])
                    gtoks.append(P.op("t", mmg, waits=w, inc="pe"))
                utoks = []
                for dcc in range(DC):
                    def mmu(t, ic=ic, dcc=dcc, pu=pu, base=base):
                        return t.matmul(pu[:, 0:512], wa[:, 22528 + dcc * 2816 + ic * 128:22528 + dcc * 2816 + ic * 128 + 128], sfm[:, base + dcc * 512:base + (dcc + 1) * 512], start=(dcc == 0), stop=(dcc == DC - 1))
                    w = [xq_tok[dcc]] + wuT_tok[dcc]
                    if dcc == 0 and gu_reader[2 + icp] is not None:
                        w.append(gu_reader[2 + icp])
                    utoks.append(P.op("t", mmu, waits=w, inc="pe"))
                hgp = hgb[:, icp * 512:icp * 512 + 512]
                hup = hgb[:, 1024 + icp * 512:1024 + icp * 512 + 512]
                thg = P.op("s", lambda s, pg=pg, hgp=hgp, ic=ic: s.activation(out=hgp, in_=pg[:, 0:512], func=AF.Silu, scale=smf[:, O_SGX + ic:O_SGX + ic + 1]), waits=[gtoks[-1], tsgx] + ([hg_war[icp]] if hg_war[icp] else []), inc="se")
                gu_reader[icp] = thg
                thu = P.op("s", lambda s, pu=pu, hup=hup, ic=ic: s.activation(out=hup, in_=pu[:, 0:512], func=AF.Identity, scale=smf[:, O_SUX + ic:O_SUX + ic + 1]), waits=[utoks[-1], tsux] + ([hu_war[icp]] if hu_war[icp] else []), inc="se")
                gu_reader[2 + icp] = thu
                hh = hho[:, icp * 512:icp * 512 + 512]
                thh = P.op("v", lambda v, hgp=hgp, hup=hup, hh=hh: v.tensor_tensor(out=hh, in0=hgp, in1=hup, op=ALU.mult), waits=[thg, thu] + ([hh_war[icp]] if hh_war[icp] else []), inc="ve")
                hg_war[icp] = thh
                tm = P.op("v", lambda v, hh=hh, hup=hup: v.tensor_scalar(out=hup, in0=hh, scalar1=1.0, scalar2=smv(C_AM), op0=ALU.mult, op1=ALU.max, accum_out=smv(C_AM)), waits=[thh, amax_last], inc="ve")
                tn = P.op("v", lambda v, hh=hh, hup=hup: v.tensor_scalar(out=hup, in0=hh, scalar1=1.0, scalar2=smv(C_AN), op0=ALU.min, op1=ALU.min, accum_out=smv(C_AN)), waits=[tm], inc="ve")
                amax_last = tn
                hu_war[icp] = tn
                tsp = P.op("s", lambda s, g=g, ic=ic, hh=hh: s.dma_start(out=sphh.ap()[g * IC + ic, :, :], in_=hh), waits=[thh], inc="aHH0" if icp == 0 else "aHH1", d=16)
                hh_sp.append(tsp)
                hh_war[icp] = tsp
            last_mm_xq[par] = gtoks[-1]
            xq_hist[par] = [utoks[-1], gtoks[-1]]

        wdT_tok = []
        for icc in range(IC):
            wdT_tok.append(P.op("q", lambda q, icc=icc: q.dma_start_transpose(wa[:, icc * 1024:(icc + 1) * 1024], pkd.ap()[:, icc * 128:(icc + 1) * 128]), waits=parks_all + [amax_last, gu_reader[0], gu_reader[1]], inc="sWD", d=16))
        ts2x, tis2 = global_reduce(amax_last, cc2i, cc2o, cc2p, 10, C_S2B, C_SD, C_IS2, extra_t8=[wdT_tok[0]])

        dn_reader = [None] * 4
        t3_war = [None, None]
        dq_war_prev = {}
        hq_mm_last = [[], []]
        x2pool = Pool(2, seed=tis2)
        opool = Pool(2, seed=tis2)
        out_toks = []
        hh_all = [("aHH0", P.cnt.get("aHH0", 0)), ("aHH1", P.cnt.get("aHH1", 0))]
        for g in range(NG):
            s_ = g // GPS
            par = g % 2
            hqb = 22528 + par * 11264
            wclear = hq_mm_last[par]
            hq_mm_last[par] = []
            def mk_tlh(q, g=g, hqb=hqb):
                src = bass.AP(sphh, g * IC * 128 * GT, [[GT, 128], [128 * GT, IC], [1, GT]])
                return q.dma_start(out=wa[:, hqb:hqb + 11264].rearrange("p (i t) -> p i t", t=GT), in_=src)
            tlh = P.op("q", mk_tlh, waits=hh_all + [tis2] + wclear + (([t for l in wuT_tok for t in l]) if g < 2 else []), inc="sL0" if par == 0 else "sL1", d=16)
            hq_tok = []
            dq_war = dq_war_prev
            for ic in range(IC):
                sl = (ic % 2) * 512
                tq1 = P.op("s", lambda s, hqb=hqb, ic=ic, sl=sl: s.activation(out=scr[:, sl:sl + 512], in_=wa[:, hqb + ic * 512:hqb + (ic + 1) * 512], func=AF.Identity, scale=smv(C_IS2), bias=smv(C_CB)), waits=[tlh, tis2] + ([dq_war[sl]] if sl in dq_war else []), inc="se")
                tq2 = P.op("v", lambda v, sl=sl: v.tensor_scalar(out=scr[:, sl:sl + 512], in0=scr[:, sl:sl + 512], scalar1=smv(C_HI), scalar2=smv(C_LO), op0=ALU.min, op1=ALU.max), waits=[tq1], inc="ve")
                tq3 = P.op("v", lambda v, hqb=hqb, ic=ic, sl=sl: v.tensor_scalar(out=wa[:, hqb + ic * 512:hqb + (ic + 1) * 512], in0=scr[:, sl:sl + 512], scalar1=smv(C_CB), scalar2=None, op0=ALU.subtract), waits=[tq2], inc="ve")
                dq_war[sl] = tq3
                dq_war_prev = dq_war
                hq_tok.append(tq3)
            for tp in range(2):
                tts = (2 * tp, 2 * tp + 1)
                gtoks = {(tt, eh): [] for tt in tts for eh in range(2)}
                for ic in range(IC):
                    for tt in tts:
                        dp = tt % 2
                        for eh in range(2):
                            pb = psum[4 + 2 * dp + eh]
                            def mmd(t, ic=ic, eh=eh, tt=tt, pb=pb, hqb=hqb):
                                return t.matmul(pb[:, 0:512], wa[:, hqb + ic * 512 + tt * 128:hqb + ic * 512 + tt * 128 + 128], wa[:, ic * 1024 + eh * 512:ic * 1024 + eh * 512 + 512], start=(ic == 0), stop=(ic == IC - 1))
                            w = [hq_tok[ic], wdT_tok[ic]]
                            if ic == 0 and dn_reader[2 * dp + eh] is not None:
                                w.append(dn_reader[2 * dp + eh])
                            gtoks[(tt, eh)].append(P.op("t", mmd, waits=w, inc="pe"))
                for tt in tts:
                    gt = g * 4 + tt
                    dp = tt % 2
                    mm_l = [gtoks[(tt, 0)][-1], gtoks[(tt, 1)][-1]]
                    hq_mm_last[par].append(mm_l[0])
                    hq_mm_last[par].append(mm_l[1])
                    x2s, wx2 = x2pool.acquire()
                    x2t = x0w[:, x2s * 1024:(x2s + 1) * 1024]
                    tlx = P.op("q", lambda q, gt=gt, x2t=x2t: q.dma_start(out=x2t, in_=spx2.ap()[gt * 128:(gt + 1) * 128, :]), waits=x2_all + [tis2] + wx2, inc="sX0" if x2s == 0 else "sX1", d=16)
                    x2pool.wrote(x2s, tlx)
                    t3t = []
                    for eh in range(2):
                        pb = psum[4 + 2 * dp + eh]
                        tok = P.op("v", lambda v, pb=pb, eh=eh, dp=dp, s_=s_: v.scalar_tensor_tensor(out=scr[:, 1024 + dp * 1024 + eh * 512:1024 + dp * 1024 + eh * 512 + 512], in0=pb[:, 0:512], scalar=smv(C_S2B), in1=repb[:, 4096 + s_ * 1024 + eh * 512:4096 + s_ * 1024 + eh * 512 + 512], op0=ALU.mult, op1=ALU.mult), waits=[mm_l[eh], ts2x, tis2] + rep_toks["scdgb"] + ([dn_reader[2 * dp + eh]] if dn_reader[2 * dp + eh] else []) + ([t3_war[dp]] if t3_war[dp] else []), inc="ve")
                        dn_reader[2 * dp + eh] = tok
                        t3t.append(tok)
                    osl, wo = opool.acquire()
                    ot = x0w[:, (2 + osl) * 1024:(3 + osl) * 1024]
                    tout = P.op("v", lambda v, x2t=x2t, ot=ot, dp=dp: v.tensor_tensor(out=ot, in0=x2t, in1=scr[:, 1024 + dp * 1024:1024 + dp * 1024 + 1024], op=ALU.add), waits=t3t + [tlx] + wo, inc="ve")
                    opool.wrote(osl, tout)
                    t3_war[dp] = tout
                    x2pool.read(x2s, tout)
                    tst = P.op("s", lambda s, gt=gt, ot=ot: s.dma_start(out=y_[gt * 128:(gt + 1) * 128, :], in_=ot), waits=[tout], inc="aO0" if osl == 0 else "aO1", d=16)
                    opool.read(osl, tst)
                    out_toks.append(tst)
            hq_mm_last[par] += t3t

        P.op("q", None, waits=out_toks)
        GROUP_SEMS = {"sS0", "sB0", "sS2", "sWT", "sWG", "sWU", "sWD", "sMB", "gBI"}

        @block.sync
        def _(q):
            P.emit("q", q, sems, GROUP_SEMS)

        @block.scalar
        def _(s):
            P.emit("s", s, sems, GROUP_SEMS)

        @block.vector
        def _(v):
            P.emit("v", v, sems, GROUP_SEMS)

        @block.gpsimd
        def _(g):
            P.emit("g", g, sems, GROUP_SEMS)

        @block.tensor
        def _(t):
            P.emit("t", t, sems, GROUP_SEMS)

    return nc


_NC_CACHE = None


def kernel(L0_attn_Z, L0_attn_decay, L0_attn_W, L0_attn_b, L0_attn_gamma,
           L0_attn_beta, L0_ffn_clip_x, L0_ffn_clip_down, L0_ffn_w_gate,
           L0_ffn_w_up, L0_ffn_w_down, L0_c, L0_wmod):
    global _NC_CACHE
    Zf = np.ascontiguousarray(np.asarray(L0_attn_Z, dtype=np.float32))
    ks = np.arange(16, 64)
    pw = (2.0 ** (-ks)).astype(np.float32).reshape(1, 48)
    cvals = (1.5 * 2.0 ** ks).astype(np.float32).reshape(1, 48)
    common = {
        "decay": np.asarray(L0_attn_decay, dtype=np.float32).reshape(1, 1),
        "W": np.ascontiguousarray(np.asarray(L0_attn_W, dtype=np.float32)),
        "bias": np.asarray(L0_attn_b, dtype=np.float32).reshape(1, D),
        "gamma": np.asarray(L0_attn_gamma, dtype=np.float32).reshape(1, D),
        "beta": np.asarray(L0_attn_beta, dtype=np.float32).reshape(1, D),
        "clip_x": np.asarray(L0_ffn_clip_x, dtype=np.float32).reshape(1, 1),
        "clip_d": np.asarray(L0_ffn_clip_down, dtype=np.float32).reshape(1, 1),
        "wg": np.ascontiguousarray(np.asarray(L0_ffn_w_gate, dtype=np.float32)),
        "wu": np.ascontiguousarray(np.asarray(L0_ffn_w_up, dtype=np.float32)),
        "wd": np.ascontiguousarray(np.asarray(L0_ffn_w_down, dtype=np.float32)),
        "wmod": np.ascontiguousarray(np.asarray(L0_wmod, dtype=np.float32)),
        "pw": pw,
        "cvals": cvals,
        "eye": np.eye(128, dtype=np.float32),
    }
    cc = np.asarray(L0_c, dtype=np.float32)
    in_maps = []
    for i in range(NCORES):
        m = dict(common)
        m["Z"] = np.ascontiguousarray(Zf[i * BL:(i + 1) * BL].reshape(R, D))
        m["cc_"] = np.ascontiguousarray(cc[i * BL:(i + 1) * BL])
        in_maps.append(m)
    if _NC_CACHE is None:
        _NC_CACHE = build_nc()
    res = run_bass_kernel_spmd(_NC_CACHE, in_maps, core_ids=list(range(NCORES)))
    out = np.empty((B, T, D), dtype=np.float32)
    for i in range(NCORES):
        out[i * BL:(i + 1) * BL] = res.results[i]["y"].reshape(BL, T, D)
    return out
